# version 01
# baseline (speedup 1.0000x reference)
import numpy as np
import ml_dtypes
import concourse.bass as bass
import concourse.mybir as mybir
from concourse.bass_utils import run_bass_kernel_spmd
from contextlib import ExitStack
from concourse.bass_types import AP as _AP


def _a(x):
    return x if isinstance(x, _AP) or isinstance(x, (int, float)) or x is None else x[:]

F32 = mybir.dt.float32
BF16 = mybir.dt.bfloat16
AF = mybir.ActivationFunctionType
ALU = mybir.AluOpType
AX = mybir.AxisListType
BF = ml_dtypes.bfloat16

T = 4096
D = 1024
NCORES = 8


class Prog:

    ENGINES = ("sync", "scalar", "vector", "tensor", "gpsimd")

    def __init__(self, name):
        self.nc = bass.Bass("TRN2", target_bir_lowering=False)
        self.plan = []
        self.es = ExitStack()
        self._names = {}

    def _uname(self, prefix):
        n = self._names.get(prefix, 0)
        self._names[prefix] = n + 1
        return f"{prefix}{n}"

    def dram_in(self, name, shape, dt=F32):
        return self.nc.dram_tensor(name, list(shape), dt, kind="ExternalInput").ap()

    def dram_out(self, name, shape, dt=F32):
        return self.nc.dram_tensor(name, list(shape), dt, kind="ExternalOutput").ap()

    def sbuf(self, name, shape, dt=F32):
        return self.es.enter_context(self.nc.sbuf_tensor(self._uname(name), list(shape), dt))

    def psum(self, name, shape, dt=F32):
        return self.es.enter_context(self.nc.psum_tensor(self._uname(name), list(shape), dt))

    def op(self, engine, fn, inc=1):
        self.plan.append((engine, fn, inc))

    def dma(self, out, in_, queue="sync"):
        self.op(queue, lambda e, o=out, i=in_: e.dma_start(out=o, in_=i), 16)

    def dma_slow(self, out, in_, queue="sync"):
        self.op(queue, lambda e, o=out, i=in_: e.dma_start(
            out=o, in_=i, allow_slow_non_contiguous=True), 16)

    def tt(self, out, a, b, alu, eng="vector"):
        self.op(eng, lambda e, o=_a(out), x=_a(a), y=_a(b), al=alu: e.tensor_tensor(o, x, y, al))

    def ts(self, out, a, s1, alu, s2=None, alu2=None, eng="vector"):
        if s2 is None:
            self.op(eng, lambda e, o=_a(out), x=_a(a), s=_a(s1), al=alu: e.tensor_scalar(o, x, s, None, al))
        else:
            self.op(eng, lambda e, o=_a(out), x=_a(a), s=_a(s1), t=_a(s2), al=alu, al2=alu2:
                    e.tensor_scalar(o, x, s, t, al, al2))

    def stt(self, out, in0, scalar, in1, op0, op1, eng="vector"):
        self.op(eng, lambda e, o=_a(out), a=_a(in0), s=_a(scalar), b=_a(in1), p0=op0, p1=op1:
                e.scalar_tensor_tensor(o, a, s, b, p0, p1))

    def copy(self, out, in_, eng="vector"):
        self.op(eng, lambda e, o=_a(out), i=_a(in_): e.tensor_copy(o, i))

    def memset(self, ap, v, eng="vector"):
        self.op(eng, lambda e, a=_a(ap), vv=v: e.memset(a, vv))

    def act(self, out, in_, fn, bias=None, scale=None, eng="scalar"):
        kw = {}
        if bias is not None:
            kw["bias"] = bias
        if scale is not None:
            kw["scale"] = scale
        self.op(eng, lambda e, o=_a(out), i=_a(in_), f=fn, k={kk: _a(vv) for kk, vv in kw.items()}: e.activation(o, i, f, **k))

    def red(self, out, in_, alu, eng="vector"):
        self.op(eng, lambda e, o=_a(out), i=_a(in_), a=alu: e.tensor_reduce(o, i, AX.X, a))

    def scan(self, out, d0, d1, init, op0, op1, eng="vector"):
        self.op(eng, lambda e, o=_a(out), a=_a(d0), b=_a(d1), s=_a(init), p=op0, q=op1:
                e.tensor_tensor_scan(o, a, b, s, p, q))

    def mm(self, out, lhsT, rhs, start=True, stop=True):
        self.op("tensor", lambda e, o=_a(out), l=_a(lhsT), r=_a(rhs), s1=start, s2=stop:
                e.matmul(o, l, r, start=s1, stop=s2))

    def transpose(self, out, in_, ident):
        self.op("tensor", lambda e, o=_a(out), i=_a(in_), d=_a(ident): e.transpose(o, i, d))

    def recip(self, out, in_):
        self.op("vector", lambda e, o=_a(out), i=_a(in_): e.reciprocal(o, i))

    def build(self):
        nc = self.nc
        sem_h = self.es.enter_context(nc.semaphore("g"))
        sem2_h = self.es.enter_context(nc.semaphore("g2"))
        totals = []
        c = 0
        for (_, _, inc) in self.plan:
            totals.append(c)
            c += inc
        final = c
        plan = self.plan
        blk = self.es.enter_context(nc.Block())

        use2 = final >= (1 << 31) - 10
        assert not use2, "semaphore overflow"

        def body_for(engname):
            def body(eng):
                last_idx = -10
                for idx, (en, fn, inc) in enumerate(plan):
                    if en != engname:
                        continue
                    eng.wait_ge(sem_h, totals[idx])
                    ins = fn(eng)
                    ins.then_inc(sem_h, inc)
                    last_idx = idx
                if engname == "sync":
                    eng.wait_ge(sem_h, final)
            return body

        blk.sync(body_for("sync"))
        blk.scalar(body_for("scalar"))
        blk.vector(body_for("vector"))
        blk.tensor(body_for("tensor"))
        blk.gpsimd(body_for("gpsimd"))
        self.es.close()
        return nc


def fm_ap(hbm, ft0, nft, t0, tn, tile_t=None):
    r = hbm.rearrange("(f p) t -> p f t", p=128)
    return r[:, ft0:ft0 + nft, t0:t0 + tn]



def ln_tokmajor(P, sx, xn_out, nf, eps_ap, scratch, also_rms=None):
    sq, s1, s2, m, var = scratch
    P.red(s1, sx, ALU.add)
    P.act(sq, sx, AF.Square)
    P.red(s2, sq, ALU.add)
    P.ts(m, s1, 1.0 / D, ALU.mult)
    P.tt(var, m, m, ALU.mult)
    P.stt(var, s2, 1.0 / D, var, ALU.mult, ALU.subtract)
    P.act(var, var, AF.Ln, bias=eps_ap)
    P.act(var, var, AF.Exp, scale=-0.5)
    P.ts(xn_out, sx, m, ALU.subtract, var, ALU.mult)
    if also_rms is not None:
        out2, eps2_ap = also_rms
        P.act(sq, xn_out, AF.Square)
        P.red(s2, sq, ALU.add)
        P.ts(s2, s2, 1.0 / D, ALU.mult)
        P.act(s2, s2, AF.Ln, bias=eps2_ap)
        P.act(s2, s2, AF.Exp, scale=-0.5)
        P.ts(out2, xn_out, s2, ALU.mult)


def softplus_psum(P, psum, bias_col, out, a1, a2):
    P.act(a1, psum, AF.Abs, bias=bias_col)
    P.act(a1, a1, AF.Exp, scale=-1.0)
    P.act(a1, a1, AF.Ln, bias=1.0)
    P.act(a2, psum, AF.Relu, bias=bias_col)
    P.tt(out, a1, a2, ALU.add)



def build_A1():
    P = Prog("A1")
    TC = 128
    NCH = T // TC
    x_h = P.dram_in("x", (T, D))
    ipw_h = P.dram_in("in_proj_w", (D, 4096))
    convw_h = P.dram_in("conv_w", (2048, 4))
    dtw_h = P.dram_in("dt_w", (2048, 2048))
    dtb_h = P.dram_in("dt_b", (2048,))
    bcw_h = P.dram_in("bc_w", (2048, 32))
    rw_h = P.dram_in("router_wT", (D, 16))
    nw_h = P.dram_in("norm_w", (D,))
    ident_h = P.dram_in("ident", (128, 128))

    xn_h = P.dram_out("xn", (D, T), BF16)
    lg_h = P.dram_out("logits0", (T, 16))
    dt_h = P.dram_out("dtS", (2048, T))
    u_h = P.dram_out("uS", (2048, T))
    xc_h = P.dram_out("xcS", (2048, T))
    bcS_h = P.dram_out("bcS", (32, T))

    ipw = P.sbuf("ipw", [128, 8 * 4096], BF16)
    dtw = P.sbuf("dtw", [128, 16 * 2048], BF16)
    bcw = P.sbuf("bcw", [128, 16 * 32], BF16)
    rw = P.sbuf("rw", [128, 8 * 16], F32)
    cw = P.sbuf("cw", [128, 16 * 4], F32)
    dtb = P.sbuf("dtb", [128, 16], F32)
    nwc = P.sbuf("nwc", [128, 8], F32)
    ident = P.sbuf("ident", [128, 128], F32)
    for k in range(8):
        P.dma(ipw[:, k * 4096:(k + 1) * 4096],
              ipw_h.rearrange("(f p) m -> p f m", p=128)[:, k], queue="gpsimd")
    for k in range(16):
        P.dma(dtw[:, k * 2048:(k + 1) * 2048],
              dtw_h.rearrange("(f p) m -> p f m", p=128)[:, k], queue="gpsimd")
    P.dma(bcw[:].rearrange("p (f m) -> p f m", m=32),
          bcw_h.rearrange("(f p) m -> p f m", p=128), queue="gpsimd")
    P.dma(rw[:].rearrange("p (f m) -> p f m", m=16),
          rw_h.rearrange("(f p) m -> p f m", p=128), queue="sync")
    P.dma(cw[:].rearrange("p (f m) -> p f m", m=4),
          convw_h.rearrange("(f p) m -> p f m", p=128), queue="sync")
    P.dma_slow(dtb[:], dtb_h.rearrange("(f p) -> p f", p=128), queue="sync")
    P.dma_slow(nwc[:], nw_h.rearrange("(f p) -> p f", p=128), queue="sync")
    P.dma(ident[:], ident_h[:], queue="sync")

    sx = P.sbuf("sx", [128, D])
    xn_tok = P.sbuf("xntok", [128, D])
    xr_tok = P.sbuf("xrtok", [128, D])
    sq = P.sbuf("sq", [128, D])
    s1 = P.sbuf("s1", [128, 1]); s2 = P.sbuf("s2", [128, 1])
    mcol = P.sbuf("mcol", [128, 1]); vcol = P.sbuf("vcol", [128, 1])
    epsln = P.sbuf("epsln", [128, 1]); epsrms = P.sbuf("epsrms", [128, 1])
    xnf = P.sbuf("xnf", [128, 8 * TC])
    xnb = P.sbuf("xnb", [128, 8 * TC], BF16)
    xrb = P.sbuf("xrb", [128, 8 * TC], BF16)
    lgs = P.sbuf("lgs", [16, TC], F32)
    lgt = P.sbuf("lgt", [128, 16], F32)
    sil = P.sbuf("sil", [128, TC])
    xi = P.sbuf("xi", [128, 16 * (TC + 3)])
    xc = P.sbuf("xc", [128, 16 * TC])
    xcb = P.sbuf("xcb", [128, 16 * TC], BF16)
    dts = P.sbuf("dts", [128, 16 * TC])
    us = P.sbuf("us", [128, 16 * TC])
    a1s = P.sbuf("a1s", [128, TC]); a2s = P.sbuf("a2s", [128, TC])
    bcs = P.sbuf("bcs", [32, TC], F32)

    pT = P.psum("pT", [128, 512])
    pA = P.psum("pA", [128, TC]); pB = P.psum("pB", [128, TC])
    pC = P.psum("pC", [128, TC]); pD = P.psum("pD", [128, TC])
    pL = P.psum("pL", [16, TC])
    pBC = P.psum("pBC", [32, TC])

    P.memset(epsln[:], 1e-5)
    P.memset(epsrms[:], 1e-6)
    P.memset(xi[:], 0.0)

    scratch = (sq, s1, s2, mcol, vcol)
    x_r = x_h.rearrange("(n p) d -> n p d", p=128)

    for c in range(NCH):
        P.dma(sx[:], x_r[c])
        ln_tokmajor(P, sx[:], xn_tok[:], 8, epsln[:, 0:1],
                    scratch, also_rms=(xr_tok[:], epsrms[:, 0:1]))
        for ft in range(0, 8, 4):
            for j in range(4):
                P.transpose(pT[:, j * 128:(j + 1) * 128],
                            xn_tok[:, (ft + j) * 128:(ft + j + 1) * 128], ident[:])
            for j in range(4):
                P.copy(xnf[:, (ft + j) * TC:(ft + j) * TC + 128],
                       pT[:, j * 128:(j + 1) * 128])
            for j in range(4):
                P.transpose(pT[:, j * 128:(j + 1) * 128],
                            xr_tok[:, (ft + j) * 128:(ft + j + 1) * 128], ident[:])
            for j in range(4):
                P.ts(xrb[:, (ft + j) * TC:(ft + j) * TC + 128],
                     pT[:, j * 128:(j + 1) * 128], nwc[:, ft + j:ft + j + 1], ALU.mult)
        P.copy(xnb[:], xnf[:])
        P.dma(fm_ap(xn_h, 0, 8, c * TC, TC),
              xnb[:].rearrange("p (f t) -> p f t", t=TC))
        for k in range(8):
            P.mm(pL[:], rw[:, k * 16:(k + 1) * 16], xnf[:, k * TC:(k + 1) * TC],
                 start=(k == 0), stop=(k == 7))
        P.copy(lgs[:], pL[:])
        P.transpose(pT[:, 0:16], lgs[:], ident[0:16, 0:16])
        P.copy(lgt[:], pT[:, 0:16])
        P.dma(lg_h[c * 128:(c + 1) * 128], lgt[:])
        for m in range(16):
            for k in range(8):
                P.mm(pA[:], ipw[:, k * 4096 + m * 128: k * 4096 + (m + 1) * 128],
                     xrb[:, k * TC:(k + 1) * TC], start=(k == 0), stop=(k == 7))
            for k in range(8):
                P.mm(pB[:], ipw[:, k * 4096 + (m + 16) * 128: k * 4096 + (m + 17) * 128],
                     xrb[:, k * TC:(k + 1) * TC], start=(k == 0), stop=(k == 7))
            P.act(sil[:], pB[:], AF.Silu)
            o = m * (TC + 3)
            P.copy(xi[:, o:o + 3], xi[:, o + TC:o + TC + 3])
            P.tt(xi[:, o + 3:o + 3 + TC], pA[:], sil[:], ALU.mult)
        for m in range(16):
            o = m * (TC + 3)
            oc = m * TC
            P.ts(xc[:, oc:oc + TC], xi[:, o:o + TC], cw[:, m * 4:m * 4 + 1], ALU.mult)
            for i in range(1, 4):
                P.stt(xc[:, oc:oc + TC], xi[:, o + i:o + i + TC],
                      cw[:, m * 4 + i:m * 4 + i + 1], xc[:, oc:oc + TC],
                      ALU.mult, ALU.add)
            P.copy(xcb[:, oc:oc + TC], xc[:, oc:oc + TC])
        for m in range(16):
            pm = pC if (m % 2 == 0) else pD
            for k in range(16):
                P.mm(pm[:], dtw[:, k * 2048 + m * 128: k * 2048 + (m + 1) * 128],
                     xcb[:, k * TC:(k + 1) * TC], start=(k == 0), stop=(k == 15))
            o = m * TC
            softplus_psum(P, pm[:], dtb[:, m:m + 1], dts[:, o:o + TC], a1s[:], a2s[:])
            P.tt(us[:, o:o + TC], dts[:, o:o + TC], xc[:, o:o + TC], ALU.mult)
        for k in range(16):
            P.mm(pBC[:], bcw[:, k * 32:(k + 1) * 32], xcb[:, k * TC:(k + 1) * TC],
                 start=(k == 0), stop=(k == 15))
        P.copy(bcs[:], pBC[:])
        P.dma(bcS_h[:, c * TC:(c + 1) * TC], bcs[:])
        for (hb, sb_) in ((dt_h, dts), (u_h, us), (xc_h, xc)):
            P.dma(fm_ap(hb, 0, 16, c * TC, TC),
                  sb_[:].rearrange("p (f t) -> p f t", t=TC))
    return P



def build_A2():
    P = Prog("A2")
    TS = 512
    NCH = T // TS
    x_h = P.dram_in("x", (T, D))
    dt_h = P.dram_in("dtS", (2048, T))
    u_h = P.dram_in("uS", (2048, T))
    xc_h = P.dram_in("xcS", (2048, T))
    bc_h = P.dram_in("bcS", (32, T))
    sel_h = P.dram_in("sel32", (32, 32 * 128))
    alog_h = P.dram_in("A_log", (2048, 16))
    dskip_h = P.dram_in("D_skip", (2048,))
    opw_h = P.dram_in("out_proj_w", (2048, D))
    ident_h = P.dram_in("ident", (128, 128))
    t1_h = P.dram_out("t1", (D, T))

    opw = P.sbuf("opw", [128, 16 * D], BF16)
    for k in range(16):
        P.dma(opw[:, k * D:(k + 1) * D],
              opw_h.rearrange("(f p) m -> p f m", p=128)[:, k], queue="gpsimd")
    expA = P.sbuf("expA", [128, 16 * 16], F32)
    P.dma(expA[:].rearrange("p (f s) -> p f s", s=16),
          alog_h.rearrange("(f p) s -> p f s", p=128))
    P.act(expA[:], expA[:], AF.Exp)
    dsk = P.sbuf("dsk", [128, 16], F32)
    P.dma_slow(dsk[:], dskip_h.rearrange("(f p) -> p f", p=128))
    ident = P.sbuf("ident", [128, 128], F32)
    P.dma(ident[:], ident_h[:])
    sel = P.sbuf("sel", [32, 32 * 128], F32)
    P.dma(sel[:], sel_h[:])

    hstate = P.sbuf("hstate", [128, 256], F32)
    P.memset(hstate[:], 0.0)

    HT = 8
    dts = P.sbuf("dts", [128, HT * TS])
    us = P.sbuf("us", [128, HT * TS])
    xcs = P.sbuf("xcs", [128, HT * TS])
    yacc = P.sbuf("yacc", [128, HT * TS])
    ys = P.sbuf("ys", [128, 16 * TS], BF16)
    bcs = P.sbuf("bcs", [32, TS], F32)
    bmr = P.sbuf("bmr", [128, TS]); cmr = P.sbuf("cmr", [128, TS])
    tmp = P.sbuf("tmp", [128, TS]); ab = P.sbuf("ab", [128, TS])
    d1 = P.sbuf("d1", [128, TS]); hh = P.sbuf("hh", [128, TS])
    pr = P.sbuf("pr", [128, TS])
    xT = P.sbuf("xT", [128, 8 * TS])
    sx = P.sbuf("sx", [128, D])
    t1s = P.sbuf("t1s", [128, 8 * TS])

    pb1 = P.psum("pb1", [128, TS])
    pb2 = P.psum("pb2", [128, TS])
    pt1 = [P.psum("pt1", [128, TS]) for _ in range(6)]

    x_r = x_h.rearrange("(n p) d -> n p d", p=128)

    for c in range(NCH):
        P.dma(bcs[:], bc_h[:, c * TS:(c + 1) * TS])
        for tt_ in range(4):
            P.dma(sx[:], x_r[c * 4 + tt_])
            for ft in range(0, 8, 4):
                for j in range(4):
                    P.transpose(pb2[:, j * 128:(j + 1) * 128],
                                sx[:, (ft + j) * 128:(ft + j + 1) * 128], ident[:])
                for j in range(4):
                    P.copy(xT[:, (ft + j) * TS + tt_ * 128:(ft + j) * TS + tt_ * 128 + 128],
                           pb2[:, j * 128:(j + 1) * 128])
        for half in range(2):
            for hb, sb_ in ((dt_h, dts), (u_h, us), (xc_h, xcs)):
                P.dma(sb_[:].rearrange("p (f t) -> p f t", t=TS),
                      fm_ap(hb, half * 8, 8, c * TS, TS))
            for m in range(HT):
                P.memset(yacc[:, m * TS:(m + 1) * TS], 0.0)
            for s in range(16):
                P.mm(pb1[:], sel[:, s * 128:(s + 1) * 128], bcs[:])
                P.copy(bmr[:], pb1[:])
                P.mm(pb1[:], sel[:, (16 + s) * 128:(17 + s) * 128], bcs[:])
                P.copy(cmr[:], pb1[:])
                for m in range(HT):
                    dt_t = dts[:, m * TS:(m + 1) * TS]
                    u_t = us[:, m * TS:(m + 1) * TS]
                    ft = half * 8 + m
                    st_col = hstate[:, s * 16 + ft: s * 16 + ft + 1]
                    P.ts(tmp[:], dt_t, expA[:, ft * 16 + s: ft * 16 + s + 1],
                         ALU.mult, eng="gpsimd")
                    P.act(ab[:], tmp[:], AF.Exp, scale=-1.0)
                    P.tt(d1[:], u_t, bmr[:], ALU.mult)
                    P.scan(hh[:], ab[:], d1[:], st_col, ALU.mult, ALU.add)
                    P.copy(st_col, hh[:, TS - 1:TS])
                    P.tt(pr[:], hh[:], cmr[:], ALU.mult)
                    P.tt(yacc[:, m * TS:(m + 1) * TS], yacc[:, m * TS:(m + 1) * TS],
                         pr[:], ALU.add, eng="gpsimd")
            for m in range(HT):
                ft = half * 8 + m
                P.stt(ys[:, ft * TS:(ft + 1) * TS], xcs[:, m * TS:(m + 1) * TS],
                      dsk[:, ft:ft + 1], yacc[:, m * TS:(m + 1) * TS],
                      ALU.mult, ALU.add)
        for grp, mlist in ((0, range(6)), (1, range(6, 8))):
            for mi, m in enumerate(mlist):
                for k in range(16):
                    P.mm(pt1[mi][:], opw[:, k * D + m * 128: k * D + (m + 1) * 128],
                         ys[:, k * TS:(k + 1) * TS], start=(k == 0), stop=(k == 15))
            for mi, m in enumerate(mlist):
                P.tt(t1s[:, m * TS:(m + 1) * TS], pt1[mi][:],
                     xT[:, m * TS:(m + 1) * TS], ALU.add)
        P.dma(fm_ap(t1_h, 0, 8, c * TS, TS),
              t1s[:].rearrange("p (f t) -> p f t", t=TS))
    return P




def emit_first_argmax(P, work, scr):
    eq, cum, zer, first, m = scr
    P.red(m, work, ALU.max)
    P.ts(eq, work, m[:, 0:1], ALU.is_equal)
    P.scan(cum, eq, zer, 0.0, ALU.add, ALU.add)
    P.ts(cum, cum, 1.0, ALU.is_equal)
    P.tt(first, eq, cum, ALU.mult)
    return first, m


def build_B():
    P = Prog("B")
    TCH = 512
    HALF = T // 2
    xn_h = P.dram_in("xn", (D, T), BF16)
    t1_h = P.dram_in("t1", (D, T))
    lg_h = P.dram_in("logits0", (T, 16))
    bias_h = P.dram_in("bias", (16,))
    gw_h = P.dram_in("gate_w", (16, D, D))
    uw_h = P.dram_in("up_w", (16, D, D))
    dw_h = P.dram_in("down_w", (16, D, D))
    shg_h = P.dram_in("sh_gate_w", (D, 2048))
    shu_h = P.dram_in("sh_up_w", (D, 2048))
    shd_h = P.dram_in("sh_down_w", (2048, D))
    sel_h = P.dram_in("sel32", (32, 32 * 128))
    ident_h = P.dram_in("ident", (128, 128))
    x1_h = P.dram_out("x1", (D, T))

    ident = P.sbuf("ident", [128, 128], F32)
    P.dma(ident[:], ident_h[:])
    selb = P.sbuf("selb", [32, 16 * 128], BF16)
    P.dma(selb[:], sel_h[0:32, 0:16 * 128], queue="gpsimd")

    lgt = P.sbuf("lgt", [128, 16], F32)
    sco = P.sbuf("sco", [128, 16], F32)
    sb_ = P.sbuf("sb_", [128, 16], F32)
    gsc = P.sbuf("gsc", [128, 4], F32)
    gmask = P.sbuf("gmask", [128, 4], F32)
    emask = P.sbuf("emask", [128, 16], F32)
    msk = P.sbuf("msk", [128, 16], F32)
    comb = P.sbuf("comb", [128, 16], F32)
    wsum = P.sbuf("wsum", [128, 1], F32)
    twc = P.sbuf("twc", [128, 1], F32)
    eq = P.sbuf("eq", [128, 16], F32)
    cum = P.sbuf("cum", [128, 16], F32)
    zer = P.sbuf("zer", [128, 16], F32)
    first = P.sbuf("first", [128, 16], F32)
    mm_ = P.sbuf("mm_", [128, 1], F32)
    brep = P.sbuf("brep", [128, 16], F32)
    combT = P.sbuf("combT", [16, T], BF16)
    pR = P.psum("pR", [128, 128])

    P.memset(zer[:], 0.0)
    ones1 = P.sbuf("ones1", [1, 128], F32)
    P.memset(ones1[:], 1.0)
    P.dma(brep[0:1, :], bias_h.unsqueeze(0))
    P.mm(pR[:, 0:16], ones1[:], brep[0:1, :])
    P.copy(brep[:], pR[:, 0:16])

    for tt_ in range(T // 128):
        P.dma(lgt[:], lg_h[tt_ * 128:(tt_ + 1) * 128])
        P.act(sco[:], lgt[:], AF.Sigmoid)
        P.tt(sb_[:], sco[:], brep[:], ALU.add)
        scr4 = (eq[:, 0:4], cum[:, 0:4], zer[:, 0:4], first[:, 0:4], mm_)
        for g in range(4):
            sg = sb_[:, g * 4:(g + 1) * 4]
            P.copy(msk[:, 0:4], sg)
            f, m = emit_first_argmax(P, msk[:, 0:4], scr4)
            P.copy(gsc[:, g:g + 1], m)
            P.stt(msk[:, 0:4], f, -1e30, msk[:, 0:4], ALU.mult, ALU.add)
            P.red(mm_, msk[:, 0:4], ALU.max)
            P.tt(gsc[:, g:g + 1], gsc[:, g:g + 1], mm_, ALU.add)
        P.memset(gmask[:], 0.0)
        P.copy(msk[:, 0:4], gsc[:])
        for _ in range(2):
            f, m = emit_first_argmax(P, msk[:, 0:4], scr4)
            P.tt(gmask[:], gmask[:], f, ALU.add)
            P.stt(msk[:, 0:4], f, -1e30, msk[:, 0:4], ALU.mult, ALU.add)
        P.copy(emask[:].rearrange("p (g r) -> p g r", r=4),
               gmask[:].unsqueeze(2).broadcast_to([128, 4, 4]))
        P.ts(msk[:], sb_[:], 1.0, ALU.add)
        P.tt(msk[:], msk[:], emask[:], ALU.mult)
        P.ts(msk[:], msk[:], 1.0, ALU.subtract)
        P.memset(comb[:], 0.0)
        P.memset(wsum[:], 0.0)
        scr16 = (eq[:], cum[:], zer[:], first[:], mm_)
        for _ in range(4):
            f, m = emit_first_argmax(P, msk[:], scr16)
            P.tt(eq[:], f, sco[:], ALU.mult)
            P.red(twc, eq[:], ALU.add)
            P.stt(comb[:], f, twc[:, 0:1], comb[:], ALU.mult, ALU.add)
            P.tt(wsum[:], wsum[:], twc, ALU.add)
            P.stt(msk[:], f, -1e30, msk[:], ALU.mult, ALU.add)
        P.ts(wsum[:], wsum[:], 1e-20, ALU.add)
        P.recip(twc, wsum[:])
        P.ts(twc, twc, 2.5, ALU.mult)
        P.ts(comb[:], comb[:], twc[:, 0:1], ALU.mult)
        P.transpose(pR[0:16, :], comb[:], ident[:])
        P.copy(combT[:, tt_ * 128:(tt_ + 1) * 128], pR[0:16, :])

    NK = 8
    wz = P.sbuf("wz", [128, 49152], BF16)
    GATE, UP, DOWN = 0, 8192, 16384
    SHG, SHU, SHD = 0, 16384, 32768
    xnc = P.sbuf("xnc", [128, 8 * TCH], BF16)
    hub = P.sbuf("hub", [128, 8 * TCH], BF16)
    silb = P.sbuf("silb", [128, TCH], F32)
    crep = P.sbuf("crep", [128, TCH], F32)
    acc = P.sbuf("acc", [128, 8 * HALF], F32)
    t1b = P.sbuf("t1b", [128, 8 * 256], F32)
    pG = P.psum("pG", [128, TCH]); pU = P.psum("pU", [128, TCH])
    pC = P.psum("pC", [128, TCH])
    pD1 = P.psum("pD1", [128, TCH]); pD2 = P.psum("pD2", [128, TCH])

    gw_r = gw_h.rearrange("e (f p) m -> e p f m", p=128)
    uw_r = uw_h.rearrange("e (f p) m -> e p f m", p=128)
    dw_r = dw_h.rearrange("e (f p) m -> e p f m", p=128)

    def expert_mm(e_, h_, sub):
        t0 = h_ * HALF + sub * TCH
        P.dma(xnc[:].rearrange("p (f t) -> p f t", t=TCH), fm_ap(xn_h, 0, 8, t0, TCH))
        for mt in range(8):
            for k in range(NK):
                P.mm(pG[:], wz[:, GATE + k * D + mt * 128: GATE + k * D + (mt + 1) * 128],
                     xnc[:, k * TCH:(k + 1) * TCH], start=(k == 0), stop=(k == NK - 1))
            for k in range(NK):
                P.mm(pU[:], wz[:, UP + k * D + mt * 128: UP + k * D + (mt + 1) * 128],
                     xnc[:, k * TCH:(k + 1) * TCH], start=(k == 0), stop=(k == NK - 1))
            P.act(silb[:], pG[:], AF.Silu)
            P.tt(hub[:, mt * TCH:(mt + 1) * TCH], silb[:], pU[:], ALU.mult)
        P.mm(pC[:], selb[0:16, e_ * 128:(e_ + 1) * 128], combT[:, t0:t0 + TCH])
        P.copy(crep[:], pC[:])
        for mt in range(8):
            P.tt(hub[:, mt * TCH:(mt + 1) * TCH], hub[:, mt * TCH:(mt + 1) * TCH],
                 crep[:], ALU.mult)
        for m in range(8):
            pd = pD1 if m % 2 == 0 else pD2
            for k in range(8):
                P.mm(pd[:], wz[:, DOWN + k * D + m * 128: DOWN + k * D + (m + 1) * 128],
                     hub[:, k * TCH:(k + 1) * TCH], start=(k == 0), stop=(k == 7))
            ao = m * (HALF) + sub * TCH
            P.tt(acc[:, ao:ao + TCH], acc[:, ao:ao + TCH], pd[:], ALU.add)

    def shared_mm(h_, sub):
        t0 = h_ * HALF + sub * TCH
        P.dma(xnc[:].rearrange("p (f t) -> p f t", t=TCH), fm_ap(xn_h, 0, 8, t0, TCH))
        for grp in range(2):
            for mt in range(8):
                mg = grp * 8 + mt
                for k in range(8):
                    P.mm(pG[:], wz[:, SHG + k * 2048 + mg * 128: SHG + k * 2048 + (mg + 1) * 128],
                         xnc[:, k * TCH:(k + 1) * TCH], start=(k == 0), stop=(k == 7))
                for k in range(8):
                    P.mm(pU[:], wz[:, SHU + k * 2048 + mg * 128: SHU + k * 2048 + (mg + 1) * 128],
                         xnc[:, k * TCH:(k + 1) * TCH], start=(k == 0), stop=(k == 7))
                P.act(silb[:], pG[:], AF.Silu)
                P.tt(hub[:, mt * TCH:(mt + 1) * TCH], silb[:], pU[:], ALU.mult)
            for m in range(8):
                pd = pD1 if m % 2 == 0 else pD2
                for k in range(8):
                    kg = grp * 8 + k
                    P.mm(pd[:], wz[:, SHD + kg * D + m * 128: SHD + kg * D + (m + 1) * 128],
                         hub[:, k * TCH:(k + 1) * TCH], start=(k == 0), stop=(k == 7))
                ao = m * HALF + sub * TCH
                if grp == 0:
                    P.copy(acc[:, ao:ao + TCH], pd[:])
                else:
                    P.tt(acc[:, ao:ao + TCH], acc[:, ao:ao + TCH], pd[:], ALU.add)

    for h_ in range(2):
        P.dma(wz[:, SHG:SHG + 16384].rearrange("p (f m) -> p f m", m=2048),
              shg_h.rearrange("(f p) m -> p f m", p=128), queue="gpsimd")
        P.dma(wz[:, SHU:SHU + 16384].rearrange("p (f m) -> p f m", m=2048),
              shu_h.rearrange("(f p) m -> p f m", p=128), queue="gpsimd")
        P.dma(wz[:, SHD:SHD + 16384].rearrange("p (f m) -> p f m", m=1024),
              shd_h.rearrange("(f p) m -> p f m", p=128), queue="gpsimd")
        for sub in range(4):
            shared_mm(h_, sub)
        for e_ in range(16):
            P.dma(wz[:, GATE:GATE + 8192].rearrange("p (f m) -> p f m", m=1024),
                  gw_r[e_], queue="gpsimd")
            P.dma(wz[:, UP:UP + 8192].rearrange("p (f m) -> p f m", m=1024),
                  uw_r[e_], queue="gpsimd")
            P.dma(wz[:, DOWN:DOWN + 8192].rearrange("p (f m) -> p f m", m=1024),
                  dw_r[e_], queue="gpsimd")
            for sub in range(4):
                expert_mm(e_, h_, sub)
        for sub in range(8):
            t0 = h_ * HALF + sub * 256
            P.dma(t1b[:].rearrange("p (f t) -> p f t", t=256), fm_ap(t1_h, 0, 8, t0, 256))
            for m in range(8):
                ao = m * HALF + sub * 256
                P.tt(t1b[:, m * 256:(m + 1) * 256], t1b[:, m * 256:(m + 1) * 256],
                     acc[:, ao:ao + 256], ALU.add)
            P.dma(fm_ap(x1_h, 0, 8, t0, 256),
                  t1b[:].rearrange("p (f t) -> p f t", t=256))
    return P




def fm_norm(P, xt, nf, TCH, sq, rowbuf, pS, ones128, epsap, out_rows):
    for k in range(nf):
        P.act(sq[:, 0:TCH], xt[:, k * TCH:(k + 1) * TCH], AF.Square)
        P.mm(pS[0:1, 0:TCH], ones128[:, 0:1], sq[:, 0:TCH],
             start=(k == 0), stop=(k == nf - 1))
    P.ts(rowbuf[0:1, 0:TCH], pS[0:1, 0:TCH], 1.0 / (nf * 128), ALU.mult)
    P.act(rowbuf[0:1, 0:TCH], rowbuf[0:1, 0:TCH], AF.Ln, bias=epsap)
    P.act(out_rows, rowbuf[0:1, 0:TCH], AF.Exp, scale=-0.5)


def build_C1a():
    P = Prog("C1a")
    TO = 512
    NS = T // TO
    x1_h = P.dram_in("x1", (D, T))
    wq_h = P.dram_in("Wq", (D, 768))
    wk_h = P.dram_in("Wk", (D, 768))
    wv_h = P.dram_in("Wv", (D, 1536))
    wg_h = P.dram_in("Wg", (D, 1536))
    wb_h = P.dram_in("Wb", (D, 8))
    wa_h = P.dram_in("Wa", (D, 8))
    alog_h = P.dram_in("A_log", (8,))
    dtb_h = P.dram_in("dt_bias", (8,))
    cq_h = P.dram_in("conv_q_w", (768, 4))
    ck_h = P.dram_in("conv_k_w", (768, 4))
    cv_h = P.dram_in("conv_v_w", (1536, 4))
    n1_h = P.dram_in("norm1_w", (D,))

    xn1_h = P.dram_out("xn1", (D, T), BF16)
    q_h = P.dram_out("qS", (768, T), BF16)
    k_h = P.dram_out("kS", (768, T), BF16)
    v_h = P.dram_out("vS", (1536, T), BF16)
    g_h = P.dram_out("gS", (1536, T), BF16)
    beta_h = P.dram_out("betaS", (8, T))
    cum_h = P.dram_out("cumS", (8, T))

    wq = P.sbuf("wq", [128, 8 * 768], BF16)
    wk = P.sbuf("wk", [128, 8 * 768], BF16)
    wv = P.sbuf("wv", [128, 8 * 1536], BF16)
    wg = P.sbuf("wg", [128, 8 * 1536], BF16)
    wba = P.sbuf("wba", [128, 8 * 16], BF16)
    for hname, hh, sb_, m in (("q", wq_h, wq, 768), ("k", wk_h, wk, 768),
                              ("v", wv_h, wv, 1536), ("g", wg_h, wg, 1536)):
        P.dma(sb_[:].rearrange("p (f m) -> p f m", m=m),
              hh.rearrange("(f p) m -> p f m", p=128), queue="gpsimd")
    P.dma(wba[:].rearrange("p (f m) -> p f m", m=16)[:, :, 0:8],
          wb_h.rearrange("(f p) m -> p f m", p=128), queue="gpsimd")
    P.dma(wba[:].rearrange("p (f m) -> p f m", m=16)[:, :, 8:16],
          wa_h.rearrange("(f p) m -> p f m", p=128), queue="gpsimd")
    cqw = P.sbuf("cqw", [96, 8 * 4], F32)
    ckw = P.sbuf("ckw", [96, 8 * 4], F32)
    cvw = P.sbuf("cvw", [96, 16 * 4], F32)
    P.dma_slow(cqw[:].rearrange("p (f m) -> p f m", m=4),
               cq_h.rearrange("(f p) m -> p f m", p=96))
    P.dma_slow(ckw[:].rearrange("p (f m) -> p f m", m=4),
               ck_h.rearrange("(f p) m -> p f m", p=96))
    P.dma_slow(cvw[:].rearrange("p (f m) -> p f m", m=4),
               cv_h.rearrange("(f p) m -> p f m", p=96))
    n1c = P.sbuf("n1c", [128, 8], F32)
    P.dma_slow(n1c[:], n1_h.rearrange("(f p) -> p f", p=128))
    nega = P.sbuf("nega", [8, 1], F32)
    P.dma(nega[:], alog_h.unsqueeze(1))
    P.act(nega[:], nega[:], AF.Exp)
    P.ts(nega[:], nega[:], -1.0, ALU.mult)
    dtbc = P.sbuf("dtbc", [8, 1], F32)
    P.dma(dtbc[:], dtb_h.unsqueeze(1))

    ones128 = P.sbuf("ones128", [128, 1], F32)
    P.memset(ones128[:], 1.0)
    ones1 = P.sbuf("ones1", [1, 128], F32)
    P.memset(ones1[:], 1.0)
    epsr = P.sbuf("epsr", [1, 1], F32)
    P.memset(epsr[:], 1e-6)
    eps5 = P.sbuf("eps5", [1, 1], F32)
    P.memset(eps5[:], 1e-5)
    epsl2 = P.sbuf("epsl2", [96, 1], F32)
    P.memset(epsl2[:], 1e-6)

    x1t = P.sbuf("x1t", [128, 8 * TO])
    xn1f = P.sbuf("xn1f", [128, 8 * TO])
    xn1b = P.sbuf("xn1b", [128, 8 * TO], BF16)
    hb = P.sbuf("hb", [128, 8 * TO], BF16)
    sq = P.sbuf("sq", [128, TO])
    rowb = P.sbuf("rowb", [1, TO], F32)
    rrep = P.sbuf("rrep", [128, TO], F32)
    qpad = P.sbuf("qpad", [96, TO + 3], F32)
    qtail = P.sbuf("qtail", [96, 48 * 3], F32)
    cnv = P.sbuf("cnv", [96, TO], F32)
    sq96 = P.sbuf("sq96", [96, TO], F32)
    row96 = P.sbuf("row96", [1, TO], F32)
    rs96 = P.sbuf("rs96", [96, TO], F32)
    qb = P.sbuf("qb", [96, 8 * TO], BF16)
    kb = P.sbuf("kb", [96, 8 * TO], BF16)
    vb = P.sbuf("vb", [96, 16 * TO], BF16)
    gb = P.sbuf("gb", [96, 16 * TO], BF16)
    ba = P.sbuf("ba", [16, TO], F32)
    bet = P.sbuf("bet", [8, TO], F32)
    gdec = P.sbuf("gdec", [8, TO], F32)
    cums = P.sbuf("cums", [8, TO], F32)
    zer8 = P.sbuf("zer8", [8, 128], F32)
    a1t = P.sbuf("a1t", [8, TO], F32)
    a2t = P.sbuf("a2t", [8, TO], F32)

    pS = P.psum("pS", [1, TO])
    pM = P.psum("pM", [128, TO])
    pP = P.psum("pP", [96, TO])
    pB = P.psum("pB", [16, TO])

    P.memset(qtail[:], 0.0)
    P.memset(zer8[:], 0.0)

    def proj_conv(wt, nheads, conv_w, tail_base, out_sb, l2norm, scale, s):
        for hh in range(nheads):
            for k in range(8):
                P.mm(pP[:], wt[:, k * (96 * nheads) + hh * 96:
                               k * (96 * nheads) + (hh + 1) * 96],
                     hb[:, k * TO:(k + 1) * TO], start=(k == 0), stop=(k == 7))
            tb = (tail_base + hh) * 3
            P.copy(qpad[:, 0:3], qtail[:, tb:tb + 3])
            P.copy(qpad[:, 3:3 + TO], pP[:])
            P.copy(qtail[:, tb:tb + 3], qpad[:, TO:TO + 3])
            P.ts(cnv[:], qpad[:, 0:TO], conv_w[:, hh * 4:hh * 4 + 1], ALU.mult)
            for i in range(1, 4):
                P.stt(cnv[:], qpad[:, i:i + TO], conv_w[:, hh * 4 + i:hh * 4 + i + 1],
                      cnv[:], ALU.mult, ALU.add)
            P.act(cnv[:], cnv[:], AF.Silu)
            if l2norm:
                P.act(sq96[:], cnv[:], AF.Square)
                P.mm(pS[0:1, :], ones128[0:96, 0:1], sq96[:], start=True, stop=True)
                P.copy(row96[0:1, :], pS[0:1, :])
                P.ts(row96[0:1, :], row96[0:1, :], epsr[0:1, 0:1], ALU.add)
                P.act(row96[0:1, :], row96[0:1, :], AF.Ln)
                P.act(row96[0:1, :], row96[0:1, :], AF.Exp, scale=-0.5)
                if scale != 1.0:
                    P.ts(row96[0:1, :], row96[0:1, :], scale, ALU.mult)
                P.mm(pP[:], ones1[0:1, 0:96], row96[0:1, :], start=True, stop=True)
                P.copy(rs96[:], pP[:])
                P.tt(out_sb[:, hh * TO:(hh + 1) * TO], cnv[:], rs96[:], ALU.mult)
            else:
                P.copy(out_sb[:, hh * TO:(hh + 1) * TO], cnv[:])

    wq_r = wq
    for s in range(NS):
        t0 = s * TO
        P.dma(x1t[:].rearrange("p (f t) -> p f t", t=TO), fm_ap(x1_h, 0, 8, t0, TO))
        for k in range(8):
            P.mm(pS[0:1, :], ones128[:, 0:1], x1t[:, k * TO:(k + 1) * TO],
                 start=(k == 0), stop=(k == 7))
        P.ts(rowb[0:1, :], pS[0:1, :], 1.0 / D, ALU.mult)
        P.mm(pM[:], ones1[:], rowb[0:1, :], start=True, stop=True)
        P.copy(rrep[:], pM[:])
        for k in range(8):
            P.tt(xn1f[:, k * TO:(k + 1) * TO], x1t[:, k * TO:(k + 1) * TO],
                 rrep[:], ALU.subtract)
        for k in range(8):
            P.act(sq[:], xn1f[:, k * TO:(k + 1) * TO], AF.Square)
            P.mm(pS[0:1, :], ones128[:, 0:1], sq[:], start=(k == 0), stop=(k == 7))
        P.ts(rowb[0:1, :], pS[0:1, :], 1.0 / D, ALU.mult)
        P.act(rowb[0:1, :], rowb[0:1, :], AF.Ln, bias=eps5[0:1, 0:1])
        P.act(rowb[0:1, :], rowb[0:1, :], AF.Exp, scale=-0.5)
        P.mm(pM[:], ones1[:], rowb[0:1, :], start=True, stop=True)
        P.copy(rrep[:], pM[:])
        for k in range(8):
            P.tt(xn1f[:, k * TO:(k + 1) * TO], xn1f[:, k * TO:(k + 1) * TO],
                 rrep[:], ALU.mult)
        P.copy(xn1b[:], xn1f[:])
        P.dma(fm_ap(xn1_h, 0, 8, t0, TO), xn1b[:].rearrange("p (f t) -> p f t", t=TO))
        fm_norm(P, xn1f, 8, TO, sq, rowb, pS, ones128, epsr[0:1, 0:1], rowb[0:1, 0:TO])
        P.mm(pM[:], ones1[:], rowb[0:1, :], start=True, stop=True)
        P.copy(rrep[:], pM[:])
        for k in range(8):
            P.tt(sq[:], xn1f[:, k * TO:(k + 1) * TO], rrep[:], ALU.mult)
            P.ts(hb[:, k * TO:(k + 1) * TO], sq[:], n1c[:, k:k + 1], ALU.mult)
        proj_conv(wq, 8, cqw, 0, qb, True, float(96 ** -0.5), s)
        proj_conv(wk, 8, ckw, 8, kb, True, 1.0, s)
        proj_conv(wv, 16, cvw, 16, vb, False, 1.0, s)
        for hh in range(16):
            for k in range(8):
                P.mm(pP[:], wg[:, k * 1536 + hh * 96: k * 1536 + (hh + 1) * 96],
                     hb[:, k * TO:(k + 1) * TO], start=(k == 0), stop=(k == 7))
            P.copy(gb[:, hh * TO:(hh + 1) * TO], pP[:])
        for k in range(8):
            P.mm(pB[:], wba[:, k * 16:(k + 1) * 16], hb[:, k * TO:(k + 1) * TO],
                 start=(k == 0), stop=(k == 7))
        P.copy(ba[:], pB[:])
        P.act(bet[:], ba[0:8, :], AF.Sigmoid)
        for k in range(8):
            P.mm(pB[0:8, :], wba[:, k * 16 + 8:k * 16 + 16],
                 hb[:, k * TO:(k + 1) * TO], start=(k == 0), stop=(k == 7))
        P.act(a1t[:], pB[0:8, :], AF.Abs, bias=dtbc[:, 0:1])
        P.act(a1t[:], a1t[:], AF.Exp, scale=-1.0)
        P.act(a1t[:], a1t[:], AF.Ln, bias=1.0)
        P.act(a2t[:], pB[0:8, :], AF.Relu, bias=dtbc[:, 0:1])
        P.tt(a1t[:], a1t[:], a2t[:], ALU.add)
        P.ts(gdec[:], a1t[:], nega[:, 0:1], ALU.mult)
        for ci in range(TO // 128):
            P.scan(cums[:, ci * 128:(ci + 1) * 128], gdec[:, ci * 128:(ci + 1) * 128],
                   zer8[:], 0.0, ALU.add, ALU.add)
        P.dma(beta_h[:, t0:t0 + TO], bet[:])
        P.dma(cum_h[:, t0:t0 + TO], cums[:])
        for hh, sb_, nf in ((q_h, qb, 8), (k_h, kb, 8), (v_h, vb, 16), (g_h, gb, 16)):
            P.dma(hh.rearrange("(f p) t -> p f t", p=96)[:, :, t0:t0 + TO],
                  sb_[:].rearrange("p (f t) -> p f t", t=TO))
    return P




def build_C1b():
    P = Prog("C1b")
    TO = 512
    NS = T // TO
    C = 128
    q_hb = P.dram_in("qS", (768, T), BF16)
    k_hb = P.dram_in("kS", (768, T), BF16)
    v_hb = P.dram_in("vS", (1536, T), BF16)
    g_hb = P.dram_in("gS", (1536, T), BF16)
    beta_h = P.dram_in("betaS", (8, T))
    cum_h = P.dram_in("cumS", (8, T))
    xn1_h = P.dram_in("xn1", (D, T), BF16)
    onw_h = P.dram_in("onorm_w", (192,))
    wo_h = P.dram_in("Wo", (1536, D))
    sel_h = P.dram_in("sel32", (32, 32 * 128))
    ident_h = P.dram_in("ident", (128, 128))
    masks_h = P.dram_in("masks", (3, 128, 128))
    xmid_h = P.dram_out("xmid", (D, T))

    ident = P.sbuf("ident", [128, 128], F32)
    P.dma(ident[:], ident_h[:])
    identb = P.sbuf("identb", [128, 128], BF16)
    P.dma(identb[:], ident_h[:], queue="gpsimd")
    mlo = P.sbuf("mlo", [128, 128], F32)
    mup = P.sbuf("mup", [128, 128], F32)
    mui = P.sbuf("mui", [128, 128], F32)
    P.dma(mlo[:], masks_h[0]); P.dma(mup[:], masks_h[1]); P.dma(mui[:], masks_h[2])
    sel8 = P.sbuf("sel8", [8, 8 * 128], F32)
    P.dma(sel8[:], sel_h[0:8, 0:8 * 128])
    ones1 = P.sbuf("ones1", [1, 128], F32)
    P.memset(ones1[:], 1.0)
    eps192 = P.sbuf("eps192", [128, 1], F32)
    P.memset(eps192[:], 1e-6)
    onwrep = P.sbuf("onwrep", [128, 192], F32)
    onwrow = P.sbuf("onwrow", [1, 192], F32)
    P.dma(onwrow[:], onw_h.unsqueeze(0))
    pO = P.psum("pO", [128, 192])
    P.mm(pO[:], ones1[:], onwrow[:])
    P.copy(onwrep[:], pO[:])
    wo = P.sbuf("wo", [96, 16 * D], BF16)
    P.dma(wo[:].rearrange("p (f m) -> p f m", m=D),
          wo_h.rearrange("(f p) m -> p f m", p=96), queue="gpsimd")

    qb = P.sbuf("qb", [96, 8 * TO], BF16)
    kb = P.sbuf("kb", [96, 8 * TO], BF16)
    vb = P.sbuf("vb", [96, 16 * TO], BF16)
    gb = P.sbuf("gb", [96, 16 * TO], BF16)
    xn1b = P.sbuf("xn1b", [128, 8 * TO], BF16)
    xn1f = P.sbuf("xn1f", [128, TO], F32)
    bet = P.sbuf("bet", [8, TO], F32)
    cums = P.sbuf("cums", [8, TO], F32)
    cbT = P.sbuf("cbT", [128, 16], F32)
    Sf = P.sbuf("Sf", [96, 8 * 192], F32)
    Sb = P.sbuf("Sb", [96, 8 * 192], BF16)
    P.memset(Sf[:], 0.0)
    ofm = P.sbuf("ofm", [96, 16 * TO], BF16)
    xmid = P.sbuf("xmid", [128, 8 * TO], F32)

    kkS = P.sbuf("kkS", [128, 128], F32)
    kqS = P.sbuf("kqS", [128, 128], F32)
    crS = P.sbuf("crS", [128, 128], F32)
    rS = P.sbuf("rS", [128, 128], F32)
    DN = P.sbuf("DN", [128, 128], F32)
    DT = P.sbuf("DT", [128, 128], F32)
    brS = P.sbuf("brS", [128, 128], F32)
    Nc = P.sbuf("Nc", [128, 128], F32)
    NTc = P.sbuf("NTc", [128, 128], F32)
    Nn = P.sbuf("Nn", [128, 128], F32)
    NTn = P.sbuf("NTn", [128, 128], F32)
    Pm = P.sbuf("Pm", [128, 128], F32)
    Vtok = P.sbuf("Vtok", [128, 192], F32)
    Gtok = P.sbuf("Gtok", [128, 192], F32)
    gKS0 = P.sbuf("gKS0", [128, 192], F32)
    RHS = P.sbuf("RHS", [128, 192], F32)
    Wf = P.sbuf("Wf", [128, 192], F32)
    Wb_ = P.sbuf("Wb_", [128, 192], BF16)
    MTb = P.sbuf("MTb", [128, 128], BF16)
    Ot = P.sbuf("Ot", [128, 192], F32)
    qs_ = P.sbuf("qs_", [128, 192], F32)
    Ktok = P.sbuf("Ktok", [128, 96], BF16)
    oftok = P.sbuf("oftok", [128, 192], BF16)
    gcol = P.sbuf("gcol", [128, 1], F32)
    dkcol = P.sbuf("dkcol", [128, 1], F32)
    grep = P.sbuf("grep", [96, 1], F32)
    ecr = P.sbuf("ecr", [1, 1], F32)
    ocol = P.sbuf("ocol", [128, 1], F32)
    sqO = P.sbuf("sqO", [128, 192], F32)

    pX = P.psum("pX", [128, 128])
    pY = P.psum("pY", [128, 192])
    pO2 = P.psum("pO2", [128, 192])
    pS2 = P.psum("pS2", [96, 192])
    pTb = P.psum("pTb", [128, 128], BF16)
    pG1 = P.psum("pG1", [96, 1])
    pW = P.psum("pW", [128, TO])

    def evac(dst, src, eng="vector"):
        P.copy(dst, src, eng=eng)

    for s in range(NS):
        t0 = s * TO
        for hh, sb_, nf, pp in ((q_hb, qb, 8, 96), (k_hb, kb, 8, 96),
                                (v_hb, vb, 16, 96), (g_hb, gb, 16, 96)):
            P.dma(sb_[:].rearrange("p (f t) -> p f t", t=TO),
                  hh.rearrange("(f p) t -> p f t", p=96)[:, :, t0:t0 + TO])
        P.dma(xn1b[:].rearrange("p (f t) -> p f t", t=TO), fm_ap(xn1_h, 0, 8, t0, TO))
        P.dma(bet[:], beta_h[:, t0:t0 + TO])
        P.dma(cums[:], cum_h[:, t0:t0 + TO])
        for ci in range(TO // C):
            co = ci * C
            P.transpose(pX[:, 0:8], cums[:, co:co + C], ident[0:8, 0:8])
            P.copy(cbT[:, 0:8], pX[:, 0:8])
            P.transpose(pX[:, 0:8], bet[:, co:co + C], ident[0:8, 0:8])
            P.copy(cbT[:, 8:16], pX[:, 0:8])
            for h in range(8):
                Ktc = kb[0:96, h * TO + co: h * TO + co + C]
                Qtc = qb[0:96, h * TO + co: h * TO + co + C]
                ccol = cbT[:, h:h + 1]
                bcol = cbT[:, 8 + h:9 + h]
                P.mm(pX[:], Ktc, Ktc); evac(kkS[:], pX[:])
                P.mm(pX[:], Ktc, Qtc); evac(kqS[:], pX[:])
                P.mm(pX[:], sel8[:, h * 128:(h + 1) * 128], cums[:, co:co + C])
                evac(crS[:], pX[:])
                P.ts(rS[:], crS[:], ccol, ALU.subtract)
                P.act(DN[:], rS[:], AF.Relu)
                P.act(DN[:], DN[:], AF.Exp, scale=-1.0)
                P.act(DT[:], rS[:], AF.Relu, scale=-1.0)
                P.act(DT[:], DT[:], AF.Exp, scale=-1.0)
                P.mm(pX[:], sel8[:, h * 128:(h + 1) * 128], bet[:, co:co + C])
                evac(brS[:], pX[:])
                P.tt(Nc[:], DN[:], kkS[:], ALU.mult)
                P.ts(Nc[:], Nc[:], bcol, ALU.mult)
                P.tt(Nc[:], Nc[:], mlo[:], ALU.mult)
                P.tt(NTc[:], DT[:], kkS[:], ALU.mult)
                P.tt(NTc[:], NTc[:], brS[:], ALU.mult)
                P.tt(NTc[:], NTc[:], mup[:], ALU.mult)
                P.tt(Pm[:], ident[:], NTc[:], ALU.subtract)
                cN, cNT, nN, nNT = Nc, NTc, Nn, NTn
                for lev in range(6):
                    P.mm(pX[:], cNT[:], cN[:]); evac(nN[:], pX[:])
                    P.mm(pX[:], cN[:], cNT[:]); evac(nNT[:], pX[:])
                    cN, nN = nN, cN
                    cNT, nNT = nNT, cNT
                    P.mm(pX[:], cN[:], Pm[:])
                    P.tt(Pm[:], Pm[:], pX[:], ALU.add)
                P.copy(Sb[:, h * 192:(h + 1) * 192], Sf[:, h * 192:(h + 1) * 192])
                P.mm(pY[:], Ktc, Sb[:, h * 192:(h + 1) * 192])
                for sub in range(2):
                    P.transpose(pTb[0:128, 0:96],
                                vb[0:96, (2 * h + sub) * TO + co: (2 * h + sub) * TO + co + C],
                                identb[0:96, 0:96])
                    P.copy(Vtok[:, sub * 96:(sub + 1) * 96], pTb[0:128, 0:96])
                P.act(gcol[:], ccol, AF.Exp)
                P.ts(gKS0[:], pY[:], gcol[:, 0:1], ALU.mult)
                P.tt(RHS[:], Vtok[:], gKS0[:], ALU.subtract)
                P.ts(RHS[:], RHS[:], bcol, ALU.mult)
                P.mm(pY[:], Pm[:], RHS[:])
                evac(Wf[:], pY[:]); P.copy(Wb_[:], Wf[:])
                P.tt(rS[:], DT[:], kqS[:], ALU.mult)
                P.tt(rS[:], rS[:], mui[:], ALU.mult)
                P.copy(MTb[:], rS[:])
                P.mm(pY[:], MTb[:], Wb_[:])
                P.mm(pO2[:], Qtc, Sb[:, h * 192:(h + 1) * 192])
                P.ts(qs_[:], pO2[:], gcol[:, 0:1], ALU.mult)
                P.tt(Ot[:], pY[:], qs_[:], ALU.add)
                P.transpose(pTb[0:128, 0:96], Ktc, identb[0:96, 0:96])
                P.copy(Ktok[:], pTb[0:128, 0:96])
                P.ts(dkcol[:], crS[:, 127:128], ccol, ALU.subtract)
                P.act(dkcol[:], dkcol[:], AF.Exp)
                P.ts(Ktok[:], Ktok[:], dkcol[:, 0:1], ALU.mult)
                P.mm(pS2[:], Ktok[:], Wb_[:])
                P.act(ecr[:], crS[0:1, 127:128], AF.Exp)
                P.mm(pG1[:], ones1[0:1, 0:96], ecr[:])
                evac(grep[:], pG1[:])
                P.stt(Sf[:, h * 192:(h + 1) * 192], Sf[:, h * 192:(h + 1) * 192],
                      grep[:, 0:1], pS2[:], ALU.mult, ALU.add)
                P.act(sqO[:], Ot[:], AF.Square)
                P.red(ocol[:], sqO[:], ALU.add)
                P.ts(ocol[:], ocol[:], 1.0 / 192, ALU.mult)
                P.act(ocol[:], ocol[:], AF.Ln, bias=eps192[:, 0:1])
                P.act(ocol[:], ocol[:], AF.Exp, scale=-0.5)
                P.ts(Ot[:], Ot[:], ocol[:, 0:1], ALU.mult)
                P.tt(Ot[:], Ot[:], onwrep[:], ALU.mult)
                for sub in range(2):
                    P.transpose(pTb[0:128, 0:96],
                                gb[0:96, (2 * h + sub) * TO + co: (2 * h + sub) * TO + co + C],
                                identb[0:96, 0:96])
                    P.copy(Gtok[:, sub * 96:(sub + 1) * 96], pTb[0:128, 0:96])
                P.act(Gtok[:], Gtok[:], AF.Silu)
                P.tt(oftok[:], Ot[:], Gtok[:], ALU.mult)
                for sub in range(2):
                    P.transpose(pTb[0:96, 0:128], oftok[:, sub * 96:(sub + 1) * 96],
                                identb[:])
                    P.copy(ofm[0:96, (2 * h + sub) * TO + co:
                           (2 * h + sub) * TO + co + C], pTb[0:96, 0:128])
        for m in range(8):
            for k in range(16):
                P.mm(pW[:], wo[:, k * D + m * 128: k * D + (m + 1) * 128],
                     ofm[:, k * TO:(k + 1) * TO], start=(k == 0), stop=(k == 15))
            P.copy(xn1f[:], xn1b[:, m * TO:(m + 1) * TO])
            P.tt(xmid[:, m * TO:(m + 1) * TO], pW[:], xn1f[:], ALU.add)
        P.dma(fm_ap(xmid_h, 0, 8, t0, TO), xmid[:].rearrange("p (f t) -> p f t", t=TO))
    return P




def build_C2():
    P = Prog("C2")
    TCH = 512
    HALF = T // 2
    xmid_h = P.dram_in("xmid", (D, T))
    x1_h = P.dram_in("x1", (D, T))
    n2_h = P.dram_in("norm2_w", (D,))
    wg_h = P.dram_in("Wgate", (D, 4096))
    wu_h = P.dram_in("Wup", (D, 4096))
    wd_h = P.dram_in("Wdown", (4096, D))
    t2_h = P.dram_out("t2", (D, T))
    h2_h = P.dram_out("h2aS", (D, T), BF16)

    n2c = P.sbuf("n2c", [128, 8], F32)
    P.dma_slow(n2c[:], n2_h.rearrange("(f p) -> p f", p=128))
    ones128 = P.sbuf("ones128", [128, 1], F32)
    P.memset(ones128[:], 1.0)
    ones1 = P.sbuf("ones1", [1, 128], F32)
    P.memset(ones1[:], 1.0)
    epsr = P.sbuf("epsr", [1, 1], F32)
    P.memset(epsr[:], 1e-6)
    xmt = P.sbuf("xmt", [128, 8 * TCH])
    sq = P.sbuf("sq", [128, TCH])
    rowb = P.sbuf("rowb", [1, TCH], F32)
    rrep = P.sbuf("rrep", [128, TCH], F32)
    h2b = P.sbuf("h2b", [128, 8 * TCH], BF16)
    pS = P.psum("pS", [1, TCH])
    pM = P.psum("pM", [128, TCH])

    for c in range(T // TCH):
        t0 = c * TCH
        P.dma(xmt[:].rearrange("p (f t) -> p f t", t=TCH), fm_ap(xmid_h, 0, 8, t0, TCH))
        fm_norm(P, xmt, 8, TCH, sq, rowb, pS, ones128, epsr[0:1, 0:1], rowb[0:1, 0:TCH])
        P.mm(pM[:], ones1[:], rowb[0:1, :], start=True, stop=True)
        P.copy(rrep[:], pM[:])
        for k in range(8):
            P.tt(sq[:], xmt[:, k * TCH:(k + 1) * TCH], rrep[:], ALU.mult)
            P.ts(h2b[:, k * TCH:(k + 1) * TCH], sq[:], n2c[:, k:k + 1], ALU.mult)
        P.dma(fm_ap(h2_h, 0, 8, t0, TCH), h2b[:].rearrange("p (f t) -> p f t", t=TCH))

    wz = P.sbuf("wz", [128, 24576], BF16)
    GA, UA, DA = 0, 8192, 16384
    acc = P.sbuf("acc", [128, 8 * HALF], F32)
    hub = P.sbuf("hub", [128, 8 * TCH], BF16)
    silb = P.sbuf("silb", [128, TCH], F32)
    pG = P.psum("pG", [128, TCH]); pU = P.psum("pU", [128, TCH])
    pD1 = P.psum("pD1", [128, TCH]); pD2 = P.psum("pD2", [128, TCH])
    wg_r = wg_h.rearrange("(f p) m -> p f m", p=128)
    wu_r = wu_h.rearrange("(f p) m -> p f m", p=128)
    wd_r = wd_h.rearrange("(f p) m -> p f m", p=128)
    t1b = P.sbuf("t1b", [128, 8 * 256], F32)
    t1c = P.sbuf("t1c", [128, 8 * 256], F32)

    for h_ in range(2):
        for grp in range(4):
            P.dma(wz[:, GA:GA + 8192].rearrange("p (f m) -> p f m", m=1024),
                  wg_r[:, :, grp * 1024:(grp + 1) * 1024], queue="gpsimd")
            P.dma(wz[:, UA:UA + 8192].rearrange("p (f m) -> p f m", m=1024),
                  wu_r[:, :, grp * 1024:(grp + 1) * 1024], queue="gpsimd")
            P.dma(wz[:, DA:DA + 8192].rearrange("p (f m) -> p f m", m=1024),
                  wd_r[:, grp * 8:(grp + 1) * 8, :], queue="gpsimd")
            for sub in range(4):
                t0 = h_ * HALF + sub * TCH
                P.dma(h2b[:].rearrange("p (f t) -> p f t", t=TCH),
                      fm_ap(h2_h, 0, 8, t0, TCH))
                for mt in range(8):
                    for k in range(8):
                        P.mm(pG[:], wz[:, GA + k * 1024 + mt * 128: GA + k * 1024 + (mt + 1) * 128],
                             h2b[:, k * TCH:(k + 1) * TCH], start=(k == 0), stop=(k == 7))
                    for k in range(8):
                        P.mm(pU[:], wz[:, UA + k * 1024 + mt * 128: UA + k * 1024 + (mt + 1) * 128],
                             h2b[:, k * TCH:(k + 1) * TCH], start=(k == 0), stop=(k == 7))
                    P.act(silb[:], pG[:], AF.Silu)
                    P.tt(hub[:, mt * TCH:(mt + 1) * TCH], silb[:], pU[:], ALU.mult)
                for m in range(8):
                    pd = pD1 if m % 2 == 0 else pD2
                    for k in range(8):
                        P.mm(pd[:], wz[:, DA + k * 1024 + m * 128: DA + k * 1024 + (m + 1) * 128],
                             hub[:, k * TCH:(k + 1) * TCH], start=(k == 0), stop=(k == 7))
                    ao = m * HALF + sub * TCH
                    if grp == 0:
                        P.copy(acc[:, ao:ao + TCH], pd[:])
                    else:
                        P.tt(acc[:, ao:ao + TCH], acc[:, ao:ao + TCH], pd[:], ALU.add)
        for sub in range(8):
            t0 = h_ * HALF + sub * 256
            P.dma(t1b[:].rearrange("p (f t) -> p f t", t=256), fm_ap(x1_h, 0, 8, t0, 256))
            P.dma(t1c[:].rearrange("p (f t) -> p f t", t=256), fm_ap(xmid_h, 0, 8, t0, 256))
            for m in range(8):
                ao = m * HALF + sub * 256
                P.tt(t1b[:, m * 256:(m + 1) * 256], t1b[:, m * 256:(m + 1) * 256],
                     t1c[:, m * 256:(m + 1) * 256], ALU.add)
                P.tt(t1b[:, m * 256:(m + 1) * 256], t1b[:, m * 256:(m + 1) * 256],
                     acc[:, ao:ao + 256], ALU.add)
            P.dma(fm_ap(t2_h, 0, 8, t0, 256), t1b[:].rearrange("p (f t) -> p f t", t=256))
    return P



def build_D1():
    P = Prog("D1")
    TO = 512
    NS = T // TO
    xn1_h = P.dram_in("xn1", (D, T), BF16)
    anw_h = P.dram_in("attn_norm_w", (D,))
    fnw_h = P.dram_in("ffn_norm_w", (D,))
    qkvw_h = P.dram_in("qkv_wP", (D, 1536))
    qkb_h = P.dram_in("qkb64", (64, 20))
    vbb_h = P.dram_in("vb128", (128, 2))
    pw_h = P.dram_in("proj_w", (D, D))
    pb_h = P.dram_in("proj_b", (D,))
    sink_h = P.dram_in("attn_sink", (16,))
    gw_h = P.dram_in("gate_w", (D, 32))
    gb_h = P.dram_in("gate_b", (32,))
    cs_h = P.dram_in("cosS", (64, T))
    sn_h = P.dram_in("sinS", (64, T))
    ident_h = P.dram_in("ident", (128, 128))
    am_h = P.dram_in("amasks", (2, 128, 256))
    xp_h = P.dram_out("xpS", (D, T))
    h2f_h = P.dram_out("h2fS", (D, T), BF16)
    glg_h = P.dram_out("glogS", (T, 32))

    ident = P.sbuf("ident", [128, 128], F32)
    P.dma(ident[:], ident_h[:])
    identb = P.sbuf("identb", [128, 128], BF16)
    P.dma(identb[:], ident_h[:], queue="gpsimd")
    am0 = P.sbuf("am0", [128, 256], F32)
    amr = P.sbuf("amr", [128, 256], F32)
    P.dma(am0[:], am_h[0]); P.dma(amr[:], am_h[1])
    anc = P.sbuf("anc", [128, 8], F32)
    P.dma_slow(anc[:], anw_h.rearrange("(f p) -> p f", p=128))
    fnc = P.sbuf("fnc", [128, 8], F32)
    P.dma_slow(fnc[:], fnw_h.rearrange("(f p) -> p f", p=128))
    qkvw = P.sbuf("qkvw", [128, 8 * 1536], BF16)
    P.dma(qkvw[:].rearrange("p (f m) -> p f m", m=1536),
          qkvw_h.rearrange("(f p) m -> p f m", p=128), queue="gpsimd")
    qkb = P.sbuf("qkb", [64, 20], F32)
    P.dma(qkb[:], qkb_h[:])
    vbb = P.sbuf("vbb", [128, 2], F32)
    P.dma(vbb[:], vbb_h[:])
    pw = P.sbuf("pw", [64, 16 * D], BF16)
    P.dma(pw[:].rearrange("p (f m) -> p f m", m=D),
          pw_h.rearrange("(f p) m -> p f m", p=64), queue="gpsimd")
    pbc = P.sbuf("pbc", [128, 8], F32)
    P.dma_slow(pbc[:], pb_h.rearrange("(f p) -> p f", p=128))
    gwc = P.sbuf("gwc", [128, 8 * 32], F32)
    P.dma(gwc[:].rearrange("p (f m) -> p f m", m=32),
          gw_h.rearrange("(f p) m -> p f m", p=128))
    gbc = P.sbuf("gbc", [32, 1], F32)
    P.dma(gbc[:], gb_h.unsqueeze(1))
    cosb = P.sbuf("cosb", [64, T], F32)
    sinb = P.sbuf("sinb", [64, T], F32)
    P.dma(cosb[:], cs_h[:]); P.dma(sinb[:], sn_h[:])
    ones128 = P.sbuf("ones128", [128, 1], F32)
    P.memset(ones128[:], 1.0)
    ones1 = P.sbuf("ones1", [1, 128], F32)
    P.memset(ones1[:], 1.0)
    epsr = P.sbuf("epsr", [1, 1], F32)
    P.memset(epsr[:], 1e-5)

    xn1b = P.sbuf("xn1b", [128, 8 * TO], BF16)
    xn1f = P.sbuf("xn1f", [128, TO], F32)
    hb = P.sbuf("hb", [128, 8 * TO], BF16)
    sq = P.sbuf("sq", [128, TO])
    rowb = P.sbuf("rowb", [1, TO], F32)
    rrep = P.sbuf("rrep", [128, TO], F32)
    qb = P.sbuf("qb", [64, 16 * TO], BF16)
    kb = P.sbuf("kb", [64, 4 * (128 + TO)], BF16)
    vtok = P.sbuf("vtok", [128, 4 * 5 * 64], BF16)
    rtmp1 = P.sbuf("rtmp1", [32, TO], F32)
    rtmp2 = P.sbuf("rtmp2", [32, TO], F32)
    lgs = P.sbuf("lgs", [128, 256], F32)
    pex = P.sbuf("pex", [128, 256], F32)
    mcol = P.sbuf("mcol", [128, 1], F32)
    nmcol = P.sbuf("nmcol", [128, 1], F32)
    scol = P.sbuf("scol", [128, 1], F32)
    dcol = P.sbuf("dcol", [128, 1], F32)
    attT = P.sbuf("attT", [128, 256], BF16)
    atb = P.sbuf("atb", [128, 256], BF16)
    yb = P.sbuf("yb", [64, 16 * TO], BF16)
    xpb = P.sbuf("xpb", [128, 8 * TO], F32)
    h2fb = P.sbuf("h2fb", [128, 8 * TO], BF16)
    glt = P.sbuf("glt", [32, TO], F32)
    glo = P.sbuf("glo", [128, 4 * 32], F32)
    sinkr = P.sbuf("sinkr", [128, 16], F32)

    pQ = P.psum("pQ", [64, TO])
    pV = P.psum("pV", [128, TO])
    pL = P.psum("pL", [128, 256])
    pTb = P.psum("pTb", [128, 128], BF16)
    pT32 = P.psum("pT32", [128, 128])
    pY = P.psum("pY", [64, 128])
    pS = P.psum("pS", [1, TO])
    pGl = P.psum("pGl", [32, TO])

    P.dma(sinkr[0:1, :], sink_h.unsqueeze(0))
    P.mm(pV[:, 0:16], ones1[:], sinkr[0:1, :])
    P.copy(sinkr[:], pV[:, 0:16])
    P.memset(kb[:], 0.0)
    P.memset(vtok[:], 0.0)

    def rope_pair(dst_even, dst_odd, src64, t0):
        P.tt(rtmp1[:], src64[0:32, :], cosb[0:32, t0:t0 + TO], ALU.mult)
        P.tt(rtmp2[:], src64[32:64, :], sinb[32:64, t0:t0 + TO], ALU.mult)
        P.tt(rtmp1[:], rtmp1[:], rtmp2[:], ALU.subtract)
        P.tt(rtmp2[:], src64[0:32, :], sinb[0:32, t0:t0 + TO], ALU.mult)
        P.copy(dst_even, rtmp1[:])
        P.tt(rtmp1[:], src64[32:64, :], cosb[32:64, t0:t0 + TO], ALU.mult)
        P.tt(rtmp1[:], rtmp1[:], rtmp2[:], ALU.add)
        P.copy(dst_odd, rtmp1[:])

    for s in range(NS):
        t0 = s * TO
        P.dma(xn1b[:].rearrange("p (f t) -> p f t", t=TO), fm_ap(xn1_h, 0, 8, t0, TO))
        for k in range(8):
            P.copy(sq[:], xn1b[:, k * TO:(k + 1) * TO])
            P.act(sq[:], sq[:], AF.Square)
            P.mm(pS[0:1, :], ones128[:, 0:1], sq[:], start=(k == 0), stop=(k == 7))
        P.ts(rowb[0:1, :], pS[0:1, :], 1.0 / D, ALU.mult)
        P.act(rowb[0:1, :], rowb[0:1, :], AF.Ln, bias=epsr[0:1, 0:1])
        P.act(rowb[0:1, :], rowb[0:1, :], AF.Exp, scale=-0.5)
        P.mm(pV[:], ones1[:], rowb[0:1, :], start=True, stop=True)
        P.copy(rrep[:], pV[:])
        for k in range(8):
            P.copy(sq[:], xn1b[:, k * TO:(k + 1) * TO])
            P.tt(sq[:], sq[:], rrep[:], ALU.mult)
            P.ts(hb[:, k * TO:(k + 1) * TO], sq[:], anc[:, k:k + 1], ALU.mult)
        for kv in range(4):
            P.copy(kb[:, kv * 640:kv * 640 + 128],
                   kb[:, kv * 640 + TO:kv * 640 + TO + 128])
            P.copy(vtok[:, kv * 320:kv * 320 + 64],
                   vtok[:, kv * 320 + 256:kv * 320 + 320])
        for hh in range(16):
            for k in range(8):
                P.mm(pQ[:], qkvw[:, k * 1536 + hh * 64: k * 1536 + (hh + 1) * 64],
                     hb[:, k * TO:(k + 1) * TO], start=(k == 0), stop=(k == 7))
            bcol = qkb[:, hh:hh + 1]
            P.ts(sq[0:64, :], pQ[:], bcol, ALU.add)
            rope_pair(qb[0:32, hh * TO:(hh + 1) * TO],
                      qb[32:64, hh * TO:(hh + 1) * TO], sq, t0)
        for kv in range(4):
            for k in range(8):
                P.mm(pQ[:], qkvw[:, k * 1536 + 1024 + kv * 64: k * 1536 + 1024 + (kv + 1) * 64],
                     hb[:, k * TO:(k + 1) * TO], start=(k == 0), stop=(k == 7))
            bcol = qkb[:, 16 + kv:17 + kv]
            P.ts(sq[0:64, :], pQ[:], bcol, ALU.add)
            ko = kv * 640 + 128
            rope_pair(kb[0:32, ko:ko + TO], kb[32:64, ko:ko + TO], sq, t0)
        for vt in range(2):
            for k in range(8):
                P.mm(pV[:], qkvw[:, k * 1536 + 1280 + vt * 128: k * 1536 + 1280 + (vt + 1) * 128],
                     hb[:, k * TO:(k + 1) * TO], start=(k == 0), stop=(k == 7))
            bcol = vbb[:, vt:vt + 1]
            P.ts(sq[:], pV[:], bcol, ALU.add)
            for tt_ in range(4):
                P.transpose(pT32[:], sq[:, tt_ * 128:(tt_ + 1) * 128], ident[:])
                P.copy(vtok[:, (2 * vt) * 320 + (tt_ + 1) * 64:
                       (2 * vt) * 320 + (tt_ + 2) * 64], pT32[:, 0:64])
                P.copy(vtok[:, (2 * vt + 1) * 320 + (tt_ + 1) * 64:
                       (2 * vt + 1) * 320 + (tt_ + 2) * 64], pT32[:, 64:128])
        for qt in range(4):
            for qh in range(16):
                kv = qh // 4
                P.mm(pL[:], qb[0:64, qh * TO + qt * 128: qh * TO + qt * 128 + 128],
                     kb[0:64, kv * 640 + qt * 128: kv * 640 + qt * 128 + 256])
                msk = am0 if (s == 0 and qt == 0) else amr
                P.stt(lgs[:], pL[:], 0.125, msk[:], ALU.mult, ALU.add)
                P.red(mcol[:], lgs[:], ALU.max)
                P.tt(mcol[:], mcol[:], sinkr[:, qh:qh + 1], ALU.max)
                P.ts(nmcol[:], mcol[:], -1.0, ALU.mult)
                P.act(pex[:], lgs[:], AF.Exp, bias=nmcol[:, 0:1])
                P.red(dcol[:], pex[:], ALU.add)
                P.tt(scol[:], sinkr[:, qh:qh + 1], mcol[:], ALU.subtract)
                P.act(scol[:], scol[:], AF.Exp)
                P.tt(dcol[:], dcol[:], scol[:], ALU.add)
                P.recip(scol[:], dcol[:])
                P.ts(atb[:], pex[:], scol[:, 0:1], ALU.mult)
                for piece in range(2):
                    P.transpose(pTb[:], atb[:, piece * 128:(piece + 1) * 128], identb[:])
                    P.copy(attT[:, piece * 128:(piece + 1) * 128], pTb[:])
                for piece in range(2):
                    P.mm(pY[:], vtok[:, kv * 320 + (qt + piece) * 64:
                                     kv * 320 + (qt + piece + 1) * 64],
                         attT[:, piece * 128:(piece + 1) * 128],
                         start=(piece == 0), stop=(piece == 1))
                P.copy(yb[0:64, qh * TO + qt * 128: qh * TO + qt * 128 + 128], pY[:])
        for m in range(8):
            for k in range(16):
                P.mm(pV[:], pw[0:64, k * D + m * 128: k * D + (m + 1) * 128],
                     yb[0:64, k * TO:(k + 1) * TO], start=(k == 0), stop=(k == 15))
            P.ts(sq[:], pV[:], pbc[:, m:m + 1], ALU.add)
            P.copy(xn1f[:], xn1b[:, m * TO:(m + 1) * TO])
            P.tt(xpb[:, m * TO:(m + 1) * TO], sq[:], xn1f[:], ALU.add)
        P.dma(fm_ap(xp_h, 0, 8, t0, TO), xpb[:].rearrange("p (f t) -> p f t", t=TO))
        for k in range(8):
            P.act(sq[:], xpb[:, k * TO:(k + 1) * TO], AF.Square)
            P.mm(pS[0:1, :], ones128[:, 0:1], sq[:], start=(k == 0), stop=(k == 7))
        P.ts(rowb[0:1, :], pS[0:1, :], 1.0 / D, ALU.mult)
        P.act(rowb[0:1, :], rowb[0:1, :], AF.Ln, bias=epsr[0:1, 0:1])
        P.act(rowb[0:1, :], rowb[0:1, :], AF.Exp, scale=-0.5)
        P.mm(pV[:], ones1[:], rowb[0:1, :], start=True, stop=True)
        P.copy(rrep[:], pV[:])
        for k in range(8):
            P.tt(sq[:], xpb[:, k * TO:(k + 1) * TO], rrep[:], ALU.mult)
            P.ts(h2fb[:, k * TO:(k + 1) * TO], sq[:], fnc[:, k:k + 1], ALU.mult)
        P.dma(fm_ap(h2f_h, 0, 8, t0, TO), h2fb[:].rearrange("p (f t) -> p f t", t=TO))
        for k in range(8):
            P.tt(sq[:], xpb[:, k * TO:(k + 1) * TO], rrep[:], ALU.mult)
            P.ts(sq[:], sq[:], fnc[:, k:k + 1], ALU.mult)
            P.mm(pGl[:], gwc[:, k * 32:(k + 1) * 32], sq[:],
                 start=(k == 0), stop=(k == 7))
        P.ts(glt[:], pGl[:], gbc[:, 0:1], ALU.add)
        for tt_ in range(4):
            P.transpose(pT32[:, 0:32], glt[:, tt_ * 128:(tt_ + 1) * 128],
                        ident[0:32, 0:32])
            P.copy(glo[:, tt_ * 32:(tt_ + 1) * 32], pT32[:, 0:32])
            P.dma(glg_h[t0 + tt_ * 128: t0 + (tt_ + 1) * 128],
                  glo[:, tt_ * 32:(tt_ + 1) * 32])
    return P




def build_D2():
    P = Prog("D2")
    TCH = 512
    HALF = T // 2
    h2_h = P.dram_in("h2fS", (D, T), BF16)
    glg_h = P.dram_in("glogS", (T, 32))
    xp_h = P.dram_in("xpS", (D, T))
    t2_h = P.dram_in("t2", (D, T))
    wu_h = P.dram_in("W_up", (32, D, 2048))
    wd_h = P.dram_in("W_down", (32, D, D))
    sel_h = P.dram_in("sel32", (32, 32 * 128))
    ident_h = P.dram_in("ident", (128, 128))
    out_h = P.dram_out("out", (T, D))

    ident = P.sbuf("ident", [128, 128], F32)
    P.dma(ident[:], ident_h[:])
    selb = P.sbuf("selb", [32, 32 * 128], BF16)
    P.dma(selb[:], sel_h[:], queue="gpsimd")

    lgt = P.sbuf("lgt", [128, 32], F32)
    msk = P.sbuf("msk", [128, 32], F32)
    eq = P.sbuf("eq", [128, 32], F32)
    cum = P.sbuf("cum", [128, 32], F32)
    zer = P.sbuf("zer", [128, 32], F32)
    first = P.sbuf("first", [128, 32], F32)
    mm_ = P.sbuf("mm_", [128, 1], F32)
    m0 = P.sbuf("m0", [128, 1], F32)
    vks = P.sbuf("vks", [128, 4], F32)
    fks = P.sbuf("fks", [128, 4 * 32], F32)
    den = P.sbuf("den", [128, 1], F32)
    comb = P.sbuf("comb", [128, 32], F32)
    combT = P.sbuf("combT", [32, T], BF16)
    pR = P.psum("pR", [128, 128])
    P.memset(zer[:], 0.0)
    scr = (eq, cum, zer, first, mm_)
    for tt_ in range(T // 128):
        P.dma(lgt[:], glg_h[tt_ * 128:(tt_ + 1) * 128])
        P.copy(msk[:], lgt[:])
        for kk in range(4):
            f, m = emit_first_argmax(P, msk[:], scr)
            P.copy(fks[:, kk * 32:(kk + 1) * 32], f[:])
            P.copy(vks[:, kk:kk + 1], m[:])
            if kk == 0:
                P.copy(m0[:], m[:])
            P.stt(msk[:], f[:], -1e30, msk[:], ALU.mult, ALU.add)
        P.ts(vks[:], vks[:], m0[:, 0:1], ALU.subtract)
        P.act(vks[:], vks[:], AF.Exp)
        P.red(den[:], vks[:], ALU.add)
        P.recip(den[:], den[:])
        P.memset(comb[:], 0.0)
        for kk in range(4):
            P.ts(mm_[:], vks[:, kk:kk + 1], den[:, 0:1], ALU.mult)
            P.stt(comb[:], fks[:, kk * 32:(kk + 1) * 32], mm_[:, 0:1], comb[:],
                  ALU.mult, ALU.add)
        P.transpose(pR[0:32, :], comb[:], ident[:])
        P.copy(combT[:, tt_ * 128:(tt_ + 1) * 128], pR[0:32, 0:128])

    wz = P.sbuf("wz", [128, 24576], BF16)
    UA, DA = 0, 16384
    acc = P.sbuf("acc", [128, 8 * HALF], F32)
    h2b = P.sbuf("h2b", [128, 8 * TCH], BF16)
    hub = P.sbuf("hub", [128, 8 * TCH], BF16)
    silb = P.sbuf("silb", [128, TCH], F32)
    crep = P.sbuf("crep", [128, TCH], F32)
    pG = P.psum("pG", [128, TCH]); pU = P.psum("pU", [128, TCH])
    pC = P.psum("pC", [128, TCH])
    pD1_ = P.psum("pD1_", [128, TCH]); pD2_ = P.psum("pD2_", [128, TCH])
    pT2 = P.psum("pT2", [128, TCH])
    wu_r = wu_h.rearrange("e (f p) m -> e p f m", p=128)
    wd_r = wd_h.rearrange("e (f p) m -> e p f m", p=128)
    fin = P.sbuf("fin", [128, 8 * TCH], F32)
    fin2 = P.sbuf("fin2", [128, 8 * TCH], F32)

    for h_ in range(2):
        for e_ in range(32):
            P.dma(wz[:, UA:UA + 16384].rearrange("p (f m) -> p f m", m=2048),
                  wu_r[e_], queue="gpsimd")
            P.dma(wz[:, DA:DA + 8192].rearrange("p (f m) -> p f m", m=1024),
                  wd_r[e_], queue="gpsimd")
            for sub in range(4):
                t0 = h_ * HALF + sub * TCH
                P.dma(h2b[:].rearrange("p (f t) -> p f t", t=TCH),
                      fm_ap(h2_h, 0, 8, t0, TCH))
                for mt in range(8):
                    for k in range(8):
                        P.mm(pU[:], wz[:, UA + k * 2048 + mt * 128:
                                       UA + k * 2048 + (mt + 1) * 128],
                             h2b[:, k * TCH:(k + 1) * TCH],
                             start=(k == 0), stop=(k == 7))
                    for k in range(8):
                        P.mm(pG[:], wz[:, UA + k * 2048 + 1024 + mt * 128:
                                       UA + k * 2048 + 1024 + (mt + 1) * 128],
                             h2b[:, k * TCH:(k + 1) * TCH],
                             start=(k == 0), stop=(k == 7))
                    P.act(silb[:], pG[:], AF.Silu)
                    P.tt(hub[:, mt * TCH:(mt + 1) * TCH], silb[:], pU[:], ALU.mult)
                P.mm(pC[:], selb[:, e_ * 128:(e_ + 1) * 128], combT[:, t0:t0 + TCH])
                P.copy(crep[:], pC[:])
                for mt in range(8):
                    P.tt(hub[:, mt * TCH:(mt + 1) * TCH],
                         hub[:, mt * TCH:(mt + 1) * TCH], crep[:], ALU.mult)
                for m in range(8):
                    pd = pD1_ if m % 2 == 0 else pD2_
                    for k in range(8):
                        P.mm(pd[:], wz[:, DA + k * 1024 + m * 128:
                                       DA + k * 1024 + (m + 1) * 128],
                             hub[:, k * TCH:(k + 1) * TCH],
                             start=(k == 0), stop=(k == 7))
                    ao = m * HALF + sub * TCH
                    if e_ == 0:
                        P.copy(acc[:, ao:ao + TCH], pd[:])
                    else:
                        P.tt(acc[:, ao:ao + TCH], acc[:, ao:ao + TCH], pd[:], ALU.add)
        for sub in range(4):
            t0 = h_ * HALF + sub * TCH
            P.dma(fin[:].rearrange("p (f t) -> p f t", t=TCH), fm_ap(t2_h, 0, 8, t0, TCH))
            P.dma(fin2[:].rearrange("p (f t) -> p f t", t=TCH), fm_ap(xp_h, 0, 8, t0, TCH))
            for m in range(8):
                ao = m * HALF + sub * TCH
                P.tt(fin[:, m * TCH:(m + 1) * TCH], fin[:, m * TCH:(m + 1) * TCH],
                     fin2[:, m * TCH:(m + 1) * TCH], ALU.add)
                P.tt(fin[:, m * TCH:(m + 1) * TCH], fin[:, m * TCH:(m + 1) * TCH],
                     acc[:, ao:ao + TCH], ALU.add)
            for tt_ in range(4):
                for m in range(8):
                    P.transpose(pT2[:, 0:128],
                                fin[:, m * TCH + tt_ * 128: m * TCH + tt_ * 128 + 128],
                                ident[:])
                    P.copy(fin2[:, m * 128:(m + 1) * 128], pT2[:, 0:128])
                P.dma(out_h[t0 + tt_ * 128: t0 + (tt_ + 1) * 128],
                      fin2[:, 0:1024])
    return P



_cache = {}


def _bf(x):
    return np.ascontiguousarray(x)


def run_launch(key, builder, in_maps):
    if key not in _cache:
        P = builder()
        _cache[key] = P.build()
    nc = _cache[key]
    res = run_bass_kernel_spmd(nc, in_maps, core_ids=list(range(len(in_maps))))
    return res.results


IDENT = np.eye(128, dtype=np.float32)
SEL32 = np.zeros((32, 32 * 128), dtype=np.float32)
for _s in range(32):
    SEL32[_s, _s * 128:(_s + 1) * 128] = 1.0
_MLO = np.tril(np.ones((128, 128), np.float32), -1)
MASKS = np.stack([_MLO, _MLO.T, np.triu(np.ones((128, 128), np.float32))])

def _amasks():
    NEG = np.float32(-3e38)
    qi = np.arange(128)[:, None]
    kj = np.arange(256)[None, :]
    valid = (kj >= qi) & (kj <= qi + 128)
    amr = np.where(valid, np.float32(0), NEG).astype(np.float32)
    am0 = amr.copy()
    am0[:, :128] = NEG
    return np.stack([am0, amr])

AMASKS = _amasks()

def _rope_tables():
    DH = 64
    pos = np.arange(T, dtype=np.float64) / 32.0
    inv = 1.0 / (150000.0 ** (np.arange(0, DH, 2, dtype=np.float64) / DH))
    fr = pos[None, :] * inv[:, None]
    c = np.cos(fr).astype(np.float32)
    s_ = np.sin(fr).astype(np.float32)
    return np.vstack([c, c]), np.vstack([s_, s_])

COSS, SINS = _rope_tables()

def _perm_qkv(qkv_w, qkv_b):
    perm = np.arange(1536)
    def headperm(base, nh):
        for h in range(nh):
            o = base + h * 64
            perm[o:o + 64] = np.concatenate([np.arange(o, o + 64, 2),
                                             np.arange(o + 1, o + 64, 2)])
    headperm(0, 16)
    headperm(1024, 4)
    wp = np.ascontiguousarray(qkv_w[:, perm])
    bpm = qkv_b[perm]
    qkb = np.ascontiguousarray(bpm[:1280].reshape(20, 64).T)
    vbb = np.ascontiguousarray(bpm[1280:].reshape(2, 128).T)
    return wp, qkb, vbb


def kernel(**ins):
    x = ins["L0_attn_x"].astype(np.float32)
    B = x.shape[0]
    bcw = np.concatenate([ins["L0_attn_B_w"], ins["L0_attn_C_w"]], axis=1)
    rwT = np.ascontiguousarray(ins["L0_ffn_router_w"].T)
    convw = np.ascontiguousarray(ins["L0_attn_conv_w"][:, 0, :])
    qkvwP, qkbP, vbbP = _perm_qkv(ins["L1_ffn_qkv_w"], ins["L1_ffn_qkv_b"])

    mapsA1 = [dict(x=x[b], in_proj_w=ins["L0_attn_in_proj_w"], conv_w=convw,
                   dt_w=ins["L0_attn_dt_w"], dt_b=ins["L0_attn_dt_b"],
                   bc_w=bcw, router_wT=rwT, norm_w=ins["L0_attn_norm_w"],
                   ident=IDENT) for b in range(B)]
    rA1 = run_launch("A1", build_A1, mapsA1)
    mapsA2 = [dict(x=x[b], dtS=rA1[b]["dtS"], uS=rA1[b]["uS"], xcS=rA1[b]["xcS"],
                   bcS=rA1[b]["bcS"], sel32=SEL32, A_log=ins["L0_attn_A_log"],
                   D_skip=ins["L0_attn_D_skip"], out_proj_w=ins["L0_attn_out_proj_w"],
                   ident=IDENT) for b in range(B)]
    rA2 = run_launch("A2", build_A2, mapsA2)
    mapsB = [dict(xn=rA1[b]["xn"], t1=rA2[b]["t1"], logits0=rA1[b]["logits0"],
                  bias=ins["L0_ffn_bias"], gate_w=ins["L0_ffn_gate_w"],
                  up_w=ins["L0_ffn_up_w"], down_w=ins["L0_ffn_down_w"],
                  sh_gate_w=ins["L0_ffn_sh_gate_w"], sh_up_w=ins["L0_ffn_sh_up_w"],
                  sh_down_w=ins["L0_ffn_sh_down_w"], sel32=SEL32, ident=IDENT)
             for b in range(B)]
    rB = run_launch("B", build_B, mapsB)
    mapsC1a = [dict(x1=rB[b]["x1"], Wq=ins["L1_attn_Wq"], Wk=ins["L1_attn_Wk"],
                    Wv=ins["L1_attn_Wv"], Wg=ins["L1_attn_Wg"], Wb=ins["L1_attn_Wb"],
                    Wa=ins["L1_attn_Wa"], A_log=ins["L1_attn_A_log"],
                    dt_bias=ins["L1_attn_dt_bias"], conv_q_w=ins["L1_attn_conv_q_w"],
                    conv_k_w=ins["L1_attn_conv_k_w"], conv_v_w=ins["L1_attn_conv_v_w"],
                    norm1_w=ins["L1_attn_norm1_w"]) for b in range(B)]
    rC1a = run_launch("C1a", build_C1a, mapsC1a)
    mapsC1b = [dict(qS=rC1a[b]["qS"], kS=rC1a[b]["kS"], vS=rC1a[b]["vS"],
                    gS=rC1a[b]["gS"], betaS=rC1a[b]["betaS"], cumS=rC1a[b]["cumS"],
                    xn1=rC1a[b]["xn1"], onorm_w=ins["L1_attn_onorm_w"],
                    Wo=ins["L1_attn_Wo"], sel32=SEL32, ident=IDENT, masks=MASKS)
               for b in range(B)]
    rC1b = run_launch("C1b", build_C1b, mapsC1b)
    mapsC2 = [dict(xmid=rC1b[b]["xmid"], x1=rB[b]["x1"], norm2_w=ins["L1_attn_norm2_w"],
                   Wgate=ins["L1_attn_Wgate"], Wup=ins["L1_attn_Wup"],
                   Wdown=ins["L1_attn_Wdown"]) for b in range(B)]
    rC2 = run_launch("C2", build_C2, mapsC2)
    mapsD1 = [dict(xn1=rC1a[b]["xn1"], attn_norm_w=ins["L1_ffn_attn_norm_w"],
                   ffn_norm_w=ins["L1_ffn_ffn_norm_w"], qkv_wP=qkvwP, qkb64=qkbP, vb128=vbbP,
                   proj_w=ins["L1_ffn_proj_w"], proj_b=ins["L1_ffn_proj_b"],
                   attn_sink=ins["L1_ffn_attn_sink"], gate_w=ins["L1_ffn_gate_w"],
                   gate_b=ins["L1_ffn_gate_b"], cosS=COSS, sinS=SINS, ident=IDENT,
                   amasks=AMASKS) for b in range(B)]
    rD1 = run_launch("D1", build_D1, mapsD1)
    mapsD2 = [dict(h2fS=rD1[b]["h2fS"], glogS=rD1[b]["glogS"], xpS=rD1[b]["xpS"],
                   t2=rC2[b]["t2"], W_up=ins["L1_ffn_W_up"], W_down=ins["L1_ffn_W_down"],
                   sel32=SEL32, ident=IDENT) for b in range(B)]
    rD2 = run_launch("D2", build_D2, mapsD2)
    out = np.stack([np.asarray(rD2[b]["out"]) for b in range(B)]).astype(np.float32)
    return out


# version 02
# speedup vs baseline: 1.0012x; 1.0012x over previous
import numpy as np
import ml_dtypes
import concourse.bass as bass
import concourse.mybir as mybir
from concourse.bass_utils import run_bass_kernel_spmd
from contextlib import ExitStack
from concourse.bass_types import AP as _AP


def _a(x):
    return x if isinstance(x, _AP) or isinstance(x, (int, float)) or x is None else x[:]

F32 = mybir.dt.float32
BF16 = mybir.dt.bfloat16
AF = mybir.ActivationFunctionType
ALU = mybir.AluOpType
AX = mybir.AxisListType
BF = ml_dtypes.bfloat16

T = 4096
D = 1024
NCORES = 8


class Prog:

    ENGINES = ("sync", "scalar", "vector", "tensor", "gpsimd")

    def __init__(self, name, fused=False):
        self.nc = bass.Bass("TRN2", target_bir_lowering=False)
        self.plan = []
        self.es = ExitStack()
        self._names = {}
        self.fused = fused
        self.drams = {}
        self._pes = None

    def _uname(self, prefix):
        n = self._names.get(prefix, 0)
        self._names[prefix] = n + 1
        return f"{prefix}{n}"

    def dram_in(self, name, shape, dt=F32):
        if name not in self.drams:
            self.drams[name] = self.nc.dram_tensor(
                name, list(shape), dt, kind="ExternalInput").ap()
        return self.drams[name]

    def dram_out(self, name, shape, dt=F32):
        if name not in self.drams:
            kind = "ExternalOutput" if (not self.fused or name == "out") else "Internal"
            self.drams[name] = self.nc.dram_tensor(name, list(shape), dt, kind=kind).ap()
        return self.drams[name]

    def begin_phase(self):
        self._pes = ExitStack()

    def end_phase(self):
        if self._pes is not None:
            self._pes.close()
            self._pes = None

    def sbuf(self, name, shape, dt=F32):
        es = self._pes if self._pes is not None else self.es
        return es.enter_context(self.nc.sbuf_tensor(self._uname(name), list(shape), dt))

    def psum(self, name, shape, dt=F32):
        es = self._pes if self._pes is not None else self.es
        return es.enter_context(self.nc.psum_tensor(self._uname(name), list(shape), dt))

    def op(self, engine, fn, inc=1):
        self.plan.append((engine, fn, inc))

    def dma(self, out, in_, queue="sync"):
        self.op(queue, lambda e, o=out, i=in_: e.dma_start(out=o, in_=i), 16)

    def dma_slow(self, out, in_, queue="sync"):
        self.op(queue, lambda e, o=out, i=in_: e.dma_start(
            out=o, in_=i, allow_slow_non_contiguous=True), 16)

    def tt(self, out, a, b, alu, eng="vector"):
        self.op(eng, lambda e, o=_a(out), x=_a(a), y=_a(b), al=alu: e.tensor_tensor(o, x, y, al))

    def ts(self, out, a, s1, alu, s2=None, alu2=None, eng="vector"):
        if s2 is None:
            self.op(eng, lambda e, o=_a(out), x=_a(a), s=_a(s1), al=alu: e.tensor_scalar(o, x, s, None, al))
        else:
            self.op(eng, lambda e, o=_a(out), x=_a(a), s=_a(s1), t=_a(s2), al=alu, al2=alu2:
                    e.tensor_scalar(o, x, s, t, al, al2))

    def stt(self, out, in0, scalar, in1, op0, op1, eng="vector"):
        self.op(eng, lambda e, o=_a(out), a=_a(in0), s=_a(scalar), b=_a(in1), p0=op0, p1=op1:
                e.scalar_tensor_tensor(o, a, s, b, p0, p1))

    def copy(self, out, in_, eng="vector"):
        self.op(eng, lambda e, o=_a(out), i=_a(in_): e.tensor_copy(o, i))

    def memset(self, ap, v, eng="vector"):
        self.op(eng, lambda e, a=_a(ap), vv=v: e.memset(a, vv))

    def act(self, out, in_, fn, bias=None, scale=None, eng="scalar"):
        kw = {}
        if bias is not None:
            kw["bias"] = bias
        if scale is not None:
            kw["scale"] = scale
        self.op(eng, lambda e, o=_a(out), i=_a(in_), f=fn, k={kk: _a(vv) for kk, vv in kw.items()}: e.activation(o, i, f, **k))

    def red(self, out, in_, alu, eng="vector"):
        self.op(eng, lambda e, o=_a(out), i=_a(in_), a=alu: e.tensor_reduce(o, i, AX.X, a))

    def scan(self, out, d0, d1, init, op0, op1, eng="vector"):
        self.op(eng, lambda e, o=_a(out), a=_a(d0), b=_a(d1), s=_a(init), p=op0, q=op1:
                e.tensor_tensor_scan(o, a, b, s, p, q))

    def mm(self, out, lhsT, rhs, start=True, stop=True):
        self.op("tensor", lambda e, o=_a(out), l=_a(lhsT), r=_a(rhs), s1=start, s2=stop:
                e.matmul(o, l, r, start=s1, stop=s2))

    def transpose(self, out, in_, ident):
        self.op("tensor", lambda e, o=_a(out), i=_a(in_), d=_a(ident): e.transpose(o, i, d))

    def recip(self, out, in_):
        self.op("vector", lambda e, o=_a(out), i=_a(in_): e.reciprocal(o, i))

    def build(self):
        nc = self.nc
        sem_h = self.es.enter_context(nc.semaphore("g"))
        sem2_h = self.es.enter_context(nc.semaphore("g2"))
        totals = []
        c = 0
        for (_, _, inc) in self.plan:
            totals.append(c)
            c += inc
        final = c
        plan = self.plan
        blk = self.es.enter_context(nc.Block())

        use2 = final >= (1 << 31) - 10
        assert not use2, "semaphore overflow"

        def body_for(engname):
            def body(eng):
                last_idx = -10
                for idx, (en, fn, inc) in enumerate(plan):
                    if en != engname:
                        continue
                    eng.wait_ge(sem_h, totals[idx])
                    ins = fn(eng)
                    ins.then_inc(sem_h, inc)
                    last_idx = idx
                if engname == "sync":
                    eng.wait_ge(sem_h, final)
            return body

        blk.sync(body_for("sync"))
        blk.scalar(body_for("scalar"))
        blk.vector(body_for("vector"))
        blk.tensor(body_for("tensor"))
        blk.gpsimd(body_for("gpsimd"))
        self.es.close()
        return nc


def fm_ap(hbm, ft0, nft, t0, tn, tile_t=None):
    r = hbm.rearrange("(f p) t -> p f t", p=128)
    return r[:, ft0:ft0 + nft, t0:t0 + tn]



def ln_tokmajor(P, sx, xn_out, nf, eps_ap, scratch, also_rms=None):
    sq, s1, s2, m, var = scratch
    P.red(s1, sx, ALU.add)
    P.act(sq, sx, AF.Square)
    P.red(s2, sq, ALU.add)
    P.ts(m, s1, 1.0 / D, ALU.mult)
    P.tt(var, m, m, ALU.mult)
    P.stt(var, s2, 1.0 / D, var, ALU.mult, ALU.subtract)
    P.act(var, var, AF.Ln, bias=eps_ap)
    P.act(var, var, AF.Exp, scale=-0.5)
    P.ts(xn_out, sx, m, ALU.subtract, var, ALU.mult)
    if also_rms is not None:
        out2, eps2_ap = also_rms
        P.act(sq, xn_out, AF.Square)
        P.red(s2, sq, ALU.add)
        P.ts(s2, s2, 1.0 / D, ALU.mult)
        P.act(s2, s2, AF.Ln, bias=eps2_ap)
        P.act(s2, s2, AF.Exp, scale=-0.5)
        P.ts(out2, xn_out, s2, ALU.mult)


def softplus_psum(P, psum, bias_col, out, a1, a2):
    P.act(a1, psum, AF.Abs, bias=bias_col)
    P.act(a1, a1, AF.Exp, scale=-1.0)
    P.act(a1, a1, AF.Ln, bias=1.0)
    P.act(a2, psum, AF.Relu, bias=bias_col)
    P.tt(out, a1, a2, ALU.add)



def build_A1(P=None):
    P = P if P is not None else Prog("A1")
    P.begin_phase()
    TC = 128
    NCH = T // TC
    x_h = P.dram_in("x", (T, D))
    ipw_h = P.dram_in("in_proj_w", (D, 4096))
    convw_h = P.dram_in("conv_w", (2048, 4))
    dtw_h = P.dram_in("dt_w", (2048, 2048))
    dtb_h = P.dram_in("dt_b", (2048,))
    bcw_h = P.dram_in("bc_w", (2048, 32))
    rw_h = P.dram_in("router_wT", (D, 16))
    nw_h = P.dram_in("norm_w", (D,))
    ident_h = P.dram_in("ident", (128, 128))

    xn_h = P.dram_out("xn", (D, T), BF16)
    lg_h = P.dram_out("logits0", (T, 16))
    dt_h = P.dram_out("dtS", (2048, T))
    u_h = P.dram_out("uS", (2048, T))
    xc_h = P.dram_out("xcS", (2048, T))
    bcS_h = P.dram_out("bcS", (32, T))

    ipw = P.sbuf("ipw", [128, 8 * 4096], BF16)
    dtw = P.sbuf("dtw", [128, 16 * 2048], BF16)
    bcw = P.sbuf("bcw", [128, 16 * 32], BF16)
    rw = P.sbuf("rw", [128, 8 * 16], F32)
    cw = P.sbuf("cw", [128, 16 * 4], F32)
    dtb = P.sbuf("dtb", [128, 16], F32)
    nwc = P.sbuf("nwc", [128, 8], F32)
    ident = P.sbuf("ident", [128, 128], F32)
    for k in range(8):
        P.dma(ipw[:, k * 4096:(k + 1) * 4096],
              ipw_h.rearrange("(f p) m -> p f m", p=128)[:, k], queue="gpsimd")
    for k in range(16):
        P.dma(dtw[:, k * 2048:(k + 1) * 2048],
              dtw_h.rearrange("(f p) m -> p f m", p=128)[:, k], queue="gpsimd")
    P.dma(bcw[:].rearrange("p (f m) -> p f m", m=32),
          bcw_h.rearrange("(f p) m -> p f m", p=128), queue="gpsimd")
    P.dma(rw[:].rearrange("p (f m) -> p f m", m=16),
          rw_h.rearrange("(f p) m -> p f m", p=128), queue="sync")
    P.dma(cw[:].rearrange("p (f m) -> p f m", m=4),
          convw_h.rearrange("(f p) m -> p f m", p=128), queue="sync")
    P.dma_slow(dtb[:], dtb_h.rearrange("(f p) -> p f", p=128), queue="sync")
    P.dma_slow(nwc[:], nw_h.rearrange("(f p) -> p f", p=128), queue="sync")
    P.dma(ident[:], ident_h[:], queue="sync")

    sx = P.sbuf("sx", [128, D])
    xn_tok = P.sbuf("xntok", [128, D])
    xr_tok = P.sbuf("xrtok", [128, D])
    sq = P.sbuf("sq", [128, D])
    s1 = P.sbuf("s1", [128, 1]); s2 = P.sbuf("s2", [128, 1])
    mcol = P.sbuf("mcol", [128, 1]); vcol = P.sbuf("vcol", [128, 1])
    epsln = P.sbuf("epsln", [128, 1]); epsrms = P.sbuf("epsrms", [128, 1])
    xnf = P.sbuf("xnf", [128, 8 * TC])
    xnb = P.sbuf("xnb", [128, 8 * TC], BF16)
    xrb = P.sbuf("xrb", [128, 8 * TC], BF16)
    lgs = P.sbuf("lgs", [16, TC], F32)
    lgt = P.sbuf("lgt", [128, 16], F32)
    sil = P.sbuf("sil", [128, TC])
    xi = P.sbuf("xi", [128, 16 * (TC + 3)])
    xc = P.sbuf("xc", [128, 16 * TC])
    xcb = P.sbuf("xcb", [128, 16 * TC], BF16)
    dts = P.sbuf("dts", [128, 16 * TC])
    us = P.sbuf("us", [128, 16 * TC])
    a1s = P.sbuf("a1s", [128, TC]); a2s = P.sbuf("a2s", [128, TC])
    bcs = P.sbuf("bcs", [32, TC], F32)

    pT = P.psum("pT", [128, 512])
    pA = P.psum("pA", [128, TC]); pB = P.psum("pB", [128, TC])
    pC = P.psum("pC", [128, TC]); pD = P.psum("pD", [128, TC])
    pL = P.psum("pL", [16, TC])
    pBC = P.psum("pBC", [32, TC])

    P.memset(epsln[:], 1e-5)
    P.memset(epsrms[:], 1e-6)
    P.memset(xi[:], 0.0)

    scratch = (sq, s1, s2, mcol, vcol)
    x_r = x_h.rearrange("(n p) d -> n p d", p=128)

    for c in range(NCH):
        P.dma(sx[:], x_r[c])
        ln_tokmajor(P, sx[:], xn_tok[:], 8, epsln[:, 0:1],
                    scratch, also_rms=(xr_tok[:], epsrms[:, 0:1]))
        for ft in range(0, 8, 4):
            for j in range(4):
                P.transpose(pT[:, j * 128:(j + 1) * 128],
                            xn_tok[:, (ft + j) * 128:(ft + j + 1) * 128], ident[:])
            for j in range(4):
                P.copy(xnf[:, (ft + j) * TC:(ft + j) * TC + 128],
                       pT[:, j * 128:(j + 1) * 128])
            for j in range(4):
                P.transpose(pT[:, j * 128:(j + 1) * 128],
                            xr_tok[:, (ft + j) * 128:(ft + j + 1) * 128], ident[:])
            for j in range(4):
                P.ts(xrb[:, (ft + j) * TC:(ft + j) * TC + 128],
                     pT[:, j * 128:(j + 1) * 128], nwc[:, ft + j:ft + j + 1], ALU.mult)
        P.copy(xnb[:], xnf[:])
        P.dma(fm_ap(xn_h, 0, 8, c * TC, TC),
              xnb[:].rearrange("p (f t) -> p f t", t=TC))
        for k in range(8):
            P.mm(pL[:], rw[:, k * 16:(k + 1) * 16], xnf[:, k * TC:(k + 1) * TC],
                 start=(k == 0), stop=(k == 7))
        P.copy(lgs[:], pL[:])
        P.transpose(pT[:, 0:16], lgs[:], ident[0:16, 0:16])
        P.copy(lgt[:], pT[:, 0:16])
        P.dma(lg_h[c * 128:(c + 1) * 128], lgt[:])
        for m in range(16):
            for k in range(8):
                P.mm(pA[:], ipw[:, k * 4096 + m * 128: k * 4096 + (m + 1) * 128],
                     xrb[:, k * TC:(k + 1) * TC], start=(k == 0), stop=(k == 7))
            for k in range(8):
                P.mm(pB[:], ipw[:, k * 4096 + (m + 16) * 128: k * 4096 + (m + 17) * 128],
                     xrb[:, k * TC:(k + 1) * TC], start=(k == 0), stop=(k == 7))
            P.act(sil[:], pB[:], AF.Silu)
            o = m * (TC + 3)
            P.copy(xi[:, o:o + 3], xi[:, o + TC:o + TC + 3])
            P.tt(xi[:, o + 3:o + 3 + TC], pA[:], sil[:], ALU.mult)
        for m in range(16):
            o = m * (TC + 3)
            oc = m * TC
            P.ts(xc[:, oc:oc + TC], xi[:, o:o + TC], cw[:, m * 4:m * 4 + 1], ALU.mult)
            for i in range(1, 4):
                P.stt(xc[:, oc:oc + TC], xi[:, o + i:o + i + TC],
                      cw[:, m * 4 + i:m * 4 + i + 1], xc[:, oc:oc + TC],
                      ALU.mult, ALU.add)
            P.copy(xcb[:, oc:oc + TC], xc[:, oc:oc + TC])
        for m in range(16):
            pm = pC if (m % 2 == 0) else pD
            for k in range(16):
                P.mm(pm[:], dtw[:, k * 2048 + m * 128: k * 2048 + (m + 1) * 128],
                     xcb[:, k * TC:(k + 1) * TC], start=(k == 0), stop=(k == 15))
            o = m * TC
            softplus_psum(P, pm[:], dtb[:, m:m + 1], dts[:, o:o + TC], a1s[:], a2s[:])
            P.tt(us[:, o:o + TC], dts[:, o:o + TC], xc[:, o:o + TC], ALU.mult)
        for k in range(16):
            P.mm(pBC[:], bcw[:, k * 32:(k + 1) * 32], xcb[:, k * TC:(k + 1) * TC],
                 start=(k == 0), stop=(k == 15))
        P.copy(bcs[:], pBC[:])
        P.dma(bcS_h[:, c * TC:(c + 1) * TC], bcs[:])
        for (hb, sb_) in ((dt_h, dts), (u_h, us), (xc_h, xc)):
            P.dma(fm_ap(hb, 0, 16, c * TC, TC),
                  sb_[:].rearrange("p (f t) -> p f t", t=TC))
    P.end_phase()
    return P



def build_A2(P=None):
    P = P if P is not None else Prog("A2")
    P.begin_phase()
    TS = 512
    NCH = T // TS
    x_h = P.dram_in("x", (T, D))
    dt_h = P.dram_in("dtS", (2048, T))
    u_h = P.dram_in("uS", (2048, T))
    xc_h = P.dram_in("xcS", (2048, T))
    bc_h = P.dram_in("bcS", (32, T))
    sel_h = P.dram_in("sel32", (32, 32 * 128))
    alog_h = P.dram_in("A_log", (2048, 16))
    dskip_h = P.dram_in("D_skip", (2048,))
    opw_h = P.dram_in("out_proj_w", (2048, D))
    ident_h = P.dram_in("ident", (128, 128))
    t1_h = P.dram_out("t1", (D, T))

    opw = P.sbuf("opw", [128, 16 * D], BF16)
    for k in range(16):
        P.dma(opw[:, k * D:(k + 1) * D],
              opw_h.rearrange("(f p) m -> p f m", p=128)[:, k], queue="gpsimd")
    expA = P.sbuf("expA", [128, 16 * 16], F32)
    P.dma(expA[:].rearrange("p (f s) -> p f s", s=16),
          alog_h.rearrange("(f p) s -> p f s", p=128))
    P.act(expA[:], expA[:], AF.Exp)
    dsk = P.sbuf("dsk", [128, 16], F32)
    P.dma_slow(dsk[:], dskip_h.rearrange("(f p) -> p f", p=128))
    ident = P.sbuf("ident", [128, 128], F32)
    P.dma(ident[:], ident_h[:])
    sel = P.sbuf("sel", [32, 32 * 128], F32)
    P.dma(sel[:], sel_h[:])

    hstate = P.sbuf("hstate", [128, 256], F32)
    P.memset(hstate[:], 0.0)

    HT = 8
    dts = P.sbuf("dts", [128, HT * TS])
    us = P.sbuf("us", [128, HT * TS])
    xcs = P.sbuf("xcs", [128, HT * TS])
    yacc = P.sbuf("yacc", [128, HT * TS])
    ys = P.sbuf("ys", [128, 16 * TS], BF16)
    bcs = P.sbuf("bcs", [32, TS], F32)
    bmr = P.sbuf("bmr", [128, TS]); cmr = P.sbuf("cmr", [128, TS])
    tmp = P.sbuf("tmp", [128, TS]); ab = P.sbuf("ab", [128, TS])
    d1 = P.sbuf("d1", [128, TS]); hh = P.sbuf("hh", [128, TS])
    pr = P.sbuf("pr", [128, TS])
    xT = P.sbuf("xT", [128, 8 * TS])
    sx = P.sbuf("sx", [128, D])
    t1s = P.sbuf("t1s", [128, 8 * TS])

    pb1 = P.psum("pb1", [128, TS])
    pb2 = P.psum("pb2", [128, TS])
    pt1 = [P.psum("pt1", [128, TS]) for _ in range(6)]

    x_r = x_h.rearrange("(n p) d -> n p d", p=128)

    for c in range(NCH):
        P.dma(bcs[:], bc_h[:, c * TS:(c + 1) * TS])
        for tt_ in range(4):
            P.dma(sx[:], x_r[c * 4 + tt_])
            for ft in range(0, 8, 4):
                for j in range(4):
                    P.transpose(pb2[:, j * 128:(j + 1) * 128],
                                sx[:, (ft + j) * 128:(ft + j + 1) * 128], ident[:])
                for j in range(4):
                    P.copy(xT[:, (ft + j) * TS + tt_ * 128:(ft + j) * TS + tt_ * 128 + 128],
                           pb2[:, j * 128:(j + 1) * 128])
        for half in range(2):
            for hb, sb_ in ((dt_h, dts), (u_h, us), (xc_h, xcs)):
                P.dma(sb_[:].rearrange("p (f t) -> p f t", t=TS),
                      fm_ap(hb, half * 8, 8, c * TS, TS))
            for m in range(HT):
                P.memset(yacc[:, m * TS:(m + 1) * TS], 0.0)
            for s in range(16):
                P.mm(pb1[:], sel[:, s * 128:(s + 1) * 128], bcs[:])
                P.copy(bmr[:], pb1[:])
                P.mm(pb1[:], sel[:, (16 + s) * 128:(17 + s) * 128], bcs[:])
                P.copy(cmr[:], pb1[:])
                for m in range(HT):
                    dt_t = dts[:, m * TS:(m + 1) * TS]
                    u_t = us[:, m * TS:(m + 1) * TS]
                    ft = half * 8 + m
                    st_col = hstate[:, s * 16 + ft: s * 16 + ft + 1]
                    P.ts(tmp[:], dt_t, expA[:, ft * 16 + s: ft * 16 + s + 1],
                         ALU.mult, eng="gpsimd")
                    P.act(ab[:], tmp[:], AF.Exp, scale=-1.0)
                    P.tt(d1[:], u_t, bmr[:], ALU.mult)
                    P.scan(hh[:], ab[:], d1[:], st_col, ALU.mult, ALU.add)
                    P.copy(st_col, hh[:, TS - 1:TS])
                    P.tt(pr[:], hh[:], cmr[:], ALU.mult)
                    P.tt(yacc[:, m * TS:(m + 1) * TS], yacc[:, m * TS:(m + 1) * TS],
                         pr[:], ALU.add, eng="gpsimd")
            for m in range(HT):
                ft = half * 8 + m
                P.stt(ys[:, ft * TS:(ft + 1) * TS], xcs[:, m * TS:(m + 1) * TS],
                      dsk[:, ft:ft + 1], yacc[:, m * TS:(m + 1) * TS],
                      ALU.mult, ALU.add)
        for grp, mlist in ((0, range(6)), (1, range(6, 8))):
            for mi, m in enumerate(mlist):
                for k in range(16):
                    P.mm(pt1[mi][:], opw[:, k * D + m * 128: k * D + (m + 1) * 128],
                         ys[:, k * TS:(k + 1) * TS], start=(k == 0), stop=(k == 15))
            for mi, m in enumerate(mlist):
                P.tt(t1s[:, m * TS:(m + 1) * TS], pt1[mi][:],
                     xT[:, m * TS:(m + 1) * TS], ALU.add)
        P.dma(fm_ap(t1_h, 0, 8, c * TS, TS),
              t1s[:].rearrange("p (f t) -> p f t", t=TS))
    P.end_phase()
    return P




def emit_first_argmax(P, work, scr):
    eq, cum, zer, first, m = scr
    P.red(m, work, ALU.max)
    P.ts(eq, work, m[:, 0:1], ALU.is_equal)
    P.scan(cum, eq, zer, 0.0, ALU.add, ALU.add)
    P.ts(cum, cum, 1.0, ALU.is_equal)
    P.tt(first, eq, cum, ALU.mult)
    return first, m


def build_B(P=None):
    P = P if P is not None else Prog("B")
    P.begin_phase()
    TCH = 512
    HALF = T // 2
    xn_h = P.dram_in("xn", (D, T), BF16)
    t1_h = P.dram_in("t1", (D, T))
    lg_h = P.dram_in("logits0", (T, 16))
    bias_h = P.dram_in("bias", (16,))
    gw_h = P.dram_in("gate_w", (16, D, D))
    uw_h = P.dram_in("up_w", (16, D, D))
    dw_h = P.dram_in("down_w", (16, D, D))
    shg_h = P.dram_in("sh_gate_w", (D, 2048))
    shu_h = P.dram_in("sh_up_w", (D, 2048))
    shd_h = P.dram_in("sh_down_w", (2048, D))
    sel_h = P.dram_in("sel32", (32, 32 * 128))
    ident_h = P.dram_in("ident", (128, 128))
    x1_h = P.dram_out("x1", (D, T))

    ident = P.sbuf("ident", [128, 128], F32)
    P.dma(ident[:], ident_h[:])
    selb = P.sbuf("selb", [32, 16 * 128], BF16)
    P.dma(selb[:], sel_h[0:32, 0:16 * 128], queue="gpsimd")

    lgt = P.sbuf("lgt", [128, 16], F32)
    sco = P.sbuf("sco", [128, 16], F32)
    sb_ = P.sbuf("sb_", [128, 16], F32)
    gsc = P.sbuf("gsc", [128, 4], F32)
    gmask = P.sbuf("gmask", [128, 4], F32)
    emask = P.sbuf("emask", [128, 16], F32)
    msk = P.sbuf("msk", [128, 16], F32)
    comb = P.sbuf("comb", [128, 16], F32)
    wsum = P.sbuf("wsum", [128, 1], F32)
    twc = P.sbuf("twc", [128, 1], F32)
    eq = P.sbuf("eq", [128, 16], F32)
    cum = P.sbuf("cum", [128, 16], F32)
    zer = P.sbuf("zer", [128, 16], F32)
    first = P.sbuf("first", [128, 16], F32)
    mm_ = P.sbuf("mm_", [128, 1], F32)
    brep = P.sbuf("brep", [128, 16], F32)
    combT = P.sbuf("combT", [16, T], BF16)
    pR = P.psum("pR", [128, 128])

    P.memset(zer[:], 0.0)
    ones1 = P.sbuf("ones1", [1, 128], F32)
    P.memset(ones1[:], 1.0)
    P.dma(brep[0:1, :], bias_h.unsqueeze(0))
    P.mm(pR[:, 0:16], ones1[:], brep[0:1, :])
    P.copy(brep[:], pR[:, 0:16])

    for tt_ in range(T // 128):
        P.dma(lgt[:], lg_h[tt_ * 128:(tt_ + 1) * 128])
        P.act(sco[:], lgt[:], AF.Sigmoid)
        P.tt(sb_[:], sco[:], brep[:], ALU.add)
        scr4 = (eq[:, 0:4], cum[:, 0:4], zer[:, 0:4], first[:, 0:4], mm_)
        for g in range(4):
            sg = sb_[:, g * 4:(g + 1) * 4]
            P.copy(msk[:, 0:4], sg)
            f, m = emit_first_argmax(P, msk[:, 0:4], scr4)
            P.copy(gsc[:, g:g + 1], m)
            P.stt(msk[:, 0:4], f, -1e30, msk[:, 0:4], ALU.mult, ALU.add)
            P.red(mm_, msk[:, 0:4], ALU.max)
            P.tt(gsc[:, g:g + 1], gsc[:, g:g + 1], mm_, ALU.add)
        P.memset(gmask[:], 0.0)
        P.copy(msk[:, 0:4], gsc[:])
        for _ in range(2):
            f, m = emit_first_argmax(P, msk[:, 0:4], scr4)
            P.tt(gmask[:], gmask[:], f, ALU.add)
            P.stt(msk[:, 0:4], f, -1e30, msk[:, 0:4], ALU.mult, ALU.add)
        P.copy(emask[:].rearrange("p (g r) -> p g r", r=4),
               gmask[:].unsqueeze(2).broadcast_to([128, 4, 4]))
        P.ts(msk[:], sb_[:], 1.0, ALU.add)
        P.tt(msk[:], msk[:], emask[:], ALU.mult)
        P.ts(msk[:], msk[:], 1.0, ALU.subtract)
        P.memset(comb[:], 0.0)
        P.memset(wsum[:], 0.0)
        scr16 = (eq[:], cum[:], zer[:], first[:], mm_)
        for _ in range(4):
            f, m = emit_first_argmax(P, msk[:], scr16)
            P.tt(eq[:], f, sco[:], ALU.mult)
            P.red(twc, eq[:], ALU.add)
            P.stt(comb[:], f, twc[:, 0:1], comb[:], ALU.mult, ALU.add)
            P.tt(wsum[:], wsum[:], twc, ALU.add)
            P.stt(msk[:], f, -1e30, msk[:], ALU.mult, ALU.add)
        P.ts(wsum[:], wsum[:], 1e-20, ALU.add)
        P.recip(twc, wsum[:])
        P.ts(twc, twc, 2.5, ALU.mult)
        P.ts(comb[:], comb[:], twc[:, 0:1], ALU.mult)
        P.transpose(pR[0:16, :], comb[:], ident[:])
        P.copy(combT[:, tt_ * 128:(tt_ + 1) * 128], pR[0:16, :])

    NK = 8
    wz = P.sbuf("wz", [128, 49152], BF16)
    GATE, UP, DOWN = 0, 8192, 16384
    SHG, SHU, SHD = 0, 16384, 32768
    xnc = P.sbuf("xnc", [128, 8 * TCH], BF16)
    hub = P.sbuf("hub", [128, 8 * TCH], BF16)
    silb = P.sbuf("silb", [128, TCH], F32)
    crep = P.sbuf("crep", [128, TCH], F32)
    acc = P.sbuf("acc", [128, 8 * HALF], F32)
    t1b = P.sbuf("t1b", [128, 8 * 256], F32)
    pG = P.psum("pG", [128, TCH]); pU = P.psum("pU", [128, TCH])
    pC = P.psum("pC", [128, TCH])
    pD1 = P.psum("pD1", [128, TCH]); pD2 = P.psum("pD2", [128, TCH])

    gw_r = gw_h.rearrange("e (f p) m -> e p f m", p=128)
    uw_r = uw_h.rearrange("e (f p) m -> e p f m", p=128)
    dw_r = dw_h.rearrange("e (f p) m -> e p f m", p=128)

    def expert_mm(e_, h_, sub):
        t0 = h_ * HALF + sub * TCH
        P.dma(xnc[:].rearrange("p (f t) -> p f t", t=TCH), fm_ap(xn_h, 0, 8, t0, TCH))
        for mt in range(8):
            for k in range(NK):
                P.mm(pG[:], wz[:, GATE + k * D + mt * 128: GATE + k * D + (mt + 1) * 128],
                     xnc[:, k * TCH:(k + 1) * TCH], start=(k == 0), stop=(k == NK - 1))
            for k in range(NK):
                P.mm(pU[:], wz[:, UP + k * D + mt * 128: UP + k * D + (mt + 1) * 128],
                     xnc[:, k * TCH:(k + 1) * TCH], start=(k == 0), stop=(k == NK - 1))
            P.act(silb[:], pG[:], AF.Silu)
            P.tt(hub[:, mt * TCH:(mt + 1) * TCH], silb[:], pU[:], ALU.mult)
        P.mm(pC[:], selb[0:16, e_ * 128:(e_ + 1) * 128], combT[:, t0:t0 + TCH])
        P.copy(crep[:], pC[:])
        for mt in range(8):
            P.tt(hub[:, mt * TCH:(mt + 1) * TCH], hub[:, mt * TCH:(mt + 1) * TCH],
                 crep[:], ALU.mult)
        for m in range(8):
            pd = pD1 if m % 2 == 0 else pD2
            for k in range(8):
                P.mm(pd[:], wz[:, DOWN + k * D + m * 128: DOWN + k * D + (m + 1) * 128],
                     hub[:, k * TCH:(k + 1) * TCH], start=(k == 0), stop=(k == 7))
            ao = m * (HALF) + sub * TCH
            P.tt(acc[:, ao:ao + TCH], acc[:, ao:ao + TCH], pd[:], ALU.add)

    def shared_mm(h_, sub):
        t0 = h_ * HALF + sub * TCH
        P.dma(xnc[:].rearrange("p (f t) -> p f t", t=TCH), fm_ap(xn_h, 0, 8, t0, TCH))
        for grp in range(2):
            for mt in range(8):
                mg = grp * 8 + mt
                for k in range(8):
                    P.mm(pG[:], wz[:, SHG + k * 2048 + mg * 128: SHG + k * 2048 + (mg + 1) * 128],
                         xnc[:, k * TCH:(k + 1) * TCH], start=(k == 0), stop=(k == 7))
                for k in range(8):
                    P.mm(pU[:], wz[:, SHU + k * 2048 + mg * 128: SHU + k * 2048 + (mg + 1) * 128],
                         xnc[:, k * TCH:(k + 1) * TCH], start=(k == 0), stop=(k == 7))
                P.act(silb[:], pG[:], AF.Silu)
                P.tt(hub[:, mt * TCH:(mt + 1) * TCH], silb[:], pU[:], ALU.mult)
            for m in range(8):
                pd = pD1 if m % 2 == 0 else pD2
                for k in range(8):
                    kg = grp * 8 + k
                    P.mm(pd[:], wz[:, SHD + kg * D + m * 128: SHD + kg * D + (m + 1) * 128],
                         hub[:, k * TCH:(k + 1) * TCH], start=(k == 0), stop=(k == 7))
                ao = m * HALF + sub * TCH
                if grp == 0:
                    P.copy(acc[:, ao:ao + TCH], pd[:])
                else:
                    P.tt(acc[:, ao:ao + TCH], acc[:, ao:ao + TCH], pd[:], ALU.add)

    for h_ in range(2):
        P.dma(wz[:, SHG:SHG + 16384].rearrange("p (f m) -> p f m", m=2048),
              shg_h.rearrange("(f p) m -> p f m", p=128), queue="gpsimd")
        P.dma(wz[:, SHU:SHU + 16384].rearrange("p (f m) -> p f m", m=2048),
              shu_h.rearrange("(f p) m -> p f m", p=128), queue="gpsimd")
        P.dma(wz[:, SHD:SHD + 16384].rearrange("p (f m) -> p f m", m=1024),
              shd_h.rearrange("(f p) m -> p f m", p=128), queue="gpsimd")
        for sub in range(4):
            shared_mm(h_, sub)
        for e_ in range(16):
            P.dma(wz[:, GATE:GATE + 8192].rearrange("p (f m) -> p f m", m=1024),
                  gw_r[e_], queue="gpsimd")
            P.dma(wz[:, UP:UP + 8192].rearrange("p (f m) -> p f m", m=1024),
                  uw_r[e_], queue="gpsimd")
            P.dma(wz[:, DOWN:DOWN + 8192].rearrange("p (f m) -> p f m", m=1024),
                  dw_r[e_], queue="gpsimd")
            for sub in range(4):
                expert_mm(e_, h_, sub)
        for sub in range(8):
            t0 = h_ * HALF + sub * 256
            P.dma(t1b[:].rearrange("p (f t) -> p f t", t=256), fm_ap(t1_h, 0, 8, t0, 256))
            for m in range(8):
                ao = m * HALF + sub * 256
                P.tt(t1b[:, m * 256:(m + 1) * 256], t1b[:, m * 256:(m + 1) * 256],
                     acc[:, ao:ao + 256], ALU.add)
            P.dma(fm_ap(x1_h, 0, 8, t0, 256),
                  t1b[:].rearrange("p (f t) -> p f t", t=256))
    P.end_phase()
    return P




def fm_norm(P, xt, nf, TCH, sq, rowbuf, pS, ones128, epsap, out_rows):
    for k in range(nf):
        P.act(sq[:, 0:TCH], xt[:, k * TCH:(k + 1) * TCH], AF.Square)
        P.mm(pS[0:1, 0:TCH], ones128[:, 0:1], sq[:, 0:TCH],
             start=(k == 0), stop=(k == nf - 1))
    P.ts(rowbuf[0:1, 0:TCH], pS[0:1, 0:TCH], 1.0 / (nf * 128), ALU.mult)
    P.act(rowbuf[0:1, 0:TCH], rowbuf[0:1, 0:TCH], AF.Ln, bias=epsap)
    P.act(out_rows, rowbuf[0:1, 0:TCH], AF.Exp, scale=-0.5)


def build_C1a(P=None):
    P = P if P is not None else Prog("C1a")
    P.begin_phase()
    TO = 512
    NS = T // TO
    x1_h = P.dram_in("x1", (D, T))
    wq_h = P.dram_in("Wq", (D, 768))
    wk_h = P.dram_in("Wk", (D, 768))
    wv_h = P.dram_in("Wv", (D, 1536))
    wg_h = P.dram_in("Wg", (D, 1536))
    wb_h = P.dram_in("Wb", (D, 8))
    wa_h = P.dram_in("Wa", (D, 8))
    alog_h = P.dram_in("A_log1", (8,))
    dtb_h = P.dram_in("dt_bias", (8,))
    cq_h = P.dram_in("conv_q_w", (768, 4))
    ck_h = P.dram_in("conv_k_w", (768, 4))
    cv_h = P.dram_in("conv_v_w", (1536, 4))
    n1_h = P.dram_in("norm1_w", (D,))

    xn1_h = P.dram_out("xn1", (D, T), BF16)
    q_h = P.dram_out("qS", (768, T), BF16)
    k_h = P.dram_out("kS", (768, T), BF16)
    v_h = P.dram_out("vS", (1536, T), BF16)
    g_h = P.dram_out("gS", (1536, T), BF16)
    beta_h = P.dram_out("betaS", (8, T))
    cum_h = P.dram_out("cumS", (8, T))

    wq = P.sbuf("wq", [128, 8 * 768], BF16)
    wk = P.sbuf("wk", [128, 8 * 768], BF16)
    wv = P.sbuf("wv", [128, 8 * 1536], BF16)
    wg = P.sbuf("wg", [128, 8 * 1536], BF16)
    wba = P.sbuf("wba", [128, 8 * 16], BF16)
    for hname, hh, sb_, m in (("q", wq_h, wq, 768), ("k", wk_h, wk, 768),
                              ("v", wv_h, wv, 1536), ("g", wg_h, wg, 1536)):
        P.dma(sb_[:].rearrange("p (f m) -> p f m", m=m),
              hh.rearrange("(f p) m -> p f m", p=128), queue="gpsimd")
    P.dma(wba[:].rearrange("p (f m) -> p f m", m=16)[:, :, 0:8],
          wb_h.rearrange("(f p) m -> p f m", p=128), queue="gpsimd")
    P.dma(wba[:].rearrange("p (f m) -> p f m", m=16)[:, :, 8:16],
          wa_h.rearrange("(f p) m -> p f m", p=128), queue="gpsimd")
    cqw = P.sbuf("cqw", [96, 8 * 4], F32)
    ckw = P.sbuf("ckw", [96, 8 * 4], F32)
    cvw = P.sbuf("cvw", [96, 16 * 4], F32)
    P.dma_slow(cqw[:].rearrange("p (f m) -> p f m", m=4),
               cq_h.rearrange("(f p) m -> p f m", p=96))
    P.dma_slow(ckw[:].rearrange("p (f m) -> p f m", m=4),
               ck_h.rearrange("(f p) m -> p f m", p=96))
    P.dma_slow(cvw[:].rearrange("p (f m) -> p f m", m=4),
               cv_h.rearrange("(f p) m -> p f m", p=96))
    n1c = P.sbuf("n1c", [128, 8], F32)
    P.dma_slow(n1c[:], n1_h.rearrange("(f p) -> p f", p=128))
    nega = P.sbuf("nega", [8, 1], F32)
    P.dma(nega[:], alog_h.unsqueeze(1))
    P.act(nega[:], nega[:], AF.Exp)
    P.ts(nega[:], nega[:], -1.0, ALU.mult)
    dtbc = P.sbuf("dtbc", [8, 1], F32)
    P.dma(dtbc[:], dtb_h.unsqueeze(1))

    ones128 = P.sbuf("ones128", [128, 1], F32)
    P.memset(ones128[:], 1.0)
    ones1 = P.sbuf("ones1", [1, 128], F32)
    P.memset(ones1[:], 1.0)
    epsr = P.sbuf("epsr", [1, 1], F32)
    P.memset(epsr[:], 1e-6)
    eps5 = P.sbuf("eps5", [1, 1], F32)
    P.memset(eps5[:], 1e-5)
    epsl2 = P.sbuf("epsl2", [96, 1], F32)
    P.memset(epsl2[:], 1e-6)

    x1t = P.sbuf("x1t", [128, 8 * TO])
    xn1f = P.sbuf("xn1f", [128, 8 * TO])
    xn1b = P.sbuf("xn1b", [128, 8 * TO], BF16)
    hb = P.sbuf("hb", [128, 8 * TO], BF16)
    sq = P.sbuf("sq", [128, TO])
    rowb = P.sbuf("rowb", [1, TO], F32)
    rrep = P.sbuf("rrep", [128, TO], F32)
    qpad = P.sbuf("qpad", [96, TO + 3], F32)
    qtail = P.sbuf("qtail", [96, 48 * 3], F32)
    cnv = P.sbuf("cnv", [96, TO], F32)
    sq96 = P.sbuf("sq96", [96, TO], F32)
    row96 = P.sbuf("row96", [1, TO], F32)
    rs96 = P.sbuf("rs96", [96, TO], F32)
    qb = P.sbuf("qb", [96, 8 * TO], BF16)
    kb = P.sbuf("kb", [96, 8 * TO], BF16)
    vb = P.sbuf("vb", [96, 16 * TO], BF16)
    gb = P.sbuf("gb", [96, 16 * TO], BF16)
    ba = P.sbuf("ba", [16, TO], F32)
    bet = P.sbuf("bet", [8, TO], F32)
    gdec = P.sbuf("gdec", [8, TO], F32)
    cums = P.sbuf("cums", [8, TO], F32)
    zer8 = P.sbuf("zer8", [8, 128], F32)
    a1t = P.sbuf("a1t", [8, TO], F32)
    a2t = P.sbuf("a2t", [8, TO], F32)

    pS = P.psum("pS", [1, TO])
    pM = P.psum("pM", [128, TO])
    pP = P.psum("pP", [96, TO])
    pB = P.psum("pB", [16, TO])

    P.memset(qtail[:], 0.0)
    P.memset(zer8[:], 0.0)

    def proj_conv(wt, nheads, conv_w, tail_base, out_sb, l2norm, scale, s):
        for hh in range(nheads):
            for k in range(8):
                P.mm(pP[:], wt[:, k * (96 * nheads) + hh * 96:
                               k * (96 * nheads) + (hh + 1) * 96],
                     hb[:, k * TO:(k + 1) * TO], start=(k == 0), stop=(k == 7))
            tb = (tail_base + hh) * 3
            P.copy(qpad[:, 0:3], qtail[:, tb:tb + 3])
            P.copy(qpad[:, 3:3 + TO], pP[:])
            P.copy(qtail[:, tb:tb + 3], qpad[:, TO:TO + 3])
            P.ts(cnv[:], qpad[:, 0:TO], conv_w[:, hh * 4:hh * 4 + 1], ALU.mult)
            for i in range(1, 4):
                P.stt(cnv[:], qpad[:, i:i + TO], conv_w[:, hh * 4 + i:hh * 4 + i + 1],
                      cnv[:], ALU.mult, ALU.add)
            P.act(cnv[:], cnv[:], AF.Silu)
            if l2norm:
                P.act(sq96[:], cnv[:], AF.Square)
                P.mm(pS[0:1, :], ones128[0:96, 0:1], sq96[:], start=True, stop=True)
                P.copy(row96[0:1, :], pS[0:1, :])
                P.ts(row96[0:1, :], row96[0:1, :], epsr[0:1, 0:1], ALU.add)
                P.act(row96[0:1, :], row96[0:1, :], AF.Ln)
                P.act(row96[0:1, :], row96[0:1, :], AF.Exp, scale=-0.5)
                if scale != 1.0:
                    P.ts(row96[0:1, :], row96[0:1, :], scale, ALU.mult)
                P.mm(pP[:], ones1[0:1, 0:96], row96[0:1, :], start=True, stop=True)
                P.copy(rs96[:], pP[:])
                P.tt(out_sb[:, hh * TO:(hh + 1) * TO], cnv[:], rs96[:], ALU.mult)
            else:
                P.copy(out_sb[:, hh * TO:(hh + 1) * TO], cnv[:])

    wq_r = wq
    for s in range(NS):
        t0 = s * TO
        P.dma(x1t[:].rearrange("p (f t) -> p f t", t=TO), fm_ap(x1_h, 0, 8, t0, TO))
        for k in range(8):
            P.mm(pS[0:1, :], ones128[:, 0:1], x1t[:, k * TO:(k + 1) * TO],
                 start=(k == 0), stop=(k == 7))
        P.ts(rowb[0:1, :], pS[0:1, :], 1.0 / D, ALU.mult)
        P.mm(pM[:], ones1[:], rowb[0:1, :], start=True, stop=True)
        P.copy(rrep[:], pM[:])
        for k in range(8):
            P.tt(xn1f[:, k * TO:(k + 1) * TO], x1t[:, k * TO:(k + 1) * TO],
                 rrep[:], ALU.subtract)
        for k in range(8):
            P.act(sq[:], xn1f[:, k * TO:(k + 1) * TO], AF.Square)
            P.mm(pS[0:1, :], ones128[:, 0:1], sq[:], start=(k == 0), stop=(k == 7))
        P.ts(rowb[0:1, :], pS[0:1, :], 1.0 / D, ALU.mult)
        P.act(rowb[0:1, :], rowb[0:1, :], AF.Ln, bias=eps5[0:1, 0:1])
        P.act(rowb[0:1, :], rowb[0:1, :], AF.Exp, scale=-0.5)
        P.mm(pM[:], ones1[:], rowb[0:1, :], start=True, stop=True)
        P.copy(rrep[:], pM[:])
        for k in range(8):
            P.tt(xn1f[:, k * TO:(k + 1) * TO], xn1f[:, k * TO:(k + 1) * TO],
                 rrep[:], ALU.mult)
        P.copy(xn1b[:], xn1f[:])
        P.dma(fm_ap(xn1_h, 0, 8, t0, TO), xn1b[:].rearrange("p (f t) -> p f t", t=TO))
        fm_norm(P, xn1f, 8, TO, sq, rowb, pS, ones128, epsr[0:1, 0:1], rowb[0:1, 0:TO])
        P.mm(pM[:], ones1[:], rowb[0:1, :], start=True, stop=True)
        P.copy(rrep[:], pM[:])
        for k in range(8):
            P.tt(sq[:], xn1f[:, k * TO:(k + 1) * TO], rrep[:], ALU.mult)
            P.ts(hb[:, k * TO:(k + 1) * TO], sq[:], n1c[:, k:k + 1], ALU.mult)
        proj_conv(wq, 8, cqw, 0, qb, True, float(96 ** -0.5), s)
        proj_conv(wk, 8, ckw, 8, kb, True, 1.0, s)
        proj_conv(wv, 16, cvw, 16, vb, False, 1.0, s)
        for hh in range(16):
            for k in range(8):
                P.mm(pP[:], wg[:, k * 1536 + hh * 96: k * 1536 + (hh + 1) * 96],
                     hb[:, k * TO:(k + 1) * TO], start=(k == 0), stop=(k == 7))
            P.copy(gb[:, hh * TO:(hh + 1) * TO], pP[:])
        for k in range(8):
            P.mm(pB[:], wba[:, k * 16:(k + 1) * 16], hb[:, k * TO:(k + 1) * TO],
                 start=(k == 0), stop=(k == 7))
        P.copy(ba[:], pB[:])
        P.act(bet[:], ba[0:8, :], AF.Sigmoid)
        for k in range(8):
            P.mm(pB[0:8, :], wba[:, k * 16 + 8:k * 16 + 16],
                 hb[:, k * TO:(k + 1) * TO], start=(k == 0), stop=(k == 7))
        P.act(a1t[:], pB[0:8, :], AF.Abs, bias=dtbc[:, 0:1])
        P.act(a1t[:], a1t[:], AF.Exp, scale=-1.0)
        P.act(a1t[:], a1t[:], AF.Ln, bias=1.0)
        P.act(a2t[:], pB[0:8, :], AF.Relu, bias=dtbc[:, 0:1])
        P.tt(a1t[:], a1t[:], a2t[:], ALU.add)
        P.ts(gdec[:], a1t[:], nega[:, 0:1], ALU.mult)
        for ci in range(TO // 128):
            P.scan(cums[:, ci * 128:(ci + 1) * 128], gdec[:, ci * 128:(ci + 1) * 128],
                   zer8[:], 0.0, ALU.add, ALU.add)
        P.dma(beta_h[:, t0:t0 + TO], bet[:])
        P.dma(cum_h[:, t0:t0 + TO], cums[:])
        for hh, sb_, nf in ((q_h, qb, 8), (k_h, kb, 8), (v_h, vb, 16), (g_h, gb, 16)):
            P.dma(hh.rearrange("(f p) t -> p f t", p=96)[:, :, t0:t0 + TO],
                  sb_[:].rearrange("p (f t) -> p f t", t=TO))
    P.end_phase()
    return P




def build_C1b(P=None):
    P = P if P is not None else Prog("C1b")
    P.begin_phase()
    TO = 512
    NS = T // TO
    C = 128
    q_hb = P.dram_in("qS", (768, T), BF16)
    k_hb = P.dram_in("kS", (768, T), BF16)
    v_hb = P.dram_in("vS", (1536, T), BF16)
    g_hb = P.dram_in("gS", (1536, T), BF16)
    beta_h = P.dram_in("betaS", (8, T))
    cum_h = P.dram_in("cumS", (8, T))
    xn1_h = P.dram_in("xn1", (D, T), BF16)
    onw_h = P.dram_in("onorm_w", (192,))
    wo_h = P.dram_in("Wo", (1536, D))
    sel_h = P.dram_in("sel32", (32, 32 * 128))
    ident_h = P.dram_in("ident", (128, 128))
    masks_h = P.dram_in("masks", (3, 128, 128))
    xmid_h = P.dram_out("xmid", (D, T))

    ident = P.sbuf("ident", [128, 128], F32)
    P.dma(ident[:], ident_h[:])
    identb = P.sbuf("identb", [128, 128], BF16)
    P.dma(identb[:], ident_h[:], queue="gpsimd")
    mlo = P.sbuf("mlo", [128, 128], F32)
    mup = P.sbuf("mup", [128, 128], F32)
    mui = P.sbuf("mui", [128, 128], F32)
    P.dma(mlo[:], masks_h[0]); P.dma(mup[:], masks_h[1]); P.dma(mui[:], masks_h[2])
    sel8 = P.sbuf("sel8", [8, 8 * 128], F32)
    P.dma(sel8[:], sel_h[0:8, 0:8 * 128])
    ones1 = P.sbuf("ones1", [1, 128], F32)
    P.memset(ones1[:], 1.0)
    eps192 = P.sbuf("eps192", [128, 1], F32)
    P.memset(eps192[:], 1e-6)
    onwrep = P.sbuf("onwrep", [128, 192], F32)
    onwrow = P.sbuf("onwrow", [1, 192], F32)
    P.dma(onwrow[:], onw_h.unsqueeze(0))
    pO = P.psum("pO", [128, 192])
    P.mm(pO[:], ones1[:], onwrow[:])
    P.copy(onwrep[:], pO[:])
    wo = P.sbuf("wo", [96, 16 * D], BF16)
    P.dma(wo[:].rearrange("p (f m) -> p f m", m=D),
          wo_h.rearrange("(f p) m -> p f m", p=96), queue="gpsimd")

    qb = P.sbuf("qb", [96, 8 * TO], BF16)
    kb = P.sbuf("kb", [96, 8 * TO], BF16)
    vb = P.sbuf("vb", [96, 16 * TO], BF16)
    gb = P.sbuf("gb", [96, 16 * TO], BF16)
    xn1b = P.sbuf("xn1b", [128, 8 * TO], BF16)
    xn1f = P.sbuf("xn1f", [128, TO], F32)
    bet = P.sbuf("bet", [8, TO], F32)
    cums = P.sbuf("cums", [8, TO], F32)
    cbT = P.sbuf("cbT", [128, 16], F32)
    Sf = P.sbuf("Sf", [96, 8 * 192], F32)
    Sb = P.sbuf("Sb", [96, 8 * 192], BF16)
    P.memset(Sf[:], 0.0)
    ofm = P.sbuf("ofm", [96, 16 * TO], BF16)
    xmid = P.sbuf("xmid", [128, 8 * TO], F32)

    kkS = P.sbuf("kkS", [128, 128], F32)
    kqS = P.sbuf("kqS", [128, 128], F32)
    crS = P.sbuf("crS", [128, 128], F32)
    rS = P.sbuf("rS", [128, 128], F32)
    DN = P.sbuf("DN", [128, 128], F32)
    DT = P.sbuf("DT", [128, 128], F32)
    brS = P.sbuf("brS", [128, 128], F32)
    Nc = P.sbuf("Nc", [128, 128], F32)
    NTc = P.sbuf("NTc", [128, 128], F32)
    Nn = P.sbuf("Nn", [128, 128], F32)
    NTn = P.sbuf("NTn", [128, 128], F32)
    Pm = P.sbuf("Pm", [128, 128], F32)
    Vtok = P.sbuf("Vtok", [128, 192], F32)
    Gtok = P.sbuf("Gtok", [128, 192], F32)
    gKS0 = P.sbuf("gKS0", [128, 192], F32)
    RHS = P.sbuf("RHS", [128, 192], F32)
    Wf = P.sbuf("Wf", [128, 192], F32)
    Wb_ = P.sbuf("Wb_", [128, 192], BF16)
    MTb = P.sbuf("MTb", [128, 128], BF16)
    Ot = P.sbuf("Ot", [128, 192], F32)
    qs_ = P.sbuf("qs_", [128, 192], F32)
    Ktok = P.sbuf("Ktok", [128, 96], BF16)
    oftok = P.sbuf("oftok", [128, 192], BF16)
    gcol = P.sbuf("gcol", [128, 1], F32)
    dkcol = P.sbuf("dkcol", [128, 1], F32)
    grep = P.sbuf("grep", [96, 1], F32)
    ecr = P.sbuf("ecr", [1, 1], F32)
    ocol = P.sbuf("ocol", [128, 1], F32)
    sqO = P.sbuf("sqO", [128, 192], F32)

    pX = P.psum("pX", [128, 128])
    pY = P.psum("pY", [128, 192])
    pO2 = P.psum("pO2", [128, 192])
    pS2 = P.psum("pS2", [96, 192])
    pTb = P.psum("pTb", [128, 128], BF16)
    pG1 = P.psum("pG1", [96, 1])
    pW = P.psum("pW", [128, TO])

    def evac(dst, src, eng="vector"):
        P.copy(dst, src, eng=eng)

    for s in range(NS):
        t0 = s * TO
        for hh, sb_, nf, pp in ((q_hb, qb, 8, 96), (k_hb, kb, 8, 96),
                                (v_hb, vb, 16, 96), (g_hb, gb, 16, 96)):
            P.dma(sb_[:].rearrange("p (f t) -> p f t", t=TO),
                  hh.rearrange("(f p) t -> p f t", p=96)[:, :, t0:t0 + TO])
        P.dma(xn1b[:].rearrange("p (f t) -> p f t", t=TO), fm_ap(xn1_h, 0, 8, t0, TO))
        P.dma(bet[:], beta_h[:, t0:t0 + TO])
        P.dma(cums[:], cum_h[:, t0:t0 + TO])
        for ci in range(TO // C):
            co = ci * C
            P.transpose(pX[:, 0:8], cums[:, co:co + C], ident[0:8, 0:8])
            P.copy(cbT[:, 0:8], pX[:, 0:8])
            P.transpose(pX[:, 0:8], bet[:, co:co + C], ident[0:8, 0:8])
            P.copy(cbT[:, 8:16], pX[:, 0:8])
            for h in range(8):
                Ktc = kb[0:96, h * TO + co: h * TO + co + C]
                Qtc = qb[0:96, h * TO + co: h * TO + co + C]
                ccol = cbT[:, h:h + 1]
                bcol = cbT[:, 8 + h:9 + h]
                P.mm(pX[:], Ktc, Ktc); evac(kkS[:], pX[:])
                P.mm(pX[:], Ktc, Qtc); evac(kqS[:], pX[:])
                P.mm(pX[:], sel8[:, h * 128:(h + 1) * 128], cums[:, co:co + C])
                evac(crS[:], pX[:])
                P.ts(rS[:], crS[:], ccol, ALU.subtract)
                P.act(DN[:], rS[:], AF.Relu)
                P.act(DN[:], DN[:], AF.Exp, scale=-1.0)
                P.act(DT[:], rS[:], AF.Relu, scale=-1.0)
                P.act(DT[:], DT[:], AF.Exp, scale=-1.0)
                P.mm(pX[:], sel8[:, h * 128:(h + 1) * 128], bet[:, co:co + C])
                evac(brS[:], pX[:])
                P.tt(Nc[:], DN[:], kkS[:], ALU.mult)
                P.ts(Nc[:], Nc[:], bcol, ALU.mult)
                P.tt(Nc[:], Nc[:], mlo[:], ALU.mult)
                P.tt(NTc[:], DT[:], kkS[:], ALU.mult)
                P.tt(NTc[:], NTc[:], brS[:], ALU.mult)
                P.tt(NTc[:], NTc[:], mup[:], ALU.mult)
                P.tt(Pm[:], ident[:], NTc[:], ALU.subtract)
                cN, cNT, nN, nNT = Nc, NTc, Nn, NTn
                for lev in range(6):
                    P.mm(pX[:], cNT[:], cN[:]); evac(nN[:], pX[:])
                    P.mm(pX[:], cN[:], cNT[:]); evac(nNT[:], pX[:])
                    cN, nN = nN, cN
                    cNT, nNT = nNT, cNT
                    P.mm(pX[:], cN[:], Pm[:])
                    P.tt(Pm[:], Pm[:], pX[:], ALU.add)
                P.copy(Sb[:, h * 192:(h + 1) * 192], Sf[:, h * 192:(h + 1) * 192])
                P.mm(pY[:], Ktc, Sb[:, h * 192:(h + 1) * 192])
                for sub in range(2):
                    P.transpose(pTb[0:128, 0:96],
                                vb[0:96, (2 * h + sub) * TO + co: (2 * h + sub) * TO + co + C],
                                identb[0:96, 0:96])
                    P.copy(Vtok[:, sub * 96:(sub + 1) * 96], pTb[0:128, 0:96])
                P.act(gcol[:], ccol, AF.Exp)
                P.ts(gKS0[:], pY[:], gcol[:, 0:1], ALU.mult)
                P.tt(RHS[:], Vtok[:], gKS0[:], ALU.subtract)
                P.ts(RHS[:], RHS[:], bcol, ALU.mult)
                P.mm(pY[:], Pm[:], RHS[:])
                evac(Wf[:], pY[:]); P.copy(Wb_[:], Wf[:])
                P.tt(rS[:], DT[:], kqS[:], ALU.mult)
                P.tt(rS[:], rS[:], mui[:], ALU.mult)
                P.copy(MTb[:], rS[:])
                P.mm(pY[:], MTb[:], Wb_[:])
                P.mm(pO2[:], Qtc, Sb[:, h * 192:(h + 1) * 192])
                P.ts(qs_[:], pO2[:], gcol[:, 0:1], ALU.mult)
                P.tt(Ot[:], pY[:], qs_[:], ALU.add)
                P.transpose(pTb[0:128, 0:96], Ktc, identb[0:96, 0:96])
                P.copy(Ktok[:], pTb[0:128, 0:96])
                P.ts(dkcol[:], crS[:, 127:128], ccol, ALU.subtract)
                P.act(dkcol[:], dkcol[:], AF.Exp)
                P.ts(Ktok[:], Ktok[:], dkcol[:, 0:1], ALU.mult)
                P.mm(pS2[:], Ktok[:], Wb_[:])
                P.act(ecr[:], crS[0:1, 127:128], AF.Exp)
                P.mm(pG1[:], ones1[0:1, 0:96], ecr[:])
                evac(grep[:], pG1[:])
                P.stt(Sf[:, h * 192:(h + 1) * 192], Sf[:, h * 192:(h + 1) * 192],
                      grep[:, 0:1], pS2[:], ALU.mult, ALU.add)
                P.act(sqO[:], Ot[:], AF.Square)
                P.red(ocol[:], sqO[:], ALU.add)
                P.ts(ocol[:], ocol[:], 1.0 / 192, ALU.mult)
                P.act(ocol[:], ocol[:], AF.Ln, bias=eps192[:, 0:1])
                P.act(ocol[:], ocol[:], AF.Exp, scale=-0.5)
                P.ts(Ot[:], Ot[:], ocol[:, 0:1], ALU.mult)
                P.tt(Ot[:], Ot[:], onwrep[:], ALU.mult)
                for sub in range(2):
                    P.transpose(pTb[0:128, 0:96],
                                gb[0:96, (2 * h + sub) * TO + co: (2 * h + sub) * TO + co + C],
                                identb[0:96, 0:96])
                    P.copy(Gtok[:, sub * 96:(sub + 1) * 96], pTb[0:128, 0:96])
                P.act(Gtok[:], Gtok[:], AF.Silu)
                P.tt(oftok[:], Ot[:], Gtok[:], ALU.mult)
                for sub in range(2):
                    P.transpose(pTb[0:96, 0:128], oftok[:, sub * 96:(sub + 1) * 96],
                                identb[:])
                    P.copy(ofm[0:96, (2 * h + sub) * TO + co:
                           (2 * h + sub) * TO + co + C], pTb[0:96, 0:128])
        for m in range(8):
            for k in range(16):
                P.mm(pW[:], wo[:, k * D + m * 128: k * D + (m + 1) * 128],
                     ofm[:, k * TO:(k + 1) * TO], start=(k == 0), stop=(k == 15))
            P.copy(xn1f[:], xn1b[:, m * TO:(m + 1) * TO])
            P.tt(xmid[:, m * TO:(m + 1) * TO], pW[:], xn1f[:], ALU.add)
        P.dma(fm_ap(xmid_h, 0, 8, t0, TO), xmid[:].rearrange("p (f t) -> p f t", t=TO))
    P.end_phase()
    return P




def build_C2(P=None):
    P = P if P is not None else Prog("C2")
    P.begin_phase()
    TCH = 512
    HALF = T // 2
    xmid_h = P.dram_in("xmid", (D, T))
    x1_h = P.dram_in("x1", (D, T))
    n2_h = P.dram_in("norm2_w", (D,))
    wg_h = P.dram_in("Wgate", (D, 4096))
    wu_h = P.dram_in("Wup", (D, 4096))
    wd_h = P.dram_in("Wdown", (4096, D))
    t2_h = P.dram_out("t2", (D, T))
    h2_h = P.dram_out("h2aS", (D, T), BF16)

    n2c = P.sbuf("n2c", [128, 8], F32)
    P.dma_slow(n2c[:], n2_h.rearrange("(f p) -> p f", p=128))
    ones128 = P.sbuf("ones128", [128, 1], F32)
    P.memset(ones128[:], 1.0)
    ones1 = P.sbuf("ones1", [1, 128], F32)
    P.memset(ones1[:], 1.0)
    epsr = P.sbuf("epsr", [1, 1], F32)
    P.memset(epsr[:], 1e-6)
    xmt = P.sbuf("xmt", [128, 8 * TCH])
    sq = P.sbuf("sq", [128, TCH])
    rowb = P.sbuf("rowb", [1, TCH], F32)
    rrep = P.sbuf("rrep", [128, TCH], F32)
    h2b = P.sbuf("h2b", [128, 8 * TCH], BF16)
    pS = P.psum("pS", [1, TCH])
    pM = P.psum("pM", [128, TCH])

    for c in range(T // TCH):
        t0 = c * TCH
        P.dma(xmt[:].rearrange("p (f t) -> p f t", t=TCH), fm_ap(xmid_h, 0, 8, t0, TCH))
        fm_norm(P, xmt, 8, TCH, sq, rowb, pS, ones128, epsr[0:1, 0:1], rowb[0:1, 0:TCH])
        P.mm(pM[:], ones1[:], rowb[0:1, :], start=True, stop=True)
        P.copy(rrep[:], pM[:])
        for k in range(8):
            P.tt(sq[:], xmt[:, k * TCH:(k + 1) * TCH], rrep[:], ALU.mult)
            P.ts(h2b[:, k * TCH:(k + 1) * TCH], sq[:], n2c[:, k:k + 1], ALU.mult)
        P.dma(fm_ap(h2_h, 0, 8, t0, TCH), h2b[:].rearrange("p (f t) -> p f t", t=TCH))

    wz = P.sbuf("wz", [128, 24576], BF16)
    GA, UA, DA = 0, 8192, 16384
    acc = P.sbuf("acc", [128, 8 * HALF], F32)
    hub = P.sbuf("hub", [128, 8 * TCH], BF16)
    silb = P.sbuf("silb", [128, TCH], F32)
    pG = P.psum("pG", [128, TCH]); pU = P.psum("pU", [128, TCH])
    pD1 = P.psum("pD1", [128, TCH]); pD2 = P.psum("pD2", [128, TCH])
    wg_r = wg_h.rearrange("(f p) m -> p f m", p=128)
    wu_r = wu_h.rearrange("(f p) m -> p f m", p=128)
    wd_r = wd_h.rearrange("(f p) m -> p f m", p=128)
    t1b = P.sbuf("t1b", [128, 8 * 256], F32)
    t1c = P.sbuf("t1c", [128, 8 * 256], F32)

    for h_ in range(2):
        for grp in range(4):
            P.dma(wz[:, GA:GA + 8192].rearrange("p (f m) -> p f m", m=1024),
                  wg_r[:, :, grp * 1024:(grp + 1) * 1024], queue="gpsimd")
            P.dma(wz[:, UA:UA + 8192].rearrange("p (f m) -> p f m", m=1024),
                  wu_r[:, :, grp * 1024:(grp + 1) * 1024], queue="gpsimd")
            P.dma(wz[:, DA:DA + 8192].rearrange("p (f m) -> p f m", m=1024),
                  wd_r[:, grp * 8:(grp + 1) * 8, :], queue="gpsimd")
            for sub in range(4):
                t0 = h_ * HALF + sub * TCH
                P.dma(h2b[:].rearrange("p (f t) -> p f t", t=TCH),
                      fm_ap(h2_h, 0, 8, t0, TCH))
                for mt in range(8):
                    for k in range(8):
                        P.mm(pG[:], wz[:, GA + k * 1024 + mt * 128: GA + k * 1024 + (mt + 1) * 128],
                             h2b[:, k * TCH:(k + 1) * TCH], start=(k == 0), stop=(k == 7))
                    for k in range(8):
                        P.mm(pU[:], wz[:, UA + k * 1024 + mt * 128: UA + k * 1024 + (mt + 1) * 128],
                             h2b[:, k * TCH:(k + 1) * TCH], start=(k == 0), stop=(k == 7))
                    P.act(silb[:], pG[:], AF.Silu)
                    P.tt(hub[:, mt * TCH:(mt + 1) * TCH], silb[:], pU[:], ALU.mult)
                for m in range(8):
                    pd = pD1 if m % 2 == 0 else pD2
                    for k in range(8):
                        P.mm(pd[:], wz[:, DA + k * 1024 + m * 128: DA + k * 1024 + (m + 1) * 128],
                             hub[:, k * TCH:(k + 1) * TCH], start=(k == 0), stop=(k == 7))
                    ao = m * HALF + sub * TCH
                    if grp == 0:
                        P.copy(acc[:, ao:ao + TCH], pd[:])
                    else:
                        P.tt(acc[:, ao:ao + TCH], acc[:, ao:ao + TCH], pd[:], ALU.add)
        for sub in range(8):
            t0 = h_ * HALF + sub * 256
            P.dma(t1b[:].rearrange("p (f t) -> p f t", t=256), fm_ap(x1_h, 0, 8, t0, 256))
            P.dma(t1c[:].rearrange("p (f t) -> p f t", t=256), fm_ap(xmid_h, 0, 8, t0, 256))
            for m in range(8):
                ao = m * HALF + sub * 256
                P.tt(t1b[:, m * 256:(m + 1) * 256], t1b[:, m * 256:(m + 1) * 256],
                     t1c[:, m * 256:(m + 1) * 256], ALU.add)
                P.tt(t1b[:, m * 256:(m + 1) * 256], t1b[:, m * 256:(m + 1) * 256],
                     acc[:, ao:ao + 256], ALU.add)
            P.dma(fm_ap(t2_h, 0, 8, t0, 256), t1b[:].rearrange("p (f t) -> p f t", t=256))
    P.end_phase()
    return P



def build_D1(P=None):
    P = P if P is not None else Prog("D1")
    P.begin_phase()
    TO = 512
    NS = T // TO
    xn1_h = P.dram_in("xn1", (D, T), BF16)
    anw_h = P.dram_in("attn_norm_w", (D,))
    fnw_h = P.dram_in("ffn_norm_w", (D,))
    qkvw_h = P.dram_in("qkv_wP", (D, 1536))
    qkb_h = P.dram_in("qkb64", (64, 20))
    vbb_h = P.dram_in("vb128", (128, 2))
    pw_h = P.dram_in("proj_w", (D, D))
    pb_h = P.dram_in("proj_b", (D,))
    sink_h = P.dram_in("attn_sink", (16,))
    gw_h = P.dram_in("gate_w32", (D, 32))
    gb_h = P.dram_in("gate_b32", (32,))
    cs_h = P.dram_in("cosS", (64, T))
    sn_h = P.dram_in("sinS", (64, T))
    ident_h = P.dram_in("ident", (128, 128))
    am_h = P.dram_in("amasks", (2, 128, 256))
    xp_h = P.dram_out("xpS", (D, T))
    h2f_h = P.dram_out("h2fS", (D, T), BF16)
    glg_h = P.dram_out("glogS", (T, 32))

    ident = P.sbuf("ident", [128, 128], F32)
    P.dma(ident[:], ident_h[:])
    identb = P.sbuf("identb", [128, 128], BF16)
    P.dma(identb[:], ident_h[:], queue="gpsimd")
    am0 = P.sbuf("am0", [128, 256], F32)
    amr = P.sbuf("amr", [128, 256], F32)
    P.dma(am0[:], am_h[0]); P.dma(amr[:], am_h[1])
    anc = P.sbuf("anc", [128, 8], F32)
    P.dma_slow(anc[:], anw_h.rearrange("(f p) -> p f", p=128))
    fnc = P.sbuf("fnc", [128, 8], F32)
    P.dma_slow(fnc[:], fnw_h.rearrange("(f p) -> p f", p=128))
    qkvw = P.sbuf("qkvw", [128, 8 * 1536], BF16)
    P.dma(qkvw[:].rearrange("p (f m) -> p f m", m=1536),
          qkvw_h.rearrange("(f p) m -> p f m", p=128), queue="gpsimd")
    qkb = P.sbuf("qkb", [64, 20], F32)
    P.dma(qkb[:], qkb_h[:])
    vbb = P.sbuf("vbb", [128, 2], F32)
    P.dma(vbb[:], vbb_h[:])
    pw = P.sbuf("pw", [64, 16 * D], BF16)
    P.dma(pw[:].rearrange("p (f m) -> p f m", m=D),
          pw_h.rearrange("(f p) m -> p f m", p=64), queue="gpsimd")
    pbc = P.sbuf("pbc", [128, 8], F32)
    P.dma_slow(pbc[:], pb_h.rearrange("(f p) -> p f", p=128))
    gwc = P.sbuf("gwc", [128, 8 * 32], F32)
    P.dma(gwc[:].rearrange("p (f m) -> p f m", m=32),
          gw_h.rearrange("(f p) m -> p f m", p=128))
    gbc = P.sbuf("gbc", [32, 1], F32)
    P.dma(gbc[:], gb_h.unsqueeze(1))
    cosb = P.sbuf("cosb", [64, T], F32)
    sinb = P.sbuf("sinb", [64, T], F32)
    P.dma(cosb[:], cs_h[:]); P.dma(sinb[:], sn_h[:])
    ones128 = P.sbuf("ones128", [128, 1], F32)
    P.memset(ones128[:], 1.0)
    ones1 = P.sbuf("ones1", [1, 128], F32)
    P.memset(ones1[:], 1.0)
    epsr = P.sbuf("epsr", [1, 1], F32)
    P.memset(epsr[:], 1e-5)

    xn1b = P.sbuf("xn1b", [128, 8 * TO], BF16)
    xn1f = P.sbuf("xn1f", [128, TO], F32)
    hb = P.sbuf("hb", [128, 8 * TO], BF16)
    sq = P.sbuf("sq", [128, TO])
    rowb = P.sbuf("rowb", [1, TO], F32)
    rrep = P.sbuf("rrep", [128, TO], F32)
    qb = P.sbuf("qb", [64, 16 * TO], BF16)
    kb = P.sbuf("kb", [64, 4 * (128 + TO)], BF16)
    vtok = P.sbuf("vtok", [128, 4 * 5 * 64], BF16)
    rtmp1 = P.sbuf("rtmp1", [32, TO], F32)
    rtmp2 = P.sbuf("rtmp2", [32, TO], F32)
    lgs = P.sbuf("lgs", [128, 256], F32)
    pex = P.sbuf("pex", [128, 256], F32)
    mcol = P.sbuf("mcol", [128, 1], F32)
    nmcol = P.sbuf("nmcol", [128, 1], F32)
    scol = P.sbuf("scol", [128, 1], F32)
    dcol = P.sbuf("dcol", [128, 1], F32)
    attT = P.sbuf("attT", [128, 256], BF16)
    atb = P.sbuf("atb", [128, 256], BF16)
    yb = P.sbuf("yb", [64, 16 * TO], BF16)
    xpb = P.sbuf("xpb", [128, 8 * TO], F32)
    h2fb = P.sbuf("h2fb", [128, 8 * TO], BF16)
    glt = P.sbuf("glt", [32, TO], F32)
    glo = P.sbuf("glo", [128, 4 * 32], F32)
    sinkr = P.sbuf("sinkr", [128, 16], F32)

    pQ = P.psum("pQ", [64, TO])
    pV = P.psum("pV", [128, TO])
    pL = P.psum("pL", [128, 256])
    pTb = P.psum("pTb", [128, 128], BF16)
    pT32 = P.psum("pT32", [128, 128])
    pY = P.psum("pY", [64, 128])
    pS = P.psum("pS", [1, TO])
    pGl = P.psum("pGl", [32, TO])

    P.dma(sinkr[0:1, :], sink_h.unsqueeze(0))
    P.mm(pV[:, 0:16], ones1[:], sinkr[0:1, :])
    P.copy(sinkr[:], pV[:, 0:16])
    P.memset(kb[:], 0.0)
    P.memset(vtok[:], 0.0)

    def rope_pair(dst_even, dst_odd, src64, t0):
        P.tt(rtmp1[:], src64[0:32, :], cosb[0:32, t0:t0 + TO], ALU.mult)
        P.tt(rtmp2[:], src64[32:64, :], sinb[32:64, t0:t0 + TO], ALU.mult)
        P.tt(rtmp1[:], rtmp1[:], rtmp2[:], ALU.subtract)
        P.tt(rtmp2[:], src64[0:32, :], sinb[0:32, t0:t0 + TO], ALU.mult)
        P.copy(dst_even, rtmp1[:])
        P.tt(rtmp1[:], src64[32:64, :], cosb[32:64, t0:t0 + TO], ALU.mult)
        P.tt(rtmp1[:], rtmp1[:], rtmp2[:], ALU.add)
        P.copy(dst_odd, rtmp1[:])

    for s in range(NS):
        t0 = s * TO
        P.dma(xn1b[:].rearrange("p (f t) -> p f t", t=TO), fm_ap(xn1_h, 0, 8, t0, TO))
        for k in range(8):
            P.copy(sq[:], xn1b[:, k * TO:(k + 1) * TO])
            P.act(sq[:], sq[:], AF.Square)
            P.mm(pS[0:1, :], ones128[:, 0:1], sq[:], start=(k == 0), stop=(k == 7))
        P.ts(rowb[0:1, :], pS[0:1, :], 1.0 / D, ALU.mult)
        P.act(rowb[0:1, :], rowb[0:1, :], AF.Ln, bias=epsr[0:1, 0:1])
        P.act(rowb[0:1, :], rowb[0:1, :], AF.Exp, scale=-0.5)
        P.mm(pV[:], ones1[:], rowb[0:1, :], start=True, stop=True)
        P.copy(rrep[:], pV[:])
        for k in range(8):
            P.copy(sq[:], xn1b[:, k * TO:(k + 1) * TO])
            P.tt(sq[:], sq[:], rrep[:], ALU.mult)
            P.ts(hb[:, k * TO:(k + 1) * TO], sq[:], anc[:, k:k + 1], ALU.mult)
        for kv in range(4):
            P.copy(kb[:, kv * 640:kv * 640 + 128],
                   kb[:, kv * 640 + TO:kv * 640 + TO + 128])
            P.copy(vtok[:, kv * 320:kv * 320 + 64],
                   vtok[:, kv * 320 + 256:kv * 320 + 320])
        for hh in range(16):
            for k in range(8):
                P.mm(pQ[:], qkvw[:, k * 1536 + hh * 64: k * 1536 + (hh + 1) * 64],
                     hb[:, k * TO:(k + 1) * TO], start=(k == 0), stop=(k == 7))
            bcol = qkb[:, hh:hh + 1]
            P.ts(sq[0:64, :], pQ[:], bcol, ALU.add)
            rope_pair(qb[0:32, hh * TO:(hh + 1) * TO],
                      qb[32:64, hh * TO:(hh + 1) * TO], sq, t0)
        for kv in range(4):
            for k in range(8):
                P.mm(pQ[:], qkvw[:, k * 1536 + 1024 + kv * 64: k * 1536 + 1024 + (kv + 1) * 64],
                     hb[:, k * TO:(k + 1) * TO], start=(k == 0), stop=(k == 7))
            bcol = qkb[:, 16 + kv:17 + kv]
            P.ts(sq[0:64, :], pQ[:], bcol, ALU.add)
            ko = kv * 640 + 128
            rope_pair(kb[0:32, ko:ko + TO], kb[32:64, ko:ko + TO], sq, t0)
        for vt in range(2):
            for k in range(8):
                P.mm(pV[:], qkvw[:, k * 1536 + 1280 + vt * 128: k * 1536 + 1280 + (vt + 1) * 128],
                     hb[:, k * TO:(k + 1) * TO], start=(k == 0), stop=(k == 7))
            bcol = vbb[:, vt:vt + 1]
            P.ts(sq[:], pV[:], bcol, ALU.add)
            for tt_ in range(4):
                P.transpose(pT32[:], sq[:, tt_ * 128:(tt_ + 1) * 128], ident[:])
                P.copy(vtok[:, (2 * vt) * 320 + (tt_ + 1) * 64:
                       (2 * vt) * 320 + (tt_ + 2) * 64], pT32[:, 0:64])
                P.copy(vtok[:, (2 * vt + 1) * 320 + (tt_ + 1) * 64:
                       (2 * vt + 1) * 320 + (tt_ + 2) * 64], pT32[:, 64:128])
        for qt in range(4):
            for qh in range(16):
                kv = qh // 4
                P.mm(pL[:], qb[0:64, qh * TO + qt * 128: qh * TO + qt * 128 + 128],
                     kb[0:64, kv * 640 + qt * 128: kv * 640 + qt * 128 + 256])
                msk = am0 if (s == 0 and qt == 0) else amr
                P.stt(lgs[:], pL[:], 0.125, msk[:], ALU.mult, ALU.add)
                P.red(mcol[:], lgs[:], ALU.max)
                P.tt(mcol[:], mcol[:], sinkr[:, qh:qh + 1], ALU.max)
                P.ts(nmcol[:], mcol[:], -1.0, ALU.mult)
                P.act(pex[:], lgs[:], AF.Exp, bias=nmcol[:, 0:1])
                P.red(dcol[:], pex[:], ALU.add)
                P.tt(scol[:], sinkr[:, qh:qh + 1], mcol[:], ALU.subtract)
                P.act(scol[:], scol[:], AF.Exp)
                P.tt(dcol[:], dcol[:], scol[:], ALU.add)
                P.recip(scol[:], dcol[:])
                P.ts(atb[:], pex[:], scol[:, 0:1], ALU.mult)
                for piece in range(2):
                    P.transpose(pTb[:], atb[:, piece * 128:(piece + 1) * 128], identb[:])
                    P.copy(attT[:, piece * 128:(piece + 1) * 128], pTb[:])
                for piece in range(2):
                    P.mm(pY[:], vtok[:, kv * 320 + (qt + piece) * 64:
                                     kv * 320 + (qt + piece + 1) * 64],
                         attT[:, piece * 128:(piece + 1) * 128],
                         start=(piece == 0), stop=(piece == 1))
                P.copy(yb[0:64, qh * TO + qt * 128: qh * TO + qt * 128 + 128], pY[:])
        for m in range(8):
            for k in range(16):
                P.mm(pV[:], pw[0:64, k * D + m * 128: k * D + (m + 1) * 128],
                     yb[0:64, k * TO:(k + 1) * TO], start=(k == 0), stop=(k == 15))
            P.ts(sq[:], pV[:], pbc[:, m:m + 1], ALU.add)
            P.copy(xn1f[:], xn1b[:, m * TO:(m + 1) * TO])
            P.tt(xpb[:, m * TO:(m + 1) * TO], sq[:], xn1f[:], ALU.add)
        P.dma(fm_ap(xp_h, 0, 8, t0, TO), xpb[:].rearrange("p (f t) -> p f t", t=TO))
        for k in range(8):
            P.act(sq[:], xpb[:, k * TO:(k + 1) * TO], AF.Square)
            P.mm(pS[0:1, :], ones128[:, 0:1], sq[:], start=(k == 0), stop=(k == 7))
        P.ts(rowb[0:1, :], pS[0:1, :], 1.0 / D, ALU.mult)
        P.act(rowb[0:1, :], rowb[0:1, :], AF.Ln, bias=epsr[0:1, 0:1])
        P.act(rowb[0:1, :], rowb[0:1, :], AF.Exp, scale=-0.5)
        P.mm(pV[:], ones1[:], rowb[0:1, :], start=True, stop=True)
        P.copy(rrep[:], pV[:])
        for k in range(8):
            P.tt(sq[:], xpb[:, k * TO:(k + 1) * TO], rrep[:], ALU.mult)
            P.ts(h2fb[:, k * TO:(k + 1) * TO], sq[:], fnc[:, k:k + 1], ALU.mult)
        P.dma(fm_ap(h2f_h, 0, 8, t0, TO), h2fb[:].rearrange("p (f t) -> p f t", t=TO))
        for k in range(8):
            P.tt(sq[:], xpb[:, k * TO:(k + 1) * TO], rrep[:], ALU.mult)
            P.ts(sq[:], sq[:], fnc[:, k:k + 1], ALU.mult)
            P.mm(pGl[:], gwc[:, k * 32:(k + 1) * 32], sq[:],
                 start=(k == 0), stop=(k == 7))
        P.ts(glt[:], pGl[:], gbc[:, 0:1], ALU.add)
        for tt_ in range(4):
            P.transpose(pT32[:, 0:32], glt[:, tt_ * 128:(tt_ + 1) * 128],
                        ident[0:32, 0:32])
            P.copy(glo[:, tt_ * 32:(tt_ + 1) * 32], pT32[:, 0:32])
            P.dma(glg_h[t0 + tt_ * 128: t0 + (tt_ + 1) * 128],
                  glo[:, tt_ * 32:(tt_ + 1) * 32])
    P.end_phase()
    return P




def build_D2(P=None):
    P = P if P is not None else Prog("D2")
    P.begin_phase()
    TCH = 512
    HALF = T // 2
    h2_h = P.dram_in("h2fS", (D, T), BF16)
    glg_h = P.dram_in("glogS", (T, 32))
    xp_h = P.dram_in("xpS", (D, T))
    t2_h = P.dram_in("t2", (D, T))
    wu_h = P.dram_in("W_up", (32, D, 2048))
    wd_h = P.dram_in("W_down", (32, D, D))
    sel_h = P.dram_in("sel32", (32, 32 * 128))
    ident_h = P.dram_in("ident", (128, 128))
    out_h = P.dram_out("out", (T, D))

    ident = P.sbuf("ident", [128, 128], F32)
    P.dma(ident[:], ident_h[:])
    selb = P.sbuf("selb", [32, 32 * 128], BF16)
    P.dma(selb[:], sel_h[:], queue="gpsimd")

    lgt = P.sbuf("lgt", [128, 32], F32)
    msk = P.sbuf("msk", [128, 32], F32)
    eq = P.sbuf("eq", [128, 32], F32)
    cum = P.sbuf("cum", [128, 32], F32)
    zer = P.sbuf("zer", [128, 32], F32)
    first = P.sbuf("first", [128, 32], F32)
    mm_ = P.sbuf("mm_", [128, 1], F32)
    m0 = P.sbuf("m0", [128, 1], F32)
    vks = P.sbuf("vks", [128, 4], F32)
    fks = P.sbuf("fks", [128, 4 * 32], F32)
    den = P.sbuf("den", [128, 1], F32)
    comb = P.sbuf("comb", [128, 32], F32)
    combT = P.sbuf("combT", [32, T], BF16)
    pR = P.psum("pR", [128, 128])
    P.memset(zer[:], 0.0)
    scr = (eq, cum, zer, first, mm_)
    for tt_ in range(T // 128):
        P.dma(lgt[:], glg_h[tt_ * 128:(tt_ + 1) * 128])
        P.copy(msk[:], lgt[:])
        for kk in range(4):
            f, m = emit_first_argmax(P, msk[:], scr)
            P.copy(fks[:, kk * 32:(kk + 1) * 32], f[:])
            P.copy(vks[:, kk:kk + 1], m[:])
            if kk == 0:
                P.copy(m0[:], m[:])
            P.stt(msk[:], f[:], -1e30, msk[:], ALU.mult, ALU.add)
        P.ts(vks[:], vks[:], m0[:, 0:1], ALU.subtract)
        P.act(vks[:], vks[:], AF.Exp)
        P.red(den[:], vks[:], ALU.add)
        P.recip(den[:], den[:])
        P.memset(comb[:], 0.0)
        for kk in range(4):
            P.ts(mm_[:], vks[:, kk:kk + 1], den[:, 0:1], ALU.mult)
            P.stt(comb[:], fks[:, kk * 32:(kk + 1) * 32], mm_[:, 0:1], comb[:],
                  ALU.mult, ALU.add)
        P.transpose(pR[0:32, :], comb[:], ident[:])
        P.copy(combT[:, tt_ * 128:(tt_ + 1) * 128], pR[0:32, 0:128])

    wz = P.sbuf("wz", [128, 24576], BF16)
    UA, DA = 0, 16384
    acc = P.sbuf("acc", [128, 8 * HALF], F32)
    h2b = P.sbuf("h2b", [128, 8 * TCH], BF16)
    hub = P.sbuf("hub", [128, 8 * TCH], BF16)
    silb = P.sbuf("silb", [128, TCH], F32)
    crep = P.sbuf("crep", [128, TCH], F32)
    pG = P.psum("pG", [128, TCH]); pU = P.psum("pU", [128, TCH])
    pC = P.psum("pC", [128, TCH])
    pD1_ = P.psum("pD1_", [128, TCH]); pD2_ = P.psum("pD2_", [128, TCH])
    pT2 = P.psum("pT2", [128, TCH])
    wu_r = wu_h.rearrange("e (f p) m -> e p f m", p=128)
    wd_r = wd_h.rearrange("e (f p) m -> e p f m", p=128)
    fin = P.sbuf("fin", [128, 8 * TCH], F32)
    fin2 = P.sbuf("fin2", [128, 8 * TCH], F32)

    for h_ in range(2):
        for e_ in range(32):
            P.dma(wz[:, UA:UA + 16384].rearrange("p (f m) -> p f m", m=2048),
                  wu_r[e_], queue="gpsimd")
            P.dma(wz[:, DA:DA + 8192].rearrange("p (f m) -> p f m", m=1024),
                  wd_r[e_], queue="gpsimd")
            for sub in range(4):
                t0 = h_ * HALF + sub * TCH
                P.dma(h2b[:].rearrange("p (f t) -> p f t", t=TCH),
                      fm_ap(h2_h, 0, 8, t0, TCH))
                for mt in range(8):
                    for k in range(8):
                        P.mm(pU[:], wz[:, UA + k * 2048 + mt * 128:
                                       UA + k * 2048 + (mt + 1) * 128],
                             h2b[:, k * TCH:(k + 1) * TCH],
                             start=(k == 0), stop=(k == 7))
                    for k in range(8):
                        P.mm(pG[:], wz[:, UA + k * 2048 + 1024 + mt * 128:
                                       UA + k * 2048 + 1024 + (mt + 1) * 128],
                             h2b[:, k * TCH:(k + 1) * TCH],
                             start=(k == 0), stop=(k == 7))
                    P.act(silb[:], pG[:], AF.Silu)
                    P.tt(hub[:, mt * TCH:(mt + 1) * TCH], silb[:], pU[:], ALU.mult)
                P.mm(pC[:], selb[:, e_ * 128:(e_ + 1) * 128], combT[:, t0:t0 + TCH])
                P.copy(crep[:], pC[:])
                for mt in range(8):
                    P.tt(hub[:, mt * TCH:(mt + 1) * TCH],
                         hub[:, mt * TCH:(mt + 1) * TCH], crep[:], ALU.mult)
                for m in range(8):
                    pd = pD1_ if m % 2 == 0 else pD2_
                    for k in range(8):
                        P.mm(pd[:], wz[:, DA + k * 1024 + m * 128:
                                       DA + k * 1024 + (m + 1) * 128],
                             hub[:, k * TCH:(k + 1) * TCH],
                             start=(k == 0), stop=(k == 7))
                    ao = m * HALF + sub * TCH
                    if e_ == 0:
                        P.copy(acc[:, ao:ao + TCH], pd[:])
                    else:
                        P.tt(acc[:, ao:ao + TCH], acc[:, ao:ao + TCH], pd[:], ALU.add)
        for sub in range(4):
            t0 = h_ * HALF + sub * TCH
            P.dma(fin[:].rearrange("p (f t) -> p f t", t=TCH), fm_ap(t2_h, 0, 8, t0, TCH))
            P.dma(fin2[:].rearrange("p (f t) -> p f t", t=TCH), fm_ap(xp_h, 0, 8, t0, TCH))
            for m in range(8):
                ao = m * HALF + sub * TCH
                P.tt(fin[:, m * TCH:(m + 1) * TCH], fin[:, m * TCH:(m + 1) * TCH],
                     fin2[:, m * TCH:(m + 1) * TCH], ALU.add)
                P.tt(fin[:, m * TCH:(m + 1) * TCH], fin[:, m * TCH:(m + 1) * TCH],
                     acc[:, ao:ao + TCH], ALU.add)
            for tt_ in range(4):
                for m in range(8):
                    P.transpose(pT2[:, 0:128],
                                fin[:, m * TCH + tt_ * 128: m * TCH + tt_ * 128 + 128],
                                ident[:])
                    P.copy(fin2[:, m * 128:(m + 1) * 128], pT2[:, 0:128])
                P.dma(out_h[t0 + tt_ * 128: t0 + (tt_ + 1) * 128],
                      fin2[:, 0:1024])
    P.end_phase()
    return P



_cache = {}


def _bf(x):
    return np.ascontiguousarray(x)


def run_launch(key, builder, in_maps):
    if key not in _cache:
        P = builder()
        _cache[key] = P.build()
    nc = _cache[key]
    res = run_bass_kernel_spmd(nc, in_maps, core_ids=list(range(len(in_maps))))
    return res.results


IDENT = np.eye(128, dtype=np.float32)
SEL32 = np.zeros((32, 32 * 128), dtype=np.float32)
for _s in range(32):
    SEL32[_s, _s * 128:(_s + 1) * 128] = 1.0
_MLO = np.tril(np.ones((128, 128), np.float32), -1)
MASKS = np.stack([_MLO, _MLO.T, np.triu(np.ones((128, 128), np.float32))])

def _amasks():
    NEG = np.float32(-3e38)
    qi = np.arange(128)[:, None]
    kj = np.arange(256)[None, :]
    valid = (kj >= qi) & (kj <= qi + 128)
    amr = np.where(valid, np.float32(0), NEG).astype(np.float32)
    am0 = amr.copy()
    am0[:, :128] = NEG
    return np.stack([am0, amr])

AMASKS = _amasks()

def _rope_tables():
    DH = 64
    pos = np.arange(T, dtype=np.float64) / 32.0
    inv = 1.0 / (150000.0 ** (np.arange(0, DH, 2, dtype=np.float64) / DH))
    fr = pos[None, :] * inv[:, None]
    c = np.cos(fr).astype(np.float32)
    s_ = np.sin(fr).astype(np.float32)
    return np.vstack([c, c]), np.vstack([s_, s_])

COSS, SINS = _rope_tables()

def _perm_qkv(qkv_w, qkv_b):
    perm = np.arange(1536)
    def headperm(base, nh):
        for h in range(nh):
            o = base + h * 64
            perm[o:o + 64] = np.concatenate([np.arange(o, o + 64, 2),
                                             np.arange(o + 1, o + 64, 2)])
    headperm(0, 16)
    headperm(1024, 4)
    wp = np.ascontiguousarray(qkv_w[:, perm])
    bpm = qkv_b[perm]
    qkb = np.ascontiguousarray(bpm[:1280].reshape(20, 64).T)
    vbb = np.ascontiguousarray(bpm[1280:].reshape(2, 128).T)
    return wp, qkb, vbb


_BUILDERS = [build_A1, build_A2, build_B, build_C1a, build_C1b, build_C2,
             build_D1, build_D2]


def build_fused():
    P = Prog("fused", fused=True)
    for b in _BUILDERS:
        b(P)
    return P


import os as _os
FUSED = _os.environ.get("MK_UNFUSED", "") == ""


def _fused_maps(ins, x, b, extra):
    m = dict(x=x[b], ident=IDENT, sel32=SEL32, masks=MASKS, amasks=AMASKS,
             cosS=COSS, sinS=SINS)
    m.update(extra)
    return m


def kernel(**ins):
    x = ins["L0_attn_x"].astype(np.float32)
    B = x.shape[0]
    bcw = np.concatenate([ins["L0_attn_B_w"], ins["L0_attn_C_w"]], axis=1)
    rwT = np.ascontiguousarray(ins["L0_ffn_router_w"].T)
    convw = np.ascontiguousarray(ins["L0_attn_conv_w"][:, 0, :])
    qkvwP, qkbP, vbbP = _perm_qkv(ins["L1_ffn_qkv_w"], ins["L1_ffn_qkv_b"])

    if FUSED:
        common = dict(
            in_proj_w=ins["L0_attn_in_proj_w"], conv_w=convw,
            dt_w=ins["L0_attn_dt_w"], dt_b=ins["L0_attn_dt_b"], bc_w=bcw,
            router_wT=rwT, norm_w=ins["L0_attn_norm_w"],
            A_log=ins["L0_attn_A_log"], D_skip=ins["L0_attn_D_skip"],
            out_proj_w=ins["L0_attn_out_proj_w"], bias=ins["L0_ffn_bias"],
            gate_w=ins["L0_ffn_gate_w"], up_w=ins["L0_ffn_up_w"],
            down_w=ins["L0_ffn_down_w"], sh_gate_w=ins["L0_ffn_sh_gate_w"],
            sh_up_w=ins["L0_ffn_sh_up_w"], sh_down_w=ins["L0_ffn_sh_down_w"],
            Wq=ins["L1_attn_Wq"], Wk=ins["L1_attn_Wk"], Wv=ins["L1_attn_Wv"],
            Wg=ins["L1_attn_Wg"], Wb=ins["L1_attn_Wb"], Wa=ins["L1_attn_Wa"],
            A_log1=ins["L1_attn_A_log"], dt_bias=ins["L1_attn_dt_bias"], conv_q_w=ins["L1_attn_conv_q_w"],
            conv_k_w=ins["L1_attn_conv_k_w"], conv_v_w=ins["L1_attn_conv_v_w"],
            norm1_w=ins["L1_attn_norm1_w"], onorm_w=ins["L1_attn_onorm_w"],
            Wo=ins["L1_attn_Wo"], norm2_w=ins["L1_attn_norm2_w"],
            Wgate=ins["L1_attn_Wgate"], Wup=ins["L1_attn_Wup"],
            Wdown=ins["L1_attn_Wdown"], attn_norm_w=ins["L1_ffn_attn_norm_w"],
            ffn_norm_w=ins["L1_ffn_ffn_norm_w"], qkv_wP=qkvwP, qkb64=qkbP,
            vb128=vbbP, proj_w=ins["L1_ffn_proj_w"], proj_b=ins["L1_ffn_proj_b"],
            attn_sink=ins["L1_ffn_attn_sink"], gate_wD=None,
            W_up=ins["L1_ffn_W_up"], W_down=ins["L1_ffn_W_down"])
        del common["gate_wD"]
        maps = []
        for b in range(B):
            m = _fused_maps(ins, x, b, common)
            maps.append(m)
        for m in maps:
            m["gate_w32"] = ins["L1_ffn_gate_w"]
            m["gate_b32"] = ins["L1_ffn_gate_b"]
        res = run_launch("FUSED", build_fused, maps)
        out = np.stack([np.asarray(r["out"]) for r in res]).astype(np.float32)
        return out

    mapsA1 = [dict(x=x[b], in_proj_w=ins["L0_attn_in_proj_w"], conv_w=convw,
                   dt_w=ins["L0_attn_dt_w"], dt_b=ins["L0_attn_dt_b"],
                   bc_w=bcw, router_wT=rwT, norm_w=ins["L0_attn_norm_w"],
                   ident=IDENT) for b in range(B)]
    rA1 = run_launch("A1", build_A1, mapsA1)
    mapsA2 = [dict(x=x[b], dtS=rA1[b]["dtS"], uS=rA1[b]["uS"], xcS=rA1[b]["xcS"],
                   bcS=rA1[b]["bcS"], sel32=SEL32, A_log=ins["L0_attn_A_log"],
                   D_skip=ins["L0_attn_D_skip"], out_proj_w=ins["L0_attn_out_proj_w"],
                   ident=IDENT) for b in range(B)]
    rA2 = run_launch("A2", build_A2, mapsA2)
    mapsB = [dict(xn=rA1[b]["xn"], t1=rA2[b]["t1"], logits0=rA1[b]["logits0"],
                  bias=ins["L0_ffn_bias"], gate_w=ins["L0_ffn_gate_w"],
                  up_w=ins["L0_ffn_up_w"], down_w=ins["L0_ffn_down_w"],
                  sh_gate_w=ins["L0_ffn_sh_gate_w"], sh_up_w=ins["L0_ffn_sh_up_w"],
                  sh_down_w=ins["L0_ffn_sh_down_w"], sel32=SEL32, ident=IDENT)
             for b in range(B)]
    rB = run_launch("B", build_B, mapsB)
    mapsC1a = [dict(x1=rB[b]["x1"], Wq=ins["L1_attn_Wq"], Wk=ins["L1_attn_Wk"],
                    Wv=ins["L1_attn_Wv"], Wg=ins["L1_attn_Wg"], Wb=ins["L1_attn_Wb"],
                    Wa=ins["L1_attn_Wa"], A_log1=ins["L1_attn_A_log"],
                    dt_bias=ins["L1_attn_dt_bias"], conv_q_w=ins["L1_attn_conv_q_w"],
                    conv_k_w=ins["L1_attn_conv_k_w"], conv_v_w=ins["L1_attn_conv_v_w"],
                    norm1_w=ins["L1_attn_norm1_w"]) for b in range(B)]
    rC1a = run_launch("C1a", build_C1a, mapsC1a)
    mapsC1b = [dict(qS=rC1a[b]["qS"], kS=rC1a[b]["kS"], vS=rC1a[b]["vS"],
                    gS=rC1a[b]["gS"], betaS=rC1a[b]["betaS"], cumS=rC1a[b]["cumS"],
                    xn1=rC1a[b]["xn1"], onorm_w=ins["L1_attn_onorm_w"],
                    Wo=ins["L1_attn_Wo"], sel32=SEL32, ident=IDENT, masks=MASKS)
               for b in range(B)]
    rC1b = run_launch("C1b", build_C1b, mapsC1b)
    mapsC2 = [dict(xmid=rC1b[b]["xmid"], x1=rB[b]["x1"], norm2_w=ins["L1_attn_norm2_w"],
                   Wgate=ins["L1_attn_Wgate"], Wup=ins["L1_attn_Wup"],
                   Wdown=ins["L1_attn_Wdown"]) for b in range(B)]
    rC2 = run_launch("C2", build_C2, mapsC2)
    mapsD1 = [dict(xn1=rC1a[b]["xn1"], attn_norm_w=ins["L1_ffn_attn_norm_w"],
                   ffn_norm_w=ins["L1_ffn_ffn_norm_w"], qkv_wP=qkvwP, qkb64=qkbP, vb128=vbbP,
                   proj_w=ins["L1_ffn_proj_w"], proj_b=ins["L1_ffn_proj_b"],
                   attn_sink=ins["L1_ffn_attn_sink"], gate_w32=ins["L1_ffn_gate_w"],
                   gate_b32=ins["L1_ffn_gate_b"], cosS=COSS, sinS=SINS, ident=IDENT,
                   amasks=AMASKS) for b in range(B)]
    rD1 = run_launch("D1", build_D1, mapsD1)
    mapsD2 = [dict(h2fS=rD1[b]["h2fS"], glogS=rD1[b]["glogS"], xpS=rD1[b]["xpS"],
                   t2=rC2[b]["t2"], W_up=ins["L1_ffn_W_up"], W_down=ins["L1_ffn_W_down"],
                   sel32=SEL32, ident=IDENT) for b in range(B)]
    rD2 = run_launch("D2", build_D2, mapsD2)
    out = np.stack([np.asarray(rD2[b]["out"]) for b in range(B)]).astype(np.float32)
    return out


# version 04
# speedup vs baseline: 1.4269x; 1.4252x over previous
import numpy as np
import ml_dtypes
import concourse.bass as bass
import concourse.mybir as mybir
from concourse.bass_utils import run_bass_kernel_spmd
from contextlib import ExitStack
from concourse.bass_types import AP as _AP


def _a(x):
    return x if isinstance(x, _AP) or isinstance(x, (int, float)) or x is None else x[:]

F32 = mybir.dt.float32
BF16 = mybir.dt.bfloat16
AF = mybir.ActivationFunctionType
ALU = mybir.AluOpType
AX = mybir.AxisListType
BF = ml_dtypes.bfloat16

T = 4096
D = 1024
NCORES = 8


class Prog:

    ENGINES = ("sync", "scalar", "vector", "tensor", "gpsimd")

    def __init__(self, name, fused=False):
        self.nc = bass.Bass("TRN2", target_bir_lowering=False)
        self.plan = []
        self.es = ExitStack()
        self._names = {}
        self.fused = fused
        self.drams = {}
        self._pes = None

    def _uname(self, prefix):
        n = self._names.get(prefix, 0)
        self._names[prefix] = n + 1
        return f"{prefix}{n}"

    def dram_in(self, name, shape, dt=F32):
        if name not in self.drams:
            self.drams[name] = self.nc.dram_tensor(
                name, list(shape), dt, kind="ExternalInput").ap()
        return self.drams[name]

    def dram_out(self, name, shape, dt=F32):
        if name not in self.drams:
            kind = "ExternalOutput" if (not self.fused or name == "out") else "Internal"
            self.drams[name] = self.nc.dram_tensor(name, list(shape), dt, kind=kind).ap()
        return self.drams[name]

    def begin_phase(self):
        self._pes = ExitStack()

    def end_phase(self):
        if self._pes is not None:
            self._pes.close()
            self._pes = None

    def sbuf(self, name, shape, dt=F32):
        es = self._pes if self._pes is not None else self.es
        return es.enter_context(self.nc.sbuf_tensor(self._uname(name), list(shape), dt))

    def psum(self, name, shape, dt=F32):
        es = self._pes if self._pes is not None else self.es
        return es.enter_context(self.nc.psum_tensor(self._uname(name), list(shape), dt))

    def op(self, engine, fn, inc=1):
        self.plan.append((engine, fn, inc))

    def dma(self, out, in_, queue="sync"):
        self.op(queue, lambda e, o=out, i=in_: e.dma_start(out=o, in_=i), 16)

    def dma_slow(self, out, in_, queue="sync"):
        self.op(queue, lambda e, o=out, i=in_: e.dma_start(
            out=o, in_=i, allow_slow_non_contiguous=True), 16)

    def tt(self, out, a, b, alu, eng="vector"):
        self.op(eng, lambda e, o=_a(out), x=_a(a), y=_a(b), al=alu: e.tensor_tensor(o, x, y, al))

    def ts(self, out, a, s1, alu, s2=None, alu2=None, eng="vector"):
        if s2 is None:
            self.op(eng, lambda e, o=_a(out), x=_a(a), s=_a(s1), al=alu: e.tensor_scalar(o, x, s, None, al))
        else:
            self.op(eng, lambda e, o=_a(out), x=_a(a), s=_a(s1), t=_a(s2), al=alu, al2=alu2:
                    e.tensor_scalar(o, x, s, t, al, al2))

    def stt(self, out, in0, scalar, in1, op0, op1, eng="vector"):
        self.op(eng, lambda e, o=_a(out), a=_a(in0), s=_a(scalar), b=_a(in1), p0=op0, p1=op1:
                e.scalar_tensor_tensor(o, a, s, b, p0, p1))

    def copy(self, out, in_, eng="vector"):
        self.op(eng, lambda e, o=_a(out), i=_a(in_): e.tensor_copy(o, i))

    def memset(self, ap, v, eng="vector"):
        self.op(eng, lambda e, a=_a(ap), vv=v: e.memset(a, vv))

    def act(self, out, in_, fn, bias=None, scale=None, eng="scalar"):
        kw = {}
        if bias is not None:
            kw["bias"] = bias
        if scale is not None:
            kw["scale"] = scale
        self.op(eng, lambda e, o=_a(out), i=_a(in_), f=fn, k={kk: _a(vv) for kk, vv in kw.items()}: e.activation(o, i, f, **k))

    def red(self, out, in_, alu, eng="vector"):
        self.op(eng, lambda e, o=_a(out), i=_a(in_), a=alu: e.tensor_reduce(o, i, AX.X, a))

    def scan(self, out, d0, d1, init, op0, op1, eng="vector"):
        self.op(eng, lambda e, o=_a(out), a=_a(d0), b=_a(d1), s=_a(init), p=op0, q=op1:
                e.tensor_tensor_scan(o, a, b, s, p, q))

    def mm(self, out, lhsT, rhs, start=True, stop=True):
        self.op("tensor", lambda e, o=_a(out), l=_a(lhsT), r=_a(rhs), s1=start, s2=stop:
                e.matmul(o, l, r, start=s1, stop=s2))

    def transpose(self, out, in_, ident):
        self.op("tensor", lambda e, o=_a(out), i=_a(in_), d=_a(ident): e.transpose(o, i, d))

    def recip(self, out, in_):
        self.op("vector", lambda e, o=_a(out), i=_a(in_): e.reciprocal(o, i))

    def build(self):
        nc = self.nc
        sem_h = self.es.enter_context(nc.semaphore("g"))
        sem2_h = self.es.enter_context(nc.semaphore("g2"))
        totals = []
        c = 0
        for (_, _, inc) in self.plan:
            totals.append(c)
            c += inc
        final = c
        plan = self.plan
        blk = self.es.enter_context(nc.Block())

        use2 = final >= (1 << 31) - 10
        assert not use2, "semaphore overflow"

        def body_for(engname):
            def body(eng):
                known = 0
                for idx, (en, fn, inc) in enumerate(plan):
                    if en != engname:
                        continue
                    w = totals[idx]
                    if w > known:
                        eng.wait_ge(sem_h, w)
                        known = w
                    ins = fn(eng)
                    ins.then_inc(sem_h, inc)
                    if inc == 1 and engname == "tensor":
                        known += inc
                if engname == "sync":
                    eng.wait_ge(sem_h, final)
            return body

        blk.sync(body_for("sync"))
        blk.scalar(body_for("scalar"))
        blk.vector(body_for("vector"))
        blk.tensor(body_for("tensor"))
        blk.gpsimd(body_for("gpsimd"))
        self.es.close()
        return nc


def fm_ap(hbm, ft0, nft, t0, tn, tile_t=None):
    r = hbm.rearrange("(f p) t -> p f t", p=128)
    return r[:, ft0:ft0 + nft, t0:t0 + tn]



def ln_tokmajor(P, sx, xn_out, nf, eps_ap, scratch, also_rms=None):
    sq, s1, s2, m, var = scratch
    P.red(s1, sx, ALU.add)
    P.act(sq, sx, AF.Square)
    P.red(s2, sq, ALU.add)
    P.ts(m, s1, 1.0 / D, ALU.mult)
    P.tt(var, m, m, ALU.mult)
    P.stt(var, s2, 1.0 / D, var, ALU.mult, ALU.subtract)
    P.act(var, var, AF.Ln, bias=eps_ap)
    P.act(var, var, AF.Exp, scale=-0.5)
    P.ts(xn_out, sx, m, ALU.subtract, var, ALU.mult)
    if also_rms is not None:
        out2, eps2_ap = also_rms
        P.act(sq, xn_out, AF.Square)
        P.red(s2, sq, ALU.add)
        P.ts(s2, s2, 1.0 / D, ALU.mult)
        P.act(s2, s2, AF.Ln, bias=eps2_ap)
        P.act(s2, s2, AF.Exp, scale=-0.5)
        P.ts(out2, xn_out, s2, ALU.mult)


def softplus_psum(P, psum, bias_col, out, a1, a2):
    P.act(a1, psum, AF.Abs, bias=bias_col)
    P.act(a1, a1, AF.Exp, scale=-1.0)
    P.act(a1, a1, AF.Ln, bias=1.0)
    P.act(a2, psum, AF.Relu, bias=bias_col)
    P.tt(out, a1, a2, ALU.add)



def build_A1(P=None):
    P = P if P is not None else Prog("A1")
    P.begin_phase()
    TC = 128
    NCH = T // TC
    x_h = P.dram_in("x", (T, D))
    ipw_h = P.dram_in("in_proj_w", (D, 4096))
    convw_h = P.dram_in("conv_w", (2048, 4))
    dtw_h = P.dram_in("dt_w", (2048, 2048))
    dtb_h = P.dram_in("dt_b", (2048,))
    bcw_h = P.dram_in("bc_w", (2048, 32))
    rw_h = P.dram_in("router_wT", (D, 16))
    nw_h = P.dram_in("norm_w", (D,))
    ident_h = P.dram_in("ident", (128, 128))

    xn_h = P.dram_out("xn", (D, T), BF16)
    lg_h = P.dram_out("logits0", (T, 16))
    dt_h = P.dram_out("dtS", (2048, T))
    u_h = P.dram_out("uS", (2048, T))
    xc_h = P.dram_out("xcS", (2048, T))
    bcS_h = P.dram_out("bcS", (32, T))

    ipw = P.sbuf("ipw", [128, 8 * 4096], BF16)
    dtw = P.sbuf("dtw", [128, 16 * 2048], BF16)
    bcw = P.sbuf("bcw", [128, 16 * 32], BF16)
    rw = P.sbuf("rw", [128, 8 * 16], F32)
    cw = P.sbuf("cw", [128, 16 * 4], F32)
    dtb = P.sbuf("dtb", [128, 16], F32)
    nwc = P.sbuf("nwc", [128, 8], F32)
    ident = P.sbuf("ident", [128, 128], F32)
    for k in range(8):
        P.dma(ipw[:, k * 4096:(k + 1) * 4096],
              ipw_h.rearrange("(f p) m -> p f m", p=128)[:, k], queue="gpsimd")
    for k in range(16):
        P.dma(dtw[:, k * 2048:(k + 1) * 2048],
              dtw_h.rearrange("(f p) m -> p f m", p=128)[:, k], queue="gpsimd")
    P.dma(bcw[:].rearrange("p (f m) -> p f m", m=32),
          bcw_h.rearrange("(f p) m -> p f m", p=128), queue="gpsimd")
    P.dma(rw[:].rearrange("p (f m) -> p f m", m=16),
          rw_h.rearrange("(f p) m -> p f m", p=128), queue="sync")
    P.dma(cw[:].rearrange("p (f m) -> p f m", m=4),
          convw_h.rearrange("(f p) m -> p f m", p=128), queue="sync")
    P.dma_slow(dtb[:], dtb_h.rearrange("(f p) -> p f", p=128), queue="sync")
    P.dma_slow(nwc[:], nw_h.rearrange("(f p) -> p f", p=128), queue="sync")
    P.dma(ident[:], ident_h[:], queue="sync")

    sx = P.sbuf("sx", [128, D])
    xn_tok = P.sbuf("xntok", [128, D])
    xr_tok = P.sbuf("xrtok", [128, D])
    sq = P.sbuf("sq", [128, D])
    s1 = P.sbuf("s1", [128, 1]); s2 = P.sbuf("s2", [128, 1])
    mcol = P.sbuf("mcol", [128, 1]); vcol = P.sbuf("vcol", [128, 1])
    epsln = P.sbuf("epsln", [128, 1]); epsrms = P.sbuf("epsrms", [128, 1])
    xnf = P.sbuf("xnf", [128, 8 * TC])
    xnb = P.sbuf("xnb", [128, 8 * TC], BF16)
    xrb = P.sbuf("xrb", [128, 8 * TC], BF16)
    lgs = P.sbuf("lgs", [16, TC], F32)
    lgt = P.sbuf("lgt", [128, 16], F32)
    sil = P.sbuf("sil", [128, TC])
    xi = P.sbuf("xi", [128, 16 * (TC + 3)])
    xc = P.sbuf("xc", [128, 16 * TC])
    xcb = P.sbuf("xcb", [128, 16 * TC], BF16)
    dts = P.sbuf("dts", [128, 16 * TC])
    us = P.sbuf("us", [128, 16 * TC])
    a1s = P.sbuf("a1s", [128, TC]); a2s = P.sbuf("a2s", [128, TC])
    bcs = P.sbuf("bcs", [32, TC], F32)

    pT = P.psum("pT", [128, 512])
    pA = P.psum("pA", [128, TC]); pB = P.psum("pB", [128, TC])
    pC = P.psum("pC", [128, TC]); pD = P.psum("pD", [128, TC])
    pL = P.psum("pL", [16, TC])
    pBC = P.psum("pBC", [32, TC])

    P.memset(epsln[:], 1e-5)
    P.memset(epsrms[:], 1e-6)
    P.memset(xi[:], 0.0)

    scratch = (sq, s1, s2, mcol, vcol)
    x_r = x_h.rearrange("(n p) d -> n p d", p=128)

    for c in range(NCH):
        P.dma(sx[:], x_r[c])
        ln_tokmajor(P, sx[:], xn_tok[:], 8, epsln[:, 0:1],
                    scratch, also_rms=(xr_tok[:], epsrms[:, 0:1]))
        for ft in range(0, 8, 4):
            for j in range(4):
                P.transpose(pT[:, j * 128:(j + 1) * 128],
                            xn_tok[:, (ft + j) * 128:(ft + j + 1) * 128], ident[:])
            for j in range(4):
                P.copy(xnf[:, (ft + j) * TC:(ft + j) * TC + 128],
                       pT[:, j * 128:(j + 1) * 128])
            for j in range(4):
                P.transpose(pT[:, j * 128:(j + 1) * 128],
                            xr_tok[:, (ft + j) * 128:(ft + j + 1) * 128], ident[:])
            for j in range(4):
                P.ts(xrb[:, (ft + j) * TC:(ft + j) * TC + 128],
                     pT[:, j * 128:(j + 1) * 128], nwc[:, ft + j:ft + j + 1], ALU.mult)
        P.copy(xnb[:], xnf[:])
        P.dma(fm_ap(xn_h, 0, 8, c * TC, TC),
              xnb[:].rearrange("p (f t) -> p f t", t=TC))
        for k in range(8):
            P.mm(pL[:], rw[:, k * 16:(k + 1) * 16], xnf[:, k * TC:(k + 1) * TC],
                 start=(k == 0), stop=(k == 7))
        P.copy(lgs[:], pL[:])
        P.transpose(pT[:, 0:16], lgs[:], ident[0:16, 0:16])
        P.copy(lgt[:], pT[:, 0:16])
        P.dma(lg_h[c * 128:(c + 1) * 128], lgt[:])
        for m in range(16):
            for k in range(8):
                P.mm(pA[:], ipw[:, k * 4096 + m * 128: k * 4096 + (m + 1) * 128],
                     xrb[:, k * TC:(k + 1) * TC], start=(k == 0), stop=(k == 7))
            for k in range(8):
                P.mm(pB[:], ipw[:, k * 4096 + (m + 16) * 128: k * 4096 + (m + 17) * 128],
                     xrb[:, k * TC:(k + 1) * TC], start=(k == 0), stop=(k == 7))
            P.act(sil[:], pB[:], AF.Silu)
            o = m * (TC + 3)
            P.copy(xi[:, o:o + 3], xi[:, o + TC:o + TC + 3])
            P.tt(xi[:, o + 3:o + 3 + TC], pA[:], sil[:], ALU.mult)
        for m in range(16):
            o = m * (TC + 3)
            oc = m * TC
            P.ts(xc[:, oc:oc + TC], xi[:, o:o + TC], cw[:, m * 4:m * 4 + 1], ALU.mult)
            for i in range(1, 4):
                P.stt(xc[:, oc:oc + TC], xi[:, o + i:o + i + TC],
                      cw[:, m * 4 + i:m * 4 + i + 1], xc[:, oc:oc + TC],
                      ALU.mult, ALU.add)
            P.copy(xcb[:, oc:oc + TC], xc[:, oc:oc + TC])
        for m in range(16):
            pm = pC if (m % 2 == 0) else pD
            for k in range(16):
                P.mm(pm[:], dtw[:, k * 2048 + m * 128: k * 2048 + (m + 1) * 128],
                     xcb[:, k * TC:(k + 1) * TC], start=(k == 0), stop=(k == 15))
            o = m * TC
            softplus_psum(P, pm[:], dtb[:, m:m + 1], dts[:, o:o + TC], a1s[:], a2s[:])
            P.tt(us[:, o:o + TC], dts[:, o:o + TC], xc[:, o:o + TC], ALU.mult)
        for k in range(16):
            P.mm(pBC[:], bcw[:, k * 32:(k + 1) * 32], xcb[:, k * TC:(k + 1) * TC],
                 start=(k == 0), stop=(k == 15))
        P.copy(bcs[:], pBC[:])
        P.dma(bcS_h[:, c * TC:(c + 1) * TC], bcs[:])
        for (hb, sb_) in ((dt_h, dts), (u_h, us), (xc_h, xc)):
            P.dma(fm_ap(hb, 0, 16, c * TC, TC),
                  sb_[:].rearrange("p (f t) -> p f t", t=TC))
    P.end_phase()
    return P



def build_A2(P=None):
    P = P if P is not None else Prog("A2")
    P.begin_phase()
    TS = 512
    NCH = T // TS
    x_h = P.dram_in("x", (T, D))
    dt_h = P.dram_in("dtS", (2048, T))
    u_h = P.dram_in("uS", (2048, T))
    xc_h = P.dram_in("xcS", (2048, T))
    bc_h = P.dram_in("bcS", (32, T))
    sel_h = P.dram_in("sel32", (32, 32 * 128))
    alog_h = P.dram_in("A_log", (2048, 16))
    dskip_h = P.dram_in("D_skip", (2048,))
    opw_h = P.dram_in("out_proj_w", (2048, D))
    ident_h = P.dram_in("ident", (128, 128))
    t1_h = P.dram_out("t1", (D, T))

    opw = P.sbuf("opw", [128, 16 * D], BF16)
    for k in range(16):
        P.dma(opw[:, k * D:(k + 1) * D],
              opw_h.rearrange("(f p) m -> p f m", p=128)[:, k], queue="gpsimd")
    expA = P.sbuf("expA", [128, 16 * 16], F32)
    P.dma(expA[:].rearrange("p (f s) -> p f s", s=16),
          alog_h.rearrange("(f p) s -> p f s", p=128))
    P.act(expA[:], expA[:], AF.Exp)
    dsk = P.sbuf("dsk", [128, 16], F32)
    P.dma_slow(dsk[:], dskip_h.rearrange("(f p) -> p f", p=128))
    ident = P.sbuf("ident", [128, 128], F32)
    P.dma(ident[:], ident_h[:])
    sel = P.sbuf("sel", [32, 32 * 128], F32)
    P.dma(sel[:], sel_h[:])

    hstate = P.sbuf("hstate", [128, 256], F32)
    P.memset(hstate[:], 0.0)

    HT = 8
    dts = P.sbuf("dts", [128, HT * TS])
    us = P.sbuf("us", [128, HT * TS])
    xcs = P.sbuf("xcs", [128, HT * TS])
    yacc = P.sbuf("yacc", [128, HT * TS])
    ys = P.sbuf("ys", [128, 16 * TS], BF16)
    bcs = P.sbuf("bcs", [32, TS], F32)
    bmr = P.sbuf("bmr", [128, TS]); cmr = P.sbuf("cmr", [128, TS])
    tmp = P.sbuf("tmp", [128, TS]); ab = P.sbuf("ab", [128, TS])
    d1 = P.sbuf("d1", [128, TS]); hh = P.sbuf("hh", [128, TS])
    pr = P.sbuf("pr", [128, TS])
    xT = P.sbuf("xT", [128, 8 * TS])
    sx = P.sbuf("sx", [128, D])
    t1s = P.sbuf("t1s", [128, 8 * TS])

    pb1 = P.psum("pb1", [128, TS])
    pb2 = P.psum("pb2", [128, TS])
    pt1 = [P.psum("pt1", [128, TS]) for _ in range(6)]

    x_r = x_h.rearrange("(n p) d -> n p d", p=128)

    for c in range(NCH):
        P.dma(bcs[:], bc_h[:, c * TS:(c + 1) * TS])
        for tt_ in range(4):
            P.dma(sx[:], x_r[c * 4 + tt_])
            for ft in range(0, 8, 4):
                for j in range(4):
                    P.transpose(pb2[:, j * 128:(j + 1) * 128],
                                sx[:, (ft + j) * 128:(ft + j + 1) * 128], ident[:])
                for j in range(4):
                    P.copy(xT[:, (ft + j) * TS + tt_ * 128:(ft + j) * TS + tt_ * 128 + 128],
                           pb2[:, j * 128:(j + 1) * 128])
        for half in range(2):
            for hb, sb_ in ((dt_h, dts), (u_h, us), (xc_h, xcs)):
                P.dma(sb_[:].rearrange("p (f t) -> p f t", t=TS),
                      fm_ap(hb, half * 8, 8, c * TS, TS))
            for m in range(HT):
                P.memset(yacc[:, m * TS:(m + 1) * TS], 0.0)
            for s in range(16):
                P.mm(pb1[:], sel[:, s * 128:(s + 1) * 128], bcs[:])
                P.copy(bmr[:], pb1[:])
                P.mm(pb1[:], sel[:, (16 + s) * 128:(17 + s) * 128], bcs[:])
                P.copy(cmr[:], pb1[:])
                for m in range(HT):
                    dt_t = dts[:, m * TS:(m + 1) * TS]
                    u_t = us[:, m * TS:(m + 1) * TS]
                    ft = half * 8 + m
                    st_col = hstate[:, s * 16 + ft: s * 16 + ft + 1]
                    P.ts(tmp[:], dt_t, expA[:, ft * 16 + s: ft * 16 + s + 1],
                         ALU.mult, eng="gpsimd")
                    P.act(ab[:], tmp[:], AF.Exp, scale=-1.0)
                    P.tt(d1[:], u_t, bmr[:], ALU.mult)
                    P.scan(hh[:], ab[:], d1[:], st_col, ALU.mult, ALU.add)
                    P.copy(st_col, hh[:, TS - 1:TS])
                    P.tt(pr[:], hh[:], cmr[:], ALU.mult)
                    P.tt(yacc[:, m * TS:(m + 1) * TS], yacc[:, m * TS:(m + 1) * TS],
                         pr[:], ALU.add, eng="gpsimd")
            for m in range(HT):
                ft = half * 8 + m
                P.stt(ys[:, ft * TS:(ft + 1) * TS], xcs[:, m * TS:(m + 1) * TS],
                      dsk[:, ft:ft + 1], yacc[:, m * TS:(m + 1) * TS],
                      ALU.mult, ALU.add)
        for grp, mlist in ((0, range(6)), (1, range(6, 8))):
            for mi, m in enumerate(mlist):
                for k in range(16):
                    P.mm(pt1[mi][:], opw[:, k * D + m * 128: k * D + (m + 1) * 128],
                         ys[:, k * TS:(k + 1) * TS], start=(k == 0), stop=(k == 15))
            for mi, m in enumerate(mlist):
                P.tt(t1s[:, m * TS:(m + 1) * TS], pt1[mi][:],
                     xT[:, m * TS:(m + 1) * TS], ALU.add)
        P.dma(fm_ap(t1_h, 0, 8, c * TS, TS),
              t1s[:].rearrange("p (f t) -> p f t", t=TS))
    P.end_phase()
    return P




def emit_first_argmax(P, work, scr):
    eq, cum, zer, first, m = scr
    P.red(m, work, ALU.max)
    P.ts(eq, work, m[:, 0:1], ALU.is_equal)
    P.scan(cum, eq, zer, 0.0, ALU.add, ALU.add)
    P.ts(cum, cum, 1.0, ALU.is_equal)
    P.tt(first, eq, cum, ALU.mult)
    return first, m


def build_B(P=None):
    P = P if P is not None else Prog("B")
    P.begin_phase()
    TCH = 512
    HALF = T // 2
    xn_h = P.dram_in("xn", (D, T), BF16)
    t1_h = P.dram_in("t1", (D, T))
    lg_h = P.dram_in("logits0", (T, 16))
    bias_h = P.dram_in("bias", (16,))
    gw_h = P.dram_in("gate_w", (16, D, D))
    uw_h = P.dram_in("up_w", (16, D, D))
    dw_h = P.dram_in("down_w", (16, D, D))
    shg_h = P.dram_in("sh_gate_w", (D, 2048))
    shu_h = P.dram_in("sh_up_w", (D, 2048))
    shd_h = P.dram_in("sh_down_w", (2048, D))
    sel_h = P.dram_in("sel32", (32, 32 * 128))
    ident_h = P.dram_in("ident", (128, 128))
    x1_h = P.dram_out("x1", (D, T))

    ident = P.sbuf("ident", [128, 128], F32)
    P.dma(ident[:], ident_h[:])
    selb = P.sbuf("selb", [32, 16 * 128], BF16)
    P.dma(selb[:], sel_h[0:32, 0:16 * 128], queue="gpsimd")

    lgt = P.sbuf("lgt", [128, 16], F32)
    sco = P.sbuf("sco", [128, 16], F32)
    sb_ = P.sbuf("sb_", [128, 16], F32)
    gsc = P.sbuf("gsc", [128, 4], F32)
    gmask = P.sbuf("gmask", [128, 4], F32)
    emask = P.sbuf("emask", [128, 16], F32)
    msk = P.sbuf("msk", [128, 16], F32)
    comb = P.sbuf("comb", [128, 16], F32)
    wsum = P.sbuf("wsum", [128, 1], F32)
    twc = P.sbuf("twc", [128, 1], F32)
    eq = P.sbuf("eq", [128, 16], F32)
    cum = P.sbuf("cum", [128, 16], F32)
    zer = P.sbuf("zer", [128, 16], F32)
    first = P.sbuf("first", [128, 16], F32)
    mm_ = P.sbuf("mm_", [128, 1], F32)
    brep = P.sbuf("brep", [128, 16], F32)
    combT = P.sbuf("combT", [16, T], BF16)
    pR = P.psum("pR", [128, 128])

    P.memset(zer[:], 0.0)
    ones1 = P.sbuf("ones1", [1, 128], F32)
    P.memset(ones1[:], 1.0)
    P.dma(brep[0:1, :], bias_h.unsqueeze(0))
    P.mm(pR[:, 0:16], ones1[:], brep[0:1, :])
    P.copy(brep[:], pR[:, 0:16])

    for tt_ in range(T // 128):
        P.dma(lgt[:], lg_h[tt_ * 128:(tt_ + 1) * 128])
        P.act(sco[:], lgt[:], AF.Sigmoid)
        P.tt(sb_[:], sco[:], brep[:], ALU.add)
        scr4 = (eq[:, 0:4], cum[:, 0:4], zer[:, 0:4], first[:, 0:4], mm_)
        for g in range(4):
            sg = sb_[:, g * 4:(g + 1) * 4]
            P.copy(msk[:, 0:4], sg)
            f, m = emit_first_argmax(P, msk[:, 0:4], scr4)
            P.copy(gsc[:, g:g + 1], m)
            P.stt(msk[:, 0:4], f, -1e30, msk[:, 0:4], ALU.mult, ALU.add)
            P.red(mm_, msk[:, 0:4], ALU.max)
            P.tt(gsc[:, g:g + 1], gsc[:, g:g + 1], mm_, ALU.add)
        P.memset(gmask[:], 0.0)
        P.copy(msk[:, 0:4], gsc[:])
        for _ in range(2):
            f, m = emit_first_argmax(P, msk[:, 0:4], scr4)
            P.tt(gmask[:], gmask[:], f, ALU.add)
            P.stt(msk[:, 0:4], f, -1e30, msk[:, 0:4], ALU.mult, ALU.add)
        P.copy(emask[:].rearrange("p (g r) -> p g r", r=4),
               gmask[:].unsqueeze(2).broadcast_to([128, 4, 4]))
        P.ts(msk[:], sb_[:], 1.0, ALU.add)
        P.tt(msk[:], msk[:], emask[:], ALU.mult)
        P.ts(msk[:], msk[:], 1.0, ALU.subtract)
        P.memset(comb[:], 0.0)
        P.memset(wsum[:], 0.0)
        scr16 = (eq[:], cum[:], zer[:], first[:], mm_)
        for _ in range(4):
            f, m = emit_first_argmax(P, msk[:], scr16)
            P.tt(eq[:], f, sco[:], ALU.mult)
            P.red(twc, eq[:], ALU.add)
            P.stt(comb[:], f, twc[:, 0:1], comb[:], ALU.mult, ALU.add)
            P.tt(wsum[:], wsum[:], twc, ALU.add)
            P.stt(msk[:], f, -1e30, msk[:], ALU.mult, ALU.add)
        P.ts(wsum[:], wsum[:], 1e-20, ALU.add)
        P.recip(twc, wsum[:])
        P.ts(twc, twc, 2.5, ALU.mult)
        P.ts(comb[:], comb[:], twc[:, 0:1], ALU.mult)
        P.transpose(pR[0:16, :], comb[:], ident[:])
        P.copy(combT[:, tt_ * 128:(tt_ + 1) * 128], pR[0:16, :])

    NK = 8
    wz = P.sbuf("wz", [128, 49152], BF16)
    GATE, UP, DOWN = 0, 8192, 16384
    SHG, SHU, SHD = 0, 16384, 32768
    xnc = P.sbuf("xnc", [128, 8 * TCH], BF16)
    hub = P.sbuf("hub", [128, 8 * TCH], BF16)
    silb = P.sbuf("silb", [128, TCH], F32)
    crep = P.sbuf("crep", [128, TCH], F32)
    acc = P.sbuf("acc", [128, 8 * HALF], F32)
    t1b = P.sbuf("t1b", [128, 8 * 256], F32)
    pG = P.psum("pG", [128, TCH]); pU = P.psum("pU", [128, TCH])
    pC = P.psum("pC", [128, TCH])
    pD1 = P.psum("pD1", [128, TCH]); pD2 = P.psum("pD2", [128, TCH])

    gw_r = gw_h.rearrange("e (f p) m -> e p f m", p=128)
    uw_r = uw_h.rearrange("e (f p) m -> e p f m", p=128)
    dw_r = dw_h.rearrange("e (f p) m -> e p f m", p=128)

    def expert_mm(e_, h_, sub):
        t0 = h_ * HALF + sub * TCH
        P.dma(xnc[:].rearrange("p (f t) -> p f t", t=TCH), fm_ap(xn_h, 0, 8, t0, TCH))
        for mt in range(8):
            for k in range(NK):
                P.mm(pG[:], wz[:, GATE + k * D + mt * 128: GATE + k * D + (mt + 1) * 128],
                     xnc[:, k * TCH:(k + 1) * TCH], start=(k == 0), stop=(k == NK - 1))
            for k in range(NK):
                P.mm(pU[:], wz[:, UP + k * D + mt * 128: UP + k * D + (mt + 1) * 128],
                     xnc[:, k * TCH:(k + 1) * TCH], start=(k == 0), stop=(k == NK - 1))
            P.act(silb[:], pG[:], AF.Silu)
            P.tt(hub[:, mt * TCH:(mt + 1) * TCH], silb[:], pU[:], ALU.mult)
        P.mm(pC[:], selb[0:16, e_ * 128:(e_ + 1) * 128], combT[:, t0:t0 + TCH])
        P.copy(crep[:], pC[:])
        for mt in range(8):
            P.tt(hub[:, mt * TCH:(mt + 1) * TCH], hub[:, mt * TCH:(mt + 1) * TCH],
                 crep[:], ALU.mult)
        for m in range(8):
            pd = pD1 if m % 2 == 0 else pD2
            for k in range(8):
                P.mm(pd[:], wz[:, DOWN + k * D + m * 128: DOWN + k * D + (m + 1) * 128],
                     hub[:, k * TCH:(k + 1) * TCH], start=(k == 0), stop=(k == 7))
            ao = m * (HALF) + sub * TCH
            P.tt(acc[:, ao:ao + TCH], acc[:, ao:ao + TCH], pd[:], ALU.add)

    def shared_mm(h_, sub):
        t0 = h_ * HALF + sub * TCH
        P.dma(xnc[:].rearrange("p (f t) -> p f t", t=TCH), fm_ap(xn_h, 0, 8, t0, TCH))
        for grp in range(2):
            for mt in range(8):
                mg = grp * 8 + mt
                for k in range(8):
                    P.mm(pG[:], wz[:, SHG + k * 2048 + mg * 128: SHG + k * 2048 + (mg + 1) * 128],
                         xnc[:, k * TCH:(k + 1) * TCH], start=(k == 0), stop=(k == 7))
                for k in range(8):
                    P.mm(pU[:], wz[:, SHU + k * 2048 + mg * 128: SHU + k * 2048 + (mg + 1) * 128],
                         xnc[:, k * TCH:(k + 1) * TCH], start=(k == 0), stop=(k == 7))
                P.act(silb[:], pG[:], AF.Silu)
                P.tt(hub[:, mt * TCH:(mt + 1) * TCH], silb[:], pU[:], ALU.mult)
            for m in range(8):
                pd = pD1 if m % 2 == 0 else pD2
                for k in range(8):
                    kg = grp * 8 + k
                    P.mm(pd[:], wz[:, SHD + kg * D + m * 128: SHD + kg * D + (m + 1) * 128],
                         hub[:, k * TCH:(k + 1) * TCH], start=(k == 0), stop=(k == 7))
                ao = m * HALF + sub * TCH
                if grp == 0:
                    P.copy(acc[:, ao:ao + TCH], pd[:])
                else:
                    P.tt(acc[:, ao:ao + TCH], acc[:, ao:ao + TCH], pd[:], ALU.add)

    for h_ in range(2):
        P.dma(wz[:, SHG:SHG + 16384].rearrange("p (f m) -> p f m", m=2048),
              shg_h.rearrange("(f p) m -> p f m", p=128), queue="gpsimd")
        P.dma(wz[:, SHU:SHU + 16384].rearrange("p (f m) -> p f m", m=2048),
              shu_h.rearrange("(f p) m -> p f m", p=128), queue="gpsimd")
        P.dma(wz[:, SHD:SHD + 16384].rearrange("p (f m) -> p f m", m=1024),
              shd_h.rearrange("(f p) m -> p f m", p=128), queue="gpsimd")
        for sub in range(4):
            shared_mm(h_, sub)
        for e_ in range(16):
            P.dma(wz[:, GATE:GATE + 8192].rearrange("p (f m) -> p f m", m=1024),
                  gw_r[e_], queue="gpsimd")
            P.dma(wz[:, UP:UP + 8192].rearrange("p (f m) -> p f m", m=1024),
                  uw_r[e_], queue="gpsimd")
            P.dma(wz[:, DOWN:DOWN + 8192].rearrange("p (f m) -> p f m", m=1024),
                  dw_r[e_], queue="gpsimd")
            for sub in range(4):
                expert_mm(e_, h_, sub)
        for sub in range(8):
            t0 = h_ * HALF + sub * 256
            P.dma(t1b[:].rearrange("p (f t) -> p f t", t=256), fm_ap(t1_h, 0, 8, t0, 256))
            for m in range(8):
                ao = m * HALF + sub * 256
                P.tt(t1b[:, m * 256:(m + 1) * 256], t1b[:, m * 256:(m + 1) * 256],
                     acc[:, ao:ao + 256], ALU.add)
            P.dma(fm_ap(x1_h, 0, 8, t0, 256),
                  t1b[:].rearrange("p (f t) -> p f t", t=256))
    P.end_phase()
    return P




def fm_norm(P, xt, nf, TCH, sq, rowbuf, pS, ones128, epsap, out_rows):
    for k in range(nf):
        P.act(sq[:, 0:TCH], xt[:, k * TCH:(k + 1) * TCH], AF.Square)
        P.mm(pS[0:1, 0:TCH], ones128[:, 0:1], sq[:, 0:TCH],
             start=(k == 0), stop=(k == nf - 1))
    P.ts(rowbuf[0:1, 0:TCH], pS[0:1, 0:TCH], 1.0 / (nf * 128), ALU.mult)
    P.act(rowbuf[0:1, 0:TCH], rowbuf[0:1, 0:TCH], AF.Ln, bias=epsap)
    P.act(out_rows, rowbuf[0:1, 0:TCH], AF.Exp, scale=-0.5)


def build_C1a(P=None):
    P = P if P is not None else Prog("C1a")
    P.begin_phase()
    TO = 512
    NS = T // TO
    x1_h = P.dram_in("x1", (D, T))
    wq_h = P.dram_in("Wq", (D, 768))
    wk_h = P.dram_in("Wk", (D, 768))
    wv_h = P.dram_in("Wv", (D, 1536))
    wg_h = P.dram_in("Wg", (D, 1536))
    wb_h = P.dram_in("Wb", (D, 8))
    wa_h = P.dram_in("Wa", (D, 8))
    alog_h = P.dram_in("A_log1", (8,))
    dtb_h = P.dram_in("dt_bias", (8,))
    cq_h = P.dram_in("conv_q_w", (768, 4))
    ck_h = P.dram_in("conv_k_w", (768, 4))
    cv_h = P.dram_in("conv_v_w", (1536, 4))
    n1_h = P.dram_in("norm1_w", (D,))

    xn1_h = P.dram_out("xn1", (D, T), BF16)
    q_h = P.dram_out("qS", (768, T), BF16)
    k_h = P.dram_out("kS", (768, T), BF16)
    v_h = P.dram_out("vS", (1536, T), BF16)
    g_h = P.dram_out("gS", (1536, T), BF16)
    beta_h = P.dram_out("betaS", (8, T))
    cum_h = P.dram_out("cumS", (8, T))

    wq = P.sbuf("wq", [128, 8 * 768], BF16)
    wk = P.sbuf("wk", [128, 8 * 768], BF16)
    wv = P.sbuf("wv", [128, 8 * 1536], BF16)
    wg = P.sbuf("wg", [128, 8 * 1536], BF16)
    wba = P.sbuf("wba", [128, 8 * 16], BF16)
    for hname, hh, sb_, m in (("q", wq_h, wq, 768), ("k", wk_h, wk, 768),
                              ("v", wv_h, wv, 1536), ("g", wg_h, wg, 1536)):
        P.dma(sb_[:].rearrange("p (f m) -> p f m", m=m),
              hh.rearrange("(f p) m -> p f m", p=128), queue="gpsimd")
    P.dma(wba[:].rearrange("p (f m) -> p f m", m=16)[:, :, 0:8],
          wb_h.rearrange("(f p) m -> p f m", p=128), queue="gpsimd")
    P.dma(wba[:].rearrange("p (f m) -> p f m", m=16)[:, :, 8:16],
          wa_h.rearrange("(f p) m -> p f m", p=128), queue="gpsimd")
    cqw = P.sbuf("cqw", [96, 8 * 4], F32)
    ckw = P.sbuf("ckw", [96, 8 * 4], F32)
    cvw = P.sbuf("cvw", [96, 16 * 4], F32)
    P.dma_slow(cqw[:].rearrange("p (f m) -> p f m", m=4),
               cq_h.rearrange("(f p) m -> p f m", p=96))
    P.dma_slow(ckw[:].rearrange("p (f m) -> p f m", m=4),
               ck_h.rearrange("(f p) m -> p f m", p=96))
    P.dma_slow(cvw[:].rearrange("p (f m) -> p f m", m=4),
               cv_h.rearrange("(f p) m -> p f m", p=96))
    n1c = P.sbuf("n1c", [128, 8], F32)
    P.dma_slow(n1c[:], n1_h.rearrange("(f p) -> p f", p=128))
    nega = P.sbuf("nega", [8, 1], F32)
    P.dma(nega[:], alog_h.unsqueeze(1))
    P.act(nega[:], nega[:], AF.Exp)
    P.ts(nega[:], nega[:], -1.0, ALU.mult)
    dtbc = P.sbuf("dtbc", [8, 1], F32)
    P.dma(dtbc[:], dtb_h.unsqueeze(1))

    ones128 = P.sbuf("ones128", [128, 1], F32)
    P.memset(ones128[:], 1.0)
    ones1 = P.sbuf("ones1", [1, 128], F32)
    P.memset(ones1[:], 1.0)
    epsr = P.sbuf("epsr", [1, 1], F32)
    P.memset(epsr[:], 1e-6)
    eps5 = P.sbuf("eps5", [1, 1], F32)
    P.memset(eps5[:], 1e-5)
    epsl2 = P.sbuf("epsl2", [96, 1], F32)
    P.memset(epsl2[:], 1e-6)

    x1t = P.sbuf("x1t", [128, 8 * TO])
    xn1f = P.sbuf("xn1f", [128, 8 * TO])
    xn1b = P.sbuf("xn1b", [128, 8 * TO], BF16)
    hb = P.sbuf("hb", [128, 8 * TO], BF16)
    sq = P.sbuf("sq", [128, TO])
    rowb = P.sbuf("rowb", [1, TO], F32)
    rrep = P.sbuf("rrep", [128, TO], F32)
    qpad = P.sbuf("qpad", [96, TO + 3], F32)
    qtail = P.sbuf("qtail", [96, 48 * 3], F32)
    cnv = P.sbuf("cnv", [96, TO], F32)
    sq96 = P.sbuf("sq96", [96, TO], F32)
    row96 = P.sbuf("row96", [1, TO], F32)
    rs96 = P.sbuf("rs96", [96, TO], F32)
    qb = P.sbuf("qb", [96, 8 * TO], BF16)
    kb = P.sbuf("kb", [96, 8 * TO], BF16)
    vb = P.sbuf("vb", [96, 16 * TO], BF16)
    gb = P.sbuf("gb", [96, 16 * TO], BF16)
    ba = P.sbuf("ba", [16, TO], F32)
    bet = P.sbuf("bet", [8, TO], F32)
    gdec = P.sbuf("gdec", [8, TO], F32)
    cums = P.sbuf("cums", [8, TO], F32)
    zer8 = P.sbuf("zer8", [8, 128], F32)
    a1t = P.sbuf("a1t", [8, TO], F32)
    a2t = P.sbuf("a2t", [8, TO], F32)

    pS = P.psum("pS", [1, TO])
    pM = P.psum("pM", [128, TO])
    pP = P.psum("pP", [96, TO])
    pB = P.psum("pB", [16, TO])

    P.memset(qtail[:], 0.0)
    P.memset(zer8[:], 0.0)

    def proj_conv(wt, nheads, conv_w, tail_base, out_sb, l2norm, scale, s):
        for hh in range(nheads):
            for k in range(8):
                P.mm(pP[:], wt[:, k * (96 * nheads) + hh * 96:
                               k * (96 * nheads) + (hh + 1) * 96],
                     hb[:, k * TO:(k + 1) * TO], start=(k == 0), stop=(k == 7))
            tb = (tail_base + hh) * 3
            P.copy(qpad[:, 0:3], qtail[:, tb:tb + 3])
            P.copy(qpad[:, 3:3 + TO], pP[:])
            P.copy(qtail[:, tb:tb + 3], qpad[:, TO:TO + 3])
            P.ts(cnv[:], qpad[:, 0:TO], conv_w[:, hh * 4:hh * 4 + 1], ALU.mult)
            for i in range(1, 4):
                P.stt(cnv[:], qpad[:, i:i + TO], conv_w[:, hh * 4 + i:hh * 4 + i + 1],
                      cnv[:], ALU.mult, ALU.add)
            P.act(cnv[:], cnv[:], AF.Silu)
            if l2norm:
                P.act(sq96[:], cnv[:], AF.Square)
                P.mm(pS[0:1, :], ones128[0:96, 0:1], sq96[:], start=True, stop=True)
                P.copy(row96[0:1, :], pS[0:1, :])
                P.ts(row96[0:1, :], row96[0:1, :], epsr[0:1, 0:1], ALU.add)
                P.act(row96[0:1, :], row96[0:1, :], AF.Ln)
                P.act(row96[0:1, :], row96[0:1, :], AF.Exp, scale=-0.5)
                if scale != 1.0:
                    P.ts(row96[0:1, :], row96[0:1, :], scale, ALU.mult)
                P.mm(pP[:], ones1[0:1, 0:96], row96[0:1, :], start=True, stop=True)
                P.copy(rs96[:], pP[:])
                P.tt(out_sb[:, hh * TO:(hh + 1) * TO], cnv[:], rs96[:], ALU.mult)
            else:
                P.copy(out_sb[:, hh * TO:(hh + 1) * TO], cnv[:])

    wq_r = wq
    for s in range(NS):
        t0 = s * TO
        P.dma(x1t[:].rearrange("p (f t) -> p f t", t=TO), fm_ap(x1_h, 0, 8, t0, TO))
        for k in range(8):
            P.mm(pS[0:1, :], ones128[:, 0:1], x1t[:, k * TO:(k + 1) * TO],
                 start=(k == 0), stop=(k == 7))
        P.ts(rowb[0:1, :], pS[0:1, :], 1.0 / D, ALU.mult)
        P.mm(pM[:], ones1[:], rowb[0:1, :], start=True, stop=True)
        P.copy(rrep[:], pM[:])
        for k in range(8):
            P.tt(xn1f[:, k * TO:(k + 1) * TO], x1t[:, k * TO:(k + 1) * TO],
                 rrep[:], ALU.subtract)
        for k in range(8):
            P.act(sq[:], xn1f[:, k * TO:(k + 1) * TO], AF.Square)
            P.mm(pS[0:1, :], ones128[:, 0:1], sq[:], start=(k == 0), stop=(k == 7))
        P.ts(rowb[0:1, :], pS[0:1, :], 1.0 / D, ALU.mult)
        P.act(rowb[0:1, :], rowb[0:1, :], AF.Ln, bias=eps5[0:1, 0:1])
        P.act(rowb[0:1, :], rowb[0:1, :], AF.Exp, scale=-0.5)
        P.mm(pM[:], ones1[:], rowb[0:1, :], start=True, stop=True)
        P.copy(rrep[:], pM[:])
        for k in range(8):
            P.tt(xn1f[:, k * TO:(k + 1) * TO], xn1f[:, k * TO:(k + 1) * TO],
                 rrep[:], ALU.mult)
        P.copy(xn1b[:], xn1f[:])
        P.dma(fm_ap(xn1_h, 0, 8, t0, TO), xn1b[:].rearrange("p (f t) -> p f t", t=TO))
        fm_norm(P, xn1f, 8, TO, sq, rowb, pS, ones128, epsr[0:1, 0:1], rowb[0:1, 0:TO])
        P.mm(pM[:], ones1[:], rowb[0:1, :], start=True, stop=True)
        P.copy(rrep[:], pM[:])
        for k in range(8):
            P.tt(sq[:], xn1f[:, k * TO:(k + 1) * TO], rrep[:], ALU.mult)
            P.ts(hb[:, k * TO:(k + 1) * TO], sq[:], n1c[:, k:k + 1], ALU.mult)
        proj_conv(wq, 8, cqw, 0, qb, True, float(96 ** -0.5), s)
        proj_conv(wk, 8, ckw, 8, kb, True, 1.0, s)
        proj_conv(wv, 16, cvw, 16, vb, False, 1.0, s)
        for hh in range(16):
            for k in range(8):
                P.mm(pP[:], wg[:, k * 1536 + hh * 96: k * 1536 + (hh + 1) * 96],
                     hb[:, k * TO:(k + 1) * TO], start=(k == 0), stop=(k == 7))
            P.copy(gb[:, hh * TO:(hh + 1) * TO], pP[:])
        for k in range(8):
            P.mm(pB[:], wba[:, k * 16:(k + 1) * 16], hb[:, k * TO:(k + 1) * TO],
                 start=(k == 0), stop=(k == 7))
        P.copy(ba[:], pB[:])
        P.act(bet[:], ba[0:8, :], AF.Sigmoid)
        for k in range(8):
            P.mm(pB[0:8, :], wba[:, k * 16 + 8:k * 16 + 16],
                 hb[:, k * TO:(k + 1) * TO], start=(k == 0), stop=(k == 7))
        P.act(a1t[:], pB[0:8, :], AF.Abs, bias=dtbc[:, 0:1])
        P.act(a1t[:], a1t[:], AF.Exp, scale=-1.0)
        P.act(a1t[:], a1t[:], AF.Ln, bias=1.0)
        P.act(a2t[:], pB[0:8, :], AF.Relu, bias=dtbc[:, 0:1])
        P.tt(a1t[:], a1t[:], a2t[:], ALU.add)
        P.ts(gdec[:], a1t[:], nega[:, 0:1], ALU.mult)
        for ci in range(TO // 128):
            P.scan(cums[:, ci * 128:(ci + 1) * 128], gdec[:, ci * 128:(ci + 1) * 128],
                   zer8[:], 0.0, ALU.add, ALU.add)
        P.dma(beta_h[:, t0:t0 + TO], bet[:])
        P.dma(cum_h[:, t0:t0 + TO], cums[:])
        for hh, sb_, nf in ((q_h, qb, 8), (k_h, kb, 8), (v_h, vb, 16), (g_h, gb, 16)):
            P.dma(hh.rearrange("(f p) t -> p f t", p=96)[:, :, t0:t0 + TO],
                  sb_[:].rearrange("p (f t) -> p f t", t=TO))
    P.end_phase()
    return P




def build_C1b(P=None):
    P = P if P is not None else Prog("C1b")
    P.begin_phase()
    TO = 512
    NS = T // TO
    C = 128
    q_hb = P.dram_in("qS", (768, T), BF16)
    k_hb = P.dram_in("kS", (768, T), BF16)
    v_hb = P.dram_in("vS", (1536, T), BF16)
    g_hb = P.dram_in("gS", (1536, T), BF16)
    beta_h = P.dram_in("betaS", (8, T))
    cum_h = P.dram_in("cumS", (8, T))
    xn1_h = P.dram_in("xn1", (D, T), BF16)
    onw_h = P.dram_in("onorm_w", (192,))
    wo_h = P.dram_in("Wo", (1536, D))
    sel_h = P.dram_in("sel32", (32, 32 * 128))
    ident_h = P.dram_in("ident", (128, 128))
    masks_h = P.dram_in("masks", (3, 128, 128))
    xmid_h = P.dram_out("xmid", (D, T))

    ident = P.sbuf("ident", [128, 128], F32)
    P.dma(ident[:], ident_h[:])
    identb = P.sbuf("identb", [128, 128], BF16)
    P.dma(identb[:], ident_h[:], queue="gpsimd")
    mlo = P.sbuf("mlo", [128, 128], F32)
    mup = P.sbuf("mup", [128, 128], F32)
    mui = P.sbuf("mui", [128, 128], F32)
    P.dma(mlo[:], masks_h[0]); P.dma(mup[:], masks_h[1]); P.dma(mui[:], masks_h[2])
    sel8 = P.sbuf("sel8", [8, 8 * 128], F32)
    P.dma(sel8[:], sel_h[0:8, 0:8 * 128])
    ones1 = P.sbuf("ones1", [1, 128], F32)
    P.memset(ones1[:], 1.0)
    eps192 = P.sbuf("eps192", [128, 1], F32)
    P.memset(eps192[:], 1e-6)
    onwrep = P.sbuf("onwrep", [128, 192], F32)
    onwrow = P.sbuf("onwrow", [1, 192], F32)
    P.dma(onwrow[:], onw_h.unsqueeze(0))
    pO = P.psum("pO", [128, 192])
    P.mm(pO[:], ones1[:], onwrow[:])
    P.copy(onwrep[:], pO[:])
    wo = P.sbuf("wo", [96, 16 * D], BF16)
    P.dma(wo[:].rearrange("p (f m) -> p f m", m=D),
          wo_h.rearrange("(f p) m -> p f m", p=96), queue="gpsimd")

    qb = P.sbuf("qb", [96, 8 * TO], BF16)
    kb = P.sbuf("kb", [96, 8 * TO], BF16)
    vb = P.sbuf("vb", [96, 16 * TO], BF16)
    gb = P.sbuf("gb", [96, 16 * TO], BF16)
    xn1b = P.sbuf("xn1b", [128, 8 * TO], BF16)
    xn1f = P.sbuf("xn1f", [128, TO], F32)
    bet = P.sbuf("bet", [8, TO], F32)
    cums = P.sbuf("cums", [8, TO], F32)
    cbT = P.sbuf("cbT", [128, 16], F32)
    Sf = P.sbuf("Sf", [96, 8 * 192], F32)
    Sb = P.sbuf("Sb", [96, 8 * 192], BF16)
    P.memset(Sf[:], 0.0)
    ofm = P.sbuf("ofm", [96, 16 * TO], BF16)
    xmid = P.sbuf("xmid", [128, 8 * TO], F32)

    kkS = P.sbuf("kkS", [128, 128], F32)
    kqS = P.sbuf("kqS", [128, 128], F32)
    crS = P.sbuf("crS", [128, 128], F32)
    rS = P.sbuf("rS", [128, 128], F32)
    DN = P.sbuf("DN", [128, 128], F32)
    DT = P.sbuf("DT", [128, 128], F32)
    brS = P.sbuf("brS", [128, 128], F32)
    Nc = P.sbuf("Nc", [128, 128], F32)
    NTc = P.sbuf("NTc", [128, 128], F32)
    Nn = P.sbuf("Nn", [128, 128], F32)
    NTn = P.sbuf("NTn", [128, 128], F32)
    Pm = P.sbuf("Pm", [128, 128], F32)
    Vtok = P.sbuf("Vtok", [128, 192], F32)
    Gtok = P.sbuf("Gtok", [128, 192], F32)
    gKS0 = P.sbuf("gKS0", [128, 192], F32)
    RHS = P.sbuf("RHS", [128, 192], F32)
    Wf = P.sbuf("Wf", [128, 192], F32)
    Wb_ = P.sbuf("Wb_", [128, 192], BF16)
    MTb = P.sbuf("MTb", [128, 128], BF16)
    Ot = P.sbuf("Ot", [128, 192], F32)
    qs_ = P.sbuf("qs_", [128, 192], F32)
    Ktok = P.sbuf("Ktok", [128, 96], BF16)
    oftok = P.sbuf("oftok", [128, 192], BF16)
    gcol = P.sbuf("gcol", [128, 1], F32)
    dkcol = P.sbuf("dkcol", [128, 1], F32)
    grep = P.sbuf("grep", [96, 1], F32)
    ecr = P.sbuf("ecr", [1, 1], F32)
    ocol = P.sbuf("ocol", [128, 1], F32)
    sqO = P.sbuf("sqO", [128, 192], F32)

    pX = P.psum("pX", [128, 128])
    pY = P.psum("pY", [128, 192])
    pO2 = P.psum("pO2", [128, 192])
    pS2 = P.psum("pS2", [96, 192])
    pTb = P.psum("pTb", [128, 128], BF16)
    pG1 = P.psum("pG1", [96, 1])
    pW = P.psum("pW", [128, TO])

    def evac(dst, src, eng="vector"):
        P.copy(dst, src, eng=eng)

    for s in range(NS):
        t0 = s * TO
        for hh, sb_, nf, pp in ((q_hb, qb, 8, 96), (k_hb, kb, 8, 96),
                                (v_hb, vb, 16, 96), (g_hb, gb, 16, 96)):
            P.dma(sb_[:].rearrange("p (f t) -> p f t", t=TO),
                  hh.rearrange("(f p) t -> p f t", p=96)[:, :, t0:t0 + TO])
        P.dma(xn1b[:].rearrange("p (f t) -> p f t", t=TO), fm_ap(xn1_h, 0, 8, t0, TO))
        P.dma(bet[:], beta_h[:, t0:t0 + TO])
        P.dma(cums[:], cum_h[:, t0:t0 + TO])
        for ci in range(TO // C):
            co = ci * C
            P.transpose(pX[:, 0:8], cums[:, co:co + C], ident[0:8, 0:8])
            P.copy(cbT[:, 0:8], pX[:, 0:8])
            P.transpose(pX[:, 0:8], bet[:, co:co + C], ident[0:8, 0:8])
            P.copy(cbT[:, 8:16], pX[:, 0:8])
            for h in range(8):
                Ktc = kb[0:96, h * TO + co: h * TO + co + C]
                Qtc = qb[0:96, h * TO + co: h * TO + co + C]
                ccol = cbT[:, h:h + 1]
                bcol = cbT[:, 8 + h:9 + h]
                P.mm(pX[:], Ktc, Ktc); evac(kkS[:], pX[:])
                P.mm(pX[:], Ktc, Qtc); evac(kqS[:], pX[:])
                P.mm(pX[:], sel8[:, h * 128:(h + 1) * 128], cums[:, co:co + C])
                evac(crS[:], pX[:])
                P.ts(rS[:], crS[:], ccol, ALU.subtract)
                P.act(DN[:], rS[:], AF.Relu)
                P.act(DN[:], DN[:], AF.Exp, scale=-1.0)
                P.act(DT[:], rS[:], AF.Relu, scale=-1.0)
                P.act(DT[:], DT[:], AF.Exp, scale=-1.0)
                P.mm(pX[:], sel8[:, h * 128:(h + 1) * 128], bet[:, co:co + C])
                evac(brS[:], pX[:])
                P.tt(Nc[:], DN[:], kkS[:], ALU.mult)
                P.ts(Nc[:], Nc[:], bcol, ALU.mult)
                P.tt(Nc[:], Nc[:], mlo[:], ALU.mult)
                P.tt(NTc[:], DT[:], kkS[:], ALU.mult)
                P.tt(NTc[:], NTc[:], brS[:], ALU.mult)
                P.tt(NTc[:], NTc[:], mup[:], ALU.mult)
                P.tt(Pm[:], ident[:], NTc[:], ALU.subtract)
                cN, cNT, nN, nNT = Nc, NTc, Nn, NTn
                for lev in range(6):
                    P.mm(pX[:], cNT[:], cN[:]); evac(nN[:], pX[:])
                    P.mm(pX[:], cN[:], cNT[:]); evac(nNT[:], pX[:])
                    cN, nN = nN, cN
                    cNT, nNT = nNT, cNT
                    P.mm(pX[:], cN[:], Pm[:])
                    P.tt(Pm[:], Pm[:], pX[:], ALU.add)
                P.copy(Sb[:, h * 192:(h + 1) * 192], Sf[:, h * 192:(h + 1) * 192])
                P.mm(pY[:], Ktc, Sb[:, h * 192:(h + 1) * 192])
                for sub in range(2):
                    P.transpose(pTb[0:128, 0:96],
                                vb[0:96, (2 * h + sub) * TO + co: (2 * h + sub) * TO + co + C],
                                identb[0:96, 0:96])
                    P.copy(Vtok[:, sub * 96:(sub + 1) * 96], pTb[0:128, 0:96])
                P.act(gcol[:], ccol, AF.Exp)
                P.ts(gKS0[:], pY[:], gcol[:, 0:1], ALU.mult)
                P.tt(RHS[:], Vtok[:], gKS0[:], ALU.subtract)
                P.ts(RHS[:], RHS[:], bcol, ALU.mult)
                P.mm(pY[:], Pm[:], RHS[:])
                evac(Wf[:], pY[:]); P.copy(Wb_[:], Wf[:])
                P.tt(rS[:], DT[:], kqS[:], ALU.mult)
                P.tt(rS[:], rS[:], mui[:], ALU.mult)
                P.copy(MTb[:], rS[:])
                P.mm(pY[:], MTb[:], Wb_[:])
                P.mm(pO2[:], Qtc, Sb[:, h * 192:(h + 1) * 192])
                P.ts(qs_[:], pO2[:], gcol[:, 0:1], ALU.mult)
                P.tt(Ot[:], pY[:], qs_[:], ALU.add)
                P.transpose(pTb[0:128, 0:96], Ktc, identb[0:96, 0:96])
                P.copy(Ktok[:], pTb[0:128, 0:96])
                P.ts(dkcol[:], crS[:, 127:128], ccol, ALU.subtract)
                P.act(dkcol[:], dkcol[:], AF.Exp)
                P.ts(Ktok[:], Ktok[:], dkcol[:, 0:1], ALU.mult)
                P.mm(pS2[:], Ktok[:], Wb_[:])
                P.act(ecr[:], crS[0:1, 127:128], AF.Exp)
                P.mm(pG1[:], ones1[0:1, 0:96], ecr[:])
                evac(grep[:], pG1[:])
                P.stt(Sf[:, h * 192:(h + 1) * 192], Sf[:, h * 192:(h + 1) * 192],
                      grep[:, 0:1], pS2[:], ALU.mult, ALU.add)
                P.act(sqO[:], Ot[:], AF.Square)
                P.red(ocol[:], sqO[:], ALU.add)
                P.ts(ocol[:], ocol[:], 1.0 / 192, ALU.mult)
                P.act(ocol[:], ocol[:], AF.Ln, bias=eps192[:, 0:1])
                P.act(ocol[:], ocol[:], AF.Exp, scale=-0.5)
                P.ts(Ot[:], Ot[:], ocol[:, 0:1], ALU.mult)
                P.tt(Ot[:], Ot[:], onwrep[:], ALU.mult)
                for sub in range(2):
                    P.transpose(pTb[0:128, 0:96],
                                gb[0:96, (2 * h + sub) * TO + co: (2 * h + sub) * TO + co + C],
                                identb[0:96, 0:96])
                    P.copy(Gtok[:, sub * 96:(sub + 1) * 96], pTb[0:128, 0:96])
                P.act(Gtok[:], Gtok[:], AF.Silu)
                P.tt(oftok[:], Ot[:], Gtok[:], ALU.mult)
                for sub in range(2):
                    P.transpose(pTb[0:96, 0:128], oftok[:, sub * 96:(sub + 1) * 96],
                                identb[:])
                    P.copy(ofm[0:96, (2 * h + sub) * TO + co:
                           (2 * h + sub) * TO + co + C], pTb[0:96, 0:128])
        for m in range(8):
            for k in range(16):
                P.mm(pW[:], wo[:, k * D + m * 128: k * D + (m + 1) * 128],
                     ofm[:, k * TO:(k + 1) * TO], start=(k == 0), stop=(k == 15))
            P.copy(xn1f[:], xn1b[:, m * TO:(m + 1) * TO])
            P.tt(xmid[:, m * TO:(m + 1) * TO], pW[:], xn1f[:], ALU.add)
        P.dma(fm_ap(xmid_h, 0, 8, t0, TO), xmid[:].rearrange("p (f t) -> p f t", t=TO))
    P.end_phase()
    return P




def build_C2(P=None):
    P = P if P is not None else Prog("C2")
    P.begin_phase()
    TCH = 512
    HALF = T // 2
    xmid_h = P.dram_in("xmid", (D, T))
    x1_h = P.dram_in("x1", (D, T))
    n2_h = P.dram_in("norm2_w", (D,))
    wg_h = P.dram_in("Wgate", (D, 4096))
    wu_h = P.dram_in("Wup", (D, 4096))
    wd_h = P.dram_in("Wdown", (4096, D))
    t2_h = P.dram_out("t2", (D, T))
    h2_h = P.dram_out("h2aS", (D, T), BF16)

    n2c = P.sbuf("n2c", [128, 8], F32)
    P.dma_slow(n2c[:], n2_h.rearrange("(f p) -> p f", p=128))
    ones128 = P.sbuf("ones128", [128, 1], F32)
    P.memset(ones128[:], 1.0)
    ones1 = P.sbuf("ones1", [1, 128], F32)
    P.memset(ones1[:], 1.0)
    epsr = P.sbuf("epsr", [1, 1], F32)
    P.memset(epsr[:], 1e-6)
    xmt = P.sbuf("xmt", [128, 8 * TCH])
    sq = P.sbuf("sq", [128, TCH])
    rowb = P.sbuf("rowb", [1, TCH], F32)
    rrep = P.sbuf("rrep", [128, TCH], F32)
    h2b = P.sbuf("h2b", [128, 8 * TCH], BF16)
    pS = P.psum("pS", [1, TCH])
    pM = P.psum("pM", [128, TCH])

    for c in range(T // TCH):
        t0 = c * TCH
        P.dma(xmt[:].rearrange("p (f t) -> p f t", t=TCH), fm_ap(xmid_h, 0, 8, t0, TCH))
        fm_norm(P, xmt, 8, TCH, sq, rowb, pS, ones128, epsr[0:1, 0:1], rowb[0:1, 0:TCH])
        P.mm(pM[:], ones1[:], rowb[0:1, :], start=True, stop=True)
        P.copy(rrep[:], pM[:])
        for k in range(8):
            P.tt(sq[:], xmt[:, k * TCH:(k + 1) * TCH], rrep[:], ALU.mult)
            P.ts(h2b[:, k * TCH:(k + 1) * TCH], sq[:], n2c[:, k:k + 1], ALU.mult)
        P.dma(fm_ap(h2_h, 0, 8, t0, TCH), h2b[:].rearrange("p (f t) -> p f t", t=TCH))

    wz = P.sbuf("wz", [128, 24576], BF16)
    GA, UA, DA = 0, 8192, 16384
    acc = P.sbuf("acc", [128, 8 * HALF], F32)
    hub = P.sbuf("hub", [128, 8 * TCH], BF16)
    silb = P.sbuf("silb", [128, TCH], F32)
    pG = P.psum("pG", [128, TCH]); pU = P.psum("pU", [128, TCH])
    pD1 = P.psum("pD1", [128, TCH]); pD2 = P.psum("pD2", [128, TCH])
    wg_r = wg_h.rearrange("(f p) m -> p f m", p=128)
    wu_r = wu_h.rearrange("(f p) m -> p f m", p=128)
    wd_r = wd_h.rearrange("(f p) m -> p f m", p=128)
    t1b = P.sbuf("t1b", [128, 8 * 256], F32)
    t1c = P.sbuf("t1c", [128, 8 * 256], F32)

    for h_ in range(2):
        for grp in range(4):
            P.dma(wz[:, GA:GA + 8192].rearrange("p (f m) -> p f m", m=1024),
                  wg_r[:, :, grp * 1024:(grp + 1) * 1024], queue="gpsimd")
            P.dma(wz[:, UA:UA + 8192].rearrange("p (f m) -> p f m", m=1024),
                  wu_r[:, :, grp * 1024:(grp + 1) * 1024], queue="gpsimd")
            P.dma(wz[:, DA:DA + 8192].rearrange("p (f m) -> p f m", m=1024),
                  wd_r[:, grp * 8:(grp + 1) * 8, :], queue="gpsimd")
            for sub in range(4):
                t0 = h_ * HALF + sub * TCH
                P.dma(h2b[:].rearrange("p (f t) -> p f t", t=TCH),
                      fm_ap(h2_h, 0, 8, t0, TCH))
                for mt in range(8):
                    for k in range(8):
                        P.mm(pG[:], wz[:, GA + k * 1024 + mt * 128: GA + k * 1024 + (mt + 1) * 128],
                             h2b[:, k * TCH:(k + 1) * TCH], start=(k == 0), stop=(k == 7))
                    for k in range(8):
                        P.mm(pU[:], wz[:, UA + k * 1024 + mt * 128: UA + k * 1024 + (mt + 1) * 128],
                             h2b[:, k * TCH:(k + 1) * TCH], start=(k == 0), stop=(k == 7))
                    P.act(silb[:], pG[:], AF.Silu)
                    P.tt(hub[:, mt * TCH:(mt + 1) * TCH], silb[:], pU[:], ALU.mult)
                for m in range(8):
                    pd = pD1 if m % 2 == 0 else pD2
                    for k in range(8):
                        P.mm(pd[:], wz[:, DA + k * 1024 + m * 128: DA + k * 1024 + (m + 1) * 128],
                             hub[:, k * TCH:(k + 1) * TCH], start=(k == 0), stop=(k == 7))
                    ao = m * HALF + sub * TCH
                    if grp == 0:
                        P.copy(acc[:, ao:ao + TCH], pd[:])
                    else:
                        P.tt(acc[:, ao:ao + TCH], acc[:, ao:ao + TCH], pd[:], ALU.add)
        for sub in range(8):
            t0 = h_ * HALF + sub * 256
            P.dma(t1b[:].rearrange("p (f t) -> p f t", t=256), fm_ap(x1_h, 0, 8, t0, 256))
            P.dma(t1c[:].rearrange("p (f t) -> p f t", t=256), fm_ap(xmid_h, 0, 8, t0, 256))
            for m in range(8):
                ao = m * HALF + sub * 256
                P.tt(t1b[:, m * 256:(m + 1) * 256], t1b[:, m * 256:(m + 1) * 256],
                     t1c[:, m * 256:(m + 1) * 256], ALU.add)
                P.tt(t1b[:, m * 256:(m + 1) * 256], t1b[:, m * 256:(m + 1) * 256],
                     acc[:, ao:ao + 256], ALU.add)
            P.dma(fm_ap(t2_h, 0, 8, t0, 256), t1b[:].rearrange("p (f t) -> p f t", t=256))
    P.end_phase()
    return P



def build_D1(P=None):
    P = P if P is not None else Prog("D1")
    P.begin_phase()
    TO = 512
    NS = T // TO
    xn1_h = P.dram_in("xn1", (D, T), BF16)
    anw_h = P.dram_in("attn_norm_w", (D,))
    fnw_h = P.dram_in("ffn_norm_w", (D,))
    qkvw_h = P.dram_in("qkv_wP", (D, 1536))
    qkb_h = P.dram_in("qkb64", (64, 20))
    vbb_h = P.dram_in("vb128", (128, 2))
    pw_h = P.dram_in("proj_w", (D, D))
    pb_h = P.dram_in("proj_b", (D,))
    sink_h = P.dram_in("attn_sink", (16,))
    gw_h = P.dram_in("gate_w32", (D, 32))
    gb_h = P.dram_in("gate_b32", (32,))
    cs_h = P.dram_in("cosS", (64, T))
    sn_h = P.dram_in("sinS", (64, T))
    ident_h = P.dram_in("ident", (128, 128))
    am_h = P.dram_in("amasks", (2, 128, 256))
    xp_h = P.dram_out("xpS", (D, T))
    h2f_h = P.dram_out("h2fS", (D, T), BF16)
    glg_h = P.dram_out("glogS", (T, 32))

    ident = P.sbuf("ident", [128, 128], F32)
    P.dma(ident[:], ident_h[:])
    identb = P.sbuf("identb", [128, 128], BF16)
    P.dma(identb[:], ident_h[:], queue="gpsimd")
    am0 = P.sbuf("am0", [128, 256], F32)
    amr = P.sbuf("amr", [128, 256], F32)
    P.dma(am0[:], am_h[0]); P.dma(amr[:], am_h[1])
    anc = P.sbuf("anc", [128, 8], F32)
    P.dma_slow(anc[:], anw_h.rearrange("(f p) -> p f", p=128))
    fnc = P.sbuf("fnc", [128, 8], F32)
    P.dma_slow(fnc[:], fnw_h.rearrange("(f p) -> p f", p=128))
    qkvw = P.sbuf("qkvw", [128, 8 * 1536], BF16)
    P.dma(qkvw[:].rearrange("p (f m) -> p f m", m=1536),
          qkvw_h.rearrange("(f p) m -> p f m", p=128), queue="gpsimd")
    qkb = P.sbuf("qkb", [64, 20], F32)
    P.dma(qkb[:], qkb_h[:])
    vbb = P.sbuf("vbb", [128, 2], F32)
    P.dma(vbb[:], vbb_h[:])
    pw = P.sbuf("pw", [64, 16 * D], BF16)
    P.dma(pw[:].rearrange("p (f m) -> p f m", m=D),
          pw_h.rearrange("(f p) m -> p f m", p=64), queue="gpsimd")
    pbc = P.sbuf("pbc", [128, 8], F32)
    P.dma_slow(pbc[:], pb_h.rearrange("(f p) -> p f", p=128))
    gwc = P.sbuf("gwc", [128, 8 * 32], F32)
    P.dma(gwc[:].rearrange("p (f m) -> p f m", m=32),
          gw_h.rearrange("(f p) m -> p f m", p=128))
    gbc = P.sbuf("gbc", [32, 1], F32)
    P.dma(gbc[:], gb_h.unsqueeze(1))
    cosb = P.sbuf("cosb", [64, T], F32)
    sinb = P.sbuf("sinb", [64, T], F32)
    P.dma(cosb[:], cs_h[:]); P.dma(sinb[:], sn_h[:])
    ones128 = P.sbuf("ones128", [128, 1], F32)
    P.memset(ones128[:], 1.0)
    ones1 = P.sbuf("ones1", [1, 128], F32)
    P.memset(ones1[:], 1.0)
    epsr = P.sbuf("epsr", [1, 1], F32)
    P.memset(epsr[:], 1e-5)

    xn1b = P.sbuf("xn1b", [128, 8 * TO], BF16)
    xn1f = P.sbuf("xn1f", [128, TO], F32)
    hb = P.sbuf("hb", [128, 8 * TO], BF16)
    sq = P.sbuf("sq", [128, TO])
    rowb = P.sbuf("rowb", [1, TO], F32)
    rrep = P.sbuf("rrep", [128, TO], F32)
    qb = P.sbuf("qb", [64, 16 * TO], BF16)
    kb = P.sbuf("kb", [64, 4 * (128 + TO)], BF16)
    vtok = P.sbuf("vtok", [128, 4 * 5 * 64], BF16)
    rtmp1 = P.sbuf("rtmp1", [32, TO], F32)
    rtmp2 = P.sbuf("rtmp2", [32, TO], F32)
    lgs = P.sbuf("lgs", [128, 256], F32)
    pex = P.sbuf("pex", [128, 256], F32)
    mcol = P.sbuf("mcol", [128, 1], F32)
    nmcol = P.sbuf("nmcol", [128, 1], F32)
    scol = P.sbuf("scol", [128, 1], F32)
    dcol = P.sbuf("dcol", [128, 1], F32)
    attT = P.sbuf("attT", [128, 256], BF16)
    atb = P.sbuf("atb", [128, 256], BF16)
    yb = P.sbuf("yb", [64, 16 * TO], BF16)
    xpb = P.sbuf("xpb", [128, 8 * TO], F32)
    h2fb = P.sbuf("h2fb", [128, 8 * TO], BF16)
    glt = P.sbuf("glt", [32, TO], F32)
    glo = P.sbuf("glo", [128, 4 * 32], F32)
    sinkr = P.sbuf("sinkr", [128, 16], F32)

    pQ = P.psum("pQ", [64, TO])
    pV = P.psum("pV", [128, TO])
    pL = P.psum("pL", [128, 256])
    pTb = P.psum("pTb", [128, 128], BF16)
    pT32 = P.psum("pT32", [128, 128])
    pY = P.psum("pY", [64, 128])
    pS = P.psum("pS", [1, TO])
    pGl = P.psum("pGl", [32, TO])

    P.dma(sinkr[0:1, :], sink_h.unsqueeze(0))
    P.mm(pV[:, 0:16], ones1[:], sinkr[0:1, :])
    P.copy(sinkr[:], pV[:, 0:16])
    P.memset(kb[:], 0.0)
    P.memset(vtok[:], 0.0)

    def rope_pair(dst_even, dst_odd, src64, t0):
        P.tt(rtmp1[:], src64[0:32, :], cosb[0:32, t0:t0 + TO], ALU.mult)
        P.tt(rtmp2[:], src64[32:64, :], sinb[32:64, t0:t0 + TO], ALU.mult)
        P.tt(rtmp1[:], rtmp1[:], rtmp2[:], ALU.subtract)
        P.tt(rtmp2[:], src64[0:32, :], sinb[0:32, t0:t0 + TO], ALU.mult)
        P.copy(dst_even, rtmp1[:])
        P.tt(rtmp1[:], src64[32:64, :], cosb[32:64, t0:t0 + TO], ALU.mult)
        P.tt(rtmp1[:], rtmp1[:], rtmp2[:], ALU.add)
        P.copy(dst_odd, rtmp1[:])

    for s in range(NS):
        t0 = s * TO
        P.dma(xn1b[:].rearrange("p (f t) -> p f t", t=TO), fm_ap(xn1_h, 0, 8, t0, TO))
        for k in range(8):
            P.copy(sq[:], xn1b[:, k * TO:(k + 1) * TO])
            P.act(sq[:], sq[:], AF.Square)
            P.mm(pS[0:1, :], ones128[:, 0:1], sq[:], start=(k == 0), stop=(k == 7))
        P.ts(rowb[0:1, :], pS[0:1, :], 1.0 / D, ALU.mult)
        P.act(rowb[0:1, :], rowb[0:1, :], AF.Ln, bias=epsr[0:1, 0:1])
        P.act(rowb[0:1, :], rowb[0:1, :], AF.Exp, scale=-0.5)
        P.mm(pV[:], ones1[:], rowb[0:1, :], start=True, stop=True)
        P.copy(rrep[:], pV[:])
        for k in range(8):
            P.copy(sq[:], xn1b[:, k * TO:(k + 1) * TO])
            P.tt(sq[:], sq[:], rrep[:], ALU.mult)
            P.ts(hb[:, k * TO:(k + 1) * TO], sq[:], anc[:, k:k + 1], ALU.mult)
        for kv in range(4):
            P.copy(kb[:, kv * 640:kv * 640 + 128],
                   kb[:, kv * 640 + TO:kv * 640 + TO + 128])
            P.copy(vtok[:, kv * 320:kv * 320 + 64],
                   vtok[:, kv * 320 + 256:kv * 320 + 320])
        for hh in range(16):
            for k in range(8):
                P.mm(pQ[:], qkvw[:, k * 1536 + hh * 64: k * 1536 + (hh + 1) * 64],
                     hb[:, k * TO:(k + 1) * TO], start=(k == 0), stop=(k == 7))
            bcol = qkb[:, hh:hh + 1]
            P.ts(sq[0:64, :], pQ[:], bcol, ALU.add)
            rope_pair(qb[0:32, hh * TO:(hh + 1) * TO],
                      qb[32:64, hh * TO:(hh + 1) * TO], sq, t0)
        for kv in range(4):
            for k in range(8):
                P.mm(pQ[:], qkvw[:, k * 1536 + 1024 + kv * 64: k * 1536 + 1024 + (kv + 1) * 64],
                     hb[:, k * TO:(k + 1) * TO], start=(k == 0), stop=(k == 7))
            bcol = qkb[:, 16 + kv:17 + kv]
            P.ts(sq[0:64, :], pQ[:], bcol, ALU.add)
            ko = kv * 640 + 128
            rope_pair(kb[0:32, ko:ko + TO], kb[32:64, ko:ko + TO], sq, t0)
        for vt in range(2):
            for k in range(8):
                P.mm(pV[:], qkvw[:, k * 1536 + 1280 + vt * 128: k * 1536 + 1280 + (vt + 1) * 128],
                     hb[:, k * TO:(k + 1) * TO], start=(k == 0), stop=(k == 7))
            bcol = vbb[:, vt:vt + 1]
            P.ts(sq[:], pV[:], bcol, ALU.add)
            for tt_ in range(4):
                P.transpose(pT32[:], sq[:, tt_ * 128:(tt_ + 1) * 128], ident[:])
                P.copy(vtok[:, (2 * vt) * 320 + (tt_ + 1) * 64:
                       (2 * vt) * 320 + (tt_ + 2) * 64], pT32[:, 0:64])
                P.copy(vtok[:, (2 * vt + 1) * 320 + (tt_ + 1) * 64:
                       (2 * vt + 1) * 320 + (tt_ + 2) * 64], pT32[:, 64:128])
        for qt in range(4):
            for qh in range(16):
                kv = qh // 4
                P.mm(pL[:], qb[0:64, qh * TO + qt * 128: qh * TO + qt * 128 + 128],
                     kb[0:64, kv * 640 + qt * 128: kv * 640 + qt * 128 + 256])
                msk = am0 if (s == 0 and qt == 0) else amr
                P.stt(lgs[:], pL[:], 0.125, msk[:], ALU.mult, ALU.add)
                P.red(mcol[:], lgs[:], ALU.max)
                P.tt(mcol[:], mcol[:], sinkr[:, qh:qh + 1], ALU.max)
                P.ts(nmcol[:], mcol[:], -1.0, ALU.mult)
                P.act(pex[:], lgs[:], AF.Exp, bias=nmcol[:, 0:1])
                P.red(dcol[:], pex[:], ALU.add)
                P.tt(scol[:], sinkr[:, qh:qh + 1], mcol[:], ALU.subtract)
                P.act(scol[:], scol[:], AF.Exp)
                P.tt(dcol[:], dcol[:], scol[:], ALU.add)
                P.recip(scol[:], dcol[:])
                P.ts(atb[:], pex[:], scol[:, 0:1], ALU.mult)
                for piece in range(2):
                    P.transpose(pTb[:], atb[:, piece * 128:(piece + 1) * 128], identb[:])
                    P.copy(attT[:, piece * 128:(piece + 1) * 128], pTb[:])
                for piece in range(2):
                    P.mm(pY[:], vtok[:, kv * 320 + (qt + piece) * 64:
                                     kv * 320 + (qt + piece + 1) * 64],
                         attT[:, piece * 128:(piece + 1) * 128],
                         start=(piece == 0), stop=(piece == 1))
                P.copy(yb[0:64, qh * TO + qt * 128: qh * TO + qt * 128 + 128], pY[:])
        for m in range(8):
            for k in range(16):
                P.mm(pV[:], pw[0:64, k * D + m * 128: k * D + (m + 1) * 128],
                     yb[0:64, k * TO:(k + 1) * TO], start=(k == 0), stop=(k == 15))
            P.ts(sq[:], pV[:], pbc[:, m:m + 1], ALU.add)
            P.copy(xn1f[:], xn1b[:, m * TO:(m + 1) * TO])
            P.tt(xpb[:, m * TO:(m + 1) * TO], sq[:], xn1f[:], ALU.add)
        P.dma(fm_ap(xp_h, 0, 8, t0, TO), xpb[:].rearrange("p (f t) -> p f t", t=TO))
        for k in range(8):
            P.act(sq[:], xpb[:, k * TO:(k + 1) * TO], AF.Square)
            P.mm(pS[0:1, :], ones128[:, 0:1], sq[:], start=(k == 0), stop=(k == 7))
        P.ts(rowb[0:1, :], pS[0:1, :], 1.0 / D, ALU.mult)
        P.act(rowb[0:1, :], rowb[0:1, :], AF.Ln, bias=epsr[0:1, 0:1])
        P.act(rowb[0:1, :], rowb[0:1, :], AF.Exp, scale=-0.5)
        P.mm(pV[:], ones1[:], rowb[0:1, :], start=True, stop=True)
        P.copy(rrep[:], pV[:])
        for k in range(8):
            P.tt(sq[:], xpb[:, k * TO:(k + 1) * TO], rrep[:], ALU.mult)
            P.ts(h2fb[:, k * TO:(k + 1) * TO], sq[:], fnc[:, k:k + 1], ALU.mult)
        P.dma(fm_ap(h2f_h, 0, 8, t0, TO), h2fb[:].rearrange("p (f t) -> p f t", t=TO))
        for k in range(8):
            P.tt(sq[:], xpb[:, k * TO:(k + 1) * TO], rrep[:], ALU.mult)
            P.ts(sq[:], sq[:], fnc[:, k:k + 1], ALU.mult)
            P.mm(pGl[:], gwc[:, k * 32:(k + 1) * 32], sq[:],
                 start=(k == 0), stop=(k == 7))
        P.ts(glt[:], pGl[:], gbc[:, 0:1], ALU.add)
        for tt_ in range(4):
            P.transpose(pT32[:, 0:32], glt[:, tt_ * 128:(tt_ + 1) * 128],
                        ident[0:32, 0:32])
            P.copy(glo[:, tt_ * 32:(tt_ + 1) * 32], pT32[:, 0:32])
            P.dma(glg_h[t0 + tt_ * 128: t0 + (tt_ + 1) * 128],
                  glo[:, tt_ * 32:(tt_ + 1) * 32])
    P.end_phase()
    return P




def build_D2(P=None):
    P = P if P is not None else Prog("D2")
    P.begin_phase()
    TCH = 512
    HALF = T // 2
    h2_h = P.dram_in("h2fS", (D, T), BF16)
    glg_h = P.dram_in("glogS", (T, 32))
    xp_h = P.dram_in("xpS", (D, T))
    t2_h = P.dram_in("t2", (D, T))
    wu_h = P.dram_in("W_up", (32, D, 2048))
    wd_h = P.dram_in("W_down", (32, D, D))
    sel_h = P.dram_in("sel32", (32, 32 * 128))
    ident_h = P.dram_in("ident", (128, 128))
    out_h = P.dram_out("out", (T, D))

    ident = P.sbuf("ident", [128, 128], F32)
    P.dma(ident[:], ident_h[:])
    selb = P.sbuf("selb", [32, 32 * 128], BF16)
    P.dma(selb[:], sel_h[:], queue="gpsimd")

    lgt = P.sbuf("lgt", [128, 32], F32)
    msk = P.sbuf("msk", [128, 32], F32)
    eq = P.sbuf("eq", [128, 32], F32)
    cum = P.sbuf("cum", [128, 32], F32)
    zer = P.sbuf("zer", [128, 32], F32)
    first = P.sbuf("first", [128, 32], F32)
    mm_ = P.sbuf("mm_", [128, 1], F32)
    m0 = P.sbuf("m0", [128, 1], F32)
    vks = P.sbuf("vks", [128, 4], F32)
    fks = P.sbuf("fks", [128, 4 * 32], F32)
    den = P.sbuf("den", [128, 1], F32)
    comb = P.sbuf("comb", [128, 32], F32)
    combT = P.sbuf("combT", [32, T], BF16)
    pR = P.psum("pR", [128, 128])
    P.memset(zer[:], 0.0)
    scr = (eq, cum, zer, first, mm_)
    for tt_ in range(T // 128):
        P.dma(lgt[:], glg_h[tt_ * 128:(tt_ + 1) * 128])
        P.copy(msk[:], lgt[:])
        for kk in range(4):
            f, m = emit_first_argmax(P, msk[:], scr)
            P.copy(fks[:, kk * 32:(kk + 1) * 32], f[:])
            P.copy(vks[:, kk:kk + 1], m[:])
            if kk == 0:
                P.copy(m0[:], m[:])
            P.stt(msk[:], f[:], -1e30, msk[:], ALU.mult, ALU.add)
        P.ts(vks[:], vks[:], m0[:, 0:1], ALU.subtract)
        P.act(vks[:], vks[:], AF.Exp)
        P.red(den[:], vks[:], ALU.add)
        P.recip(den[:], den[:])
        P.memset(comb[:], 0.0)
        for kk in range(4):
            P.ts(mm_[:], vks[:, kk:kk + 1], den[:, 0:1], ALU.mult)
            P.stt(comb[:], fks[:, kk * 32:(kk + 1) * 32], mm_[:, 0:1], comb[:],
                  ALU.mult, ALU.add)
        P.transpose(pR[0:32, :], comb[:], ident[:])
        P.copy(combT[:, tt_ * 128:(tt_ + 1) * 128], pR[0:32, 0:128])

    wz = P.sbuf("wz", [128, 24576], BF16)
    UA, DA = 0, 16384
    acc = P.sbuf("acc", [128, 8 * HALF], F32)
    h2b = P.sbuf("h2b", [128, 8 * TCH], BF16)
    hub = P.sbuf("hub", [128, 8 * TCH], BF16)
    silb = P.sbuf("silb", [128, TCH], F32)
    crep = P.sbuf("crep", [128, TCH], F32)
    pG = P.psum("pG", [128, TCH]); pU = P.psum("pU", [128, TCH])
    pC = P.psum("pC", [128, TCH])
    pD1_ = P.psum("pD1_", [128, TCH]); pD2_ = P.psum("pD2_", [128, TCH])
    pT2 = P.psum("pT2", [128, TCH])
    wu_r = wu_h.rearrange("e (f p) m -> e p f m", p=128)
    wd_r = wd_h.rearrange("e (f p) m -> e p f m", p=128)
    fin = P.sbuf("fin", [128, 8 * TCH], F32)
    fin2 = P.sbuf("fin2", [128, 8 * TCH], F32)

    for h_ in range(2):
        for e_ in range(32):
            P.dma(wz[:, UA:UA + 16384].rearrange("p (f m) -> p f m", m=2048),
                  wu_r[e_], queue="gpsimd")
            P.dma(wz[:, DA:DA + 8192].rearrange("p (f m) -> p f m", m=1024),
                  wd_r[e_], queue="gpsimd")
            for sub in range(4):
                t0 = h_ * HALF + sub * TCH
                P.dma(h2b[:].rearrange("p (f t) -> p f t", t=TCH),
                      fm_ap(h2_h, 0, 8, t0, TCH))
                for mt in range(8):
                    for k in range(8):
                        P.mm(pU[:], wz[:, UA + k * 2048 + mt * 128:
                                       UA + k * 2048 + (mt + 1) * 128],
                             h2b[:, k * TCH:(k + 1) * TCH],
                             start=(k == 0), stop=(k == 7))
                    for k in range(8):
                        P.mm(pG[:], wz[:, UA + k * 2048 + 1024 + mt * 128:
                                       UA + k * 2048 + 1024 + (mt + 1) * 128],
                             h2b[:, k * TCH:(k + 1) * TCH],
                             start=(k == 0), stop=(k == 7))
                    P.act(silb[:], pG[:], AF.Silu)
                    P.tt(hub[:, mt * TCH:(mt + 1) * TCH], silb[:], pU[:], ALU.mult)
                P.mm(pC[:], selb[:, e_ * 128:(e_ + 1) * 128], combT[:, t0:t0 + TCH])
                P.copy(crep[:], pC[:])
                for mt in range(8):
                    P.tt(hub[:, mt * TCH:(mt + 1) * TCH],
                         hub[:, mt * TCH:(mt + 1) * TCH], crep[:], ALU.mult)
                for m in range(8):
                    pd = pD1_ if m % 2 == 0 else pD2_
                    for k in range(8):
                        P.mm(pd[:], wz[:, DA + k * 1024 + m * 128:
                                       DA + k * 1024 + (m + 1) * 128],
                             hub[:, k * TCH:(k + 1) * TCH],
                             start=(k == 0), stop=(k == 7))
                    ao = m * HALF + sub * TCH
                    if e_ == 0:
                        P.copy(acc[:, ao:ao + TCH], pd[:])
                    else:
                        P.tt(acc[:, ao:ao + TCH], acc[:, ao:ao + TCH], pd[:], ALU.add)
        for sub in range(4):
            t0 = h_ * HALF + sub * TCH
            P.dma(fin[:].rearrange("p (f t) -> p f t", t=TCH), fm_ap(t2_h, 0, 8, t0, TCH))
            P.dma(fin2[:].rearrange("p (f t) -> p f t", t=TCH), fm_ap(xp_h, 0, 8, t0, TCH))
            for m in range(8):
                ao = m * HALF + sub * TCH
                P.tt(fin[:, m * TCH:(m + 1) * TCH], fin[:, m * TCH:(m + 1) * TCH],
                     fin2[:, m * TCH:(m + 1) * TCH], ALU.add)
                P.tt(fin[:, m * TCH:(m + 1) * TCH], fin[:, m * TCH:(m + 1) * TCH],
                     acc[:, ao:ao + TCH], ALU.add)
            for tt_ in range(4):
                for m in range(8):
                    P.transpose(pT2[:, 0:128],
                                fin[:, m * TCH + tt_ * 128: m * TCH + tt_ * 128 + 128],
                                ident[:])
                    P.copy(fin2[:, m * 128:(m + 1) * 128], pT2[:, 0:128])
                P.dma(out_h[t0 + tt_ * 128: t0 + (tt_ + 1) * 128],
                      fin2[:, 0:1024])
    P.end_phase()
    return P



_cache = {}


def _bf(x):
    return np.ascontiguousarray(x)


def run_launch(key, builder, in_maps):
    if key not in _cache:
        P = builder()
        _cache[key] = P.build()
    nc = _cache[key]
    res = run_bass_kernel_spmd(nc, in_maps, core_ids=list(range(len(in_maps))))
    return res.results


IDENT = np.eye(128, dtype=np.float32)
SEL32 = np.zeros((32, 32 * 128), dtype=np.float32)
for _s in range(32):
    SEL32[_s, _s * 128:(_s + 1) * 128] = 1.0
_MLO = np.tril(np.ones((128, 128), np.float32), -1)
MASKS = np.stack([_MLO, _MLO.T, np.triu(np.ones((128, 128), np.float32))])

def _amasks():
    NEG = np.float32(-3e38)
    qi = np.arange(128)[:, None]
    kj = np.arange(256)[None, :]
    valid = (kj >= qi) & (kj <= qi + 128)
    amr = np.where(valid, np.float32(0), NEG).astype(np.float32)
    am0 = amr.copy()
    am0[:, :128] = NEG
    return np.stack([am0, amr])

AMASKS = _amasks()

def _rope_tables():
    DH = 64
    pos = np.arange(T, dtype=np.float64) / 32.0
    inv = 1.0 / (150000.0 ** (np.arange(0, DH, 2, dtype=np.float64) / DH))
    fr = pos[None, :] * inv[:, None]
    c = np.cos(fr).astype(np.float32)
    s_ = np.sin(fr).astype(np.float32)
    return np.vstack([c, c]), np.vstack([s_, s_])

COSS, SINS = _rope_tables()

def _perm_qkv(qkv_w, qkv_b):
    perm = np.arange(1536)
    def headperm(base, nh):
        for h in range(nh):
            o = base + h * 64
            perm[o:o + 64] = np.concatenate([np.arange(o, o + 64, 2),
                                             np.arange(o + 1, o + 64, 2)])
    headperm(0, 16)
    headperm(1024, 4)
    wp = np.ascontiguousarray(qkv_w[:, perm])
    bpm = qkv_b[perm]
    qkb = np.ascontiguousarray(bpm[:1280].reshape(20, 64).T)
    vbb = np.ascontiguousarray(bpm[1280:].reshape(2, 128).T)
    return wp, qkb, vbb


_BUILDERS = [build_A1, build_A2, build_B, build_C1a, build_C1b, build_C2,
             build_D1, build_D2]


def build_fused():
    P = Prog("fused", fused=True)
    for b in _BUILDERS:
        b(P)
    return P


import os as _os
FUSED = _os.environ.get("MK_UNFUSED", "") == ""


def _fused_maps(ins, x, b, extra):
    m = dict(x=x[b], ident=IDENT, sel32=SEL32, masks=MASKS, amasks=AMASKS,
             cosS=COSS, sinS=SINS)
    m.update(extra)
    return m


def kernel(**ins):
    x = ins["L0_attn_x"].astype(np.float32)
    B = x.shape[0]
    bcw = np.concatenate([ins["L0_attn_B_w"], ins["L0_attn_C_w"]], axis=1)
    rwT = np.ascontiguousarray(ins["L0_ffn_router_w"].T)
    convw = np.ascontiguousarray(ins["L0_attn_conv_w"][:, 0, :])
    qkvwP, qkbP, vbbP = _perm_qkv(ins["L1_ffn_qkv_w"], ins["L1_ffn_qkv_b"])

    if FUSED:
        common = dict(
            in_proj_w=ins["L0_attn_in_proj_w"], conv_w=convw,
            dt_w=ins["L0_attn_dt_w"], dt_b=ins["L0_attn_dt_b"], bc_w=bcw,
            router_wT=rwT, norm_w=ins["L0_attn_norm_w"],
            A_log=ins["L0_attn_A_log"], D_skip=ins["L0_attn_D_skip"],
            out_proj_w=ins["L0_attn_out_proj_w"], bias=ins["L0_ffn_bias"],
            gate_w=ins["L0_ffn_gate_w"], up_w=ins["L0_ffn_up_w"],
            down_w=ins["L0_ffn_down_w"], sh_gate_w=ins["L0_ffn_sh_gate_w"],
            sh_up_w=ins["L0_ffn_sh_up_w"], sh_down_w=ins["L0_ffn_sh_down_w"],
            Wq=ins["L1_attn_Wq"], Wk=ins["L1_attn_Wk"], Wv=ins["L1_attn_Wv"],
            Wg=ins["L1_attn_Wg"], Wb=ins["L1_attn_Wb"], Wa=ins["L1_attn_Wa"],
            A_log1=ins["L1_attn_A_log"], dt_bias=ins["L1_attn_dt_bias"], conv_q_w=ins["L1_attn_conv_q_w"],
            conv_k_w=ins["L1_attn_conv_k_w"], conv_v_w=ins["L1_attn_conv_v_w"],
            norm1_w=ins["L1_attn_norm1_w"], onorm_w=ins["L1_attn_onorm_w"],
            Wo=ins["L1_attn_Wo"], norm2_w=ins["L1_attn_norm2_w"],
            Wgate=ins["L1_attn_Wgate"], Wup=ins["L1_attn_Wup"],
            Wdown=ins["L1_attn_Wdown"], attn_norm_w=ins["L1_ffn_attn_norm_w"],
            ffn_norm_w=ins["L1_ffn_ffn_norm_w"], qkv_wP=qkvwP, qkb64=qkbP,
            vb128=vbbP, proj_w=ins["L1_ffn_proj_w"], proj_b=ins["L1_ffn_proj_b"],
            attn_sink=ins["L1_ffn_attn_sink"], gate_wD=None,
            W_up=ins["L1_ffn_W_up"], W_down=ins["L1_ffn_W_down"])
        del common["gate_wD"]
        maps = []
        for b in range(B):
            m = _fused_maps(ins, x, b, common)
            maps.append(m)
        for m in maps:
            m["gate_w32"] = ins["L1_ffn_gate_w"]
            m["gate_b32"] = ins["L1_ffn_gate_b"]
        res = run_launch("FUSED", build_fused, maps)
        out = np.stack([np.asarray(r["out"]) for r in res]).astype(np.float32)
        return out

    mapsA1 = [dict(x=x[b], in_proj_w=ins["L0_attn_in_proj_w"], conv_w=convw,
                   dt_w=ins["L0_attn_dt_w"], dt_b=ins["L0_attn_dt_b"],
                   bc_w=bcw, router_wT=rwT, norm_w=ins["L0_attn_norm_w"],
                   ident=IDENT) for b in range(B)]
    rA1 = run_launch("A1", build_A1, mapsA1)
    mapsA2 = [dict(x=x[b], dtS=rA1[b]["dtS"], uS=rA1[b]["uS"], xcS=rA1[b]["xcS"],
                   bcS=rA1[b]["bcS"], sel32=SEL32, A_log=ins["L0_attn_A_log"],
                   D_skip=ins["L0_attn_D_skip"], out_proj_w=ins["L0_attn_out_proj_w"],
                   ident=IDENT) for b in range(B)]
    rA2 = run_launch("A2", build_A2, mapsA2)
    mapsB = [dict(xn=rA1[b]["xn"], t1=rA2[b]["t1"], logits0=rA1[b]["logits0"],
                  bias=ins["L0_ffn_bias"], gate_w=ins["L0_ffn_gate_w"],
                  up_w=ins["L0_ffn_up_w"], down_w=ins["L0_ffn_down_w"],
                  sh_gate_w=ins["L0_ffn_sh_gate_w"], sh_up_w=ins["L0_ffn_sh_up_w"],
                  sh_down_w=ins["L0_ffn_sh_down_w"], sel32=SEL32, ident=IDENT)
             for b in range(B)]
    rB = run_launch("B", build_B, mapsB)
    mapsC1a = [dict(x1=rB[b]["x1"], Wq=ins["L1_attn_Wq"], Wk=ins["L1_attn_Wk"],
                    Wv=ins["L1_attn_Wv"], Wg=ins["L1_attn_Wg"], Wb=ins["L1_attn_Wb"],
                    Wa=ins["L1_attn_Wa"], A_log1=ins["L1_attn_A_log"],
                    dt_bias=ins["L1_attn_dt_bias"], conv_q_w=ins["L1_attn_conv_q_w"],
                    conv_k_w=ins["L1_attn_conv_k_w"], conv_v_w=ins["L1_attn_conv_v_w"],
                    norm1_w=ins["L1_attn_norm1_w"]) for b in range(B)]
    rC1a = run_launch("C1a", build_C1a, mapsC1a)
    mapsC1b = [dict(qS=rC1a[b]["qS"], kS=rC1a[b]["kS"], vS=rC1a[b]["vS"],
                    gS=rC1a[b]["gS"], betaS=rC1a[b]["betaS"], cumS=rC1a[b]["cumS"],
                    xn1=rC1a[b]["xn1"], onorm_w=ins["L1_attn_onorm_w"],
                    Wo=ins["L1_attn_Wo"], sel32=SEL32, ident=IDENT, masks=MASKS)
               for b in range(B)]
    rC1b = run_launch("C1b", build_C1b, mapsC1b)
    mapsC2 = [dict(xmid=rC1b[b]["xmid"], x1=rB[b]["x1"], norm2_w=ins["L1_attn_norm2_w"],
                   Wgate=ins["L1_attn_Wgate"], Wup=ins["L1_attn_Wup"],
                   Wdown=ins["L1_attn_Wdown"]) for b in range(B)]
    rC2 = run_launch("C2", build_C2, mapsC2)
    mapsD1 = [dict(xn1=rC1a[b]["xn1"], attn_norm_w=ins["L1_ffn_attn_norm_w"],
                   ffn_norm_w=ins["L1_ffn_ffn_norm_w"], qkv_wP=qkvwP, qkb64=qkbP, vb128=vbbP,
                   proj_w=ins["L1_ffn_proj_w"], proj_b=ins["L1_ffn_proj_b"],
                   attn_sink=ins["L1_ffn_attn_sink"], gate_w32=ins["L1_ffn_gate_w"],
                   gate_b32=ins["L1_ffn_gate_b"], cosS=COSS, sinS=SINS, ident=IDENT,
                   amasks=AMASKS) for b in range(B)]
    rD1 = run_launch("D1", build_D1, mapsD1)
    mapsD2 = [dict(h2fS=rD1[b]["h2fS"], glogS=rD1[b]["glogS"], xpS=rD1[b]["xpS"],
                   t2=rC2[b]["t2"], W_up=ins["L1_ffn_W_up"], W_down=ins["L1_ffn_W_down"],
                   sel32=SEL32, ident=IDENT) for b in range(B)]
    rD2 = run_launch("D2", build_D2, mapsD2)
    out = np.stack([np.asarray(rD2[b]["out"]) for b in range(B)]).astype(np.float32)
    return out


# version 09
# speedup vs baseline: 1.4900x; 1.0443x over previous
import numpy as np
import ml_dtypes
import concourse.bass as bass
import concourse.mybir as mybir
from concourse.bass_utils import run_bass_kernel_spmd
from contextlib import ExitStack
from concourse.bass_types import AP as _AP


def _a(x):
    return x if isinstance(x, _AP) or isinstance(x, (int, float)) or x is None else x[:]

F32 = mybir.dt.float32
BF16 = mybir.dt.bfloat16
AF = mybir.ActivationFunctionType
ALU = mybir.AluOpType
AX = mybir.AxisListType
BF = ml_dtypes.bfloat16

T = 4096
D = 1024
NCORES = 8


class Prog:

    ENGINES = ("sync", "scalar", "vector", "tensor", "gpsimd")

    def __init__(self, name, fused=False):
        self.nc = bass.Bass("TRN2", target_bir_lowering=False)
        self.plan = []
        self.es = ExitStack()
        self._names = {}
        self.fused = fused
        self.drams = {}
        self._pes = None
        self._counts = [0, 0, 0, 0, 0]
        self.last = None

    def _uname(self, prefix):
        n = self._names.get(prefix, 0)
        self._names[prefix] = n + 1
        return f"{prefix}{n}"

    def dram_in(self, name, shape, dt=F32):
        if name not in self.drams:
            self.drams[name] = self.nc.dram_tensor(
                name, list(shape), dt, kind="ExternalInput").ap()
        return self.drams[name]

    def dram_out(self, name, shape, dt=F32):
        if name not in self.drams:
            kind = "ExternalOutput" if (not self.fused or name == "out") else "Internal"
            self.drams[name] = self.nc.dram_tensor(name, list(shape), dt, kind=kind).ap()
        return self.drams[name]

    def begin_phase(self):
        self._pes = ExitStack()

    def end_phase(self):
        if self._pes is not None:
            self._pes.close()
            self._pes = None

    def sbuf(self, name, shape, dt=F32):
        es = self._pes if self._pes is not None else self.es
        return es.enter_context(self.nc.sbuf_tensor(self._uname(name), list(shape), dt))

    def psum(self, name, shape, dt=F32):
        es = self._pes if self._pes is not None else self.es
        return es.enter_context(self.nc.psum_tensor(self._uname(name), list(shape), dt))

    EIDX = {"sync": 0, "scalar": 1, "vector": 2, "tensor": 3, "gpsimd": 4}

    def now(self):
        return [(i, c) for i, c in enumerate(self._counts) if c > 0]

    def op(self, engine, fn, inc=1, at=None):
        ei = self.EIDX[engine]
        if at is None:
            deps = self.now()
        elif at == 0:
            deps = []
        elif isinstance(at, tuple):
            deps = [at]
        else:
            deps = [t for t in at if t != 0]
        if engine != "tensor" and self._counts[ei] > 0:
            deps = deps + [(ei, self._counts[ei])]
        self.plan.append((engine, fn, inc, deps))
        self._counts[ei] += inc
        self.last = (ei, self._counts[ei])
        return self.last

    def dma(self, out, in_, queue="sync", at=None):
        return self.op(queue, lambda e, o=out, i=in_: e.dma_start(out=o, in_=i), 16, at)

    def dma_slow(self, out, in_, queue="sync"):
        self.op(queue, lambda e, o=out, i=in_: e.dma_start(
            out=o, in_=i, allow_slow_non_contiguous=True), 16)

    def tt(self, out, a, b, alu, eng="vector", at=None):
        return self.op(eng, lambda e, o=_a(out), x=_a(a), y=_a(b), al=alu: e.tensor_tensor(o, x, y, al), 1, at)

    def ts(self, out, a, s1, alu, s2=None, alu2=None, eng="vector", at=None):
        if s2 is None:
            return self.op(eng, lambda e, o=_a(out), x=_a(a), s=_a(s1), al=alu: e.tensor_scalar(o, x, s, None, al), 1, at)
        return self.op(eng, lambda e, o=_a(out), x=_a(a), s=_a(s1), t=_a(s2), al=alu, al2=alu2:
                       e.tensor_scalar(o, x, s, t, al, al2), 1, at)

    def stt(self, out, in0, scalar, in1, op0, op1, eng="vector", at=None):
        return self.op(eng, lambda e, o=_a(out), a=_a(in0), s=_a(scalar), b=_a(in1), p0=op0, p1=op1:
                       e.scalar_tensor_tensor(o, a, s, b, p0, p1), 1, at)

    def copy(self, out, in_, eng="vector", at=None):
        return self.op(eng, lambda e, o=_a(out), i=_a(in_): e.tensor_copy(o, i), 1, at)

    def memset(self, ap, v, eng="vector"):
        self.op(eng, lambda e, a=_a(ap), vv=v: e.memset(a, vv))

    def act(self, out, in_, fn, bias=None, scale=None, eng="scalar", at=None):
        kw = {}
        if bias is not None:
            kw["bias"] = bias
        if scale is not None:
            kw["scale"] = scale
        return self.op(eng, lambda e, o=_a(out), i=_a(in_), f=fn, k={kk: _a(vv) for kk, vv in kw.items()}: e.activation(o, i, f, **k), 1, at)

    def red(self, out, in_, alu, eng="vector"):
        self.op(eng, lambda e, o=_a(out), i=_a(in_), a=alu: e.tensor_reduce(o, i, AX.X, a))

    def scan(self, out, d0, d1, init, op0, op1, eng="vector"):
        self.op(eng, lambda e, o=_a(out), a=_a(d0), b=_a(d1), s=_a(init), p=op0, q=op1:
                e.tensor_tensor_scan(o, a, b, s, p, q))

    def mm(self, out, lhsT, rhs, start=True, stop=True, at=None):
        return self.op("tensor", lambda e, o=_a(out), l=_a(lhsT), r=_a(rhs), s1=start, s2=stop:
                       e.matmul(o, l, r, start=s1, stop=s2), 1, at)

    def transpose(self, out, in_, ident):
        self.op("tensor", lambda e, o=_a(out), i=_a(in_), d=_a(ident): e.transpose(o, i, d))

    def recip(self, out, in_):
        self.op("vector", lambda e, o=_a(out), i=_a(in_): e.reciprocal(o, i))

    def build(self):
        nc = self.nc
        sems = [self.es.enter_context(nc.semaphore(f"s{i}")) for i in range(5)]
        finals = list(self._counts)
        plan = self.plan
        blk = self.es.enter_context(nc.Block())
        EIDX = self.EIDX

        def body_for(engname):
            ei = EIDX[engname]
            def body(eng):
                known = [0, 0, 0, 0, 0]
                for (en, fn, inc, deps) in plan:
                    if en != engname:
                        continue
                    for (di, dv) in deps:
                        if di == ei and en == "tensor":
                            continue
                        if dv > known[di]:
                            eng.wait_ge(sems[di], dv)
                            known[di] = dv
                    ins = fn(eng)
                    ins.then_inc(sems[ei], inc)
                    if engname == "tensor":
                        known[ei] += inc
                if engname == "sync":
                    for i in range(5):
                        if finals[i] > known[i]:
                            eng.wait_ge(sems[i], finals[i])
            return body

        blk.sync(body_for("sync"))
        blk.scalar(body_for("scalar"))
        blk.vector(body_for("vector"))
        blk.tensor(body_for("tensor"))
        blk.gpsimd(body_for("gpsimd"))
        self.es.close()
        return nc


def fm_ap(hbm, ft0, nft, t0, tn, tile_t=None):
    r = hbm.rearrange("(f p) t -> p f t", p=128)
    return r[:, ft0:ft0 + nft, t0:t0 + tn]



def ln_tokmajor(P, sx, xn_out, nf, eps_ap, scratch, also_rms=None):
    sq, s1, s2, m, var = scratch
    P.red(s1, sx, ALU.add)
    P.act(sq, sx, AF.Square)
    P.red(s2, sq, ALU.add)
    P.ts(m, s1, 1.0 / D, ALU.mult)
    P.tt(var, m, m, ALU.mult)
    P.stt(var, s2, 1.0 / D, var, ALU.mult, ALU.subtract)
    P.act(var, var, AF.Ln, bias=eps_ap)
    P.act(var, var, AF.Exp, scale=-0.5)
    P.ts(xn_out, sx, m, ALU.subtract, var, ALU.mult)
    if also_rms is not None:
        out2, eps2_ap = also_rms
        P.act(sq, xn_out, AF.Square)
        P.red(s2, sq, ALU.add)
        P.ts(s2, s2, 1.0 / D, ALU.mult)
        P.act(s2, s2, AF.Ln, bias=eps2_ap)
        P.act(s2, s2, AF.Exp, scale=-0.5)
        P.ts(out2, xn_out, s2, ALU.mult)


def softplus_psum(P, psum, bias_col, out, a1, a2):
    P.act(a1, psum, AF.Abs, bias=bias_col)
    P.act(a1, a1, AF.Exp, scale=-1.0)
    P.act(a1, a1, AF.Ln, bias=1.0)
    P.act(a2, psum, AF.Relu, bias=bias_col)
    P.tt(out, a1, a2, ALU.add)



def build_A1(P=None):
    P = P if P is not None else Prog("A1")
    P.begin_phase()
    TC = 128
    NCH = T // TC
    x_h = P.dram_in("x", (T, D))
    ipw_h = P.dram_in("in_proj_w", (D, 4096))
    convw_h = P.dram_in("conv_w", (2048, 4))
    dtw_h = P.dram_in("dt_w", (2048, 2048))
    dtb_h = P.dram_in("dt_b", (2048,))
    bcw_h = P.dram_in("bc_w", (2048, 32))
    rw_h = P.dram_in("router_wT", (D, 16))
    nw_h = P.dram_in("norm_w", (D,))
    ident_h = P.dram_in("ident", (128, 128))

    xn_h = P.dram_out("xn", (D, T), BF16)
    lg_h = P.dram_out("logits0", (T, 16))
    dt_h = P.dram_out("dtS", (2048, T))
    u_h = P.dram_out("uS", (2048, T))
    xc_h = P.dram_out("xcS", (2048, T))
    bcS_h = P.dram_out("bcS", (32, T))

    ipw = P.sbuf("ipw", [128, 8 * 4096], BF16)
    dtw = P.sbuf("dtw", [128, 16 * 2048], BF16)
    bcw = P.sbuf("bcw", [128, 16 * 32], BF16)
    rw = P.sbuf("rw", [128, 8 * 16], F32)
    cw = P.sbuf("cw", [128, 16 * 4], F32)
    dtb = P.sbuf("dtb", [128, 16], F32)
    nwc = P.sbuf("nwc", [128, 8], F32)
    ident = P.sbuf("ident", [128, 128], F32)
    for k in range(8):
        P.dma(ipw[:, k * 4096:(k + 1) * 4096],
              ipw_h.rearrange("(f p) m -> p f m", p=128)[:, k], queue="gpsimd")
    for k in range(16):
        P.dma(dtw[:, k * 2048:(k + 1) * 2048],
              dtw_h.rearrange("(f p) m -> p f m", p=128)[:, k], queue="gpsimd")
    P.dma(bcw[:].rearrange("p (f m) -> p f m", m=32),
          bcw_h.rearrange("(f p) m -> p f m", p=128), queue="gpsimd")
    P.dma(rw[:].rearrange("p (f m) -> p f m", m=16),
          rw_h.rearrange("(f p) m -> p f m", p=128), queue="sync")
    P.dma(cw[:].rearrange("p (f m) -> p f m", m=4),
          convw_h.rearrange("(f p) m -> p f m", p=128), queue="sync")
    P.dma_slow(dtb[:], dtb_h.rearrange("(f p) -> p f", p=128), queue="sync")
    P.dma_slow(nwc[:], nw_h.rearrange("(f p) -> p f", p=128), queue="sync")
    P.dma(ident[:], ident_h[:], queue="sync")

    sx = P.sbuf("sx", [128, D])
    xn_tok = P.sbuf("xntok", [128, D])
    xr_tok = P.sbuf("xrtok", [128, D])
    sq = P.sbuf("sq", [128, D])
    s1 = P.sbuf("s1", [128, 1]); s2 = P.sbuf("s2", [128, 1])
    mcol = P.sbuf("mcol", [128, 1]); vcol = P.sbuf("vcol", [128, 1])
    epsln = P.sbuf("epsln", [128, 1]); epsrms = P.sbuf("epsrms", [128, 1])
    xnf = P.sbuf("xnf", [128, 8 * TC])
    xnb = P.sbuf("xnb", [128, 8 * TC], BF16)
    xrb = P.sbuf("xrb", [128, 8 * TC], BF16)
    lgs = P.sbuf("lgs", [16, TC], F32)
    lgt = P.sbuf("lgt", [128, 16], F32)
    sil = P.sbuf("sil", [128, TC])
    xi = P.sbuf("xi", [128, 16 * (TC + 3)])
    xc = P.sbuf("xc", [128, 16 * TC])
    xcb = P.sbuf("xcb", [128, 16 * TC], BF16)
    dts = P.sbuf("dts", [128, 16 * TC])
    us = P.sbuf("us", [128, 16 * TC])
    a1s = P.sbuf("a1s", [128, TC]); a2s = P.sbuf("a2s", [128, TC])
    bcs = P.sbuf("bcs", [32, TC], F32)

    pT = P.psum("pT", [128, 512])
    pA = P.psum("pA", [128, TC]); pB = P.psum("pB", [128, TC])
    pC = P.psum("pC", [128, TC]); pD = P.psum("pD", [128, TC])
    pL = P.psum("pL", [16, TC])
    pBC = P.psum("pBC", [32, TC])

    P.memset(epsln[:], 1e-5)
    P.memset(epsrms[:], 1e-6)
    P.memset(xi[:], 0.0)

    scratch = (sq, s1, s2, mcol, vcol)
    x_r = x_h.rearrange("(n p) d -> n p d", p=128)

    for c in range(NCH):
        P.dma(sx[:], x_r[c])
        ln_tokmajor(P, sx[:], xn_tok[:], 8, epsln[:, 0:1],
                    scratch, also_rms=(xr_tok[:], epsrms[:, 0:1]))
        for ft in range(0, 8, 4):
            for j in range(4):
                P.transpose(pT[:, j * 128:(j + 1) * 128],
                            xn_tok[:, (ft + j) * 128:(ft + j + 1) * 128], ident[:])
            for j in range(4):
                P.copy(xnf[:, (ft + j) * TC:(ft + j) * TC + 128],
                       pT[:, j * 128:(j + 1) * 128])
            for j in range(4):
                P.transpose(pT[:, j * 128:(j + 1) * 128],
                            xr_tok[:, (ft + j) * 128:(ft + j + 1) * 128], ident[:])
            for j in range(4):
                P.ts(xrb[:, (ft + j) * TC:(ft + j) * TC + 128],
                     pT[:, j * 128:(j + 1) * 128], nwc[:, ft + j:ft + j + 1], ALU.mult)
        P.copy(xnb[:], xnf[:])
        P.dma(fm_ap(xn_h, 0, 8, c * TC, TC),
              xnb[:].rearrange("p (f t) -> p f t", t=TC))
        for k in range(8):
            P.mm(pL[:], rw[:, k * 16:(k + 1) * 16], xnf[:, k * TC:(k + 1) * TC],
                 start=(k == 0), stop=(k == 7))
        P.copy(lgs[:], pL[:])
        P.transpose(pT[:, 0:16], lgs[:], ident[0:16, 0:16])
        P.copy(lgt[:], pT[:, 0:16])
        P.dma(lg_h[c * 128:(c + 1) * 128], lgt[:])
        for m in range(16):
            for k in range(8):
                P.mm(pA[:], ipw[:, k * 4096 + m * 128: k * 4096 + (m + 1) * 128],
                     xrb[:, k * TC:(k + 1) * TC], start=(k == 0), stop=(k == 7))
            for k in range(8):
                P.mm(pB[:], ipw[:, k * 4096 + (m + 16) * 128: k * 4096 + (m + 17) * 128],
                     xrb[:, k * TC:(k + 1) * TC], start=(k == 0), stop=(k == 7))
            P.act(sil[:], pB[:], AF.Silu)
            o = m * (TC + 3)
            P.copy(xi[:, o:o + 3], xi[:, o + TC:o + TC + 3])
            P.tt(xi[:, o + 3:o + 3 + TC], pA[:], sil[:], ALU.mult)
        for m in range(16):
            o = m * (TC + 3)
            oc = m * TC
            P.ts(xc[:, oc:oc + TC], xi[:, o:o + TC], cw[:, m * 4:m * 4 + 1], ALU.mult)
            for i in range(1, 4):
                P.stt(xc[:, oc:oc + TC], xi[:, o + i:o + i + TC],
                      cw[:, m * 4 + i:m * 4 + i + 1], xc[:, oc:oc + TC],
                      ALU.mult, ALU.add)
            P.copy(xcb[:, oc:oc + TC], xc[:, oc:oc + TC])
        for m in range(16):
            pm = pC if (m % 2 == 0) else pD
            for k in range(16):
                P.mm(pm[:], dtw[:, k * 2048 + m * 128: k * 2048 + (m + 1) * 128],
                     xcb[:, k * TC:(k + 1) * TC], start=(k == 0), stop=(k == 15))
            o = m * TC
            softplus_psum(P, pm[:], dtb[:, m:m + 1], dts[:, o:o + TC], a1s[:], a2s[:])
            P.tt(us[:, o:o + TC], dts[:, o:o + TC], xc[:, o:o + TC], ALU.mult)
        for k in range(16):
            P.mm(pBC[:], bcw[:, k * 32:(k + 1) * 32], xcb[:, k * TC:(k + 1) * TC],
                 start=(k == 0), stop=(k == 15))
        P.copy(bcs[:], pBC[:])
        P.dma(bcS_h[:, c * TC:(c + 1) * TC], bcs[:])
        for (hb, sb_) in ((dt_h, dts), (u_h, us), (xc_h, xc)):
            P.dma(fm_ap(hb, 0, 16, c * TC, TC),
                  sb_[:].rearrange("p (f t) -> p f t", t=TC))
    P.end_phase()
    return P



def build_A2(P=None):
    P = P if P is not None else Prog("A2")
    P.begin_phase()
    TS = 512
    NCH = T // TS
    x_h = P.dram_in("x", (T, D))
    dt_h = P.dram_in("dtS", (2048, T))
    u_h = P.dram_in("uS", (2048, T))
    xc_h = P.dram_in("xcS", (2048, T))
    bc_h = P.dram_in("bcS", (32, T))
    sel_h = P.dram_in("sel32", (32, 32 * 128))
    alog_h = P.dram_in("A_log", (2048, 16))
    dskip_h = P.dram_in("D_skip", (2048,))
    opw_h = P.dram_in("out_proj_w", (2048, D))
    ident_h = P.dram_in("ident", (128, 128))
    t1_h = P.dram_out("t1", (D, T))

    opw = P.sbuf("opw", [128, 16 * D], BF16)
    for k in range(16):
        P.dma(opw[:, k * D:(k + 1) * D],
              opw_h.rearrange("(f p) m -> p f m", p=128)[:, k], queue="gpsimd")
    expA = P.sbuf("expA", [128, 16 * 16], F32)
    P.dma(expA[:].rearrange("p (f s) -> p f s", s=16),
          alog_h.rearrange("(f p) s -> p f s", p=128))
    P.act(expA[:], expA[:], AF.Exp)
    dsk = P.sbuf("dsk", [128, 16], F32)
    P.dma_slow(dsk[:], dskip_h.rearrange("(f p) -> p f", p=128))
    ident = P.sbuf("ident", [128, 128], F32)
    P.dma(ident[:], ident_h[:])
    sel = P.sbuf("sel", [32, 32 * 128], F32)
    P.dma(sel[:], sel_h[:])

    hstate = P.sbuf("hstate", [128, 256], F32)
    P.memset(hstate[:], 0.0)

    HT = 8
    dts = P.sbuf("dts", [128, HT * TS])
    us = P.sbuf("us", [128, HT * TS])
    xcs = P.sbuf("xcs", [128, HT * TS])
    yacc = P.sbuf("yacc", [128, HT * TS])
    ys = P.sbuf("ys", [128, 16 * TS], BF16)
    bcs = P.sbuf("bcs", [32, TS], F32)
    bmr = P.sbuf("bmr", [128, TS]); cmr = P.sbuf("cmr", [128, TS])
    tmp = P.sbuf("tmp", [128, TS]); ab = P.sbuf("ab", [128, TS])
    d1 = P.sbuf("d1", [128, TS]); hh = P.sbuf("hh", [128, TS])
    pr = P.sbuf("pr", [128, TS])
    xT = P.sbuf("xT", [128, 8 * TS])
    sx = P.sbuf("sx", [128, D])
    t1s = P.sbuf("t1s", [128, 8 * TS])

    pb1 = P.psum("pb1", [128, TS])
    pb2 = P.psum("pb2", [128, TS])
    pt1 = [P.psum("pt1", [128, TS]) for _ in range(6)]

    x_r = x_h.rearrange("(n p) d -> n p d", p=128)

    for c in range(NCH):
        P.dma(bcs[:], bc_h[:, c * TS:(c + 1) * TS])
        for tt_ in range(4):
            P.dma(sx[:], x_r[c * 4 + tt_])
            for ft in range(0, 8, 4):
                for j in range(4):
                    P.transpose(pb2[:, j * 128:(j + 1) * 128],
                                sx[:, (ft + j) * 128:(ft + j + 1) * 128], ident[:])
                for j in range(4):
                    P.copy(xT[:, (ft + j) * TS + tt_ * 128:(ft + j) * TS + tt_ * 128 + 128],
                           pb2[:, j * 128:(j + 1) * 128])
        for half in range(2):
            for hb, sb_ in ((dt_h, dts), (u_h, us), (xc_h, xcs)):
                P.dma(sb_[:].rearrange("p (f t) -> p f t", t=TS),
                      fm_ap(hb, half * 8, 8, c * TS, TS))
            for m in range(HT):
                P.memset(yacc[:, m * TS:(m + 1) * TS], 0.0)
            for s in range(16):
                P.mm(pb1[:], sel[:, s * 128:(s + 1) * 128], bcs[:])
                P.copy(bmr[:], pb1[:])
                P.mm(pb1[:], sel[:, (16 + s) * 128:(17 + s) * 128], bcs[:])
                P.copy(cmr[:], pb1[:])
                for m in range(HT):
                    dt_t = dts[:, m * TS:(m + 1) * TS]
                    u_t = us[:, m * TS:(m + 1) * TS]
                    ft = half * 8 + m
                    st_col = hstate[:, s * 16 + ft: s * 16 + ft + 1]
                    P.ts(tmp[:], dt_t, expA[:, ft * 16 + s: ft * 16 + s + 1],
                         ALU.mult, eng="gpsimd")
                    P.act(ab[:], tmp[:], AF.Exp, scale=-1.0)
                    P.tt(d1[:], u_t, bmr[:], ALU.mult)
                    P.scan(hh[:], ab[:], d1[:], st_col, ALU.mult, ALU.add)
                    P.copy(st_col, hh[:, TS - 1:TS])
                    P.tt(pr[:], hh[:], cmr[:], ALU.mult)
                    P.tt(yacc[:, m * TS:(m + 1) * TS], yacc[:, m * TS:(m + 1) * TS],
                         pr[:], ALU.add, eng="gpsimd")
            for m in range(HT):
                ft = half * 8 + m
                P.stt(ys[:, ft * TS:(ft + 1) * TS], xcs[:, m * TS:(m + 1) * TS],
                      dsk[:, ft:ft + 1], yacc[:, m * TS:(m + 1) * TS],
                      ALU.mult, ALU.add)
        for grp, mlist in ((0, range(6)), (1, range(6, 8))):
            for mi, m in enumerate(mlist):
                for k in range(16):
                    P.mm(pt1[mi][:], opw[:, k * D + m * 128: k * D + (m + 1) * 128],
                         ys[:, k * TS:(k + 1) * TS], start=(k == 0), stop=(k == 15))
            for mi, m in enumerate(mlist):
                P.tt(t1s[:, m * TS:(m + 1) * TS], pt1[mi][:],
                     xT[:, m * TS:(m + 1) * TS], ALU.add)
        P.dma(fm_ap(t1_h, 0, 8, c * TS, TS),
              t1s[:].rearrange("p (f t) -> p f t", t=TS))
    P.end_phase()
    return P




def emit_first_argmax(P, work, scr):
    eq, cum, zer, first, m = scr
    P.red(m, work, ALU.max)
    P.ts(eq, work, m[:, 0:1], ALU.is_equal)
    P.scan(cum, eq, zer, 0.0, ALU.add, ALU.add)
    P.ts(cum, cum, 1.0, ALU.is_equal)
    P.tt(first, eq, cum, ALU.mult)
    return first, m


def build_B(P=None):
    P = P if P is not None else Prog("B")
    P.begin_phase()
    TCH = 512
    HALF = T // 2
    xn_h = P.dram_in("xn", (D, T), BF16)
    t1_h = P.dram_in("t1", (D, T))
    lg_h = P.dram_in("logits0", (T, 16))
    bias_h = P.dram_in("bias", (16,))
    gw_h = P.dram_in("gate_w", (16, D, D))
    uw_h = P.dram_in("up_w", (16, D, D))
    dw_h = P.dram_in("down_w", (16, D, D))
    shg_h = P.dram_in("sh_gate_w", (D, 2048))
    shu_h = P.dram_in("sh_up_w", (D, 2048))
    shd_h = P.dram_in("sh_down_w", (2048, D))
    sel_h = P.dram_in("sel32", (32, 32 * 128))
    ident_h = P.dram_in("ident", (128, 128))
    x1_h = P.dram_out("x1", (D, T))

    ident = P.sbuf("ident", [128, 128], F32)
    P.dma(ident[:], ident_h[:])
    selb = P.sbuf("selb", [32, 16 * 128], BF16)
    P.dma(selb[:], sel_h[0:32, 0:16 * 128], queue="gpsimd")

    lgt = P.sbuf("lgt", [128, 16], F32)
    sco = P.sbuf("sco", [128, 16], F32)
    sb_ = P.sbuf("sb_", [128, 16], F32)
    gsc = P.sbuf("gsc", [128, 4], F32)
    gmask = P.sbuf("gmask", [128, 4], F32)
    emask = P.sbuf("emask", [128, 16], F32)
    msk = P.sbuf("msk", [128, 16], F32)
    comb = P.sbuf("comb", [128, 16], F32)
    wsum = P.sbuf("wsum", [128, 1], F32)
    twc = P.sbuf("twc", [128, 1], F32)
    eq = P.sbuf("eq", [128, 16], F32)
    cum = P.sbuf("cum", [128, 16], F32)
    zer = P.sbuf("zer", [128, 16], F32)
    first = P.sbuf("first", [128, 16], F32)
    mm_ = P.sbuf("mm_", [128, 1], F32)
    brep = P.sbuf("brep", [128, 16], F32)
    combT = P.sbuf("combT", [16, T], BF16)
    pR = P.psum("pR", [128, 128])

    P.memset(zer[:], 0.0)
    ones1 = P.sbuf("ones1", [1, 128], F32)
    P.memset(ones1[:], 1.0)
    P.dma(brep[0:1, :], bias_h.unsqueeze(0))
    P.mm(pR[:, 0:16], ones1[:], brep[0:1, :])
    P.copy(brep[:], pR[:, 0:16])

    for tt_ in range(T // 128):
        P.dma(lgt[:], lg_h[tt_ * 128:(tt_ + 1) * 128])
        P.act(sco[:], lgt[:], AF.Sigmoid)
        P.tt(sb_[:], sco[:], brep[:], ALU.add)
        scr4 = (eq[:, 0:4], cum[:, 0:4], zer[:, 0:4], first[:, 0:4], mm_)
        for g in range(4):
            sg = sb_[:, g * 4:(g + 1) * 4]
            P.copy(msk[:, 0:4], sg)
            f, m = emit_first_argmax(P, msk[:, 0:4], scr4)
            P.copy(gsc[:, g:g + 1], m)
            P.stt(msk[:, 0:4], f, -1e30, msk[:, 0:4], ALU.mult, ALU.add)
            P.red(mm_, msk[:, 0:4], ALU.max)
            P.tt(gsc[:, g:g + 1], gsc[:, g:g + 1], mm_, ALU.add)
        P.memset(gmask[:], 0.0)
        P.copy(msk[:, 0:4], gsc[:])
        for _ in range(2):
            f, m = emit_first_argmax(P, msk[:, 0:4], scr4)
            P.tt(gmask[:], gmask[:], f, ALU.add)
            P.stt(msk[:, 0:4], f, -1e30, msk[:, 0:4], ALU.mult, ALU.add)
        P.copy(emask[:].rearrange("p (g r) -> p g r", r=4),
               gmask[:].unsqueeze(2).broadcast_to([128, 4, 4]))
        P.ts(msk[:], sb_[:], 1.0, ALU.add)
        P.tt(msk[:], msk[:], emask[:], ALU.mult)
        P.ts(msk[:], msk[:], 1.0, ALU.subtract)
        P.memset(comb[:], 0.0)
        P.memset(wsum[:], 0.0)
        scr16 = (eq[:], cum[:], zer[:], first[:], mm_)
        for _ in range(4):
            f, m = emit_first_argmax(P, msk[:], scr16)
            P.tt(eq[:], f, sco[:], ALU.mult)
            P.red(twc, eq[:], ALU.add)
            P.stt(comb[:], f, twc[:, 0:1], comb[:], ALU.mult, ALU.add)
            P.tt(wsum[:], wsum[:], twc, ALU.add)
            P.stt(msk[:], f, -1e30, msk[:], ALU.mult, ALU.add)
        P.ts(wsum[:], wsum[:], 1e-20, ALU.add)
        P.recip(twc, wsum[:])
        P.ts(twc, twc, 2.5, ALU.mult)
        P.ts(comb[:], comb[:], twc[:, 0:1], ALU.mult)
        P.transpose(pR[0:16, :], comb[:], ident[:])
        P.copy(combT[:, tt_ * 128:(tt_ + 1) * 128], pR[0:16, :])

    NK = 8
    wz = P.sbuf("wz", [128, 49152], BF16)
    GATE, UP, DOWN = 0, 8192, 16384
    SHG, SHU, SHD = 0, 16384, 32768
    xnc = P.sbuf("xnc", [128, 8 * TCH], BF16)
    hub = P.sbuf("hub", [128, 8 * TCH], BF16)
    silb = P.sbuf("silb", [128, TCH], F32)
    crep = P.sbuf("crep", [128, TCH], F32)
    acc = P.sbuf("acc", [128, 8 * HALF], F32)
    t1b = P.sbuf("t1b", [128, 8 * 256], F32)
    pG = P.psum("pG", [128, TCH]); pU = P.psum("pU", [128, TCH])
    pC = P.psum("pC", [128, TCH])
    pD1 = P.psum("pD1", [128, TCH]); pD2 = P.psum("pD2", [128, TCH])

    gw_r = gw_h.rearrange("e (f p) m -> e p f m", p=128)
    uw_r = uw_h.rearrange("e (f p) m -> e p f m", p=128)
    dw_r = dw_h.rearrange("e (f p) m -> e p f m", p=128)

    def expert_mm(e_, h_, sub):
        t0 = h_ * HALF + sub * TCH
        P.dma(xnc[:].rearrange("p (f t) -> p f t", t=TCH), fm_ap(xn_h, 0, 8, t0, TCH))
        for mt in range(8):
            for k in range(NK):
                P.mm(pG[:], wz[:, GATE + k * D + mt * 128: GATE + k * D + (mt + 1) * 128],
                     xnc[:, k * TCH:(k + 1) * TCH], start=(k == 0), stop=(k == NK - 1))
            for k in range(NK):
                P.mm(pU[:], wz[:, UP + k * D + mt * 128: UP + k * D + (mt + 1) * 128],
                     xnc[:, k * TCH:(k + 1) * TCH], start=(k == 0), stop=(k == NK - 1))
            P.act(silb[:], pG[:], AF.Silu)
            P.tt(hub[:, mt * TCH:(mt + 1) * TCH], silb[:], pU[:], ALU.mult)
        P.mm(pC[:], selb[0:16, e_ * 128:(e_ + 1) * 128], combT[:, t0:t0 + TCH])
        P.copy(crep[:], pC[:])
        for mt in range(8):
            P.tt(hub[:, mt * TCH:(mt + 1) * TCH], hub[:, mt * TCH:(mt + 1) * TCH],
                 crep[:], ALU.mult)
        for m in range(8):
            pd = pD1 if m % 2 == 0 else pD2
            for k in range(8):
                P.mm(pd[:], wz[:, DOWN + k * D + m * 128: DOWN + k * D + (m + 1) * 128],
                     hub[:, k * TCH:(k + 1) * TCH], start=(k == 0), stop=(k == 7))
            ao = m * (HALF) + sub * TCH
            P.tt(acc[:, ao:ao + TCH], acc[:, ao:ao + TCH], pd[:], ALU.add)

    def shared_mm(h_, sub):
        t0 = h_ * HALF + sub * TCH
        P.dma(xnc[:].rearrange("p (f t) -> p f t", t=TCH), fm_ap(xn_h, 0, 8, t0, TCH))
        for grp in range(2):
            for mt in range(8):
                mg = grp * 8 + mt
                for k in range(8):
                    P.mm(pG[:], wz[:, SHG + k * 2048 + mg * 128: SHG + k * 2048 + (mg + 1) * 128],
                         xnc[:, k * TCH:(k + 1) * TCH], start=(k == 0), stop=(k == 7))
                for k in range(8):
                    P.mm(pU[:], wz[:, SHU + k * 2048 + mg * 128: SHU + k * 2048 + (mg + 1) * 128],
                         xnc[:, k * TCH:(k + 1) * TCH], start=(k == 0), stop=(k == 7))
                P.act(silb[:], pG[:], AF.Silu)
                P.tt(hub[:, mt * TCH:(mt + 1) * TCH], silb[:], pU[:], ALU.mult)
            for m in range(8):
                pd = pD1 if m % 2 == 0 else pD2
                for k in range(8):
                    kg = grp * 8 + k
                    P.mm(pd[:], wz[:, SHD + kg * D + m * 128: SHD + kg * D + (m + 1) * 128],
                         hub[:, k * TCH:(k + 1) * TCH], start=(k == 0), stop=(k == 7))
                ao = m * HALF + sub * TCH
                if grp == 0:
                    P.copy(acc[:, ao:ao + TCH], pd[:])
                else:
                    P.tt(acc[:, ao:ao + TCH], acc[:, ao:ao + TCH], pd[:], ALU.add)

    for h_ in range(2):
        P.dma(wz[:, SHG:SHG + 16384].rearrange("p (f m) -> p f m", m=2048),
              shg_h.rearrange("(f p) m -> p f m", p=128), queue="gpsimd")
        P.dma(wz[:, SHU:SHU + 16384].rearrange("p (f m) -> p f m", m=2048),
              shu_h.rearrange("(f p) m -> p f m", p=128), queue="gpsimd")
        P.dma(wz[:, SHD:SHD + 16384].rearrange("p (f m) -> p f m", m=1024),
              shd_h.rearrange("(f p) m -> p f m", p=128), queue="gpsimd")
        for sub in range(4):
            shared_mm(h_, sub)
        for e_ in range(16):
            P.dma(wz[:, GATE:GATE + 8192].rearrange("p (f m) -> p f m", m=1024),
                  gw_r[e_], queue="gpsimd")
            P.dma(wz[:, UP:UP + 8192].rearrange("p (f m) -> p f m", m=1024),
                  uw_r[e_], queue="gpsimd")
            P.dma(wz[:, DOWN:DOWN + 8192].rearrange("p (f m) -> p f m", m=1024),
                  dw_r[e_], queue="gpsimd")
            for sub in range(4):
                expert_mm(e_, h_, sub)
        for sub in range(8):
            t0 = h_ * HALF + sub * 256
            P.dma(t1b[:].rearrange("p (f t) -> p f t", t=256), fm_ap(t1_h, 0, 8, t0, 256))
            for m in range(8):
                ao = m * HALF + sub * 256
                P.tt(t1b[:, m * 256:(m + 1) * 256], t1b[:, m * 256:(m + 1) * 256],
                     acc[:, ao:ao + 256], ALU.add)
            P.dma(fm_ap(x1_h, 0, 8, t0, 256),
                  t1b[:].rearrange("p (f t) -> p f t", t=256))
    P.end_phase()
    return P




def fm_norm(P, xt, nf, TCH, sq, rowbuf, pS, ones128, epsap, out_rows):
    for k in range(nf):
        P.act(sq[:, 0:TCH], xt[:, k * TCH:(k + 1) * TCH], AF.Square)
        P.mm(pS[0:1, 0:TCH], ones128[:, 0:1], sq[:, 0:TCH],
             start=(k == 0), stop=(k == nf - 1))
    P.ts(rowbuf[0:1, 0:TCH], pS[0:1, 0:TCH], 1.0 / (nf * 128), ALU.mult)
    P.act(rowbuf[0:1, 0:TCH], rowbuf[0:1, 0:TCH], AF.Ln, bias=epsap)
    P.act(out_rows, rowbuf[0:1, 0:TCH], AF.Exp, scale=-0.5)


def build_C1a(P=None):
    P = P if P is not None else Prog("C1a")
    P.begin_phase()
    TO = 512
    NS = T // TO
    x1_h = P.dram_in("x1", (D, T))
    wq_h = P.dram_in("Wq", (D, 768))
    wk_h = P.dram_in("Wk", (D, 768))
    wv_h = P.dram_in("Wv", (D, 1536))
    wg_h = P.dram_in("Wg", (D, 1536))
    wb_h = P.dram_in("Wb", (D, 8))
    wa_h = P.dram_in("Wa", (D, 8))
    alog_h = P.dram_in("A_log1", (8,))
    dtb_h = P.dram_in("dt_bias", (8,))
    cq_h = P.dram_in("conv_q_w", (768, 4))
    ck_h = P.dram_in("conv_k_w", (768, 4))
    cv_h = P.dram_in("conv_v_w", (1536, 4))
    n1_h = P.dram_in("norm1_w", (D,))

    xn1_h = P.dram_out("xn1", (D, T), BF16)
    q_h = P.dram_out("qS", (768, T), BF16)
    k_h = P.dram_out("kS", (768, T), BF16)
    v_h = P.dram_out("vS", (1536, T), BF16)
    g_h = P.dram_out("gS", (1536, T), BF16)
    beta_h = P.dram_out("betaS", (8, T))
    cum_h = P.dram_out("cumS", (8, T))

    wq = P.sbuf("wq", [128, 8 * 768], BF16)
    wk = P.sbuf("wk", [128, 8 * 768], BF16)
    wv = P.sbuf("wv", [128, 8 * 1536], BF16)
    wg = P.sbuf("wg", [128, 8 * 1536], BF16)
    wba = P.sbuf("wba", [128, 8 * 16], BF16)
    for hname, hh, sb_, m in (("q", wq_h, wq, 768), ("k", wk_h, wk, 768),
                              ("v", wv_h, wv, 1536), ("g", wg_h, wg, 1536)):
        P.dma(sb_[:].rearrange("p (f m) -> p f m", m=m),
              hh.rearrange("(f p) m -> p f m", p=128), queue="gpsimd")
    P.dma(wba[:].rearrange("p (f m) -> p f m", m=16)[:, :, 0:8],
          wb_h.rearrange("(f p) m -> p f m", p=128), queue="gpsimd")
    P.dma(wba[:].rearrange("p (f m) -> p f m", m=16)[:, :, 8:16],
          wa_h.rearrange("(f p) m -> p f m", p=128), queue="gpsimd")
    cqw = P.sbuf("cqw", [96, 8 * 4], F32)
    ckw = P.sbuf("ckw", [96, 8 * 4], F32)
    cvw = P.sbuf("cvw", [96, 16 * 4], F32)
    P.dma_slow(cqw[:].rearrange("p (f m) -> p f m", m=4),
               cq_h.rearrange("(f p) m -> p f m", p=96))
    P.dma_slow(ckw[:].rearrange("p (f m) -> p f m", m=4),
               ck_h.rearrange("(f p) m -> p f m", p=96))
    P.dma_slow(cvw[:].rearrange("p (f m) -> p f m", m=4),
               cv_h.rearrange("(f p) m -> p f m", p=96))
    n1c = P.sbuf("n1c", [128, 8], F32)
    P.dma_slow(n1c[:], n1_h.rearrange("(f p) -> p f", p=128))
    nega = P.sbuf("nega", [8, 1], F32)
    P.dma(nega[:], alog_h.unsqueeze(1))
    P.act(nega[:], nega[:], AF.Exp)
    P.ts(nega[:], nega[:], -1.0, ALU.mult)
    dtbc = P.sbuf("dtbc", [8, 1], F32)
    P.dma(dtbc[:], dtb_h.unsqueeze(1))

    ones128 = P.sbuf("ones128", [128, 1], F32)
    P.memset(ones128[:], 1.0)
    ones1 = P.sbuf("ones1", [1, 128], F32)
    P.memset(ones1[:], 1.0)
    epsr = P.sbuf("epsr", [1, 1], F32)
    P.memset(epsr[:], 1e-6)
    eps5 = P.sbuf("eps5", [1, 1], F32)
    P.memset(eps5[:], 1e-5)
    epsl2 = P.sbuf("epsl2", [96, 1], F32)
    P.memset(epsl2[:], 1e-6)

    x1t = P.sbuf("x1t", [128, 8 * TO])
    xn1f = P.sbuf("xn1f", [128, 8 * TO])
    xn1b = P.sbuf("xn1b", [128, 8 * TO], BF16)
    hb = P.sbuf("hb", [128, 8 * TO], BF16)
    sq = P.sbuf("sq", [128, TO])
    rowb = P.sbuf("rowb", [1, TO], F32)
    rrep = P.sbuf("rrep", [128, TO], F32)
    qpad = P.sbuf("qpad", [96, TO + 3], F32)
    qtail = P.sbuf("qtail", [96, 48 * 3], F32)
    cnv = P.sbuf("cnv", [96, TO], F32)
    sq96 = P.sbuf("sq96", [96, TO], F32)
    row96 = P.sbuf("row96", [1, TO], F32)
    rs96 = P.sbuf("rs96", [96, TO], F32)
    qb = P.sbuf("qb", [96, 8 * TO], BF16)
    kb = P.sbuf("kb", [96, 8 * TO], BF16)
    vb = P.sbuf("vb", [96, 16 * TO], BF16)
    gb = P.sbuf("gb", [96, 16 * TO], BF16)
    ba = P.sbuf("ba", [16, TO], F32)
    bet = P.sbuf("bet", [8, TO], F32)
    gdec = P.sbuf("gdec", [8, TO], F32)
    cums = P.sbuf("cums", [8, TO], F32)
    zer8 = P.sbuf("zer8", [8, 128], F32)
    a1t = P.sbuf("a1t", [8, TO], F32)
    a2t = P.sbuf("a2t", [8, TO], F32)

    pS = P.psum("pS", [1, TO])
    pM = P.psum("pM", [128, TO])
    pP = P.psum("pP", [96, TO])
    pB = P.psum("pB", [16, TO])

    P.memset(qtail[:], 0.0)
    P.memset(zer8[:], 0.0)

    def proj_conv(wt, nheads, conv_w, tail_base, out_sb, l2norm, scale, s):
        for hh in range(nheads):
            for k in range(8):
                P.mm(pP[:], wt[:, k * (96 * nheads) + hh * 96:
                               k * (96 * nheads) + (hh + 1) * 96],
                     hb[:, k * TO:(k + 1) * TO], start=(k == 0), stop=(k == 7))
            tb = (tail_base + hh) * 3
            P.copy(qpad[:, 0:3], qtail[:, tb:tb + 3])
            P.copy(qpad[:, 3:3 + TO], pP[:])
            P.copy(qtail[:, tb:tb + 3], qpad[:, TO:TO + 3])
            P.ts(cnv[:], qpad[:, 0:TO], conv_w[:, hh * 4:hh * 4 + 1], ALU.mult)
            for i in range(1, 4):
                P.stt(cnv[:], qpad[:, i:i + TO], conv_w[:, hh * 4 + i:hh * 4 + i + 1],
                      cnv[:], ALU.mult, ALU.add)
            P.act(cnv[:], cnv[:], AF.Silu)
            if l2norm:
                P.act(sq96[:], cnv[:], AF.Square)
                P.mm(pS[0:1, :], ones128[0:96, 0:1], sq96[:], start=True, stop=True)
                P.copy(row96[0:1, :], pS[0:1, :])
                P.ts(row96[0:1, :], row96[0:1, :], epsr[0:1, 0:1], ALU.add)
                P.act(row96[0:1, :], row96[0:1, :], AF.Ln)
                P.act(row96[0:1, :], row96[0:1, :], AF.Exp, scale=-0.5)
                if scale != 1.0:
                    P.ts(row96[0:1, :], row96[0:1, :], scale, ALU.mult)
                P.mm(pP[:], ones1[0:1, 0:96], row96[0:1, :], start=True, stop=True)
                P.copy(rs96[:], pP[:])
                P.tt(out_sb[:, hh * TO:(hh + 1) * TO], cnv[:], rs96[:], ALU.mult)
            else:
                P.copy(out_sb[:, hh * TO:(hh + 1) * TO], cnv[:])

    wq_r = wq
    for s in range(NS):
        t0 = s * TO
        P.dma(x1t[:].rearrange("p (f t) -> p f t", t=TO), fm_ap(x1_h, 0, 8, t0, TO))
        for k in range(8):
            P.mm(pS[0:1, :], ones128[:, 0:1], x1t[:, k * TO:(k + 1) * TO],
                 start=(k == 0), stop=(k == 7))
        P.ts(rowb[0:1, :], pS[0:1, :], 1.0 / D, ALU.mult)
        P.mm(pM[:], ones1[:], rowb[0:1, :], start=True, stop=True)
        P.copy(rrep[:], pM[:])
        for k in range(8):
            P.tt(xn1f[:, k * TO:(k + 1) * TO], x1t[:, k * TO:(k + 1) * TO],
                 rrep[:], ALU.subtract)
        for k in range(8):
            P.act(sq[:], xn1f[:, k * TO:(k + 1) * TO], AF.Square)
            P.mm(pS[0:1, :], ones128[:, 0:1], sq[:], start=(k == 0), stop=(k == 7))
        P.ts(rowb[0:1, :], pS[0:1, :], 1.0 / D, ALU.mult)
        P.act(rowb[0:1, :], rowb[0:1, :], AF.Ln, bias=eps5[0:1, 0:1])
        P.act(rowb[0:1, :], rowb[0:1, :], AF.Exp, scale=-0.5)
        P.mm(pM[:], ones1[:], rowb[0:1, :], start=True, stop=True)
        P.copy(rrep[:], pM[:])
        for k in range(8):
            P.tt(xn1f[:, k * TO:(k + 1) * TO], xn1f[:, k * TO:(k + 1) * TO],
                 rrep[:], ALU.mult)
        P.copy(xn1b[:], xn1f[:])
        P.dma(fm_ap(xn1_h, 0, 8, t0, TO), xn1b[:].rearrange("p (f t) -> p f t", t=TO))
        fm_norm(P, xn1f, 8, TO, sq, rowb, pS, ones128, epsr[0:1, 0:1], rowb[0:1, 0:TO])
        P.mm(pM[:], ones1[:], rowb[0:1, :], start=True, stop=True)
        P.copy(rrep[:], pM[:])
        for k in range(8):
            P.tt(sq[:], xn1f[:, k * TO:(k + 1) * TO], rrep[:], ALU.mult)
            P.ts(hb[:, k * TO:(k + 1) * TO], sq[:], n1c[:, k:k + 1], ALU.mult)
        proj_conv(wq, 8, cqw, 0, qb, True, float(96 ** -0.5), s)
        proj_conv(wk, 8, ckw, 8, kb, True, 1.0, s)
        proj_conv(wv, 16, cvw, 16, vb, False, 1.0, s)
        for hh in range(16):
            for k in range(8):
                P.mm(pP[:], wg[:, k * 1536 + hh * 96: k * 1536 + (hh + 1) * 96],
                     hb[:, k * TO:(k + 1) * TO], start=(k == 0), stop=(k == 7))
            P.copy(gb[:, hh * TO:(hh + 1) * TO], pP[:])
        for k in range(8):
            P.mm(pB[:], wba[:, k * 16:(k + 1) * 16], hb[:, k * TO:(k + 1) * TO],
                 start=(k == 0), stop=(k == 7))
        P.copy(ba[:], pB[:])
        P.act(bet[:], ba[0:8, :], AF.Sigmoid)
        for k in range(8):
            P.mm(pB[0:8, :], wba[:, k * 16 + 8:k * 16 + 16],
                 hb[:, k * TO:(k + 1) * TO], start=(k == 0), stop=(k == 7))
        P.act(a1t[:], pB[0:8, :], AF.Abs, bias=dtbc[:, 0:1])
        P.act(a1t[:], a1t[:], AF.Exp, scale=-1.0)
        P.act(a1t[:], a1t[:], AF.Ln, bias=1.0)
        P.act(a2t[:], pB[0:8, :], AF.Relu, bias=dtbc[:, 0:1])
        P.tt(a1t[:], a1t[:], a2t[:], ALU.add)
        P.ts(gdec[:], a1t[:], nega[:, 0:1], ALU.mult)
        for ci in range(TO // 128):
            P.scan(cums[:, ci * 128:(ci + 1) * 128], gdec[:, ci * 128:(ci + 1) * 128],
                   zer8[:], 0.0, ALU.add, ALU.add)
        P.dma(beta_h[:, t0:t0 + TO], bet[:])
        P.dma(cum_h[:, t0:t0 + TO], cums[:])
        for hh, sb_, nf in ((q_h, qb, 8), (k_h, kb, 8), (v_h, vb, 16), (g_h, gb, 16)):
            P.dma(hh.rearrange("(f p) t -> p f t", p=96)[:, :, t0:t0 + TO],
                  sb_[:].rearrange("p (f t) -> p f t", t=TO))
    P.end_phase()
    return P




def build_C1b(P=None):
    P = P if P is not None else Prog("C1b")
    P.begin_phase()
    TO = 512
    NS = T // TO
    C = 128
    q_hb = P.dram_in("qS", (768, T), BF16)
    k_hb = P.dram_in("kS", (768, T), BF16)
    v_hb = P.dram_in("vS", (1536, T), BF16)
    g_hb = P.dram_in("gS", (1536, T), BF16)
    beta_h = P.dram_in("betaS", (8, T))
    cum_h = P.dram_in("cumS", (8, T))
    xn1_h = P.dram_in("xn1", (D, T), BF16)
    onw_h = P.dram_in("onorm_w", (192,))
    wo_h = P.dram_in("Wo", (1536, D))
    sel_h = P.dram_in("sel32", (32, 32 * 128))
    ident_h = P.dram_in("ident", (128, 128))
    masks_h = P.dram_in("masks", (3, 128, 128))
    xmid_h = P.dram_out("xmid", (D, T))

    ident = P.sbuf("ident", [128, 128], F32)
    P.dma(ident[:], ident_h[:])
    identb = P.sbuf("identb", [128, 128], BF16)
    P.dma(identb[:], ident_h[:], queue="gpsimd")
    mlo = P.sbuf("mlo", [128, 128], F32)
    mup = P.sbuf("mup", [128, 128], F32)
    mui = P.sbuf("mui", [128, 128], F32)
    P.dma(mlo[:], masks_h[0]); P.dma(mup[:], masks_h[1]); P.dma(mui[:], masks_h[2])
    sel8 = P.sbuf("sel8", [8, 8 * 128], F32)
    P.dma(sel8[:], sel_h[0:8, 0:8 * 128])
    ones1 = P.sbuf("ones1", [1, 128], F32)
    P.memset(ones1[:], 1.0)
    eps192 = P.sbuf("eps192", [128, 1], F32)
    P.memset(eps192[:], 1e-6)
    onwrep = P.sbuf("onwrep", [128, 192], F32)
    onwrow = P.sbuf("onwrow", [1, 192], F32)
    P.dma(onwrow[:], onw_h.unsqueeze(0))
    pO = P.psum("pO", [128, 192])
    P.mm(pO[:], ones1[:], onwrow[:])
    P.copy(onwrep[:], pO[:])
    wo = P.sbuf("wo", [96, 16 * D], BF16)
    P.dma(wo[:].rearrange("p (f m) -> p f m", m=D),
          wo_h.rearrange("(f p) m -> p f m", p=96), queue="gpsimd")

    qb = P.sbuf("qb", [96, 8 * TO], BF16)
    kb = P.sbuf("kb", [96, 8 * TO], BF16)
    vb = P.sbuf("vb", [96, 16 * TO], BF16)
    gb = P.sbuf("gb", [96, 16 * TO], BF16)
    xn1b = P.sbuf("xn1b", [128, 8 * TO], BF16)
    xn1f = P.sbuf("xn1f", [128, TO], F32)
    bet = P.sbuf("bet", [8, TO], F32)
    cums = P.sbuf("cums", [8, TO], F32)
    cbT = P.sbuf("cbT", [128, 16], F32)
    Sf = P.sbuf("Sf", [96, 8 * 192], F32)
    Sb = P.sbuf("Sb", [96, 8 * 192], BF16)
    P.memset(Sf[:], 0.0)
    ofm = P.sbuf("ofm", [96, 16 * TO], BF16)
    xmid = P.sbuf("xmid", [128, 8 * TO], F32)

    kkS = P.sbuf("kkS", [128, 128], F32)
    kqS = P.sbuf("kqS", [128, 128], F32)
    crS = P.sbuf("crS", [128, 128], F32)
    rS = P.sbuf("rS", [128, 128], F32)
    DN = P.sbuf("DN", [128, 128], F32)
    DT = P.sbuf("DT", [128, 128], F32)
    brS = P.sbuf("brS", [128, 128], F32)
    Nc = P.sbuf("Nc", [128, 128], F32)
    NTc = P.sbuf("NTc", [128, 128], F32)
    Nn = P.sbuf("Nn", [128, 128], F32)
    NTn = P.sbuf("NTn", [128, 128], F32)
    Pm = P.sbuf("Pm", [128, 128], F32)
    Vtok = P.sbuf("Vtok", [128, 192], F32)
    Gtok = P.sbuf("Gtok", [128, 192], F32)
    gKS0 = P.sbuf("gKS0", [128, 192], F32)
    RHS = P.sbuf("RHS", [128, 192], F32)
    Wf = P.sbuf("Wf", [128, 192], F32)
    Wb_ = P.sbuf("Wb_", [128, 192], BF16)
    MTb = P.sbuf("MTb", [128, 128], BF16)
    Ot = P.sbuf("Ot", [128, 192], F32)
    qs_ = P.sbuf("qs_", [128, 192], F32)
    Ktok = P.sbuf("Ktok", [128, 96], BF16)
    oftok = P.sbuf("oftok", [128, 192], BF16)
    gcol = P.sbuf("gcol", [128, 1], F32)
    dkcol = P.sbuf("dkcol", [128, 1], F32)
    grep = P.sbuf("grep", [96, 1], F32)
    ecr = P.sbuf("ecr", [1, 1], F32)
    ocol = P.sbuf("ocol", [128, 1], F32)
    sqO = P.sbuf("sqO", [128, 192], F32)

    pX = P.psum("pX", [128, 128])
    pY = P.psum("pY", [128, 192])
    pO2 = P.psum("pO2", [128, 192])
    pS2 = P.psum("pS2", [96, 192])
    pTb = P.psum("pTb", [128, 128], BF16)
    pG1 = P.psum("pG1", [96, 1])
    pW = P.psum("pW", [128, TO])

    def evac(dst, src, eng="vector"):
        P.copy(dst, src, eng=eng)

    for s in range(NS):
        t0 = s * TO
        for hh, sb_, nf, pp in ((q_hb, qb, 8, 96), (k_hb, kb, 8, 96),
                                (v_hb, vb, 16, 96), (g_hb, gb, 16, 96)):
            P.dma(sb_[:].rearrange("p (f t) -> p f t", t=TO),
                  hh.rearrange("(f p) t -> p f t", p=96)[:, :, t0:t0 + TO])
        P.dma(xn1b[:].rearrange("p (f t) -> p f t", t=TO), fm_ap(xn1_h, 0, 8, t0, TO))
        P.dma(bet[:], beta_h[:, t0:t0 + TO])
        P.dma(cums[:], cum_h[:, t0:t0 + TO])
        for ci in range(TO // C):
            co = ci * C
            P.transpose(pX[:, 0:8], cums[:, co:co + C], ident[0:8, 0:8])
            P.copy(cbT[:, 0:8], pX[:, 0:8])
            P.transpose(pX[:, 0:8], bet[:, co:co + C], ident[0:8, 0:8])
            P.copy(cbT[:, 8:16], pX[:, 0:8])
            for h in range(8):
                Ktc = kb[0:96, h * TO + co: h * TO + co + C]
                Qtc = qb[0:96, h * TO + co: h * TO + co + C]
                ccol = cbT[:, h:h + 1]
                bcol = cbT[:, 8 + h:9 + h]
                P.mm(pX[:], Ktc, Ktc); evac(kkS[:], pX[:])
                P.mm(pX[:], Ktc, Qtc); evac(kqS[:], pX[:])
                P.mm(pX[:], sel8[:, h * 128:(h + 1) * 128], cums[:, co:co + C])
                evac(crS[:], pX[:])
                P.ts(rS[:], crS[:], ccol, ALU.subtract)
                P.act(DN[:], rS[:], AF.Relu)
                P.act(DN[:], DN[:], AF.Exp, scale=-1.0)
                P.act(DT[:], rS[:], AF.Relu, scale=-1.0)
                P.act(DT[:], DT[:], AF.Exp, scale=-1.0)
                P.mm(pX[:], sel8[:, h * 128:(h + 1) * 128], bet[:, co:co + C])
                evac(brS[:], pX[:])
                P.tt(Nc[:], DN[:], kkS[:], ALU.mult)
                P.ts(Nc[:], Nc[:], bcol, ALU.mult)
                P.tt(Nc[:], Nc[:], mlo[:], ALU.mult)
                P.tt(NTc[:], DT[:], kkS[:], ALU.mult)
                P.tt(NTc[:], NTc[:], brS[:], ALU.mult)
                P.tt(NTc[:], NTc[:], mup[:], ALU.mult)
                P.tt(Pm[:], ident[:], NTc[:], ALU.subtract)
                cN, cNT, nN, nNT = Nc, NTc, Nn, NTn
                for lev in range(6):
                    P.mm(pX[:], cNT[:], cN[:]); evac(nN[:], pX[:])
                    P.mm(pX[:], cN[:], cNT[:]); evac(nNT[:], pX[:])
                    cN, nN = nN, cN
                    cNT, nNT = nNT, cNT
                    P.mm(pX[:], cN[:], Pm[:])
                    P.tt(Pm[:], Pm[:], pX[:], ALU.add)
                P.copy(Sb[:, h * 192:(h + 1) * 192], Sf[:, h * 192:(h + 1) * 192])
                P.mm(pY[:], Ktc, Sb[:, h * 192:(h + 1) * 192])
                for sub in range(2):
                    P.transpose(pTb[0:128, 0:96],
                                vb[0:96, (2 * h + sub) * TO + co: (2 * h + sub) * TO + co + C],
                                identb[0:96, 0:96])
                    P.copy(Vtok[:, sub * 96:(sub + 1) * 96], pTb[0:128, 0:96])
                P.act(gcol[:], ccol, AF.Exp)
                P.ts(gKS0[:], pY[:], gcol[:, 0:1], ALU.mult)
                P.tt(RHS[:], Vtok[:], gKS0[:], ALU.subtract)
                P.ts(RHS[:], RHS[:], bcol, ALU.mult)
                P.mm(pY[:], Pm[:], RHS[:])
                evac(Wf[:], pY[:]); P.copy(Wb_[:], Wf[:])
                P.tt(rS[:], DT[:], kqS[:], ALU.mult)
                P.tt(rS[:], rS[:], mui[:], ALU.mult)
                P.copy(MTb[:], rS[:])
                P.mm(pY[:], MTb[:], Wb_[:])
                P.mm(pO2[:], Qtc, Sb[:, h * 192:(h + 1) * 192])
                P.ts(qs_[:], pO2[:], gcol[:, 0:1], ALU.mult)
                P.tt(Ot[:], pY[:], qs_[:], ALU.add)
                P.transpose(pTb[0:128, 0:96], Ktc, identb[0:96, 0:96])
                P.copy(Ktok[:], pTb[0:128, 0:96])
                P.ts(dkcol[:], crS[:, 127:128], ccol, ALU.subtract)
                P.act(dkcol[:], dkcol[:], AF.Exp)
                P.ts(Ktok[:], Ktok[:], dkcol[:, 0:1], ALU.mult)
                P.mm(pS2[:], Ktok[:], Wb_[:])
                P.act(ecr[:], crS[0:1, 127:128], AF.Exp)
                P.mm(pG1[:], ones1[0:1, 0:96], ecr[:])
                evac(grep[:], pG1[:])
                P.stt(Sf[:, h * 192:(h + 1) * 192], Sf[:, h * 192:(h + 1) * 192],
                      grep[:, 0:1], pS2[:], ALU.mult, ALU.add)
                P.act(sqO[:], Ot[:], AF.Square)
                P.red(ocol[:], sqO[:], ALU.add)
                P.ts(ocol[:], ocol[:], 1.0 / 192, ALU.mult)
                P.act(ocol[:], ocol[:], AF.Ln, bias=eps192[:, 0:1])
                P.act(ocol[:], ocol[:], AF.Exp, scale=-0.5)
                P.ts(Ot[:], Ot[:], ocol[:, 0:1], ALU.mult)
                P.tt(Ot[:], Ot[:], onwrep[:], ALU.mult)
                for sub in range(2):
                    P.transpose(pTb[0:128, 0:96],
                                gb[0:96, (2 * h + sub) * TO + co: (2 * h + sub) * TO + co + C],
                                identb[0:96, 0:96])
                    P.copy(Gtok[:, sub * 96:(sub + 1) * 96], pTb[0:128, 0:96])
                P.act(Gtok[:], Gtok[:], AF.Silu)
                P.tt(oftok[:], Ot[:], Gtok[:], ALU.mult)
                for sub in range(2):
                    P.transpose(pTb[0:96, 0:128], oftok[:, sub * 96:(sub + 1) * 96],
                                identb[:])
                    P.copy(ofm[0:96, (2 * h + sub) * TO + co:
                           (2 * h + sub) * TO + co + C], pTb[0:96, 0:128])
        for m in range(8):
            for k in range(16):
                P.mm(pW[:], wo[:, k * D + m * 128: k * D + (m + 1) * 128],
                     ofm[:, k * TO:(k + 1) * TO], start=(k == 0), stop=(k == 15))
            P.copy(xn1f[:], xn1b[:, m * TO:(m + 1) * TO])
            P.tt(xmid[:, m * TO:(m + 1) * TO], pW[:], xn1f[:], ALU.add)
        P.dma(fm_ap(xmid_h, 0, 8, t0, TO), xmid[:].rearrange("p (f t) -> p f t", t=TO))
    P.end_phase()
    return P




def build_C2(P=None):
    P = P if P is not None else Prog("C2")
    P.begin_phase()
    TCH = 512
    HALF = T // 2
    xmid_h = P.dram_in("xmid", (D, T))
    x1_h = P.dram_in("x1", (D, T))
    n2_h = P.dram_in("norm2_w", (D,))
    wg_h = P.dram_in("Wgate", (D, 4096))
    wu_h = P.dram_in("Wup", (D, 4096))
    wd_h = P.dram_in("Wdown", (4096, D))
    t2_h = P.dram_out("t2", (D, T))
    h2_h = P.dram_out("h2aS", (D, T), BF16)

    n2c = P.sbuf("n2c", [128, 8], F32)
    P.dma_slow(n2c[:], n2_h.rearrange("(f p) -> p f", p=128))
    ones128 = P.sbuf("ones128", [128, 1], F32)
    P.memset(ones128[:], 1.0)
    ones1 = P.sbuf("ones1", [1, 128], F32)
    P.memset(ones1[:], 1.0)
    epsr = P.sbuf("epsr", [1, 1], F32)
    P.memset(epsr[:], 1e-6)
    xmt = P.sbuf("xmt", [128, 8 * TCH])
    sq = P.sbuf("sq", [128, TCH])
    rowb = P.sbuf("rowb", [1, TCH], F32)
    rrep = P.sbuf("rrep", [128, TCH], F32)
    h2b = P.sbuf("h2b", [128, 8 * TCH], BF16)
    pS = P.psum("pS", [1, TCH])
    pM = P.psum("pM", [128, TCH])

    for c in range(T // TCH):
        t0 = c * TCH
        P.dma(xmt[:].rearrange("p (f t) -> p f t", t=TCH), fm_ap(xmid_h, 0, 8, t0, TCH))
        fm_norm(P, xmt, 8, TCH, sq, rowb, pS, ones128, epsr[0:1, 0:1], rowb[0:1, 0:TCH])
        P.mm(pM[:], ones1[:], rowb[0:1, :], start=True, stop=True)
        P.copy(rrep[:], pM[:])
        for k in range(8):
            P.tt(sq[:], xmt[:, k * TCH:(k + 1) * TCH], rrep[:], ALU.mult)
            P.ts(h2b[:, k * TCH:(k + 1) * TCH], sq[:], n2c[:, k:k + 1], ALU.mult)
        P.dma(fm_ap(h2_h, 0, 8, t0, TCH), h2b[:].rearrange("p (f t) -> p f t", t=TCH))

    wz = P.sbuf("wz", [128, 24576], BF16)
    GA, UA, DA = 0, 8192, 16384
    acc = P.sbuf("acc", [128, 8 * HALF], F32)
    hub = P.sbuf("hub", [128, 8 * TCH], BF16)
    silb = P.sbuf("silb", [128, TCH], F32)
    pG = P.psum("pG", [128, TCH]); pU = P.psum("pU", [128, TCH])
    pD1 = P.psum("pD1", [128, TCH]); pD2 = P.psum("pD2", [128, TCH])
    wg_r = wg_h.rearrange("(f p) m -> p f m", p=128)
    wu_r = wu_h.rearrange("(f p) m -> p f m", p=128)
    wd_r = wd_h.rearrange("(f p) m -> p f m", p=128)
    t1b = P.sbuf("t1b", [128, 8 * 256], F32)
    t1c = P.sbuf("t1c", [128, 8 * 256], F32)

    for h_ in range(2):
        for grp in range(4):
            P.dma(wz[:, GA:GA + 8192].rearrange("p (f m) -> p f m", m=1024),
                  wg_r[:, :, grp * 1024:(grp + 1) * 1024], queue="gpsimd")
            P.dma(wz[:, UA:UA + 8192].rearrange("p (f m) -> p f m", m=1024),
                  wu_r[:, :, grp * 1024:(grp + 1) * 1024], queue="gpsimd")
            P.dma(wz[:, DA:DA + 8192].rearrange("p (f m) -> p f m", m=1024),
                  wd_r[:, grp * 8:(grp + 1) * 8, :], queue="gpsimd")
            for sub in range(4):
                t0 = h_ * HALF + sub * TCH
                P.dma(h2b[:].rearrange("p (f t) -> p f t", t=TCH),
                      fm_ap(h2_h, 0, 8, t0, TCH))
                for mt in range(8):
                    for k in range(8):
                        P.mm(pG[:], wz[:, GA + k * 1024 + mt * 128: GA + k * 1024 + (mt + 1) * 128],
                             h2b[:, k * TCH:(k + 1) * TCH], start=(k == 0), stop=(k == 7))
                    for k in range(8):
                        P.mm(pU[:], wz[:, UA + k * 1024 + mt * 128: UA + k * 1024 + (mt + 1) * 128],
                             h2b[:, k * TCH:(k + 1) * TCH], start=(k == 0), stop=(k == 7))
                    P.act(silb[:], pG[:], AF.Silu)
                    P.tt(hub[:, mt * TCH:(mt + 1) * TCH], silb[:], pU[:], ALU.mult)
                for m in range(8):
                    pd = pD1 if m % 2 == 0 else pD2
                    for k in range(8):
                        P.mm(pd[:], wz[:, DA + k * 1024 + m * 128: DA + k * 1024 + (m + 1) * 128],
                             hub[:, k * TCH:(k + 1) * TCH], start=(k == 0), stop=(k == 7))
                    ao = m * HALF + sub * TCH
                    if grp == 0:
                        P.copy(acc[:, ao:ao + TCH], pd[:])
                    else:
                        P.tt(acc[:, ao:ao + TCH], acc[:, ao:ao + TCH], pd[:], ALU.add)
        for sub in range(8):
            t0 = h_ * HALF + sub * 256
            P.dma(t1b[:].rearrange("p (f t) -> p f t", t=256), fm_ap(x1_h, 0, 8, t0, 256))
            P.dma(t1c[:].rearrange("p (f t) -> p f t", t=256), fm_ap(xmid_h, 0, 8, t0, 256))
            for m in range(8):
                ao = m * HALF + sub * 256
                P.tt(t1b[:, m * 256:(m + 1) * 256], t1b[:, m * 256:(m + 1) * 256],
                     t1c[:, m * 256:(m + 1) * 256], ALU.add)
                P.tt(t1b[:, m * 256:(m + 1) * 256], t1b[:, m * 256:(m + 1) * 256],
                     acc[:, ao:ao + 256], ALU.add)
            P.dma(fm_ap(t2_h, 0, 8, t0, 256), t1b[:].rearrange("p (f t) -> p f t", t=256))
    P.end_phase()
    return P



def build_D1(P=None):
    P = P if P is not None else Prog("D1")
    P.begin_phase()
    TO = 512
    NS = T // TO
    xn1_h = P.dram_in("xn1", (D, T), BF16)
    anw_h = P.dram_in("attn_norm_w", (D,))
    fnw_h = P.dram_in("ffn_norm_w", (D,))
    qkvw_h = P.dram_in("qkv_wP", (D, 1536))
    qkb_h = P.dram_in("qkb64", (64, 20))
    vbb_h = P.dram_in("vb128", (128, 2))
    pw_h = P.dram_in("proj_w", (D, D))
    pb_h = P.dram_in("proj_b", (D,))
    sink_h = P.dram_in("attn_sink", (16,))
    gw_h = P.dram_in("gate_w32", (D, 32))
    gb_h = P.dram_in("gate_b32", (32,))
    cs_h = P.dram_in("cosS", (64, T))
    sn_h = P.dram_in("sinS", (64, T))
    ident_h = P.dram_in("ident", (128, 128))
    am_h = P.dram_in("amasks", (2, 128, 256))
    xp_h = P.dram_out("xpS", (D, T))
    h2f_h = P.dram_out("h2fS", (D, T), BF16)
    glg_h = P.dram_out("glogS", (T, 32))

    ident = P.sbuf("ident", [128, 128], F32)
    P.dma(ident[:], ident_h[:])
    identb = P.sbuf("identb", [128, 128], BF16)
    P.dma(identb[:], ident_h[:], queue="gpsimd")
    am0 = P.sbuf("am0", [128, 256], F32)
    amr = P.sbuf("amr", [128, 256], F32)
    P.dma(am0[:], am_h[0]); P.dma(amr[:], am_h[1])
    anc = P.sbuf("anc", [128, 8], F32)
    P.dma_slow(anc[:], anw_h.rearrange("(f p) -> p f", p=128))
    fnc = P.sbuf("fnc", [128, 8], F32)
    P.dma_slow(fnc[:], fnw_h.rearrange("(f p) -> p f", p=128))
    qkvw = P.sbuf("qkvw", [128, 8 * 1536], BF16)
    P.dma(qkvw[:].rearrange("p (f m) -> p f m", m=1536),
          qkvw_h.rearrange("(f p) m -> p f m", p=128), queue="gpsimd")
    qkb = P.sbuf("qkb", [64, 20], F32)
    P.dma(qkb[:], qkb_h[:])
    vbb = P.sbuf("vbb", [128, 2], F32)
    P.dma(vbb[:], vbb_h[:])
    pw = P.sbuf("pw", [64, 16 * D], BF16)
    P.dma(pw[:].rearrange("p (f m) -> p f m", m=D),
          pw_h.rearrange("(f p) m -> p f m", p=64), queue="gpsimd")
    pbc = P.sbuf("pbc", [128, 8], F32)
    P.dma_slow(pbc[:], pb_h.rearrange("(f p) -> p f", p=128))
    gwc = P.sbuf("gwc", [128, 8 * 32], F32)
    P.dma(gwc[:].rearrange("p (f m) -> p f m", m=32),
          gw_h.rearrange("(f p) m -> p f m", p=128))
    gbc = P.sbuf("gbc", [32, 1], F32)
    P.dma(gbc[:], gb_h.unsqueeze(1))
    cosb = P.sbuf("cosb", [64, T], F32)
    sinb = P.sbuf("sinb", [64, T], F32)
    P.dma(cosb[:], cs_h[:]); P.dma(sinb[:], sn_h[:])
    ones128 = P.sbuf("ones128", [128, 1], F32)
    P.memset(ones128[:], 1.0)
    ones1 = P.sbuf("ones1", [1, 128], F32)
    P.memset(ones1[:], 1.0)
    epsr = P.sbuf("epsr", [1, 1], F32)
    P.memset(epsr[:], 1e-5)

    xn1b = P.sbuf("xn1b", [128, 8 * TO], BF16)
    xn1f = P.sbuf("xn1f", [128, TO], F32)
    hb = P.sbuf("hb", [128, 8 * TO], BF16)
    sq = P.sbuf("sq", [128, TO])
    rowb = P.sbuf("rowb", [1, TO], F32)
    rrep = P.sbuf("rrep", [128, TO], F32)
    qb = P.sbuf("qb", [64, 16 * TO], BF16)
    kb = P.sbuf("kb", [64, 4 * (128 + TO)], BF16)
    vtok = P.sbuf("vtok", [128, 4 * 5 * 64], BF16)
    rtmp1 = P.sbuf("rtmp1", [32, TO], F32)
    rtmp2 = P.sbuf("rtmp2", [32, TO], F32)
    lgs = P.sbuf("lgs", [128, 256], F32)
    pex = P.sbuf("pex", [128, 256], F32)
    mcol = P.sbuf("mcol", [128, 1], F32)
    nmcol = P.sbuf("nmcol", [128, 1], F32)
    scol = P.sbuf("scol", [128, 1], F32)
    dcol = P.sbuf("dcol", [128, 1], F32)
    attT = P.sbuf("attT", [128, 256], BF16)
    atb = P.sbuf("atb", [128, 256], BF16)
    yb = P.sbuf("yb", [64, 16 * TO], BF16)
    xpb = P.sbuf("xpb", [128, 8 * TO], F32)
    h2fb = P.sbuf("h2fb", [128, 8 * TO], BF16)
    glt = P.sbuf("glt", [32, TO], F32)
    glo = P.sbuf("glo", [128, 4 * 32], F32)
    sinkr = P.sbuf("sinkr", [128, 16], F32)

    pQ = P.psum("pQ", [64, TO])
    pV = P.psum("pV", [128, TO])
    pL = P.psum("pL", [128, 256])
    pTb = P.psum("pTb", [128, 128], BF16)
    pT32 = P.psum("pT32", [128, 128])
    pY = P.psum("pY", [64, 128])
    pS = P.psum("pS", [1, TO])
    pGl = P.psum("pGl", [32, TO])

    P.dma(sinkr[0:1, :], sink_h.unsqueeze(0))
    P.mm(pV[:, 0:16], ones1[:], sinkr[0:1, :])
    P.copy(sinkr[:], pV[:, 0:16])
    P.memset(kb[:], 0.0)
    P.memset(vtok[:], 0.0)

    def rope_pair(dst_even, dst_odd, src64, t0):
        P.tt(rtmp1[:], src64[0:32, :], cosb[0:32, t0:t0 + TO], ALU.mult)
        P.tt(rtmp2[:], src64[32:64, :], sinb[32:64, t0:t0 + TO], ALU.mult)
        P.tt(rtmp1[:], rtmp1[:], rtmp2[:], ALU.subtract)
        P.tt(rtmp2[:], src64[0:32, :], sinb[0:32, t0:t0 + TO], ALU.mult)
        P.copy(dst_even, rtmp1[:])
        P.tt(rtmp1[:], src64[32:64, :], cosb[32:64, t0:t0 + TO], ALU.mult)
        P.tt(rtmp1[:], rtmp1[:], rtmp2[:], ALU.add)
        P.copy(dst_odd, rtmp1[:])

    for s in range(NS):
        t0 = s * TO
        P.dma(xn1b[:].rearrange("p (f t) -> p f t", t=TO), fm_ap(xn1_h, 0, 8, t0, TO))
        for k in range(8):
            P.copy(sq[:], xn1b[:, k * TO:(k + 1) * TO])
            P.act(sq[:], sq[:], AF.Square)
            P.mm(pS[0:1, :], ones128[:, 0:1], sq[:], start=(k == 0), stop=(k == 7))
        P.ts(rowb[0:1, :], pS[0:1, :], 1.0 / D, ALU.mult)
        P.act(rowb[0:1, :], rowb[0:1, :], AF.Ln, bias=epsr[0:1, 0:1])
        P.act(rowb[0:1, :], rowb[0:1, :], AF.Exp, scale=-0.5)
        P.mm(pV[:], ones1[:], rowb[0:1, :], start=True, stop=True)
        P.copy(rrep[:], pV[:])
        for k in range(8):
            P.copy(sq[:], xn1b[:, k * TO:(k + 1) * TO])
            P.tt(sq[:], sq[:], rrep[:], ALU.mult)
            P.ts(hb[:, k * TO:(k + 1) * TO], sq[:], anc[:, k:k + 1], ALU.mult)
        for kv in range(4):
            P.copy(kb[:, kv * 640:kv * 640 + 128],
                   kb[:, kv * 640 + TO:kv * 640 + TO + 128])
            P.copy(vtok[:, kv * 320:kv * 320 + 64],
                   vtok[:, kv * 320 + 256:kv * 320 + 320])
        for hh in range(16):
            for k in range(8):
                P.mm(pQ[:], qkvw[:, k * 1536 + hh * 64: k * 1536 + (hh + 1) * 64],
                     hb[:, k * TO:(k + 1) * TO], start=(k == 0), stop=(k == 7))
            bcol = qkb[:, hh:hh + 1]
            P.ts(sq[0:64, :], pQ[:], bcol, ALU.add)
            rope_pair(qb[0:32, hh * TO:(hh + 1) * TO],
                      qb[32:64, hh * TO:(hh + 1) * TO], sq, t0)
        for kv in range(4):
            for k in range(8):
                P.mm(pQ[:], qkvw[:, k * 1536 + 1024 + kv * 64: k * 1536 + 1024 + (kv + 1) * 64],
                     hb[:, k * TO:(k + 1) * TO], start=(k == 0), stop=(k == 7))
            bcol = qkb[:, 16 + kv:17 + kv]
            P.ts(sq[0:64, :], pQ[:], bcol, ALU.add)
            ko = kv * 640 + 128
            rope_pair(kb[0:32, ko:ko + TO], kb[32:64, ko:ko + TO], sq, t0)
        for vt in range(2):
            for k in range(8):
                P.mm(pV[:], qkvw[:, k * 1536 + 1280 + vt * 128: k * 1536 + 1280 + (vt + 1) * 128],
                     hb[:, k * TO:(k + 1) * TO], start=(k == 0), stop=(k == 7))
            bcol = vbb[:, vt:vt + 1]
            P.ts(sq[:], pV[:], bcol, ALU.add)
            for tt_ in range(4):
                P.transpose(pT32[:], sq[:, tt_ * 128:(tt_ + 1) * 128], ident[:])
                P.copy(vtok[:, (2 * vt) * 320 + (tt_ + 1) * 64:
                       (2 * vt) * 320 + (tt_ + 2) * 64], pT32[:, 0:64])
                P.copy(vtok[:, (2 * vt + 1) * 320 + (tt_ + 1) * 64:
                       (2 * vt + 1) * 320 + (tt_ + 2) * 64], pT32[:, 64:128])
        for qt in range(4):
            for qh in range(16):
                kv = qh // 4
                P.mm(pL[:], qb[0:64, qh * TO + qt * 128: qh * TO + qt * 128 + 128],
                     kb[0:64, kv * 640 + qt * 128: kv * 640 + qt * 128 + 256])
                msk = am0 if (s == 0 and qt == 0) else amr
                P.stt(lgs[:], pL[:], 0.125, msk[:], ALU.mult, ALU.add)
                P.red(mcol[:], lgs[:], ALU.max)
                P.tt(mcol[:], mcol[:], sinkr[:, qh:qh + 1], ALU.max)
                P.ts(nmcol[:], mcol[:], -1.0, ALU.mult)
                P.act(pex[:], lgs[:], AF.Exp, bias=nmcol[:, 0:1])
                P.red(dcol[:], pex[:], ALU.add)
                P.tt(scol[:], sinkr[:, qh:qh + 1], mcol[:], ALU.subtract)
                P.act(scol[:], scol[:], AF.Exp)
                P.tt(dcol[:], dcol[:], scol[:], ALU.add)
                P.recip(scol[:], dcol[:])
                P.ts(atb[:], pex[:], scol[:, 0:1], ALU.mult)
                for piece in range(2):
                    P.transpose(pTb[:], atb[:, piece * 128:(piece + 1) * 128], identb[:])
                    P.copy(attT[:, piece * 128:(piece + 1) * 128], pTb[:])
                for piece in range(2):
                    P.mm(pY[:], vtok[:, kv * 320 + (qt + piece) * 64:
                                     kv * 320 + (qt + piece + 1) * 64],
                         attT[:, piece * 128:(piece + 1) * 128],
                         start=(piece == 0), stop=(piece == 1))
                P.copy(yb[0:64, qh * TO + qt * 128: qh * TO + qt * 128 + 128], pY[:])
        for m in range(8):
            for k in range(16):
                P.mm(pV[:], pw[0:64, k * D + m * 128: k * D + (m + 1) * 128],
                     yb[0:64, k * TO:(k + 1) * TO], start=(k == 0), stop=(k == 15))
            P.ts(sq[:], pV[:], pbc[:, m:m + 1], ALU.add)
            P.copy(xn1f[:], xn1b[:, m * TO:(m + 1) * TO])
            P.tt(xpb[:, m * TO:(m + 1) * TO], sq[:], xn1f[:], ALU.add)
        P.dma(fm_ap(xp_h, 0, 8, t0, TO), xpb[:].rearrange("p (f t) -> p f t", t=TO))
        for k in range(8):
            P.act(sq[:], xpb[:, k * TO:(k + 1) * TO], AF.Square)
            P.mm(pS[0:1, :], ones128[:, 0:1], sq[:], start=(k == 0), stop=(k == 7))
        P.ts(rowb[0:1, :], pS[0:1, :], 1.0 / D, ALU.mult)
        P.act(rowb[0:1, :], rowb[0:1, :], AF.Ln, bias=epsr[0:1, 0:1])
        P.act(rowb[0:1, :], rowb[0:1, :], AF.Exp, scale=-0.5)
        P.mm(pV[:], ones1[:], rowb[0:1, :], start=True, stop=True)
        P.copy(rrep[:], pV[:])
        for k in range(8):
            P.tt(sq[:], xpb[:, k * TO:(k + 1) * TO], rrep[:], ALU.mult)
            P.ts(h2fb[:, k * TO:(k + 1) * TO], sq[:], fnc[:, k:k + 1], ALU.mult)
        P.dma(fm_ap(h2f_h, 0, 8, t0, TO), h2fb[:].rearrange("p (f t) -> p f t", t=TO))
        for k in range(8):
            P.tt(sq[:], xpb[:, k * TO:(k + 1) * TO], rrep[:], ALU.mult)
            P.ts(sq[:], sq[:], fnc[:, k:k + 1], ALU.mult)
            P.mm(pGl[:], gwc[:, k * 32:(k + 1) * 32], sq[:],
                 start=(k == 0), stop=(k == 7))
        P.ts(glt[:], pGl[:], gbc[:, 0:1], ALU.add)
        for tt_ in range(4):
            P.transpose(pT32[:, 0:32], glt[:, tt_ * 128:(tt_ + 1) * 128],
                        ident[0:32, 0:32])
            P.copy(glo[:, tt_ * 32:(tt_ + 1) * 32], pT32[:, 0:32])
            P.dma(glg_h[t0 + tt_ * 128: t0 + (tt_ + 1) * 128],
                  glo[:, tt_ * 32:(tt_ + 1) * 32])
    P.end_phase()
    return P




def build_D2(P=None):
    P = P if P is not None else Prog("D2")
    P.begin_phase()
    TCH = 512
    HALF = T // 2
    h2_h = P.dram_in("h2fS", (D, T), BF16)
    glg_h = P.dram_in("glogS", (T, 32))
    xp_h = P.dram_in("xpS", (D, T))
    t2_h = P.dram_in("t2", (D, T))
    wu_h = P.dram_in("W_up", (32, D, 2048))
    wd_h = P.dram_in("W_down", (32, D, D))
    sel_h = P.dram_in("sel32", (32, 32 * 128))
    ident_h = P.dram_in("ident", (128, 128))
    out_h = P.dram_out("out", (T, D))

    ident = P.sbuf("ident", [128, 128], F32)
    P.dma(ident[:], ident_h[:])
    selb = P.sbuf("selb", [32, 32 * 128], BF16)
    P.dma(selb[:], sel_h[:], queue="gpsimd")

    lgt = P.sbuf("lgt", [128, 32], F32)
    msk = P.sbuf("msk", [128, 32], F32)
    eq = P.sbuf("eq", [128, 32], F32)
    cum = P.sbuf("cum", [128, 32], F32)
    zer = P.sbuf("zer", [128, 32], F32)
    first = P.sbuf("first", [128, 32], F32)
    mm_ = P.sbuf("mm_", [128, 1], F32)
    m0 = P.sbuf("m0", [128, 1], F32)
    vks = P.sbuf("vks", [128, 4], F32)
    fks = P.sbuf("fks", [128, 4 * 32], F32)
    den = P.sbuf("den", [128, 1], F32)
    comb = P.sbuf("comb", [128, 32], F32)
    combT = P.sbuf("combT", [32, T], BF16)
    pR = P.psum("pR", [128, 128])
    P.memset(zer[:], 0.0)
    scr = (eq, cum, zer, first, mm_)
    for tt_ in range(T // 128):
        P.dma(lgt[:], glg_h[tt_ * 128:(tt_ + 1) * 128])
        P.copy(msk[:], lgt[:])
        for kk in range(4):
            f, m = emit_first_argmax(P, msk[:], scr)
            P.copy(fks[:, kk * 32:(kk + 1) * 32], f[:])
            P.copy(vks[:, kk:kk + 1], m[:])
            if kk == 0:
                P.copy(m0[:], m[:])
            P.stt(msk[:], f[:], -1e30, msk[:], ALU.mult, ALU.add)
        P.ts(vks[:], vks[:], m0[:, 0:1], ALU.subtract)
        P.act(vks[:], vks[:], AF.Exp)
        P.red(den[:], vks[:], ALU.add)
        P.recip(den[:], den[:])
        P.memset(comb[:], 0.0)
        for kk in range(4):
            P.ts(mm_[:], vks[:, kk:kk + 1], den[:, 0:1], ALU.mult)
            P.stt(comb[:], fks[:, kk * 32:(kk + 1) * 32], mm_[:, 0:1], comb[:],
                  ALU.mult, ALU.add)
        P.transpose(pR[0:32, :], comb[:], ident[:])
        P.copy(combT[:, tt_ * 128:(tt_ + 1) * 128], pR[0:32, 0:128])

    wz = P.sbuf("wz", [128, 24576], BF16)
    UA, DA = 0, 16384
    acc = P.sbuf("acc", [128, 8 * HALF], F32)
    h2b = [P.sbuf("h2b", [128, 8 * TCH], BF16) for _ in range(2)]
    hub = [P.sbuf("hub", [128, 8 * TCH], BF16) for _ in range(2)]
    silb = [P.sbuf("silb", [128, TCH], F32) for _ in range(2)]
    crep = P.sbuf("crep", [128, TCH], F32)
    pU = [P.psum("pU", [128, TCH]) for _ in range(2)]
    pG = [P.psum("pG", [128, TCH]) for _ in range(2)]
    pD = [P.psum("pD", [128, TCH]) for _ in range(2)]
    pC = P.psum("pC", [128, TCH])
    pT2 = pC
    wu_r = wu_h.rearrange("e (f p) m -> e p f m", p=128)
    wd_r = wd_h.rearrange("e (f p) m -> e p f m", p=128)
    fin = P.sbuf("fin", [128, 8 * TCH], F32)
    fin2 = P.sbuf("fin2", [128, 8 * TCH], F32)

    for h_ in range(2):
        tWfree = 0
        tH = [0, 0]
        tHfree = [0, 0]
        tPU = [0, 0]
        tPG = [0, 0]
        tPD = [0, 0]
        tCfree = 0
        for e_ in range(32):
            twf = tWfree if tWfree != 0 else 0
            tWa = P.dma(wz[:, UA:UA + 16384].rearrange("p (f m) -> p f m", m=2048),
                        wu_r[e_], queue="gpsimd", at=twf)
            tWb = P.dma(wz[:, DA:DA + 8192].rearrange("p (f m) -> p f m", m=1024),
                        wd_r[e_], queue="gpsimd", at=twf)
            tW = [tWa, tWb]
            lastg = 0
            for sub in range(4):
                t0 = h_ * HALF + sub * TCH
                hb = sub % 2
                tH[hb] = P.dma(h2b[hb][:].rearrange("p (f t) -> p f t", t=TCH),
                               fm_ap(h2_h, 0, 8, t0, TCH),
                               at=[tHfree[hb]] if tHfree[hb] != 0 else 0)
                tHuAll = []
                for mt in range(8):
                    pp = mt % 2
                    P.mm(pU[pp][:], wz[:, UA + mt * 128: UA + (mt + 1) * 128],
                         h2b[hb][:, 0:TCH], start=True, stop=False,
                         at=tW + [tH[hb], tPU[pp]])
                    for k in range(1, 8):
                        P.mm(pU[pp][:], wz[:, UA + k * 2048 + mt * 128:
                                           UA + k * 2048 + (mt + 1) * 128],
                             h2b[hb][:, k * TCH:(k + 1) * TCH],
                             start=False, stop=(k == 7), at=0)
                    tMMu = P.last
                    P.mm(pG[pp][:], wz[:, UA + 1024 + mt * 128: UA + 1024 + (mt + 1) * 128],
                         h2b[hb][:, 0:TCH], start=True, stop=False,
                         at=[tPG[pp]] if tPG[pp] != 0 else 0)
                    for k in range(1, 8):
                        P.mm(pG[pp][:], wz[:, UA + k * 2048 + 1024 + mt * 128:
                                           UA + k * 2048 + 1024 + (mt + 1) * 128],
                             h2b[hb][:, k * TCH:(k + 1) * TCH],
                             start=False, stop=(k == 7), at=0)
                    tMMg = P.last
                    lastg = tMMg
                    tSil = P.act(silb[pp][:], pG[pp][:], AF.Silu, at=[tMMg])
                    tHu = P.tt(hub[hb][:, mt * TCH:(mt + 1) * TCH], silb[pp][:],
                               pU[pp][:], ALU.mult, at=[tSil, tMMu])
                    tPU[pp] = tHu
                    tPG[pp] = tSil
                    tHuAll.append(tHu)
                tHfree[hb] = lastg
                tC = P.mm(pC[:], selb[:, e_ * 128:(e_ + 1) * 128],
                          combT[:, t0:t0 + TCH],
                          at=[tCfree] if tCfree != 0 else 0)
                tCs = P.copy(crep[:], pC[:], at=[tC])
                tSc = 0
                for mt in range(8):
                    tSc = P.tt(hub[hb][:, mt * TCH:(mt + 1) * TCH],
                               hub[hb][:, mt * TCH:(mt + 1) * TCH], crep[:],
                               ALU.mult, eng="gpsimd", at=[tCs] + tHuAll)
                tCfree = tSc
                for m in range(8):
                    pp = m % 2
                    P.mm(pD[pp][:], wz[:, DA + m * 128: DA + (m + 1) * 128],
                         hub[hb][:, 0:TCH], start=True, stop=False,
                         at=[tSc, tPD[pp]] if tPD[pp] != 0 else [tSc])
                    for k in range(1, 8):
                        P.mm(pD[pp][:], wz[:, DA + k * 1024 + m * 128:
                                           DA + k * 1024 + (m + 1) * 128],
                             hub[hb][:, k * TCH:(k + 1) * TCH],
                             start=False, stop=(k == 7), at=0)
                    tDn = P.last
                    ao = m * HALF + sub * TCH
                    if e_ == 0:
                        tAdd = P.copy(acc[:, ao:ao + TCH], pD[pp][:], at=[tDn])
                    else:
                        tAdd = P.tt(acc[:, ao:ao + TCH], acc[:, ao:ao + TCH],
                                    pD[pp][:], ALU.add, at=[tDn])
                    tPD[pp] = tAdd
                tWfree = [P.last]
        for sub in range(4):
            t0 = h_ * HALF + sub * TCH
            P.dma(fin[:].rearrange("p (f t) -> p f t", t=TCH), fm_ap(t2_h, 0, 8, t0, TCH))
            P.dma(fin2[:].rearrange("p (f t) -> p f t", t=TCH), fm_ap(xp_h, 0, 8, t0, TCH))
            for m in range(8):
                ao = m * HALF + sub * TCH
                P.tt(fin[:, m * TCH:(m + 1) * TCH], fin[:, m * TCH:(m + 1) * TCH],
                     fin2[:, m * TCH:(m + 1) * TCH], ALU.add)
                P.tt(fin[:, m * TCH:(m + 1) * TCH], fin[:, m * TCH:(m + 1) * TCH],
                     acc[:, ao:ao + TCH], ALU.add)
            for tt_ in range(4):
                for m in range(8):
                    P.transpose(pT2[:, 0:128],
                                fin[:, m * TCH + tt_ * 128: m * TCH + tt_ * 128 + 128],
                                ident[:])
                    P.copy(fin2[:, m * 128:(m + 1) * 128], pT2[:, 0:128])
                P.dma(out_h[t0 + tt_ * 128: t0 + (tt_ + 1) * 128],
                      fin2[:, 0:1024])
    P.end_phase()
    return P



_cache = {}


def _bf(x):
    return np.ascontiguousarray(x)


def run_launch(key, builder, in_maps):
    if key not in _cache:
        P = builder()
        _cache[key] = P.build()
    nc = _cache[key]
    res = run_bass_kernel_spmd(nc, in_maps, core_ids=list(range(len(in_maps))))
    return res.results


IDENT = np.eye(128, dtype=np.float32)
SEL32 = np.zeros((32, 32 * 128), dtype=np.float32)
for _s in range(32):
    SEL32[_s, _s * 128:(_s + 1) * 128] = 1.0
_MLO = np.tril(np.ones((128, 128), np.float32), -1)
MASKS = np.stack([_MLO, _MLO.T, np.triu(np.ones((128, 128), np.float32))])

def _amasks():
    NEG = np.float32(-3e38)
    qi = np.arange(128)[:, None]
    kj = np.arange(256)[None, :]
    valid = (kj >= qi) & (kj <= qi + 128)
    amr = np.where(valid, np.float32(0), NEG).astype(np.float32)
    am0 = amr.copy()
    am0[:, :128] = NEG
    return np.stack([am0, amr])

AMASKS = _amasks()

def _rope_tables():
    DH = 64
    pos = np.arange(T, dtype=np.float64) / 32.0
    inv = 1.0 / (150000.0 ** (np.arange(0, DH, 2, dtype=np.float64) / DH))
    fr = pos[None, :] * inv[:, None]
    c = np.cos(fr).astype(np.float32)
    s_ = np.sin(fr).astype(np.float32)
    return np.vstack([c, c]), np.vstack([s_, s_])

COSS, SINS = _rope_tables()

def _perm_qkv(qkv_w, qkv_b):
    perm = np.arange(1536)
    def headperm(base, nh):
        for h in range(nh):
            o = base + h * 64
            perm[o:o + 64] = np.concatenate([np.arange(o, o + 64, 2),
                                             np.arange(o + 1, o + 64, 2)])
    headperm(0, 16)
    headperm(1024, 4)
    wp = np.ascontiguousarray(qkv_w[:, perm])
    bpm = qkv_b[perm]
    qkb = np.ascontiguousarray(bpm[:1280].reshape(20, 64).T)
    vbb = np.ascontiguousarray(bpm[1280:].reshape(2, 128).T)
    return wp, qkb, vbb


_BUILDERS = [build_A1, build_A2, build_B, build_C1a, build_C1b, build_C2,
             build_D1, build_D2]


def build_fused():
    P = Prog("fused", fused=True)
    for b in _BUILDERS:
        b(P)
    return P


import os as _os
FUSED = _os.environ.get("MK_UNFUSED", "") == ""


def _fused_maps(ins, x, b, extra):
    m = dict(x=x[b], ident=IDENT, sel32=SEL32, masks=MASKS, amasks=AMASKS,
             cosS=COSS, sinS=SINS)
    m.update(extra)
    return m


def kernel(**ins):
    x = ins["L0_attn_x"].astype(np.float32)
    B = x.shape[0]
    bcw = np.concatenate([ins["L0_attn_B_w"], ins["L0_attn_C_w"]], axis=1)
    rwT = np.ascontiguousarray(ins["L0_ffn_router_w"].T)
    convw = np.ascontiguousarray(ins["L0_attn_conv_w"][:, 0, :])
    qkvwP, qkbP, vbbP = _perm_qkv(ins["L1_ffn_qkv_w"], ins["L1_ffn_qkv_b"])

    if FUSED:
        common = dict(
            in_proj_w=ins["L0_attn_in_proj_w"], conv_w=convw,
            dt_w=ins["L0_attn_dt_w"], dt_b=ins["L0_attn_dt_b"], bc_w=bcw,
            router_wT=rwT, norm_w=ins["L0_attn_norm_w"],
            A_log=ins["L0_attn_A_log"], D_skip=ins["L0_attn_D_skip"],
            out_proj_w=ins["L0_attn_out_proj_w"], bias=ins["L0_ffn_bias"],
            gate_w=ins["L0_ffn_gate_w"], up_w=ins["L0_ffn_up_w"],
            down_w=ins["L0_ffn_down_w"], sh_gate_w=ins["L0_ffn_sh_gate_w"],
            sh_up_w=ins["L0_ffn_sh_up_w"], sh_down_w=ins["L0_ffn_sh_down_w"],
            Wq=ins["L1_attn_Wq"], Wk=ins["L1_attn_Wk"], Wv=ins["L1_attn_Wv"],
            Wg=ins["L1_attn_Wg"], Wb=ins["L1_attn_Wb"], Wa=ins["L1_attn_Wa"],
            A_log1=ins["L1_attn_A_log"], dt_bias=ins["L1_attn_dt_bias"], conv_q_w=ins["L1_attn_conv_q_w"],
            conv_k_w=ins["L1_attn_conv_k_w"], conv_v_w=ins["L1_attn_conv_v_w"],
            norm1_w=ins["L1_attn_norm1_w"], onorm_w=ins["L1_attn_onorm_w"],
            Wo=ins["L1_attn_Wo"], norm2_w=ins["L1_attn_norm2_w"],
            Wgate=ins["L1_attn_Wgate"], Wup=ins["L1_attn_Wup"],
            Wdown=ins["L1_attn_Wdown"], attn_norm_w=ins["L1_ffn_attn_norm_w"],
            ffn_norm_w=ins["L1_ffn_ffn_norm_w"], qkv_wP=qkvwP, qkb64=qkbP,
            vb128=vbbP, proj_w=ins["L1_ffn_proj_w"], proj_b=ins["L1_ffn_proj_b"],
            attn_sink=ins["L1_ffn_attn_sink"], gate_wD=None,
            W_up=ins["L1_ffn_W_up"], W_down=ins["L1_ffn_W_down"])
        del common["gate_wD"]
        maps = []
        for b in range(B):
            m = _fused_maps(ins, x, b, common)
            maps.append(m)
        for m in maps:
            m["gate_w32"] = ins["L1_ffn_gate_w"]
            m["gate_b32"] = ins["L1_ffn_gate_b"]
        res = run_launch("FUSED", build_fused, maps)
        out = np.stack([np.asarray(r["out"]) for r in res]).astype(np.float32)
        return out

    mapsA1 = [dict(x=x[b], in_proj_w=ins["L0_attn_in_proj_w"], conv_w=convw,
                   dt_w=ins["L0_attn_dt_w"], dt_b=ins["L0_attn_dt_b"],
                   bc_w=bcw, router_wT=rwT, norm_w=ins["L0_attn_norm_w"],
                   ident=IDENT) for b in range(B)]
    rA1 = run_launch("A1", build_A1, mapsA1)
    mapsA2 = [dict(x=x[b], dtS=rA1[b]["dtS"], uS=rA1[b]["uS"], xcS=rA1[b]["xcS"],
                   bcS=rA1[b]["bcS"], sel32=SEL32, A_log=ins["L0_attn_A_log"],
                   D_skip=ins["L0_attn_D_skip"], out_proj_w=ins["L0_attn_out_proj_w"],
                   ident=IDENT) for b in range(B)]
    rA2 = run_launch("A2", build_A2, mapsA2)
    mapsB = [dict(xn=rA1[b]["xn"], t1=rA2[b]["t1"], logits0=rA1[b]["logits0"],
                  bias=ins["L0_ffn_bias"], gate_w=ins["L0_ffn_gate_w"],
                  up_w=ins["L0_ffn_up_w"], down_w=ins["L0_ffn_down_w"],
                  sh_gate_w=ins["L0_ffn_sh_gate_w"], sh_up_w=ins["L0_ffn_sh_up_w"],
                  sh_down_w=ins["L0_ffn_sh_down_w"], sel32=SEL32, ident=IDENT)
             for b in range(B)]
    rB = run_launch("B", build_B, mapsB)
    mapsC1a = [dict(x1=rB[b]["x1"], Wq=ins["L1_attn_Wq"], Wk=ins["L1_attn_Wk"],
                    Wv=ins["L1_attn_Wv"], Wg=ins["L1_attn_Wg"], Wb=ins["L1_attn_Wb"],
                    Wa=ins["L1_attn_Wa"], A_log1=ins["L1_attn_A_log"],
                    dt_bias=ins["L1_attn_dt_bias"], conv_q_w=ins["L1_attn_conv_q_w"],
                    conv_k_w=ins["L1_attn_conv_k_w"], conv_v_w=ins["L1_attn_conv_v_w"],
                    norm1_w=ins["L1_attn_norm1_w"]) for b in range(B)]
    rC1a = run_launch("C1a", build_C1a, mapsC1a)
    mapsC1b = [dict(qS=rC1a[b]["qS"], kS=rC1a[b]["kS"], vS=rC1a[b]["vS"],
                    gS=rC1a[b]["gS"], betaS=rC1a[b]["betaS"], cumS=rC1a[b]["cumS"],
                    xn1=rC1a[b]["xn1"], onorm_w=ins["L1_attn_onorm_w"],
                    Wo=ins["L1_attn_Wo"], sel32=SEL32, ident=IDENT, masks=MASKS)
               for b in range(B)]
    rC1b = run_launch("C1b", build_C1b, mapsC1b)
    mapsC2 = [dict(xmid=rC1b[b]["xmid"], x1=rB[b]["x1"], norm2_w=ins["L1_attn_norm2_w"],
                   Wgate=ins["L1_attn_Wgate"], Wup=ins["L1_attn_Wup"],
                   Wdown=ins["L1_attn_Wdown"]) for b in range(B)]
    rC2 = run_launch("C2", build_C2, mapsC2)
    mapsD1 = [dict(xn1=rC1a[b]["xn1"], attn_norm_w=ins["L1_ffn_attn_norm_w"],
                   ffn_norm_w=ins["L1_ffn_ffn_norm_w"], qkv_wP=qkvwP, qkb64=qkbP, vb128=vbbP,
                   proj_w=ins["L1_ffn_proj_w"], proj_b=ins["L1_ffn_proj_b"],
                   attn_sink=ins["L1_ffn_attn_sink"], gate_w32=ins["L1_ffn_gate_w"],
                   gate_b32=ins["L1_ffn_gate_b"], cosS=COSS, sinS=SINS, ident=IDENT,
                   amasks=AMASKS) for b in range(B)]
    rD1 = run_launch("D1", build_D1, mapsD1)
    mapsD2 = [dict(h2fS=rD1[b]["h2fS"], glogS=rD1[b]["glogS"], xpS=rD1[b]["xpS"],
                   t2=rC2[b]["t2"], W_up=ins["L1_ffn_W_up"], W_down=ins["L1_ffn_W_down"],
                   sel32=SEL32, ident=IDENT) for b in range(B)]
    rD2 = run_launch("D2", build_D2, mapsD2)
    out = np.stack([np.asarray(rD2[b]["out"]) for b in range(B)]).astype(np.float32)
    return out


# version 11
# speedup vs baseline: 1.5621x; 1.0483x over previous
import numpy as np
import ml_dtypes
import concourse.bass as bass
import concourse.mybir as mybir
from concourse.bass_utils import run_bass_kernel_spmd
from contextlib import ExitStack
from concourse.bass_types import AP as _AP


def _a(x):
    return x if isinstance(x, _AP) or isinstance(x, (int, float)) or x is None else x[:]

F32 = mybir.dt.float32
BF16 = mybir.dt.bfloat16
AF = mybir.ActivationFunctionType
ALU = mybir.AluOpType
AX = mybir.AxisListType
BF = ml_dtypes.bfloat16

T = 4096
D = 1024
NCORES = 8


class Prog:

    ENGINES = ("sync", "scalar", "vector", "tensor", "gpsimd")

    def __init__(self, name, fused=False):
        self.nc = bass.Bass("TRN2", target_bir_lowering=False)
        self.plan = []
        self.es = ExitStack()
        self._names = {}
        self.fused = fused
        self.drams = {}
        self._pes = None
        self._counts = [0, 0, 0, 0, 0]
        self.last = None

    def _uname(self, prefix):
        n = self._names.get(prefix, 0)
        self._names[prefix] = n + 1
        return f"{prefix}{n}"

    def dram_in(self, name, shape, dt=F32):
        if name not in self.drams:
            self.drams[name] = self.nc.dram_tensor(
                name, list(shape), dt, kind="ExternalInput").ap()
        return self.drams[name]

    def dram_out(self, name, shape, dt=F32):
        if name not in self.drams:
            kind = "ExternalOutput" if (not self.fused or name == "out") else "Internal"
            self.drams[name] = self.nc.dram_tensor(name, list(shape), dt, kind=kind).ap()
        return self.drams[name]

    def begin_phase(self):
        self._pes = ExitStack()
        self._bar = self.now()

    def end_phase(self):
        if self._pes is not None:
            self._pes.close()
            self._pes = None

    def sbuf(self, name, shape, dt=F32):
        es = self._pes if self._pes is not None else self.es
        return es.enter_context(self.nc.sbuf_tensor(self._uname(name), list(shape), dt))

    def psum(self, name, shape, dt=F32):
        es = self._pes if self._pes is not None else self.es
        return es.enter_context(self.nc.psum_tensor(self._uname(name), list(shape), dt))

    EIDX = {"sync": 0, "scalar": 1, "vector": 2, "tensor": 3, "gpsimd": 4}

    def now(self):
        return [(i, c) for i, c in enumerate(self._counts) if c > 0]

    def op(self, engine, fn, inc=1, at=None):
        ei = self.EIDX[engine]
        if at is None:
            deps = self.now()
        else:
            if at == 0:
                deps = []
            elif isinstance(at, tuple):
                deps = [at]
            else:
                deps = [t for t in at if t != 0]
            deps = deps + getattr(self, "_bar", [])
        if engine != "tensor" and self._counts[ei] > 0:
            deps = deps + [(ei, self._counts[ei])]
        self.plan.append((engine, fn, inc, deps))
        self._counts[ei] += inc
        self.last = (ei, self._counts[ei])
        return self.last

    def dma(self, out, in_, queue="sync", at=None):
        return self.op(queue, lambda e, o=out, i=in_: e.dma_start(out=o, in_=i), 16, at)

    def dma_slow(self, out, in_, queue="sync"):
        self.op(queue, lambda e, o=out, i=in_: e.dma_start(
            out=o, in_=i, allow_slow_non_contiguous=True), 16)

    def tt(self, out, a, b, alu, eng="vector", at=None):
        return self.op(eng, lambda e, o=_a(out), x=_a(a), y=_a(b), al=alu: e.tensor_tensor(o, x, y, al), 1, at)

    def ts(self, out, a, s1, alu, s2=None, alu2=None, eng="vector", at=None):
        if s2 is None:
            return self.op(eng, lambda e, o=_a(out), x=_a(a), s=_a(s1), al=alu: e.tensor_scalar(o, x, s, None, al), 1, at)
        return self.op(eng, lambda e, o=_a(out), x=_a(a), s=_a(s1), t=_a(s2), al=alu, al2=alu2:
                       e.tensor_scalar(o, x, s, t, al, al2), 1, at)

    def stt(self, out, in0, scalar, in1, op0, op1, eng="vector", at=None):
        return self.op(eng, lambda e, o=_a(out), a=_a(in0), s=_a(scalar), b=_a(in1), p0=op0, p1=op1:
                       e.scalar_tensor_tensor(o, a, s, b, p0, p1), 1, at)

    def copy(self, out, in_, eng="vector", at=None):
        return self.op(eng, lambda e, o=_a(out), i=_a(in_): e.tensor_copy(o, i), 1, at)

    def memset(self, ap, v, eng="vector"):
        self.op(eng, lambda e, a=_a(ap), vv=v: e.memset(a, vv))

    def act(self, out, in_, fn, bias=None, scale=None, eng="scalar", at=None):
        kw = {}
        if bias is not None:
            kw["bias"] = bias
        if scale is not None:
            kw["scale"] = scale
        return self.op(eng, lambda e, o=_a(out), i=_a(in_), f=fn, k={kk: _a(vv) for kk, vv in kw.items()}: e.activation(o, i, f, **k), 1, at)

    def red(self, out, in_, alu, eng="vector"):
        self.op(eng, lambda e, o=_a(out), i=_a(in_), a=alu: e.tensor_reduce(o, i, AX.X, a))

    def scan(self, out, d0, d1, init, op0, op1, eng="vector"):
        self.op(eng, lambda e, o=_a(out), a=_a(d0), b=_a(d1), s=_a(init), p=op0, q=op1:
                e.tensor_tensor_scan(o, a, b, s, p, q))

    def mm(self, out, lhsT, rhs, start=True, stop=True, at=None):
        return self.op("tensor", lambda e, o=_a(out), l=_a(lhsT), r=_a(rhs), s1=start, s2=stop:
                       e.matmul(o, l, r, start=s1, stop=s2), 1, at)

    def transpose(self, out, in_, ident):
        self.op("tensor", lambda e, o=_a(out), i=_a(in_), d=_a(ident): e.transpose(o, i, d))

    def recip(self, out, in_):
        self.op("vector", lambda e, o=_a(out), i=_a(in_): e.reciprocal(o, i))

    def build(self):
        nc = self.nc
        sems = [self.es.enter_context(nc.semaphore(f"s{i}")) for i in range(5)]
        finals = list(self._counts)
        plan = self.plan
        blk = self.es.enter_context(nc.Block())
        EIDX = self.EIDX

        def body_for(engname):
            ei = EIDX[engname]
            def body(eng):
                known = [0, 0, 0, 0, 0]
                for (en, fn, inc, deps) in plan:
                    if en != engname:
                        continue
                    for (di, dv) in deps:
                        if di == ei and en == "tensor":
                            continue
                        if dv > known[di]:
                            eng.wait_ge(sems[di], dv)
                            known[di] = dv
                    ins = fn(eng)
                    ins.then_inc(sems[ei], inc)
                    if engname == "tensor":
                        known[ei] += inc
                if engname == "sync":
                    for i in range(5):
                        if finals[i] > known[i]:
                            eng.wait_ge(sems[i], finals[i])
            return body

        blk.sync(body_for("sync"))
        blk.scalar(body_for("scalar"))
        blk.vector(body_for("vector"))
        blk.tensor(body_for("tensor"))
        blk.gpsimd(body_for("gpsimd"))
        self.es.close()
        return nc


def fm_ap(hbm, ft0, nft, t0, tn, tile_t=None):
    r = hbm.rearrange("(f p) t -> p f t", p=128)
    return r[:, ft0:ft0 + nft, t0:t0 + tn]



def ln_tokmajor(P, sx, xn_out, nf, eps_ap, scratch, also_rms=None):
    sq, s1, s2, m, var = scratch
    P.red(s1, sx, ALU.add)
    P.act(sq, sx, AF.Square)
    P.red(s2, sq, ALU.add)
    P.ts(m, s1, 1.0 / D, ALU.mult)
    P.tt(var, m, m, ALU.mult)
    P.stt(var, s2, 1.0 / D, var, ALU.mult, ALU.subtract)
    P.act(var, var, AF.Ln, bias=eps_ap)
    P.act(var, var, AF.Exp, scale=-0.5)
    P.ts(xn_out, sx, m, ALU.subtract, var, ALU.mult)
    if also_rms is not None:
        out2, eps2_ap = also_rms
        P.act(sq, xn_out, AF.Square)
        P.red(s2, sq, ALU.add)
        P.ts(s2, s2, 1.0 / D, ALU.mult)
        P.act(s2, s2, AF.Ln, bias=eps2_ap)
        P.act(s2, s2, AF.Exp, scale=-0.5)
        P.ts(out2, xn_out, s2, ALU.mult)


def softplus_psum(P, psum, bias_col, out, a1, a2):
    P.act(a1, psum, AF.Abs, bias=bias_col)
    P.act(a1, a1, AF.Exp, scale=-1.0)
    P.act(a1, a1, AF.Ln, bias=1.0)
    P.act(a2, psum, AF.Relu, bias=bias_col)
    P.tt(out, a1, a2, ALU.add)



def build_A1(P=None):
    P = P if P is not None else Prog("A1")
    P.begin_phase()
    TC = 128
    NCH = T // TC
    x_h = P.dram_in("x", (T, D))
    ipw_h = P.dram_in("in_proj_w", (D, 4096))
    convw_h = P.dram_in("conv_w", (2048, 4))
    dtw_h = P.dram_in("dt_w", (2048, 2048))
    dtb_h = P.dram_in("dt_b", (2048,))
    bcw_h = P.dram_in("bc_w", (2048, 32))
    rw_h = P.dram_in("router_wT", (D, 16))
    nw_h = P.dram_in("norm_w", (D,))
    ident_h = P.dram_in("ident", (128, 128))

    xn_h = P.dram_out("xn", (D, T), BF16)
    lg_h = P.dram_out("logits0", (T, 16))
    dt_h = P.dram_out("dtS", (2048, T))
    u_h = P.dram_out("uS", (2048, T))
    xc_h = P.dram_out("xcS", (2048, T))
    bcS_h = P.dram_out("bcS", (32, T))

    ipw = P.sbuf("ipw", [128, 8 * 4096], BF16)
    dtw = P.sbuf("dtw", [128, 16 * 2048], BF16)
    bcw = P.sbuf("bcw", [128, 16 * 32], BF16)
    rw = P.sbuf("rw", [128, 8 * 16], F32)
    cw = P.sbuf("cw", [128, 16 * 4], F32)
    dtb = P.sbuf("dtb", [128, 16], F32)
    nwc = P.sbuf("nwc", [128, 8], F32)
    ident = P.sbuf("ident", [128, 128], F32)
    for k in range(8):
        P.dma(ipw[:, k * 4096:(k + 1) * 4096],
              ipw_h.rearrange("(f p) m -> p f m", p=128)[:, k], queue="gpsimd")
    for k in range(16):
        P.dma(dtw[:, k * 2048:(k + 1) * 2048],
              dtw_h.rearrange("(f p) m -> p f m", p=128)[:, k], queue="gpsimd")
    P.dma(bcw[:].rearrange("p (f m) -> p f m", m=32),
          bcw_h.rearrange("(f p) m -> p f m", p=128), queue="gpsimd")
    P.dma(rw[:].rearrange("p (f m) -> p f m", m=16),
          rw_h.rearrange("(f p) m -> p f m", p=128), queue="sync")
    P.dma(cw[:].rearrange("p (f m) -> p f m", m=4),
          convw_h.rearrange("(f p) m -> p f m", p=128), queue="sync")
    P.dma_slow(dtb[:], dtb_h.rearrange("(f p) -> p f", p=128), queue="sync")
    P.dma_slow(nwc[:], nw_h.rearrange("(f p) -> p f", p=128), queue="sync")
    P.dma(ident[:], ident_h[:], queue="sync")

    sx = P.sbuf("sx", [128, D])
    xn_tok = P.sbuf("xntok", [128, D])
    xr_tok = P.sbuf("xrtok", [128, D])
    sq = P.sbuf("sq", [128, D])
    s1 = P.sbuf("s1", [128, 1]); s2 = P.sbuf("s2", [128, 1])
    mcol = P.sbuf("mcol", [128, 1]); vcol = P.sbuf("vcol", [128, 1])
    epsln = P.sbuf("epsln", [128, 1]); epsrms = P.sbuf("epsrms", [128, 1])
    xnf = P.sbuf("xnf", [128, 8 * TC])
    xnb = P.sbuf("xnb", [128, 8 * TC], BF16)
    xrb = P.sbuf("xrb", [128, 8 * TC], BF16)
    lgs = P.sbuf("lgs", [16, TC], F32)
    lgt = P.sbuf("lgt", [128, 16], F32)
    sil = P.sbuf("sil", [128, TC])
    xi = P.sbuf("xi", [128, 16 * (TC + 3)])
    xc = P.sbuf("xc", [128, 16 * TC])
    xcb = P.sbuf("xcb", [128, 16 * TC], BF16)
    dts = P.sbuf("dts", [128, 16 * TC])
    us = P.sbuf("us", [128, 16 * TC])
    a1s = P.sbuf("a1s", [128, TC]); a2s = P.sbuf("a2s", [128, TC])
    bcs = P.sbuf("bcs", [32, TC], F32)

    pT = P.psum("pT", [128, 512])
    pA = P.psum("pA", [128, TC]); pB = P.psum("pB", [128, TC])
    pC = P.psum("pC", [128, TC]); pD = P.psum("pD", [128, TC])
    pL = P.psum("pL", [16, TC])
    pBC = P.psum("pBC", [32, TC])

    P.memset(epsln[:], 1e-5)
    P.memset(epsrms[:], 1e-6)
    P.memset(xi[:], 0.0)

    scratch = (sq, s1, s2, mcol, vcol)
    x_r = x_h.rearrange("(n p) d -> n p d", p=128)

    for c in range(NCH):
        P.dma(sx[:], x_r[c])
        ln_tokmajor(P, sx[:], xn_tok[:], 8, epsln[:, 0:1],
                    scratch, also_rms=(xr_tok[:], epsrms[:, 0:1]))
        for ft in range(0, 8, 4):
            for j in range(4):
                P.transpose(pT[:, j * 128:(j + 1) * 128],
                            xn_tok[:, (ft + j) * 128:(ft + j + 1) * 128], ident[:])
            for j in range(4):
                P.copy(xnf[:, (ft + j) * TC:(ft + j) * TC + 128],
                       pT[:, j * 128:(j + 1) * 128])
            for j in range(4):
                P.transpose(pT[:, j * 128:(j + 1) * 128],
                            xr_tok[:, (ft + j) * 128:(ft + j + 1) * 128], ident[:])
            for j in range(4):
                P.ts(xrb[:, (ft + j) * TC:(ft + j) * TC + 128],
                     pT[:, j * 128:(j + 1) * 128], nwc[:, ft + j:ft + j + 1], ALU.mult)
        P.copy(xnb[:], xnf[:])
        P.dma(fm_ap(xn_h, 0, 8, c * TC, TC),
              xnb[:].rearrange("p (f t) -> p f t", t=TC))
        for k in range(8):
            P.mm(pL[:], rw[:, k * 16:(k + 1) * 16], xnf[:, k * TC:(k + 1) * TC],
                 start=(k == 0), stop=(k == 7))
        P.copy(lgs[:], pL[:])
        P.transpose(pT[:, 0:16], lgs[:], ident[0:16, 0:16])
        P.copy(lgt[:], pT[:, 0:16])
        P.dma(lg_h[c * 128:(c + 1) * 128], lgt[:])
        for m in range(16):
            for k in range(8):
                P.mm(pA[:], ipw[:, k * 4096 + m * 128: k * 4096 + (m + 1) * 128],
                     xrb[:, k * TC:(k + 1) * TC], start=(k == 0), stop=(k == 7))
            for k in range(8):
                P.mm(pB[:], ipw[:, k * 4096 + (m + 16) * 128: k * 4096 + (m + 17) * 128],
                     xrb[:, k * TC:(k + 1) * TC], start=(k == 0), stop=(k == 7))
            P.act(sil[:], pB[:], AF.Silu)
            o = m * (TC + 3)
            P.copy(xi[:, o:o + 3], xi[:, o + TC:o + TC + 3])
            P.tt(xi[:, o + 3:o + 3 + TC], pA[:], sil[:], ALU.mult)
        for m in range(16):
            o = m * (TC + 3)
            oc = m * TC
            P.ts(xc[:, oc:oc + TC], xi[:, o:o + TC], cw[:, m * 4:m * 4 + 1], ALU.mult)
            for i in range(1, 4):
                P.stt(xc[:, oc:oc + TC], xi[:, o + i:o + i + TC],
                      cw[:, m * 4 + i:m * 4 + i + 1], xc[:, oc:oc + TC],
                      ALU.mult, ALU.add)
            P.copy(xcb[:, oc:oc + TC], xc[:, oc:oc + TC])
        for m in range(16):
            pm = pC if (m % 2 == 0) else pD
            for k in range(16):
                P.mm(pm[:], dtw[:, k * 2048 + m * 128: k * 2048 + (m + 1) * 128],
                     xcb[:, k * TC:(k + 1) * TC], start=(k == 0), stop=(k == 15))
            o = m * TC
            softplus_psum(P, pm[:], dtb[:, m:m + 1], dts[:, o:o + TC], a1s[:], a2s[:])
            P.tt(us[:, o:o + TC], dts[:, o:o + TC], xc[:, o:o + TC], ALU.mult)
        for k in range(16):
            P.mm(pBC[:], bcw[:, k * 32:(k + 1) * 32], xcb[:, k * TC:(k + 1) * TC],
                 start=(k == 0), stop=(k == 15))
        P.copy(bcs[:], pBC[:])
        P.dma(bcS_h[:, c * TC:(c + 1) * TC], bcs[:])
        for (hb, sb_) in ((dt_h, dts), (u_h, us), (xc_h, xc)):
            P.dma(fm_ap(hb, 0, 16, c * TC, TC),
                  sb_[:].rearrange("p (f t) -> p f t", t=TC))
    P.end_phase()
    return P



def build_A2(P=None):
    P = P if P is not None else Prog("A2")
    P.begin_phase()
    TS = 512
    NCH = T // TS
    x_h = P.dram_in("x", (T, D))
    dt_h = P.dram_in("dtS", (2048, T))
    u_h = P.dram_in("uS", (2048, T))
    xc_h = P.dram_in("xcS", (2048, T))
    bc_h = P.dram_in("bcS", (32, T))
    sel_h = P.dram_in("sel32", (32, 32 * 128))
    alog_h = P.dram_in("A_log", (2048, 16))
    dskip_h = P.dram_in("D_skip", (2048,))
    opw_h = P.dram_in("out_proj_w", (2048, D))
    ident_h = P.dram_in("ident", (128, 128))
    t1_h = P.dram_out("t1", (D, T))

    opw = P.sbuf("opw", [128, 16 * D], BF16)
    for k in range(16):
        P.dma(opw[:, k * D:(k + 1) * D],
              opw_h.rearrange("(f p) m -> p f m", p=128)[:, k], queue="gpsimd")
    expA = P.sbuf("expA", [128, 16 * 16], F32)
    P.dma(expA[:].rearrange("p (f s) -> p f s", s=16),
          alog_h.rearrange("(f p) s -> p f s", p=128))
    P.act(expA[:], expA[:], AF.Exp)
    dsk = P.sbuf("dsk", [128, 16], F32)
    P.dma_slow(dsk[:], dskip_h.rearrange("(f p) -> p f", p=128))
    ident = P.sbuf("ident", [128, 128], F32)
    P.dma(ident[:], ident_h[:])
    sel = P.sbuf("sel", [32, 32 * 128], F32)
    P.dma(sel[:], sel_h[:])

    hstate = P.sbuf("hstate", [128, 256], F32)
    P.memset(hstate[:], 0.0)

    HT = 8
    dts = P.sbuf("dts", [128, HT * TS])
    us = P.sbuf("us", [128, HT * TS])
    xcs = P.sbuf("xcs", [128, HT * TS])
    yacc = P.sbuf("yacc", [128, HT * TS])
    ys = P.sbuf("ys", [128, 16 * TS], BF16)
    bcs = P.sbuf("bcs", [32, TS], F32)
    bmr = P.sbuf("bmr", [128, TS]); cmr = P.sbuf("cmr", [128, TS])
    tmp = P.sbuf("tmp", [128, TS]); ab = P.sbuf("ab", [128, TS])
    d1 = P.sbuf("d1", [128, TS]); hh = P.sbuf("hh", [128, TS])
    pr = P.sbuf("pr", [128, TS])
    xT = P.sbuf("xT", [128, 8 * TS])
    sx = P.sbuf("sx", [128, D])
    t1s = P.sbuf("t1s", [128, 8 * TS])

    pb1 = P.psum("pb1", [128, TS])
    pb2 = P.psum("pb2", [128, TS])
    pt1 = [P.psum("pt1", [128, TS]) for _ in range(6)]

    x_r = x_h.rearrange("(n p) d -> n p d", p=128)

    for c in range(NCH):
        P.dma(bcs[:], bc_h[:, c * TS:(c + 1) * TS])
        for tt_ in range(4):
            P.dma(sx[:], x_r[c * 4 + tt_])
            for ft in range(0, 8, 4):
                for j in range(4):
                    P.transpose(pb2[:, j * 128:(j + 1) * 128],
                                sx[:, (ft + j) * 128:(ft + j + 1) * 128], ident[:])
                for j in range(4):
                    P.copy(xT[:, (ft + j) * TS + tt_ * 128:(ft + j) * TS + tt_ * 128 + 128],
                           pb2[:, j * 128:(j + 1) * 128])
        for half in range(2):
            for hb, sb_ in ((dt_h, dts), (u_h, us), (xc_h, xcs)):
                P.dma(sb_[:].rearrange("p (f t) -> p f t", t=TS),
                      fm_ap(hb, half * 8, 8, c * TS, TS))
            for m in range(HT):
                P.memset(yacc[:, m * TS:(m + 1) * TS], 0.0)
            for s in range(16):
                P.mm(pb1[:], sel[:, s * 128:(s + 1) * 128], bcs[:])
                P.copy(bmr[:], pb1[:])
                P.mm(pb1[:], sel[:, (16 + s) * 128:(17 + s) * 128], bcs[:])
                P.copy(cmr[:], pb1[:])
                for m in range(HT):
                    dt_t = dts[:, m * TS:(m + 1) * TS]
                    u_t = us[:, m * TS:(m + 1) * TS]
                    ft = half * 8 + m
                    st_col = hstate[:, s * 16 + ft: s * 16 + ft + 1]
                    P.ts(tmp[:], dt_t, expA[:, ft * 16 + s: ft * 16 + s + 1],
                         ALU.mult, eng="gpsimd")
                    P.act(ab[:], tmp[:], AF.Exp, scale=-1.0)
                    P.tt(d1[:], u_t, bmr[:], ALU.mult)
                    P.scan(hh[:], ab[:], d1[:], st_col, ALU.mult, ALU.add)
                    P.copy(st_col, hh[:, TS - 1:TS])
                    P.tt(pr[:], hh[:], cmr[:], ALU.mult)
                    P.tt(yacc[:, m * TS:(m + 1) * TS], yacc[:, m * TS:(m + 1) * TS],
                         pr[:], ALU.add, eng="gpsimd")
            for m in range(HT):
                ft = half * 8 + m
                P.stt(ys[:, ft * TS:(ft + 1) * TS], xcs[:, m * TS:(m + 1) * TS],
                      dsk[:, ft:ft + 1], yacc[:, m * TS:(m + 1) * TS],
                      ALU.mult, ALU.add)
        for grp, mlist in ((0, range(6)), (1, range(6, 8))):
            for mi, m in enumerate(mlist):
                for k in range(16):
                    P.mm(pt1[mi][:], opw[:, k * D + m * 128: k * D + (m + 1) * 128],
                         ys[:, k * TS:(k + 1) * TS], start=(k == 0), stop=(k == 15))
            for mi, m in enumerate(mlist):
                P.tt(t1s[:, m * TS:(m + 1) * TS], pt1[mi][:],
                     xT[:, m * TS:(m + 1) * TS], ALU.add)
        P.dma(fm_ap(t1_h, 0, 8, c * TS, TS),
              t1s[:].rearrange("p (f t) -> p f t", t=TS))
    P.end_phase()
    return P




def emit_first_argmax(P, work, scr):
    eq, cum, zer, first, m = scr
    P.red(m, work, ALU.max)
    P.ts(eq, work, m[:, 0:1], ALU.is_equal)
    P.scan(cum, eq, zer, 0.0, ALU.add, ALU.add)
    P.ts(cum, cum, 1.0, ALU.is_equal)
    P.tt(first, eq, cum, ALU.mult)
    return first, m


def build_B(P=None):
    P = P if P is not None else Prog("B")
    P.begin_phase()
    TCH = 512
    HALF = T // 2
    xn_h = P.dram_in("xn", (D, T), BF16)
    t1_h = P.dram_in("t1", (D, T))
    lg_h = P.dram_in("logits0", (T, 16))
    bias_h = P.dram_in("bias", (16,))
    gw_h = P.dram_in("gate_w", (16, D, D))
    uw_h = P.dram_in("up_w", (16, D, D))
    dw_h = P.dram_in("down_w", (16, D, D))
    shg_h = P.dram_in("sh_gate_w", (D, 2048))
    shu_h = P.dram_in("sh_up_w", (D, 2048))
    shd_h = P.dram_in("sh_down_w", (2048, D))
    sel_h = P.dram_in("sel32", (32, 32 * 128))
    ident_h = P.dram_in("ident", (128, 128))
    x1_h = P.dram_out("x1", (D, T))

    ident = P.sbuf("ident", [128, 128], F32)
    P.dma(ident[:], ident_h[:])
    selb = P.sbuf("selb", [32, 16 * 128], BF16)
    P.dma(selb[:], sel_h[0:32, 0:16 * 128], queue="gpsimd")

    lgt = P.sbuf("lgt", [128, 16], F32)
    sco = P.sbuf("sco", [128, 16], F32)
    sb_ = P.sbuf("sb_", [128, 16], F32)
    gsc = P.sbuf("gsc", [128, 4], F32)
    gmask = P.sbuf("gmask", [128, 4], F32)
    emask = P.sbuf("emask", [128, 16], F32)
    msk = P.sbuf("msk", [128, 16], F32)
    comb = P.sbuf("comb", [128, 16], F32)
    wsum = P.sbuf("wsum", [128, 1], F32)
    twc = P.sbuf("twc", [128, 1], F32)
    eq = P.sbuf("eq", [128, 16], F32)
    cum = P.sbuf("cum", [128, 16], F32)
    zer = P.sbuf("zer", [128, 16], F32)
    first = P.sbuf("first", [128, 16], F32)
    mm_ = P.sbuf("mm_", [128, 1], F32)
    brep = P.sbuf("brep", [128, 16], F32)
    combT = P.sbuf("combT", [16, T], BF16)
    pR = P.psum("pR", [128, 128])

    P.memset(zer[:], 0.0)
    ones1 = P.sbuf("ones1", [1, 128], F32)
    P.memset(ones1[:], 1.0)
    P.dma(brep[0:1, :], bias_h.unsqueeze(0))
    P.mm(pR[:, 0:16], ones1[:], brep[0:1, :])
    P.copy(brep[:], pR[:, 0:16])

    for tt_ in range(T // 128):
        P.dma(lgt[:], lg_h[tt_ * 128:(tt_ + 1) * 128])
        P.act(sco[:], lgt[:], AF.Sigmoid)
        P.tt(sb_[:], sco[:], brep[:], ALU.add)
        scr4 = (eq[:, 0:4], cum[:, 0:4], zer[:, 0:4], first[:, 0:4], mm_)
        for g in range(4):
            sg = sb_[:, g * 4:(g + 1) * 4]
            P.copy(msk[:, 0:4], sg)
            f, m = emit_first_argmax(P, msk[:, 0:4], scr4)
            P.copy(gsc[:, g:g + 1], m)
            P.stt(msk[:, 0:4], f, -1e30, msk[:, 0:4], ALU.mult, ALU.add)
            P.red(mm_, msk[:, 0:4], ALU.max)
            P.tt(gsc[:, g:g + 1], gsc[:, g:g + 1], mm_, ALU.add)
        P.memset(gmask[:], 0.0)
        P.copy(msk[:, 0:4], gsc[:])
        for _ in range(2):
            f, m = emit_first_argmax(P, msk[:, 0:4], scr4)
            P.tt(gmask[:], gmask[:], f, ALU.add)
            P.stt(msk[:, 0:4], f, -1e30, msk[:, 0:4], ALU.mult, ALU.add)
        P.copy(emask[:].rearrange("p (g r) -> p g r", r=4),
               gmask[:].unsqueeze(2).broadcast_to([128, 4, 4]))
        P.ts(msk[:], sb_[:], 1.0, ALU.add)
        P.tt(msk[:], msk[:], emask[:], ALU.mult)
        P.ts(msk[:], msk[:], 1.0, ALU.subtract)
        P.memset(comb[:], 0.0)
        P.memset(wsum[:], 0.0)
        scr16 = (eq[:], cum[:], zer[:], first[:], mm_)
        for _ in range(4):
            f, m = emit_first_argmax(P, msk[:], scr16)
            P.tt(eq[:], f, sco[:], ALU.mult)
            P.red(twc, eq[:], ALU.add)
            P.stt(comb[:], f, twc[:, 0:1], comb[:], ALU.mult, ALU.add)
            P.tt(wsum[:], wsum[:], twc, ALU.add)
            P.stt(msk[:], f, -1e30, msk[:], ALU.mult, ALU.add)
        P.ts(wsum[:], wsum[:], 1e-20, ALU.add)
        P.recip(twc, wsum[:])
        P.ts(twc, twc, 2.5, ALU.mult)
        P.ts(comb[:], comb[:], twc[:, 0:1], ALU.mult)
        P.transpose(pR[0:16, :], comb[:], ident[:])
        P.copy(combT[:, tt_ * 128:(tt_ + 1) * 128], pR[0:16, :])

    NK = 8
    wz = P.sbuf("wz", [128, 24576], BF16)
    GATE, UP, DOWN = 0, 8192, 16384
    xnc = [P.sbuf("xnc", [128, 8 * TCH], BF16) for _ in range(2)]
    hub = [P.sbuf("hub", [128, 8 * TCH], BF16) for _ in range(2)]
    silb = [P.sbuf("silb", [128, TCH], F32) for _ in range(2)]
    crep = P.sbuf("crep", [128, TCH], F32)
    acc = P.sbuf("acc", [128, 8 * HALF], F32)
    t1b = P.sbuf("t1b", [128, 8 * 256], F32)
    pG1 = P.psum("pG1", [128, TCH]); pU1 = P.psum("pU1", [128, TCH])
    pG2 = P.psum("pG2", [128, TCH]); pU2 = P.psum("pU2", [128, TCH])
    pC = P.psum("pC", [128, TCH])
    pD1 = P.psum("pD1", [128, TCH]); pD2 = P.psum("pD2", [128, TCH])

    gw_r = gw_h.rearrange("e (f p) m -> e p f m", p=128)
    uw_r = uw_h.rearrange("e (f p) m -> e p f m", p=128)
    dw_r = dw_h.rearrange("e (f p) m -> e p f m", p=128)

    st = dict(tW=0, tH=[0, 0], tHfree=[0, 0], tPU=[0, 0], tPG=[0, 0],
              tPD=[0, 0], tCfree=0, tHuAll=[])

    def expert_mm(e_, h_, sub, hb):
        t0 = h_ * HALF + sub * TCH
        st["tH"][hb] = P.dma(xnc[hb][:].rearrange("p (f t) -> p f t", t=TCH),
                             fm_ap(xn_h, 0, 8, t0, TCH),
                             at=[st["tHfree"][hb]] if st["tHfree"][hb] != 0 else 0)
        tHuAll = []
        lastg = 0
        for mt in range(8):
            pp = mt % 2
            pg, pu = (pG1, pU1) if pp == 0 else (pG2, pU2)
            tw = st["tW"] if st["tW"] != 0 else []
            P.mm(pg[:], wz[:, GATE + mt * 128: GATE + (mt + 1) * 128],
                 xnc[hb][:, 0:TCH], start=True, stop=False,
                 at=tw + [t for t in (st["tH"][hb], st["tPG"][pp]) if t != 0])
            for k in range(1, NK):
                P.mm(pg[:], wz[:, GATE + k * D + mt * 128: GATE + k * D + (mt + 1) * 128],
                     xnc[hb][:, k * TCH:(k + 1) * TCH], start=False,
                     stop=(k == NK - 1), at=0)
            tMMg = P.last
            P.mm(pu[:], wz[:, UP + mt * 128: UP + (mt + 1) * 128],
                 xnc[hb][:, 0:TCH], start=True, stop=False,
                 at=[st["tPU"][pp]] if st["tPU"][pp] != 0 else 0)
            for k in range(1, NK):
                P.mm(pu[:], wz[:, UP + k * D + mt * 128: UP + k * D + (mt + 1) * 128],
                     xnc[hb][:, k * TCH:(k + 1) * TCH], start=False,
                     stop=(k == NK - 1), at=0)
            tMMu = P.last
            lastg = tMMu
            tSil = P.act(silb[pp][:], pg[:], AF.Silu, at=[tMMg])
            tHu = P.tt(hub[hb][:, mt * TCH:(mt + 1) * TCH], silb[pp][:], pu[:],
                       ALU.mult, at=[tSil, tMMu])
            st["tPG"][pp] = tSil
            st["tPU"][pp] = tHu
            tHuAll.append(tHu)
        st["tHfree"][hb] = lastg
        tC = P.mm(pC[:], selb[0:16, e_ * 128:(e_ + 1) * 128], combT[:, t0:t0 + TCH],
                  at=[st["tCfree"]] if st["tCfree"] != 0 else 0)
        tCs = P.copy(crep[:], pC[:], at=[tC])
        tSc = 0
        for mt in range(8):
            tSc = P.tt(hub[hb][:, mt * TCH:(mt + 1) * TCH],
                       hub[hb][:, mt * TCH:(mt + 1) * TCH], crep[:], ALU.mult,
                       eng="gpsimd", at=[tCs] + tHuAll)
        st["tCfree"] = tSc
        for m in range(8):
            pp = m % 2
            pd = pD1 if pp == 0 else pD2
            P.mm(pd[:], wz[:, DOWN + m * 128: DOWN + (m + 1) * 128],
                 hub[hb][:, 0:TCH], start=True, stop=False,
                 at=[tSc] + ([st["tPD"][pp]] if st["tPD"][pp] != 0 else []))
            for k in range(1, 8):
                P.mm(pd[:], wz[:, DOWN + k * D + m * 128: DOWN + k * D + (m + 1) * 128],
                     hub[hb][:, k * TCH:(k + 1) * TCH], start=False, stop=(k == 7), at=0)
            tDn = P.last
            ao = m * (HALF) + sub * TCH
            P.tt(acc[:, ao:ao + TCH], acc[:, ao:ao + TCH], pd[:], ALU.add, at=[tDn])
            st["tPD"][pp] = P.last
        st["tW"] = [P.last]

    def shared_mm(h_, sub, grp):
        t0 = h_ * HALF + sub * TCH
        P.dma(xnc[0][:].rearrange("p (f t) -> p f t", t=TCH), fm_ap(xn_h, 0, 8, t0, TCH))
        for mt in range(8):
            for k in range(8):
                P.mm(pG1[:], wz[:, GATE + k * 1024 + mt * 128: GATE + k * 1024 + (mt + 1) * 128],
                     xnc[0][:, k * TCH:(k + 1) * TCH], start=(k == 0), stop=(k == 7))
            for k in range(8):
                P.mm(pU1[:], wz[:, UP + k * 1024 + mt * 128: UP + k * 1024 + (mt + 1) * 128],
                     xnc[0][:, k * TCH:(k + 1) * TCH], start=(k == 0), stop=(k == 7))
            P.act(silb[0][:], pG1[:], AF.Silu)
            P.tt(hub[0][:, mt * TCH:(mt + 1) * TCH], silb[0][:], pU1[:], ALU.mult)
        for m in range(8):
            pd = pD1 if m % 2 == 0 else pD2
            for k in range(8):
                P.mm(pd[:], wz[:, DOWN + k * 1024 + m * 128: DOWN + k * 1024 + (m + 1) * 128],
                     hub[0][:, k * TCH:(k + 1) * TCH], start=(k == 0), stop=(k == 7))
            ao = m * HALF + sub * TCH
            if grp == 0:
                P.copy(acc[:, ao:ao + TCH], pd[:])
            else:
                P.tt(acc[:, ao:ao + TCH], acc[:, ao:ao + TCH], pd[:], ALU.add)

    shg_r = shg_h.rearrange("(f p) m -> p f m", p=128)
    shu_r = shu_h.rearrange("(f p) m -> p f m", p=128)
    shd_r = shd_h.rearrange("(f p) m -> p f m", p=128)
    for h_ in range(2):
        for grp in range(2):
            P.dma(wz[:, GATE:GATE + 8192].rearrange("p (f m) -> p f m", m=1024),
                  shg_r[:, :, grp * 1024:(grp + 1) * 1024], queue="gpsimd")
            P.dma(wz[:, UP:UP + 8192].rearrange("p (f m) -> p f m", m=1024),
                  shu_r[:, :, grp * 1024:(grp + 1) * 1024], queue="gpsimd")
            P.dma(wz[:, DOWN:DOWN + 8192].rearrange("p (f m) -> p f m", m=1024),
                  shd_r[:, grp * 8:(grp + 1) * 8, :], queue="gpsimd")
            for sub in range(4):
                shared_mm(h_, sub, grp)
        st["tW"] = [P.last]
        st["tHfree"] = [P.last, P.last]
        st["tPG"] = [P.last, P.last]
        st["tPU"] = [P.last, P.last]
        st["tPD"] = [P.last, P.last]
        st["tCfree"] = P.last
        for e_ in range(16):
            twf = st["tW"] if st["tW"] != 0 else 0
            t1_ = P.dma(wz[:, GATE:GATE + 8192].rearrange("p (f m) -> p f m", m=1024),
                        gw_r[e_], queue="gpsimd", at=twf)
            t2_ = P.dma(wz[:, UP:UP + 8192].rearrange("p (f m) -> p f m", m=1024),
                        uw_r[e_], queue="gpsimd", at=0)
            t3_ = P.dma(wz[:, DOWN:DOWN + 8192].rearrange("p (f m) -> p f m", m=1024),
                        dw_r[e_], queue="gpsimd", at=0)
            st["tW"] = [t1_, t2_, t3_]
            for sub in range(4):
                expert_mm(e_, h_, sub, sub % 2)
        for sub in range(8):
            t0 = h_ * HALF + sub * 256
            P.dma(t1b[:].rearrange("p (f t) -> p f t", t=256), fm_ap(t1_h, 0, 8, t0, 256))
            for m in range(8):
                ao = m * HALF + sub * 256
                P.tt(t1b[:, m * 256:(m + 1) * 256], t1b[:, m * 256:(m + 1) * 256],
                     acc[:, ao:ao + 256], ALU.add)
            P.dma(fm_ap(x1_h, 0, 8, t0, 256),
                  t1b[:].rearrange("p (f t) -> p f t", t=256))
    P.end_phase()
    return P




def fm_norm(P, xt, nf, TCH, sq, rowbuf, pS, ones128, epsap, out_rows):
    for k in range(nf):
        P.act(sq[:, 0:TCH], xt[:, k * TCH:(k + 1) * TCH], AF.Square)
        P.mm(pS[0:1, 0:TCH], ones128[:, 0:1], sq[:, 0:TCH],
             start=(k == 0), stop=(k == nf - 1))
    P.ts(rowbuf[0:1, 0:TCH], pS[0:1, 0:TCH], 1.0 / (nf * 128), ALU.mult)
    P.act(rowbuf[0:1, 0:TCH], rowbuf[0:1, 0:TCH], AF.Ln, bias=epsap)
    P.act(out_rows, rowbuf[0:1, 0:TCH], AF.Exp, scale=-0.5)


def build_C1a(P=None):
    P = P if P is not None else Prog("C1a")
    P.begin_phase()
    TO = 512
    NS = T // TO
    x1_h = P.dram_in("x1", (D, T))
    wq_h = P.dram_in("Wq", (D, 768))
    wk_h = P.dram_in("Wk", (D, 768))
    wv_h = P.dram_in("Wv", (D, 1536))
    wg_h = P.dram_in("Wg", (D, 1536))
    wb_h = P.dram_in("Wb", (D, 8))
    wa_h = P.dram_in("Wa", (D, 8))
    alog_h = P.dram_in("A_log1", (8,))
    dtb_h = P.dram_in("dt_bias", (8,))
    cq_h = P.dram_in("conv_q_w", (768, 4))
    ck_h = P.dram_in("conv_k_w", (768, 4))
    cv_h = P.dram_in("conv_v_w", (1536, 4))
    n1_h = P.dram_in("norm1_w", (D,))

    xn1_h = P.dram_out("xn1", (D, T), BF16)
    q_h = P.dram_out("qS", (768, T), BF16)
    k_h = P.dram_out("kS", (768, T), BF16)
    v_h = P.dram_out("vS", (1536, T), BF16)
    g_h = P.dram_out("gS", (1536, T), BF16)
    beta_h = P.dram_out("betaS", (8, T))
    cum_h = P.dram_out("cumS", (8, T))

    wq = P.sbuf("wq", [128, 8 * 768], BF16)
    wk = P.sbuf("wk", [128, 8 * 768], BF16)
    wv = P.sbuf("wv", [128, 8 * 1536], BF16)
    wg = P.sbuf("wg", [128, 8 * 1536], BF16)
    wba = P.sbuf("wba", [128, 8 * 16], BF16)
    for hname, hh, sb_, m in (("q", wq_h, wq, 768), ("k", wk_h, wk, 768),
                              ("v", wv_h, wv, 1536), ("g", wg_h, wg, 1536)):
        P.dma(sb_[:].rearrange("p (f m) -> p f m", m=m),
              hh.rearrange("(f p) m -> p f m", p=128), queue="gpsimd")
    P.dma(wba[:].rearrange("p (f m) -> p f m", m=16)[:, :, 0:8],
          wb_h.rearrange("(f p) m -> p f m", p=128), queue="gpsimd")
    P.dma(wba[:].rearrange("p (f m) -> p f m", m=16)[:, :, 8:16],
          wa_h.rearrange("(f p) m -> p f m", p=128), queue="gpsimd")
    cqw = P.sbuf("cqw", [96, 8 * 4], F32)
    ckw = P.sbuf("ckw", [96, 8 * 4], F32)
    cvw = P.sbuf("cvw", [96, 16 * 4], F32)
    P.dma_slow(cqw[:].rearrange("p (f m) -> p f m", m=4),
               cq_h.rearrange("(f p) m -> p f m", p=96))
    P.dma_slow(ckw[:].rearrange("p (f m) -> p f m", m=4),
               ck_h.rearrange("(f p) m -> p f m", p=96))
    P.dma_slow(cvw[:].rearrange("p (f m) -> p f m", m=4),
               cv_h.rearrange("(f p) m -> p f m", p=96))
    n1c = P.sbuf("n1c", [128, 8], F32)
    P.dma_slow(n1c[:], n1_h.rearrange("(f p) -> p f", p=128))
    nega = P.sbuf("nega", [8, 1], F32)
    P.dma(nega[:], alog_h.unsqueeze(1))
    P.act(nega[:], nega[:], AF.Exp)
    P.ts(nega[:], nega[:], -1.0, ALU.mult)
    dtbc = P.sbuf("dtbc", [8, 1], F32)
    P.dma(dtbc[:], dtb_h.unsqueeze(1))

    ones128 = P.sbuf("ones128", [128, 1], F32)
    P.memset(ones128[:], 1.0)
    ones1 = P.sbuf("ones1", [1, 128], F32)
    P.memset(ones1[:], 1.0)
    epsr = P.sbuf("epsr", [1, 1], F32)
    P.memset(epsr[:], 1e-6)
    eps5 = P.sbuf("eps5", [1, 1], F32)
    P.memset(eps5[:], 1e-5)
    epsl2 = P.sbuf("epsl2", [96, 1], F32)
    P.memset(epsl2[:], 1e-6)

    x1t = P.sbuf("x1t", [128, 8 * TO])
    xn1f = P.sbuf("xn1f", [128, 8 * TO])
    xn1b = P.sbuf("xn1b", [128, 8 * TO], BF16)
    hb = P.sbuf("hb", [128, 8 * TO], BF16)
    sq = P.sbuf("sq", [128, TO])
    rowb = P.sbuf("rowb", [1, TO], F32)
    rrep = P.sbuf("rrep", [128, TO], F32)
    qpad = P.sbuf("qpad", [96, TO + 3], F32)
    qtail = P.sbuf("qtail", [96, 48 * 3], F32)
    cnv = P.sbuf("cnv", [96, TO], F32)
    sq96 = P.sbuf("sq96", [96, TO], F32)
    row96 = P.sbuf("row96", [1, TO], F32)
    rs96 = P.sbuf("rs96", [96, TO], F32)
    qb = P.sbuf("qb", [96, 8 * TO], BF16)
    kb = P.sbuf("kb", [96, 8 * TO], BF16)
    vb = P.sbuf("vb", [96, 16 * TO], BF16)
    gb = P.sbuf("gb", [96, 16 * TO], BF16)
    ba = P.sbuf("ba", [16, TO], F32)
    bet = P.sbuf("bet", [8, TO], F32)
    gdec = P.sbuf("gdec", [8, TO], F32)
    cums = P.sbuf("cums", [8, TO], F32)
    zer8 = P.sbuf("zer8", [8, 128], F32)
    a1t = P.sbuf("a1t", [8, TO], F32)
    a2t = P.sbuf("a2t", [8, TO], F32)

    pS = P.psum("pS", [1, TO])
    pM = P.psum("pM", [128, TO])
    pP = P.psum("pP", [96, TO])
    pB = P.psum("pB", [16, TO])

    P.memset(qtail[:], 0.0)
    P.memset(zer8[:], 0.0)

    def proj_conv(wt, nheads, conv_w, tail_base, out_sb, l2norm, scale, s):
        for hh in range(nheads):
            for k in range(8):
                P.mm(pP[:], wt[:, k * (96 * nheads) + hh * 96:
                               k * (96 * nheads) + (hh + 1) * 96],
                     hb[:, k * TO:(k + 1) * TO], start=(k == 0), stop=(k == 7))
            tb = (tail_base + hh) * 3
            P.copy(qpad[:, 0:3], qtail[:, tb:tb + 3])
            P.copy(qpad[:, 3:3 + TO], pP[:])
            P.copy(qtail[:, tb:tb + 3], qpad[:, TO:TO + 3])
            P.ts(cnv[:], qpad[:, 0:TO], conv_w[:, hh * 4:hh * 4 + 1], ALU.mult)
            for i in range(1, 4):
                P.stt(cnv[:], qpad[:, i:i + TO], conv_w[:, hh * 4 + i:hh * 4 + i + 1],
                      cnv[:], ALU.mult, ALU.add)
            P.act(cnv[:], cnv[:], AF.Silu)
            if l2norm:
                P.act(sq96[:], cnv[:], AF.Square)
                P.mm(pS[0:1, :], ones128[0:96, 0:1], sq96[:], start=True, stop=True)
                P.copy(row96[0:1, :], pS[0:1, :])
                P.ts(row96[0:1, :], row96[0:1, :], epsr[0:1, 0:1], ALU.add)
                P.act(row96[0:1, :], row96[0:1, :], AF.Ln)
                P.act(row96[0:1, :], row96[0:1, :], AF.Exp, scale=-0.5)
                if scale != 1.0:
                    P.ts(row96[0:1, :], row96[0:1, :], scale, ALU.mult)
                P.mm(pP[:], ones1[0:1, 0:96], row96[0:1, :], start=True, stop=True)
                P.copy(rs96[:], pP[:])
                P.tt(out_sb[:, hh * TO:(hh + 1) * TO], cnv[:], rs96[:], ALU.mult)
            else:
                P.copy(out_sb[:, hh * TO:(hh + 1) * TO], cnv[:])

    wq_r = wq
    for s in range(NS):
        t0 = s * TO
        P.dma(x1t[:].rearrange("p (f t) -> p f t", t=TO), fm_ap(x1_h, 0, 8, t0, TO))
        for k in range(8):
            P.mm(pS[0:1, :], ones128[:, 0:1], x1t[:, k * TO:(k + 1) * TO],
                 start=(k == 0), stop=(k == 7))
        P.ts(rowb[0:1, :], pS[0:1, :], 1.0 / D, ALU.mult)
        P.mm(pM[:], ones1[:], rowb[0:1, :], start=True, stop=True)
        P.copy(rrep[:], pM[:])
        for k in range(8):
            P.tt(xn1f[:, k * TO:(k + 1) * TO], x1t[:, k * TO:(k + 1) * TO],
                 rrep[:], ALU.subtract)
        for k in range(8):
            P.act(sq[:], xn1f[:, k * TO:(k + 1) * TO], AF.Square)
            P.mm(pS[0:1, :], ones128[:, 0:1], sq[:], start=(k == 0), stop=(k == 7))
        P.ts(rowb[0:1, :], pS[0:1, :], 1.0 / D, ALU.mult)
        P.act(rowb[0:1, :], rowb[0:1, :], AF.Ln, bias=eps5[0:1, 0:1])
        P.act(rowb[0:1, :], rowb[0:1, :], AF.Exp, scale=-0.5)
        P.mm(pM[:], ones1[:], rowb[0:1, :], start=True, stop=True)
        P.copy(rrep[:], pM[:])
        for k in range(8):
            P.tt(xn1f[:, k * TO:(k + 1) * TO], xn1f[:, k * TO:(k + 1) * TO],
                 rrep[:], ALU.mult)
        P.copy(xn1b[:], xn1f[:])
        P.dma(fm_ap(xn1_h, 0, 8, t0, TO), xn1b[:].rearrange("p (f t) -> p f t", t=TO))
        fm_norm(P, xn1f, 8, TO, sq, rowb, pS, ones128, epsr[0:1, 0:1], rowb[0:1, 0:TO])
        P.mm(pM[:], ones1[:], rowb[0:1, :], start=True, stop=True)
        P.copy(rrep[:], pM[:])
        for k in range(8):
            P.tt(sq[:], xn1f[:, k * TO:(k + 1) * TO], rrep[:], ALU.mult)
            P.ts(hb[:, k * TO:(k + 1) * TO], sq[:], n1c[:, k:k + 1], ALU.mult)
        proj_conv(wq, 8, cqw, 0, qb, True, float(96 ** -0.5), s)
        proj_conv(wk, 8, ckw, 8, kb, True, 1.0, s)
        proj_conv(wv, 16, cvw, 16, vb, False, 1.0, s)
        for hh in range(16):
            for k in range(8):
                P.mm(pP[:], wg[:, k * 1536 + hh * 96: k * 1536 + (hh + 1) * 96],
                     hb[:, k * TO:(k + 1) * TO], start=(k == 0), stop=(k == 7))
            P.copy(gb[:, hh * TO:(hh + 1) * TO], pP[:])
        for k in range(8):
            P.mm(pB[:], wba[:, k * 16:(k + 1) * 16], hb[:, k * TO:(k + 1) * TO],
                 start=(k == 0), stop=(k == 7))
        P.copy(ba[:], pB[:])
        P.act(bet[:], ba[0:8, :], AF.Sigmoid)
        for k in range(8):
            P.mm(pB[0:8, :], wba[:, k * 16 + 8:k * 16 + 16],
                 hb[:, k * TO:(k + 1) * TO], start=(k == 0), stop=(k == 7))
        P.act(a1t[:], pB[0:8, :], AF.Abs, bias=dtbc[:, 0:1])
        P.act(a1t[:], a1t[:], AF.Exp, scale=-1.0)
        P.act(a1t[:], a1t[:], AF.Ln, bias=1.0)
        P.act(a2t[:], pB[0:8, :], AF.Relu, bias=dtbc[:, 0:1])
        P.tt(a1t[:], a1t[:], a2t[:], ALU.add)
        P.ts(gdec[:], a1t[:], nega[:, 0:1], ALU.mult)
        for ci in range(TO // 128):
            P.scan(cums[:, ci * 128:(ci + 1) * 128], gdec[:, ci * 128:(ci + 1) * 128],
                   zer8[:], 0.0, ALU.add, ALU.add)
        P.dma(beta_h[:, t0:t0 + TO], bet[:])
        P.dma(cum_h[:, t0:t0 + TO], cums[:])
        for hh, sb_, nf in ((q_h, qb, 8), (k_h, kb, 8), (v_h, vb, 16), (g_h, gb, 16)):
            P.dma(hh.rearrange("(f p) t -> p f t", p=96)[:, :, t0:t0 + TO],
                  sb_[:].rearrange("p (f t) -> p f t", t=TO))
    P.end_phase()
    return P




def build_C1b(P=None):
    P = P if P is not None else Prog("C1b")
    P.begin_phase()
    TO = 512
    NS = T // TO
    C = 128
    q_hb = P.dram_in("qS", (768, T), BF16)
    k_hb = P.dram_in("kS", (768, T), BF16)
    v_hb = P.dram_in("vS", (1536, T), BF16)
    g_hb = P.dram_in("gS", (1536, T), BF16)
    beta_h = P.dram_in("betaS", (8, T))
    cum_h = P.dram_in("cumS", (8, T))
    xn1_h = P.dram_in("xn1", (D, T), BF16)
    onw_h = P.dram_in("onorm_w", (192,))
    wo_h = P.dram_in("Wo", (1536, D))
    sel_h = P.dram_in("sel32", (32, 32 * 128))
    ident_h = P.dram_in("ident", (128, 128))
    masks_h = P.dram_in("masks", (3, 128, 128))
    xmid_h = P.dram_out("xmid", (D, T))

    ident = P.sbuf("ident", [128, 128], F32)
    P.dma(ident[:], ident_h[:])
    identb = P.sbuf("identb", [128, 128], BF16)
    P.dma(identb[:], ident_h[:], queue="gpsimd")
    mlo = P.sbuf("mlo", [128, 128], F32)
    mup = P.sbuf("mup", [128, 128], F32)
    mui = P.sbuf("mui", [128, 128], F32)
    P.dma(mlo[:], masks_h[0]); P.dma(mup[:], masks_h[1]); P.dma(mui[:], masks_h[2])
    sel8 = P.sbuf("sel8", [8, 8 * 128], F32)
    P.dma(sel8[:], sel_h[0:8, 0:8 * 128])
    ones1 = P.sbuf("ones1", [1, 128], F32)
    P.memset(ones1[:], 1.0)
    eps192 = P.sbuf("eps192", [128, 1], F32)
    P.memset(eps192[:], 1e-6)
    onwrep = P.sbuf("onwrep", [128, 192], F32)
    onwrow = P.sbuf("onwrow", [1, 192], F32)
    P.dma(onwrow[:], onw_h.unsqueeze(0))
    pO = P.psum("pO", [128, 192])
    P.mm(pO[:], ones1[:], onwrow[:])
    P.copy(onwrep[:], pO[:])
    wo = P.sbuf("wo", [96, 16 * D], BF16)
    P.dma(wo[:].rearrange("p (f m) -> p f m", m=D),
          wo_h.rearrange("(f p) m -> p f m", p=96), queue="gpsimd")

    qb = P.sbuf("qb", [96, 8 * TO], BF16)
    kb = P.sbuf("kb", [96, 8 * TO], BF16)
    vb = P.sbuf("vb", [96, 16 * TO], BF16)
    gb = P.sbuf("gb", [96, 16 * TO], BF16)
    xn1b = P.sbuf("xn1b", [128, 8 * TO], BF16)
    xn1f = P.sbuf("xn1f", [128, TO], F32)
    bet = P.sbuf("bet", [8, TO], F32)
    cums = P.sbuf("cums", [8, TO], F32)
    cbT = P.sbuf("cbT", [128, 16], F32)
    Sf = P.sbuf("Sf", [96, 8 * 192], F32)
    Sb = P.sbuf("Sb", [96, 8 * 192], BF16)
    P.memset(Sf[:], 0.0)
    ofm = P.sbuf("ofm", [96, 16 * TO], BF16)
    xmid = P.sbuf("xmid", [128, 8 * TO], F32)

    kkS = P.sbuf("kkS", [128, 128], F32)
    kqS = P.sbuf("kqS", [128, 128], F32)
    crS = P.sbuf("crS", [128, 128], F32)
    rS = P.sbuf("rS", [128, 128], F32)
    DN = P.sbuf("DN", [128, 128], F32)
    DT = P.sbuf("DT", [128, 128], F32)
    brS = P.sbuf("brS", [128, 128], F32)
    Nc = P.sbuf("Nc", [128, 128], F32)
    NTc = P.sbuf("NTc", [128, 128], F32)
    Nn = P.sbuf("Nn", [128, 128], F32)
    NTn = P.sbuf("NTn", [128, 128], F32)
    Pm = P.sbuf("Pm", [128, 128], F32)
    Vtok = P.sbuf("Vtok", [128, 192], F32)
    Gtok = P.sbuf("Gtok", [128, 192], F32)
    gKS0 = P.sbuf("gKS0", [128, 192], F32)
    RHS = P.sbuf("RHS", [128, 192], F32)
    Wf = P.sbuf("Wf", [128, 192], F32)
    Wb_ = P.sbuf("Wb_", [128, 192], BF16)
    MTb = P.sbuf("MTb", [128, 128], BF16)
    Ot = P.sbuf("Ot", [128, 192], F32)
    qs_ = P.sbuf("qs_", [128, 192], F32)
    Ktok = P.sbuf("Ktok", [128, 96], BF16)
    oftok = P.sbuf("oftok", [128, 192], BF16)
    gcol = P.sbuf("gcol", [128, 1], F32)
    dkcol = P.sbuf("dkcol", [128, 1], F32)
    grep = P.sbuf("grep", [96, 1], F32)
    ecr = P.sbuf("ecr", [1, 1], F32)
    ocol = P.sbuf("ocol", [128, 1], F32)
    sqO = P.sbuf("sqO", [128, 192], F32)

    pX = P.psum("pX", [128, 128])
    pY = P.psum("pY", [128, 192])
    pO2 = P.psum("pO2", [128, 192])
    pS2 = P.psum("pS2", [96, 192])
    pTb = P.psum("pTb", [128, 128], BF16)
    pG1 = P.psum("pG1", [96, 1])
    pW = P.psum("pW", [128, TO])

    def evac(dst, src, eng="vector"):
        P.copy(dst, src, eng=eng)

    for s in range(NS):
        t0 = s * TO
        for hh, sb_, nf, pp in ((q_hb, qb, 8, 96), (k_hb, kb, 8, 96),
                                (v_hb, vb, 16, 96), (g_hb, gb, 16, 96)):
            P.dma(sb_[:].rearrange("p (f t) -> p f t", t=TO),
                  hh.rearrange("(f p) t -> p f t", p=96)[:, :, t0:t0 + TO])
        P.dma(xn1b[:].rearrange("p (f t) -> p f t", t=TO), fm_ap(xn1_h, 0, 8, t0, TO))
        P.dma(bet[:], beta_h[:, t0:t0 + TO])
        P.dma(cums[:], cum_h[:, t0:t0 + TO])
        for ci in range(TO // C):
            co = ci * C
            P.transpose(pX[:, 0:8], cums[:, co:co + C], ident[0:8, 0:8])
            P.copy(cbT[:, 0:8], pX[:, 0:8])
            P.transpose(pX[:, 0:8], bet[:, co:co + C], ident[0:8, 0:8])
            P.copy(cbT[:, 8:16], pX[:, 0:8])
            for h in range(8):
                Ktc = kb[0:96, h * TO + co: h * TO + co + C]
                Qtc = qb[0:96, h * TO + co: h * TO + co + C]
                ccol = cbT[:, h:h + 1]
                bcol = cbT[:, 8 + h:9 + h]
                P.mm(pX[:], Ktc, Ktc); evac(kkS[:], pX[:])
                P.mm(pX[:], Ktc, Qtc); evac(kqS[:], pX[:])
                P.mm(pX[:], sel8[:, h * 128:(h + 1) * 128], cums[:, co:co + C])
                evac(crS[:], pX[:])
                P.ts(rS[:], crS[:], ccol, ALU.subtract)
                P.act(DN[:], rS[:], AF.Relu)
                P.act(DN[:], DN[:], AF.Exp, scale=-1.0)
                P.act(DT[:], rS[:], AF.Relu, scale=-1.0)
                P.act(DT[:], DT[:], AF.Exp, scale=-1.0)
                P.mm(pX[:], sel8[:, h * 128:(h + 1) * 128], bet[:, co:co + C])
                evac(brS[:], pX[:])
                P.tt(Nc[:], DN[:], kkS[:], ALU.mult)
                P.ts(Nc[:], Nc[:], bcol, ALU.mult)
                P.tt(Nc[:], Nc[:], mlo[:], ALU.mult)
                P.tt(NTc[:], DT[:], kkS[:], ALU.mult)
                P.tt(NTc[:], NTc[:], brS[:], ALU.mult)
                P.tt(NTc[:], NTc[:], mup[:], ALU.mult)
                P.tt(Pm[:], ident[:], NTc[:], ALU.subtract)
                cN, cNT, nN, nNT = Nc, NTc, Nn, NTn
                for lev in range(6):
                    P.mm(pX[:], cNT[:], cN[:]); evac(nN[:], pX[:])
                    P.mm(pX[:], cN[:], cNT[:]); evac(nNT[:], pX[:])
                    cN, nN = nN, cN
                    cNT, nNT = nNT, cNT
                    P.mm(pX[:], cN[:], Pm[:])
                    P.tt(Pm[:], Pm[:], pX[:], ALU.add)
                P.copy(Sb[:, h * 192:(h + 1) * 192], Sf[:, h * 192:(h + 1) * 192])
                P.mm(pY[:], Ktc, Sb[:, h * 192:(h + 1) * 192])
                for sub in range(2):
                    P.transpose(pTb[0:128, 0:96],
                                vb[0:96, (2 * h + sub) * TO + co: (2 * h + sub) * TO + co + C],
                                identb[0:96, 0:96])
                    P.copy(Vtok[:, sub * 96:(sub + 1) * 96], pTb[0:128, 0:96])
                P.act(gcol[:], ccol, AF.Exp)
                P.ts(gKS0[:], pY[:], gcol[:, 0:1], ALU.mult)
                P.tt(RHS[:], Vtok[:], gKS0[:], ALU.subtract)
                P.ts(RHS[:], RHS[:], bcol, ALU.mult)
                P.mm(pY[:], Pm[:], RHS[:])
                evac(Wf[:], pY[:]); P.copy(Wb_[:], Wf[:])
                P.tt(rS[:], DT[:], kqS[:], ALU.mult)
                P.tt(rS[:], rS[:], mui[:], ALU.mult)
                P.copy(MTb[:], rS[:])
                P.mm(pY[:], MTb[:], Wb_[:])
                P.mm(pO2[:], Qtc, Sb[:, h * 192:(h + 1) * 192])
                P.ts(qs_[:], pO2[:], gcol[:, 0:1], ALU.mult)
                P.tt(Ot[:], pY[:], qs_[:], ALU.add)
                P.transpose(pTb[0:128, 0:96], Ktc, identb[0:96, 0:96])
                P.copy(Ktok[:], pTb[0:128, 0:96])
                P.ts(dkcol[:], crS[:, 127:128], ccol, ALU.subtract)
                P.act(dkcol[:], dkcol[:], AF.Exp)
                P.ts(Ktok[:], Ktok[:], dkcol[:, 0:1], ALU.mult)
                P.mm(pS2[:], Ktok[:], Wb_[:])
                P.act(ecr[:], crS[0:1, 127:128], AF.Exp)
                P.mm(pG1[:], ones1[0:1, 0:96], ecr[:])
                evac(grep[:], pG1[:])
                P.stt(Sf[:, h * 192:(h + 1) * 192], Sf[:, h * 192:(h + 1) * 192],
                      grep[:, 0:1], pS2[:], ALU.mult, ALU.add)
                P.act(sqO[:], Ot[:], AF.Square)
                P.red(ocol[:], sqO[:], ALU.add)
                P.ts(ocol[:], ocol[:], 1.0 / 192, ALU.mult)
                P.act(ocol[:], ocol[:], AF.Ln, bias=eps192[:, 0:1])
                P.act(ocol[:], ocol[:], AF.Exp, scale=-0.5)
                P.ts(Ot[:], Ot[:], ocol[:, 0:1], ALU.mult)
                P.tt(Ot[:], Ot[:], onwrep[:], ALU.mult)
                for sub in range(2):
                    P.transpose(pTb[0:128, 0:96],
                                gb[0:96, (2 * h + sub) * TO + co: (2 * h + sub) * TO + co + C],
                                identb[0:96, 0:96])
                    P.copy(Gtok[:, sub * 96:(sub + 1) * 96], pTb[0:128, 0:96])
                P.act(Gtok[:], Gtok[:], AF.Silu)
                P.tt(oftok[:], Ot[:], Gtok[:], ALU.mult)
                for sub in range(2):
                    P.transpose(pTb[0:96, 0:128], oftok[:, sub * 96:(sub + 1) * 96],
                                identb[:])
                    P.copy(ofm[0:96, (2 * h + sub) * TO + co:
                           (2 * h + sub) * TO + co + C], pTb[0:96, 0:128])
        for m in range(8):
            for k in range(16):
                P.mm(pW[:], wo[:, k * D + m * 128: k * D + (m + 1) * 128],
                     ofm[:, k * TO:(k + 1) * TO], start=(k == 0), stop=(k == 15))
            P.copy(xn1f[:], xn1b[:, m * TO:(m + 1) * TO])
            P.tt(xmid[:, m * TO:(m + 1) * TO], pW[:], xn1f[:], ALU.add)
        P.dma(fm_ap(xmid_h, 0, 8, t0, TO), xmid[:].rearrange("p (f t) -> p f t", t=TO))
    P.end_phase()
    return P




def build_C2(P=None):
    P = P if P is not None else Prog("C2")
    P.begin_phase()
    TCH = 512
    HALF = T // 2
    xmid_h = P.dram_in("xmid", (D, T))
    x1_h = P.dram_in("x1", (D, T))
    n2_h = P.dram_in("norm2_w", (D,))
    wg_h = P.dram_in("Wgate", (D, 4096))
    wu_h = P.dram_in("Wup", (D, 4096))
    wd_h = P.dram_in("Wdown", (4096, D))
    t2_h = P.dram_out("t2", (D, T))
    h2_h = P.dram_out("h2aS", (D, T), BF16)

    n2c = P.sbuf("n2c", [128, 8], F32)
    P.dma_slow(n2c[:], n2_h.rearrange("(f p) -> p f", p=128))
    ones128 = P.sbuf("ones128", [128, 1], F32)
    P.memset(ones128[:], 1.0)
    ones1 = P.sbuf("ones1", [1, 128], F32)
    P.memset(ones1[:], 1.0)
    epsr = P.sbuf("epsr", [1, 1], F32)
    P.memset(epsr[:], 1e-6)
    xmt = P.sbuf("xmt", [128, 8 * TCH])
    sq = P.sbuf("sq", [128, TCH])
    rowb = P.sbuf("rowb", [1, TCH], F32)
    rrep = P.sbuf("rrep", [128, TCH], F32)
    h2b = [P.sbuf("h2b", [128, 8 * TCH], BF16) for _ in range(2)]
    pS = P.psum("pS", [1, TCH])
    pM = P.psum("pM", [128, TCH])

    for c in range(T // TCH):
        t0 = c * TCH
        P.dma(xmt[:].rearrange("p (f t) -> p f t", t=TCH), fm_ap(xmid_h, 0, 8, t0, TCH))
        fm_norm(P, xmt, 8, TCH, sq, rowb, pS, ones128, epsr[0:1, 0:1], rowb[0:1, 0:TCH])
        P.mm(pM[:], ones1[:], rowb[0:1, :], start=True, stop=True)
        P.copy(rrep[:], pM[:])
        for k in range(8):
            P.tt(sq[:], xmt[:, k * TCH:(k + 1) * TCH], rrep[:], ALU.mult)
            P.ts(h2b[0][:, k * TCH:(k + 1) * TCH], sq[:], n2c[:, k:k + 1], ALU.mult)
        P.dma(fm_ap(h2_h, 0, 8, t0, TCH), h2b[0][:].rearrange("p (f t) -> p f t", t=TCH))

    wz = P.sbuf("wz", [128, 24576], BF16)
    GA, UA, DA = 0, 8192, 16384
    acc = P.sbuf("acc", [128, 8 * HALF], F32)
    hub = [P.sbuf("hub", [128, 8 * TCH], BF16) for _ in range(2)]
    silb = [P.sbuf("silb", [128, TCH], F32) for _ in range(2)]
    pG1 = P.psum("pG1", [128, TCH]); pU1 = P.psum("pU1", [128, TCH])
    pG2 = P.psum("pG2", [128, TCH]); pU2 = P.psum("pU2", [128, TCH])
    pD1 = P.psum("pD1", [128, TCH]); pD2 = P.psum("pD2", [128, TCH])
    wg_r = wg_h.rearrange("(f p) m -> p f m", p=128)
    wu_r = wu_h.rearrange("(f p) m -> p f m", p=128)
    wd_r = wd_h.rearrange("(f p) m -> p f m", p=128)
    t1b = P.sbuf("t1b", [128, 8 * 256], F32)
    t1c = P.sbuf("t1c", [128, 8 * 256], F32)

    st = dict(tW=0, tH=[0, 0], tHfree=[0, 0], tPU=[0, 0], tPG=[0, 0],
              tPD=[0, 0])
    for h_ in range(2):
        for grp in range(4):
            twf = st["tW"] if st["tW"] != 0 else 0
            t1_ = P.dma(wz[:, GA:GA + 8192].rearrange("p (f m) -> p f m", m=1024),
                        wg_r[:, :, grp * 1024:(grp + 1) * 1024], queue="gpsimd", at=twf)
            t2_ = P.dma(wz[:, UA:UA + 8192].rearrange("p (f m) -> p f m", m=1024),
                        wu_r[:, :, grp * 1024:(grp + 1) * 1024], queue="gpsimd", at=0)
            t3_ = P.dma(wz[:, DA:DA + 8192].rearrange("p (f m) -> p f m", m=1024),
                        wd_r[:, grp * 8:(grp + 1) * 8, :], queue="gpsimd", at=0)
            tW = [t1_, t2_, t3_]
            for sub in range(4):
                t0 = h_ * HALF + sub * TCH
                hb = sub % 2
                st["tH"][hb] = P.dma(h2b[hb][:].rearrange("p (f t) -> p f t", t=TCH),
                                     fm_ap(h2_h, 0, 8, t0, TCH),
                                     at=[st["tHfree"][hb]] if st["tHfree"][hb] != 0 else 0)
                lastg = 0
                for mt in range(8):
                    pp = mt % 2
                    pg, pu = (pG1, pU1) if pp == 0 else (pG2, pU2)
                    P.mm(pg[:], wz[:, GA + mt * 128: GA + (mt + 1) * 128],
                         h2b[hb][:, 0:TCH], start=True, stop=False,
                         at=tW + [t for t in (st["tH"][hb], st["tPG"][pp]) if t != 0])
                    for k in range(1, 8):
                        P.mm(pg[:], wz[:, GA + k * 1024 + mt * 128: GA + k * 1024 + (mt + 1) * 128],
                             h2b[hb][:, k * TCH:(k + 1) * TCH], start=False, stop=(k == 7), at=0)
                    tMMg = P.last
                    P.mm(pu[:], wz[:, UA + mt * 128: UA + (mt + 1) * 128],
                         h2b[hb][:, 0:TCH], start=True, stop=False,
                         at=[st["tPU"][pp]] if st["tPU"][pp] != 0 else 0)
                    for k in range(1, 8):
                        P.mm(pu[:], wz[:, UA + k * 1024 + mt * 128: UA + k * 1024 + (mt + 1) * 128],
                             h2b[hb][:, k * TCH:(k + 1) * TCH], start=False, stop=(k == 7), at=0)
                    tMMu = P.last
                    lastg = tMMu
                    tSil = P.act(silb[pp][:], pg[:], AF.Silu, at=[tMMg])
                    tHu = P.tt(hub[hb][:, mt * TCH:(mt + 1) * TCH], silb[pp][:], pu[:],
                               ALU.mult, at=[tSil, tMMu])
                    st["tPG"][pp] = tSil
                    st["tPU"][pp] = tHu
                st["tHfree"][hb] = lastg
                for m in range(8):
                    pp = m % 2
                    pd = pD1 if pp == 0 else pD2
                    P.mm(pd[:], wz[:, DA + m * 128: DA + (m + 1) * 128],
                         hub[hb][:, 0:TCH], start=True, stop=False,
                         at=[st["tPU"][0], st["tPU"][1]] +
                            ([st["tPD"][pp]] if st["tPD"][pp] != 0 else []))
                    for k in range(1, 8):
                        P.mm(pd[:], wz[:, DA + k * 1024 + m * 128: DA + k * 1024 + (m + 1) * 128],
                             hub[hb][:, k * TCH:(k + 1) * TCH], start=False, stop=(k == 7), at=0)
                    tDn = P.last
                    ao = m * HALF + sub * TCH
                    if grp == 0:
                        P.copy(acc[:, ao:ao + TCH], pd[:], at=[tDn])
                    else:
                        P.tt(acc[:, ao:ao + TCH], acc[:, ao:ao + TCH], pd[:],
                             ALU.add, at=[tDn])
                    st["tPD"][pp] = P.last
                st["tW"] = [P.last]
        for sub in range(8):
            t0 = h_ * HALF + sub * 256
            P.dma(t1b[:].rearrange("p (f t) -> p f t", t=256), fm_ap(x1_h, 0, 8, t0, 256))
            P.dma(t1c[:].rearrange("p (f t) -> p f t", t=256), fm_ap(xmid_h, 0, 8, t0, 256))
            for m in range(8):
                ao = m * HALF + sub * 256
                P.tt(t1b[:, m * 256:(m + 1) * 256], t1b[:, m * 256:(m + 1) * 256],
                     t1c[:, m * 256:(m + 1) * 256], ALU.add)
                P.tt(t1b[:, m * 256:(m + 1) * 256], t1b[:, m * 256:(m + 1) * 256],
                     acc[:, ao:ao + 256], ALU.add)
            P.dma(fm_ap(t2_h, 0, 8, t0, 256), t1b[:].rearrange("p (f t) -> p f t", t=256))
    P.end_phase()
    return P



def build_D1(P=None):
    P = P if P is not None else Prog("D1")
    P.begin_phase()
    TO = 512
    NS = T // TO
    xn1_h = P.dram_in("xn1", (D, T), BF16)
    anw_h = P.dram_in("attn_norm_w", (D,))
    fnw_h = P.dram_in("ffn_norm_w", (D,))
    qkvw_h = P.dram_in("qkv_wP", (D, 1536))
    qkb_h = P.dram_in("qkb64", (64, 20))
    vbb_h = P.dram_in("vb128", (128, 2))
    pw_h = P.dram_in("proj_w", (D, D))
    pb_h = P.dram_in("proj_b", (D,))
    sink_h = P.dram_in("attn_sink", (16,))
    gw_h = P.dram_in("gate_w32", (D, 32))
    gb_h = P.dram_in("gate_b32", (32,))
    cs_h = P.dram_in("cosS", (64, T))
    sn_h = P.dram_in("sinS", (64, T))
    ident_h = P.dram_in("ident", (128, 128))
    am_h = P.dram_in("amasks", (2, 128, 256))
    xp_h = P.dram_out("xpS", (D, T))
    h2f_h = P.dram_out("h2fS", (D, T), BF16)
    glg_h = P.dram_out("glogS", (T, 32))

    ident = P.sbuf("ident", [128, 128], F32)
    P.dma(ident[:], ident_h[:])
    identb = P.sbuf("identb", [128, 128], BF16)
    P.dma(identb[:], ident_h[:], queue="gpsimd")
    am0 = P.sbuf("am0", [128, 256], F32)
    amr = P.sbuf("amr", [128, 256], F32)
    P.dma(am0[:], am_h[0]); P.dma(amr[:], am_h[1])
    anc = P.sbuf("anc", [128, 8], F32)
    P.dma_slow(anc[:], anw_h.rearrange("(f p) -> p f", p=128))
    fnc = P.sbuf("fnc", [128, 8], F32)
    P.dma_slow(fnc[:], fnw_h.rearrange("(f p) -> p f", p=128))
    qkvw = P.sbuf("qkvw", [128, 8 * 1536], BF16)
    P.dma(qkvw[:].rearrange("p (f m) -> p f m", m=1536),
          qkvw_h.rearrange("(f p) m -> p f m", p=128), queue="gpsimd")
    qkb = P.sbuf("qkb", [64, 20], F32)
    P.dma(qkb[:], qkb_h[:])
    vbb = P.sbuf("vbb", [128, 2], F32)
    P.dma(vbb[:], vbb_h[:])
    pw = P.sbuf("pw", [64, 16 * D], BF16)
    P.dma(pw[:].rearrange("p (f m) -> p f m", m=D),
          pw_h.rearrange("(f p) m -> p f m", p=64), queue="gpsimd")
    pbc = P.sbuf("pbc", [128, 8], F32)
    P.dma_slow(pbc[:], pb_h.rearrange("(f p) -> p f", p=128))
    gwc = P.sbuf("gwc", [128, 8 * 32], F32)
    P.dma(gwc[:].rearrange("p (f m) -> p f m", m=32),
          gw_h.rearrange("(f p) m -> p f m", p=128))
    gbc = P.sbuf("gbc", [32, 1], F32)
    P.dma(gbc[:], gb_h.unsqueeze(1))
    cosb = P.sbuf("cosb", [64, T], F32)
    sinb = P.sbuf("sinb", [64, T], F32)
    P.dma(cosb[:], cs_h[:]); P.dma(sinb[:], sn_h[:])
    ones128 = P.sbuf("ones128", [128, 1], F32)
    P.memset(ones128[:], 1.0)
    ones1 = P.sbuf("ones1", [1, 128], F32)
    P.memset(ones1[:], 1.0)
    epsr = P.sbuf("epsr", [1, 1], F32)
    P.memset(epsr[:], 1e-5)

    xn1b = P.sbuf("xn1b", [128, 8 * TO], BF16)
    xn1f = P.sbuf("xn1f", [128, TO], F32)
    hb = P.sbuf("hb", [128, 8 * TO], BF16)
    sq = P.sbuf("sq", [128, TO])
    rowb = P.sbuf("rowb", [1, TO], F32)
    rrep = P.sbuf("rrep", [128, TO], F32)
    qb = P.sbuf("qb", [64, 16 * TO], BF16)
    kb = P.sbuf("kb", [64, 4 * (128 + TO)], BF16)
    vtok = P.sbuf("vtok", [128, 4 * 5 * 64], BF16)
    rtmp1 = P.sbuf("rtmp1", [32, TO], F32)
    rtmp2 = P.sbuf("rtmp2", [32, TO], F32)
    lgs = P.sbuf("lgs", [128, 256], F32)
    pex = P.sbuf("pex", [128, 256], F32)
    mcol = P.sbuf("mcol", [128, 1], F32)
    nmcol = P.sbuf("nmcol", [128, 1], F32)
    scol = P.sbuf("scol", [128, 1], F32)
    dcol = P.sbuf("dcol", [128, 1], F32)
    attT = P.sbuf("attT", [128, 256], BF16)
    atb = P.sbuf("atb", [128, 256], BF16)
    yb = P.sbuf("yb", [64, 16 * TO], BF16)
    xpb = P.sbuf("xpb", [128, 8 * TO], F32)
    h2fb = P.sbuf("h2fb", [128, 8 * TO], BF16)
    glt = P.sbuf("glt", [32, TO], F32)
    glo = P.sbuf("glo", [128, 4 * 32], F32)
    sinkr = P.sbuf("sinkr", [128, 16], F32)

    pQ = P.psum("pQ", [64, TO])
    pV = P.psum("pV", [128, TO])
    pL = P.psum("pL", [128, 256])
    pTb = P.psum("pTb", [128, 128], BF16)
    pT32 = P.psum("pT32", [128, 128])
    pY = P.psum("pY", [64, 128])
    pS = P.psum("pS", [1, TO])
    pGl = P.psum("pGl", [32, TO])

    P.dma(sinkr[0:1, :], sink_h.unsqueeze(0))
    P.mm(pV[:, 0:16], ones1[:], sinkr[0:1, :])
    P.copy(sinkr[:], pV[:, 0:16])
    P.memset(kb[:], 0.0)
    P.memset(vtok[:], 0.0)

    def rope_pair(dst_even, dst_odd, src64, t0):
        P.tt(rtmp1[:], src64[0:32, :], cosb[0:32, t0:t0 + TO], ALU.mult)
        P.tt(rtmp2[:], src64[32:64, :], sinb[32:64, t0:t0 + TO], ALU.mult)
        P.tt(rtmp1[:], rtmp1[:], rtmp2[:], ALU.subtract)
        P.tt(rtmp2[:], src64[0:32, :], sinb[0:32, t0:t0 + TO], ALU.mult)
        P.copy(dst_even, rtmp1[:])
        P.tt(rtmp1[:], src64[32:64, :], cosb[32:64, t0:t0 + TO], ALU.mult)
        P.tt(rtmp1[:], rtmp1[:], rtmp2[:], ALU.add)
        P.copy(dst_odd, rtmp1[:])

    for s in range(NS):
        t0 = s * TO
        P.dma(xn1b[:].rearrange("p (f t) -> p f t", t=TO), fm_ap(xn1_h, 0, 8, t0, TO))
        for k in range(8):
            P.copy(sq[:], xn1b[:, k * TO:(k + 1) * TO])
            P.act(sq[:], sq[:], AF.Square)
            P.mm(pS[0:1, :], ones128[:, 0:1], sq[:], start=(k == 0), stop=(k == 7))
        P.ts(rowb[0:1, :], pS[0:1, :], 1.0 / D, ALU.mult)
        P.act(rowb[0:1, :], rowb[0:1, :], AF.Ln, bias=epsr[0:1, 0:1])
        P.act(rowb[0:1, :], rowb[0:1, :], AF.Exp, scale=-0.5)
        P.mm(pV[:], ones1[:], rowb[0:1, :], start=True, stop=True)
        P.copy(rrep[:], pV[:])
        for k in range(8):
            P.copy(sq[:], xn1b[:, k * TO:(k + 1) * TO])
            P.tt(sq[:], sq[:], rrep[:], ALU.mult)
            P.ts(hb[:, k * TO:(k + 1) * TO], sq[:], anc[:, k:k + 1], ALU.mult)
        for kv in range(4):
            P.copy(kb[:, kv * 640:kv * 640 + 128],
                   kb[:, kv * 640 + TO:kv * 640 + TO + 128])
            P.copy(vtok[:, kv * 320:kv * 320 + 64],
                   vtok[:, kv * 320 + 256:kv * 320 + 320])
        for hh in range(16):
            for k in range(8):
                P.mm(pQ[:], qkvw[:, k * 1536 + hh * 64: k * 1536 + (hh + 1) * 64],
                     hb[:, k * TO:(k + 1) * TO], start=(k == 0), stop=(k == 7))
            bcol = qkb[:, hh:hh + 1]
            P.ts(sq[0:64, :], pQ[:], bcol, ALU.add)
            rope_pair(qb[0:32, hh * TO:(hh + 1) * TO],
                      qb[32:64, hh * TO:(hh + 1) * TO], sq, t0)
        for kv in range(4):
            for k in range(8):
                P.mm(pQ[:], qkvw[:, k * 1536 + 1024 + kv * 64: k * 1536 + 1024 + (kv + 1) * 64],
                     hb[:, k * TO:(k + 1) * TO], start=(k == 0), stop=(k == 7))
            bcol = qkb[:, 16 + kv:17 + kv]
            P.ts(sq[0:64, :], pQ[:], bcol, ALU.add)
            ko = kv * 640 + 128
            rope_pair(kb[0:32, ko:ko + TO], kb[32:64, ko:ko + TO], sq, t0)
        for vt in range(2):
            for k in range(8):
                P.mm(pV[:], qkvw[:, k * 1536 + 1280 + vt * 128: k * 1536 + 1280 + (vt + 1) * 128],
                     hb[:, k * TO:(k + 1) * TO], start=(k == 0), stop=(k == 7))
            bcol = vbb[:, vt:vt + 1]
            P.ts(sq[:], pV[:], bcol, ALU.add)
            for tt_ in range(4):
                P.transpose(pT32[:], sq[:, tt_ * 128:(tt_ + 1) * 128], ident[:])
                P.copy(vtok[:, (2 * vt) * 320 + (tt_ + 1) * 64:
                       (2 * vt) * 320 + (tt_ + 2) * 64], pT32[:, 0:64])
                P.copy(vtok[:, (2 * vt + 1) * 320 + (tt_ + 1) * 64:
                       (2 * vt + 1) * 320 + (tt_ + 2) * 64], pT32[:, 64:128])
        for qt in range(4):
            for qh in range(16):
                kv = qh // 4
                P.mm(pL[:], qb[0:64, qh * TO + qt * 128: qh * TO + qt * 128 + 128],
                     kb[0:64, kv * 640 + qt * 128: kv * 640 + qt * 128 + 256])
                msk = am0 if (s == 0 and qt == 0) else amr
                P.stt(lgs[:], pL[:], 0.125, msk[:], ALU.mult, ALU.add)
                P.red(mcol[:], lgs[:], ALU.max)
                P.tt(mcol[:], mcol[:], sinkr[:, qh:qh + 1], ALU.max)
                P.ts(nmcol[:], mcol[:], -1.0, ALU.mult)
                P.act(pex[:], lgs[:], AF.Exp, bias=nmcol[:, 0:1])
                P.red(dcol[:], pex[:], ALU.add)
                P.tt(scol[:], sinkr[:, qh:qh + 1], mcol[:], ALU.subtract)
                P.act(scol[:], scol[:], AF.Exp)
                P.tt(dcol[:], dcol[:], scol[:], ALU.add)
                P.recip(scol[:], dcol[:])
                P.ts(atb[:], pex[:], scol[:, 0:1], ALU.mult)
                for piece in range(2):
                    P.transpose(pTb[:], atb[:, piece * 128:(piece + 1) * 128], identb[:])
                    P.copy(attT[:, piece * 128:(piece + 1) * 128], pTb[:])
                for piece in range(2):
                    P.mm(pY[:], vtok[:, kv * 320 + (qt + piece) * 64:
                                     kv * 320 + (qt + piece + 1) * 64],
                         attT[:, piece * 128:(piece + 1) * 128],
                         start=(piece == 0), stop=(piece == 1))
                P.copy(yb[0:64, qh * TO + qt * 128: qh * TO + qt * 128 + 128], pY[:])
        for m in range(8):
            for k in range(16):
                P.mm(pV[:], pw[0:64, k * D + m * 128: k * D + (m + 1) * 128],
                     yb[0:64, k * TO:(k + 1) * TO], start=(k == 0), stop=(k == 15))
            P.ts(sq[:], pV[:], pbc[:, m:m + 1], ALU.add)
            P.copy(xn1f[:], xn1b[:, m * TO:(m + 1) * TO])
            P.tt(xpb[:, m * TO:(m + 1) * TO], sq[:], xn1f[:], ALU.add)
        P.dma(fm_ap(xp_h, 0, 8, t0, TO), xpb[:].rearrange("p (f t) -> p f t", t=TO))
        for k in range(8):
            P.act(sq[:], xpb[:, k * TO:(k + 1) * TO], AF.Square)
            P.mm(pS[0:1, :], ones128[:, 0:1], sq[:], start=(k == 0), stop=(k == 7))
        P.ts(rowb[0:1, :], pS[0:1, :], 1.0 / D, ALU.mult)
        P.act(rowb[0:1, :], rowb[0:1, :], AF.Ln, bias=epsr[0:1, 0:1])
        P.act(rowb[0:1, :], rowb[0:1, :], AF.Exp, scale=-0.5)
        P.mm(pV[:], ones1[:], rowb[0:1, :], start=True, stop=True)
        P.copy(rrep[:], pV[:])
        for k in range(8):
            P.tt(sq[:], xpb[:, k * TO:(k + 1) * TO], rrep[:], ALU.mult)
            P.ts(h2fb[:, k * TO:(k + 1) * TO], sq[:], fnc[:, k:k + 1], ALU.mult)
        P.dma(fm_ap(h2f_h, 0, 8, t0, TO), h2fb[:].rearrange("p (f t) -> p f t", t=TO))
        for k in range(8):
            P.tt(sq[:], xpb[:, k * TO:(k + 1) * TO], rrep[:], ALU.mult)
            P.ts(sq[:], sq[:], fnc[:, k:k + 1], ALU.mult)
            P.mm(pGl[:], gwc[:, k * 32:(k + 1) * 32], sq[:],
                 start=(k == 0), stop=(k == 7))
        P.ts(glt[:], pGl[:], gbc[:, 0:1], ALU.add)
        for tt_ in range(4):
            P.transpose(pT32[:, 0:32], glt[:, tt_ * 128:(tt_ + 1) * 128],
                        ident[0:32, 0:32])
            P.copy(glo[:, tt_ * 32:(tt_ + 1) * 32], pT32[:, 0:32])
            P.dma(glg_h[t0 + tt_ * 128: t0 + (tt_ + 1) * 128],
                  glo[:, tt_ * 32:(tt_ + 1) * 32])
    P.end_phase()
    return P




def build_D2(P=None):
    P = P if P is not None else Prog("D2")
    P.begin_phase()
    TCH = 512
    HALF = T // 2
    h2_h = P.dram_in("h2fS", (D, T), BF16)
    glg_h = P.dram_in("glogS", (T, 32))
    xp_h = P.dram_in("xpS", (D, T))
    t2_h = P.dram_in("t2", (D, T))
    wu_h = P.dram_in("W_up", (32, D, 2048))
    wd_h = P.dram_in("W_down", (32, D, D))
    sel_h = P.dram_in("sel32", (32, 32 * 128))
    ident_h = P.dram_in("ident", (128, 128))
    out_h = P.dram_out("out", (T, D))

    ident = P.sbuf("ident", [128, 128], F32)
    P.dma(ident[:], ident_h[:])
    selb = P.sbuf("selb", [32, 32 * 128], BF16)
    P.dma(selb[:], sel_h[:], queue="gpsimd")

    lgt = P.sbuf("lgt", [128, 32], F32)
    msk = P.sbuf("msk", [128, 32], F32)
    eq = P.sbuf("eq", [128, 32], F32)
    cum = P.sbuf("cum", [128, 32], F32)
    zer = P.sbuf("zer", [128, 32], F32)
    first = P.sbuf("first", [128, 32], F32)
    mm_ = P.sbuf("mm_", [128, 1], F32)
    m0 = P.sbuf("m0", [128, 1], F32)
    vks = P.sbuf("vks", [128, 4], F32)
    fks = P.sbuf("fks", [128, 4 * 32], F32)
    den = P.sbuf("den", [128, 1], F32)
    comb = P.sbuf("comb", [128, 32], F32)
    combT = P.sbuf("combT", [32, T], BF16)
    pR = P.psum("pR", [128, 128])
    P.memset(zer[:], 0.0)
    scr = (eq, cum, zer, first, mm_)
    for tt_ in range(T // 128):
        P.dma(lgt[:], glg_h[tt_ * 128:(tt_ + 1) * 128])
        P.copy(msk[:], lgt[:])
        for kk in range(4):
            f, m = emit_first_argmax(P, msk[:], scr)
            P.copy(fks[:, kk * 32:(kk + 1) * 32], f[:])
            P.copy(vks[:, kk:kk + 1], m[:])
            if kk == 0:
                P.copy(m0[:], m[:])
            P.stt(msk[:], f[:], -1e30, msk[:], ALU.mult, ALU.add)
        P.ts(vks[:], vks[:], m0[:, 0:1], ALU.subtract)
        P.act(vks[:], vks[:], AF.Exp)
        P.red(den[:], vks[:], ALU.add)
        P.recip(den[:], den[:])
        P.memset(comb[:], 0.0)
        for kk in range(4):
            P.ts(mm_[:], vks[:, kk:kk + 1], den[:, 0:1], ALU.mult)
            P.stt(comb[:], fks[:, kk * 32:(kk + 1) * 32], mm_[:, 0:1], comb[:],
                  ALU.mult, ALU.add)
        P.transpose(pR[0:32, :], comb[:], ident[:])
        P.copy(combT[:, tt_ * 128:(tt_ + 1) * 128], pR[0:32, 0:128])

    wz = P.sbuf("wz", [128, 24576], BF16)
    UA, DA = 0, 16384
    acc = P.sbuf("acc", [128, 8 * HALF], F32)
    h2b = [P.sbuf("h2b", [128, 8 * TCH], BF16) for _ in range(2)]
    hub = [P.sbuf("hub", [128, 8 * TCH], BF16) for _ in range(2)]
    silb = [P.sbuf("silb", [128, TCH], F32) for _ in range(2)]
    crep = P.sbuf("crep", [128, TCH], F32)
    pU = [P.psum("pU", [128, TCH]) for _ in range(2)]
    pG = [P.psum("pG", [128, TCH]) for _ in range(2)]
    pD = [P.psum("pD", [128, TCH]) for _ in range(2)]
    pC = P.psum("pC", [128, TCH])
    pT2 = pC
    wu_r = wu_h.rearrange("e (f p) m -> e p f m", p=128)
    wd_r = wd_h.rearrange("e (f p) m -> e p f m", p=128)
    fin = P.sbuf("fin", [128, 8 * TCH], F32)
    fin2 = P.sbuf("fin2", [128, 8 * TCH], F32)

    tWfree = 0
    tH = [0, 0]
    tHfree = [0, 0]
    tPU = [0, 0]
    tPG = [0, 0]
    tPD = [0, 0]
    tCfree = 0
    for h_ in range(2):
        for e_ in range(32):
            twf = tWfree if tWfree != 0 else 0
            tWa = P.dma(wz[:, UA:UA + 16384].rearrange("p (f m) -> p f m", m=2048),
                        wu_r[e_], queue="gpsimd", at=twf)
            tWb = P.dma(wz[:, DA:DA + 8192].rearrange("p (f m) -> p f m", m=1024),
                        wd_r[e_], queue="gpsimd", at=twf)
            tW = [tWa, tWb]
            lastg = 0
            for sub in range(4):
                t0 = h_ * HALF + sub * TCH
                hb = sub % 2
                tH[hb] = P.dma(h2b[hb][:].rearrange("p (f t) -> p f t", t=TCH),
                               fm_ap(h2_h, 0, 8, t0, TCH),
                               at=[tHfree[hb]] if tHfree[hb] != 0 else 0)
                tHuAll = []
                for mt in range(8):
                    pp = mt % 2
                    P.mm(pU[pp][:], wz[:, UA + mt * 128: UA + (mt + 1) * 128],
                         h2b[hb][:, 0:TCH], start=True, stop=False,
                         at=tW + [tH[hb], tPU[pp]])
                    for k in range(1, 8):
                        P.mm(pU[pp][:], wz[:, UA + k * 2048 + mt * 128:
                                           UA + k * 2048 + (mt + 1) * 128],
                             h2b[hb][:, k * TCH:(k + 1) * TCH],
                             start=False, stop=(k == 7), at=0)
                    tMMu = P.last
                    P.mm(pG[pp][:], wz[:, UA + 1024 + mt * 128: UA + 1024 + (mt + 1) * 128],
                         h2b[hb][:, 0:TCH], start=True, stop=False,
                         at=[tPG[pp]] if tPG[pp] != 0 else 0)
                    for k in range(1, 8):
                        P.mm(pG[pp][:], wz[:, UA + k * 2048 + 1024 + mt * 128:
                                           UA + k * 2048 + 1024 + (mt + 1) * 128],
                             h2b[hb][:, k * TCH:(k + 1) * TCH],
                             start=False, stop=(k == 7), at=0)
                    tMMg = P.last
                    lastg = tMMg
                    tSil = P.act(silb[pp][:], pG[pp][:], AF.Silu, at=[tMMg])
                    tHu = P.tt(hub[hb][:, mt * TCH:(mt + 1) * TCH], silb[pp][:],
                               pU[pp][:], ALU.mult, at=[tSil, tMMu])
                    tPU[pp] = tHu
                    tPG[pp] = tSil
                    tHuAll.append(tHu)
                tHfree[hb] = lastg
                tC = P.mm(pC[:], selb[:, e_ * 128:(e_ + 1) * 128],
                          combT[:, t0:t0 + TCH],
                          at=[tCfree] if tCfree != 0 else 0)
                tCs = P.copy(crep[:], pC[:], at=[tC])
                tSc = 0
                for mt in range(8):
                    tSc = P.tt(hub[hb][:, mt * TCH:(mt + 1) * TCH],
                               hub[hb][:, mt * TCH:(mt + 1) * TCH], crep[:],
                               ALU.mult, eng="gpsimd", at=[tCs] + tHuAll)
                tCfree = tSc
                for m in range(8):
                    pp = m % 2
                    P.mm(pD[pp][:], wz[:, DA + m * 128: DA + (m + 1) * 128],
                         hub[hb][:, 0:TCH], start=True, stop=False,
                         at=[tSc, tPD[pp]] if tPD[pp] != 0 else [tSc])
                    for k in range(1, 8):
                        P.mm(pD[pp][:], wz[:, DA + k * 1024 + m * 128:
                                           DA + k * 1024 + (m + 1) * 128],
                             hub[hb][:, k * TCH:(k + 1) * TCH],
                             start=False, stop=(k == 7), at=0)
                    tDn = P.last
                    ao = m * HALF + sub * TCH
                    if e_ == 0:
                        tAdd = P.copy(acc[:, ao:ao + TCH], pD[pp][:], at=[tDn])
                    else:
                        tAdd = P.tt(acc[:, ao:ao + TCH], acc[:, ao:ao + TCH],
                                    pD[pp][:], ALU.add, at=[tDn])
                    tPD[pp] = tAdd
                tWfree = [P.last]
        for sub in range(4):
            t0 = h_ * HALF + sub * TCH
            P.dma(fin[:].rearrange("p (f t) -> p f t", t=TCH), fm_ap(t2_h, 0, 8, t0, TCH))
            P.dma(fin2[:].rearrange("p (f t) -> p f t", t=TCH), fm_ap(xp_h, 0, 8, t0, TCH))
            for m in range(8):
                ao = m * HALF + sub * TCH
                P.tt(fin[:, m * TCH:(m + 1) * TCH], fin[:, m * TCH:(m + 1) * TCH],
                     fin2[:, m * TCH:(m + 1) * TCH], ALU.add)
                P.tt(fin[:, m * TCH:(m + 1) * TCH], fin[:, m * TCH:(m + 1) * TCH],
                     acc[:, ao:ao + TCH], ALU.add)
            for tt_ in range(4):
                for m in range(8):
                    P.transpose(pT2[:, 0:128],
                                fin[:, m * TCH + tt_ * 128: m * TCH + tt_ * 128 + 128],
                                ident[:])
                    P.copy(fin2[:, m * 128:(m + 1) * 128], pT2[:, 0:128])
                P.dma(out_h[t0 + tt_ * 128: t0 + (tt_ + 1) * 128],
                      fin2[:, 0:1024])
        tCfree = P.last
    P.end_phase()
    return P



_cache = {}


def _bf(x):
    return np.ascontiguousarray(x)


def run_launch(key, builder, in_maps):
    if key not in _cache:
        P = builder()
        _cache[key] = P.build()
    nc = _cache[key]
    res = run_bass_kernel_spmd(nc, in_maps, core_ids=list(range(len(in_maps))))
    return res.results


IDENT = np.eye(128, dtype=np.float32)
SEL32 = np.zeros((32, 32 * 128), dtype=np.float32)
for _s in range(32):
    SEL32[_s, _s * 128:(_s + 1) * 128] = 1.0
_MLO = np.tril(np.ones((128, 128), np.float32), -1)
MASKS = np.stack([_MLO, _MLO.T, np.triu(np.ones((128, 128), np.float32))])

def _amasks():
    NEG = np.float32(-3e38)
    qi = np.arange(128)[:, None]
    kj = np.arange(256)[None, :]
    valid = (kj >= qi) & (kj <= qi + 128)
    amr = np.where(valid, np.float32(0), NEG).astype(np.float32)
    am0 = amr.copy()
    am0[:, :128] = NEG
    return np.stack([am0, amr])

AMASKS = _amasks()

def _rope_tables():
    DH = 64
    pos = np.arange(T, dtype=np.float64) / 32.0
    inv = 1.0 / (150000.0 ** (np.arange(0, DH, 2, dtype=np.float64) / DH))
    fr = pos[None, :] * inv[:, None]
    c = np.cos(fr).astype(np.float32)
    s_ = np.sin(fr).astype(np.float32)
    return np.vstack([c, c]), np.vstack([s_, s_])

COSS, SINS = _rope_tables()

def _perm_qkv(qkv_w, qkv_b):
    perm = np.arange(1536)
    def headperm(base, nh):
        for h in range(nh):
            o = base + h * 64
            perm[o:o + 64] = np.concatenate([np.arange(o, o + 64, 2),
                                             np.arange(o + 1, o + 64, 2)])
    headperm(0, 16)
    headperm(1024, 4)
    wp = np.ascontiguousarray(qkv_w[:, perm])
    bpm = qkv_b[perm]
    qkb = np.ascontiguousarray(bpm[:1280].reshape(20, 64).T)
    vbb = np.ascontiguousarray(bpm[1280:].reshape(2, 128).T)
    return wp, qkb, vbb


_BUILDERS = [build_A1, build_A2, build_B, build_C1a, build_C1b, build_C2,
             build_D1, build_D2]


def build_fused():
    P = Prog("fused", fused=True)
    for b in _BUILDERS:
        b(P)
    return P


import os as _os
FUSED = _os.environ.get("MK_UNFUSED", "") == ""


def _fused_maps(ins, x, b, extra):
    m = dict(x=x[b], ident=IDENT, sel32=SEL32, masks=MASKS, amasks=AMASKS,
             cosS=COSS, sinS=SINS)
    m.update(extra)
    return m


def kernel(**ins):
    x = ins["L0_attn_x"].astype(np.float32)
    B = x.shape[0]
    bcw = np.concatenate([ins["L0_attn_B_w"], ins["L0_attn_C_w"]], axis=1)
    rwT = np.ascontiguousarray(ins["L0_ffn_router_w"].T)
    convw = np.ascontiguousarray(ins["L0_attn_conv_w"][:, 0, :])
    qkvwP, qkbP, vbbP = _perm_qkv(ins["L1_ffn_qkv_w"], ins["L1_ffn_qkv_b"])

    if FUSED:
        common = dict(
            in_proj_w=ins["L0_attn_in_proj_w"], conv_w=convw,
            dt_w=ins["L0_attn_dt_w"], dt_b=ins["L0_attn_dt_b"], bc_w=bcw,
            router_wT=rwT, norm_w=ins["L0_attn_norm_w"],
            A_log=ins["L0_attn_A_log"], D_skip=ins["L0_attn_D_skip"],
            out_proj_w=ins["L0_attn_out_proj_w"], bias=ins["L0_ffn_bias"],
            gate_w=ins["L0_ffn_gate_w"], up_w=ins["L0_ffn_up_w"],
            down_w=ins["L0_ffn_down_w"], sh_gate_w=ins["L0_ffn_sh_gate_w"],
            sh_up_w=ins["L0_ffn_sh_up_w"], sh_down_w=ins["L0_ffn_sh_down_w"],
            Wq=ins["L1_attn_Wq"], Wk=ins["L1_attn_Wk"], Wv=ins["L1_attn_Wv"],
            Wg=ins["L1_attn_Wg"], Wb=ins["L1_attn_Wb"], Wa=ins["L1_attn_Wa"],
            A_log1=ins["L1_attn_A_log"], dt_bias=ins["L1_attn_dt_bias"], conv_q_w=ins["L1_attn_conv_q_w"],
            conv_k_w=ins["L1_attn_conv_k_w"], conv_v_w=ins["L1_attn_conv_v_w"],
            norm1_w=ins["L1_attn_norm1_w"], onorm_w=ins["L1_attn_onorm_w"],
            Wo=ins["L1_attn_Wo"], norm2_w=ins["L1_attn_norm2_w"],
            Wgate=ins["L1_attn_Wgate"], Wup=ins["L1_attn_Wup"],
            Wdown=ins["L1_attn_Wdown"], attn_norm_w=ins["L1_ffn_attn_norm_w"],
            ffn_norm_w=ins["L1_ffn_ffn_norm_w"], qkv_wP=qkvwP, qkb64=qkbP,
            vb128=vbbP, proj_w=ins["L1_ffn_proj_w"], proj_b=ins["L1_ffn_proj_b"],
            attn_sink=ins["L1_ffn_attn_sink"], gate_wD=None,
            W_up=ins["L1_ffn_W_up"], W_down=ins["L1_ffn_W_down"])
        del common["gate_wD"]
        maps = []
        for b in range(B):
            m = _fused_maps(ins, x, b, common)
            maps.append(m)
        for m in maps:
            m["gate_w32"] = ins["L1_ffn_gate_w"]
            m["gate_b32"] = ins["L1_ffn_gate_b"]
        res = run_launch("FUSED", build_fused, maps)
        out = np.stack([np.asarray(r["out"]) for r in res]).astype(np.float32)
        return out

    mapsA1 = [dict(x=x[b], in_proj_w=ins["L0_attn_in_proj_w"], conv_w=convw,
                   dt_w=ins["L0_attn_dt_w"], dt_b=ins["L0_attn_dt_b"],
                   bc_w=bcw, router_wT=rwT, norm_w=ins["L0_attn_norm_w"],
                   ident=IDENT) for b in range(B)]
    rA1 = run_launch("A1", build_A1, mapsA1)
    mapsA2 = [dict(x=x[b], dtS=rA1[b]["dtS"], uS=rA1[b]["uS"], xcS=rA1[b]["xcS"],
                   bcS=rA1[b]["bcS"], sel32=SEL32, A_log=ins["L0_attn_A_log"],
                   D_skip=ins["L0_attn_D_skip"], out_proj_w=ins["L0_attn_out_proj_w"],
                   ident=IDENT) for b in range(B)]
    rA2 = run_launch("A2", build_A2, mapsA2)
    mapsB = [dict(xn=rA1[b]["xn"], t1=rA2[b]["t1"], logits0=rA1[b]["logits0"],
                  bias=ins["L0_ffn_bias"], gate_w=ins["L0_ffn_gate_w"],
                  up_w=ins["L0_ffn_up_w"], down_w=ins["L0_ffn_down_w"],
                  sh_gate_w=ins["L0_ffn_sh_gate_w"], sh_up_w=ins["L0_ffn_sh_up_w"],
                  sh_down_w=ins["L0_ffn_sh_down_w"], sel32=SEL32, ident=IDENT)
             for b in range(B)]
    rB = run_launch("B", build_B, mapsB)
    mapsC1a = [dict(x1=rB[b]["x1"], Wq=ins["L1_attn_Wq"], Wk=ins["L1_attn_Wk"],
                    Wv=ins["L1_attn_Wv"], Wg=ins["L1_attn_Wg"], Wb=ins["L1_attn_Wb"],
                    Wa=ins["L1_attn_Wa"], A_log1=ins["L1_attn_A_log"],
                    dt_bias=ins["L1_attn_dt_bias"], conv_q_w=ins["L1_attn_conv_q_w"],
                    conv_k_w=ins["L1_attn_conv_k_w"], conv_v_w=ins["L1_attn_conv_v_w"],
                    norm1_w=ins["L1_attn_norm1_w"]) for b in range(B)]
    rC1a = run_launch("C1a", build_C1a, mapsC1a)
    mapsC1b = [dict(qS=rC1a[b]["qS"], kS=rC1a[b]["kS"], vS=rC1a[b]["vS"],
                    gS=rC1a[b]["gS"], betaS=rC1a[b]["betaS"], cumS=rC1a[b]["cumS"],
                    xn1=rC1a[b]["xn1"], onorm_w=ins["L1_attn_onorm_w"],
                    Wo=ins["L1_attn_Wo"], sel32=SEL32, ident=IDENT, masks=MASKS)
               for b in range(B)]
    rC1b = run_launch("C1b", build_C1b, mapsC1b)
    mapsC2 = [dict(xmid=rC1b[b]["xmid"], x1=rB[b]["x1"], norm2_w=ins["L1_attn_norm2_w"],
                   Wgate=ins["L1_attn_Wgate"], Wup=ins["L1_attn_Wup"],
                   Wdown=ins["L1_attn_Wdown"]) for b in range(B)]
    rC2 = run_launch("C2", build_C2, mapsC2)
    mapsD1 = [dict(xn1=rC1a[b]["xn1"], attn_norm_w=ins["L1_ffn_attn_norm_w"],
                   ffn_norm_w=ins["L1_ffn_ffn_norm_w"], qkv_wP=qkvwP, qkb64=qkbP, vb128=vbbP,
                   proj_w=ins["L1_ffn_proj_w"], proj_b=ins["L1_ffn_proj_b"],
                   attn_sink=ins["L1_ffn_attn_sink"], gate_w32=ins["L1_ffn_gate_w"],
                   gate_b32=ins["L1_ffn_gate_b"], cosS=COSS, sinS=SINS, ident=IDENT,
                   amasks=AMASKS) for b in range(B)]
    rD1 = run_launch("D1", build_D1, mapsD1)
    mapsD2 = [dict(h2fS=rD1[b]["h2fS"], glogS=rD1[b]["glogS"], xpS=rD1[b]["xpS"],
                   t2=rC2[b]["t2"], W_up=ins["L1_ffn_W_up"], W_down=ins["L1_ffn_W_down"],
                   sel32=SEL32, ident=IDENT) for b in range(B)]
    rD2 = run_launch("D2", build_D2, mapsD2)
    out = np.stack([np.asarray(rD2[b]["out"]) for b in range(B)]).astype(np.float32)
    return out
